# version 04
# speedup vs baseline: 1.2921x; 1.0191x over previous
import numpy as np
import ml_dtypes
from contextlib import ExitStack

import concourse.bass as bass
import concourse.mybir as mybir
from concourse.bass_utils import run_bass_kernel_spmd

F32 = mybir.dt.float32
BF16 = mybir.dt.bfloat16
AF = mybir.ActivationFunctionType
ALU = mybir.AluOpType
AX = mybir.AxisListType

P = 128


class Prog:
    ENGINES = ("sync", "tensor", "vector", "scalar", "gpsimd")

    def __init__(self, nc):
        self.nc = nc
        self.ctx = ExitStack()
        self.q = {e: [] for e in self.ENGINES}
        self._sems = {}
        self.cnt = {}

    def sem(self, name):
        if name not in self._sems:
            self._sems[name] = self.ctx.enter_context(self.nc.semaphore(name))
            self.cnt[name] = 0
        return self._sems[name]

    def sbuf(self, name, shape, dtype):
        ctx = getattr(self, "scope_ctx", None) or self.ctx
        return ctx.enter_context(self.nc.sbuf_tensor(name, shape, dtype))

    def psum(self, name, shape, dtype=F32):
        ctx = getattr(self, "scope_ctx", None) or self.ctx
        return ctx.enter_context(self.nc.psum_tensor(name, shape, dtype))

    def push_scope(self):
        self.scope_ctx = ExitStack()

    def pop_scope(self):
        if getattr(self, "scope_ctx", None) is not None:
            self.scope_ctx.close()
            self.scope_ctx = None

    def barrier(self):
        snap = [(k, v) for k, v in self.cnt.items() if v > 0]
        for eng in self.ENGINES:
            for k, v in snap:
                self.wait(eng, k, v)

    def on(self, eng, fn, inc=None):
        tgt = None
        if inc is not None:
            sname, n = inc
            self.sem(sname)
            self.cnt[sname] += n
            tgt = self.cnt[sname]
            sh = self._sems[sname]

            def run(e, fn=fn, sh=sh, n=n):
                fn(e).then_inc(sh, n)
        else:
            def run(e, fn=fn):
                fn(e)
        self.q[eng].append(run)
        return tgt

    def wait(self, eng, sname, val=None):
        self.sem(sname)
        if val is None:
            val = self.cnt[sname]
        if val <= 0:
            return
        sh = self._sems[sname]
        self.q[eng].append(lambda e, sh=sh, v=val: e.wait_ge(sh, v))

    def dma(self, eng, out, in_, sname=None, **kw):
        if sname is None:
            sname = "dq_" + eng
        return self.on(eng, lambda e, o=out, i=in_, kw=kw: e.dma_start(out=o, in_=i, **kw),
                       inc=(sname, 16))

    def finalize(self):
        blk = self.ctx.enter_context(self.nc.Block())
        for e in self.ENGINES:
            lst = self.q[e]
            if not lst:
                continue
            getattr(blk, e)(lambda eng, lst=lst: [f(eng) for f in lst])
        self.ctx.close()


def make_nc():
    return bass.Bass("TRN2", target_bir_lowering=False, debug=False)



def emit_bcast_row(p, pb, ones1f, src_row, dst, width):
    ng = (width + 511) // 512
    for g in range(ng):
        n = min(512, width - g * 512)
        t = p.on("tensor", lambda e, g=g, n=n: e.matmul(
            pb[:, 0:n], ones1f[:], src_row[:, g * 512:g * 512 + n], start=True, stop=True),
            inc=("bcmm", 1))
        p.wait("vector", "bcmm", t)
        ev = p.on("vector", lambda e, dst=dst, g=g, n=n: e.tensor_copy(
            dst[:, g * 512:g * 512 + n], pb[:, 0:n]), inc=("bcev", 1))
        p.wait("tensor", "bcev", ev)


def emit_ln_stats_math(p, s_sum, s_sq, s_m, s_var, s_rstd, s_nmr, D, eps_col):
    p.wait("vector", "pv")
    p.on("vector", lambda e: e.tensor_single_scalar(s_m, s_sum, 1.0 / D, ALU.mult),
         inc=("stv", 1))
    p.on("vector", lambda e: e.tensor_single_scalar(s_var, s_sq, 1.0 / D, ALU.mult),
         inc=("stv", 1))
    p.wait("vector", "stv")
    p.on("vector", lambda e: e.tensor_mul(s_nmr, s_m, s_m), inc=("stv", 1))
    p.wait("vector", "stv")
    v = p.on("vector", lambda e: e.tensor_sub(s_sq, s_var, s_nmr), inc=("stv", 1))
    p.wait("scalar", "stv", v)
    s1 = p.on("scalar", lambda e: e.activation(s_var, s_sq, AF.Ln, bias=eps_col),
              inc=("sts", 1))
    p.wait("scalar", "sts", s1)
    s2 = p.on("scalar", lambda e: e.activation(s_rstd, s_var, AF.Exp, scale=-0.5),
              inc=("sts", 1))
    p.wait("vector", "sts", s2)
    p.wait("vector", "stv")
    v = p.on("vector", lambda e: e.scalar_tensor_tensor(s_nmr, s_m, -1.0, s_rstd,
                                                        ALU.mult, ALU.mult),
             inc=("stv", 1))
    return v, s2


def emit_rms_stats_math(p, s_sq, s_rstd, D, eps_col):
    p.wait("vector", "ps")
    p.wait("vector", "stv")
    v = p.on("vector", lambda e: e.tensor_single_scalar(s_rstd, s_sq, 1.0 / D, ALU.mult),
             inc=("stv", 1))
    p.wait("scalar", "stv", v)
    s1 = p.on("scalar", lambda e: e.activation(s_sq, s_rstd, AF.Ln, bias=eps_col),
              inc=("sts", 1))
    p.wait("scalar", "sts", s1)
    s2 = p.on("scalar", lambda e: e.activation(s_rstd, s_sq, AF.Exp, scale=-0.5),
              inc=("sts", 1))
    return s2


def emit_transpose_tiles(p, src_tile_fns, dst_slab, pt2, ident, chunks, W, coloff=0):
    ntt = len(src_tile_fns)
    for ci, c in enumerate(chunks):
        for tt in range(ntt):
            j = p.cnt.get("trT", 0) % 2
            if p.cnt.get("trT", 0) >= 2:
                p.wait("tensor", "trV", p.cnt["trV"] - 1)
            t = p.on("tensor", lambda e, c=c, tt=tt, j=j:
                     e.transpose(pt2[j][:, 0:128], src_tile_fns[tt](c), ident[:]),
                     inc=("trT", 1))
            p.wait("vector", "trT", t)
            p.on("vector", lambda e, ci=ci, tt=tt, j=j:
                 e.tensor_copy(dst_slab[:, ci * W + coloff + tt * 128:
                                        ci * W + coloff + (tt + 1) * 128],
                               pt2[j][:, 0:128]),
                 inc=("trV", 1))
    p.wait("tensor", "trV")



def build_l1ffn(p, T, x_in, x_out, win, nseq=2, dbg=None):
    nc = p.nc
    D, FF = 2048, 5632
    NT = T // 128
    assert NT % 4 == 0
    NTG = NT // 4
    NC, NFC = 16, 44
    EPS_LN, EPS_RMS = 1e-5, 1e-6

    wv_bf = nc.dram_tensor("s_wv_bf", [D, D], BF16, kind="Internal").ap()
    wo_bf = nc.dram_tensor("s_wo_bf", [D, D], BF16, kind="Internal").ap()
    wg_bf = nc.dram_tensor("s_wg_bf", [NFC, P, NC * P], BF16, kind="Internal").ap()
    wu_bf = nc.dram_tensor("s_wu_bf", [NFC, P, NC * P], BF16, kind="Internal").ap()
    wd_bf = nc.dram_tensor("s_wd_bf", [FF, D], BF16, kind="Internal").ap()

    cst = p.sbuf("cst", [P, 2], F32)
    ident = p.sbuf("ident", [P, P], BF16)
    n1_row = p.sbuf("n1_row", [P, D], BF16)
    n2_row = p.sbuf("n2_row", [P, D], BF16)
    bv_row = p.sbuf("bv_row", [P, D], BF16)
    wr_row = p.sbuf("wr_row", [P, D], F32)
    br_col = p.sbuf("br_col", [P, 1], F32)
    wr1 = p.sbuf("wr1", [P, 1], F32)

    xb = [p.sbuf(f"xb{i}", [P, D], F32) for i in range(2)]
    xob = [p.sbuf(f"xob{i}", [P, D], F32) for i in range(2)]
    yb = [p.sbuf(f"yb{i}", [P, D], BF16) for i in range(4)]
    hb = [p.sbuf(f"hb{i}", [P, D], BF16) for i in range(4)]
    db = [p.sbuf(f"db{i}", [P, D], BF16) for i in range(4)]
    h1f = p.sbuf("h1f", [P, NC * 512], BF16)
    h2f = p.sbuf("h2f", [P, NC * 512], BF16)
    ufm = p.sbuf("ufm", [P, NFC * 512], BF16)
    junkb = p.sbuf("junkb", [P, D], BF16)
    junkv = p.sbuf("junkv", [P, D], BF16)
    wring = [p.sbuf(f"wring{i}", [P, 512], BF16) for i in range(4)]
    gring = [p.sbuf(f"gring{i}", [P, NC * P], BF16) for i in range(2)]
    sgb = p.sbuf("sgb", [P, 512], BF16)

    s_sum = p.sbuf("s_sum", [P, NT], F32)
    s_sq = p.sbuf("s_sq", [P, NT], F32)
    s_rsc = p.sbuf("s_rsc", [P, NT], F32)
    s_m = p.sbuf("s_m", [P, NT], F32)
    s_var = p.sbuf("s_var", [P, NT], F32)
    s_rstd = p.sbuf("s_rstd", [P, NT], F32)
    s_nmr = p.sbuf("s_nmr", [P, NT], F32)
    s_mask = p.sbuf("s_mask", [P, NT], F32)
    g_sq = p.sbuf("g_sq", [P, 4], F32)
    g_r1 = p.sbuf("g_r1", [P, 4], F32)
    g_sq2 = p.sbuf("g_sq2", [P, 4], F32)
    g_r2 = p.sbuf("g_r2", [P, 4], F32)
    o_sum = p.sbuf("o_sum", [P, 4], F32)
    o_sq = p.sbuf("o_sq", [P, 4], F32)
    o_m = p.sbuf("o_m", [P, 4], F32)
    o_var = p.sbuf("o_var", [P, 4], F32)
    o_rstd = p.sbuf("o_rstd", [P, 4], F32)
    o_nmr = p.sbuf("o_nmr", [P, 4], F32)

    pacc = [p.psum(f"pacc{i}", [P, 512], F32) for i in range(4)]
    pg = p.psum("pg", [P, 512], F32)
    pu = p.psum("pu", [P, 512], F32)
    pb = pg
    pt2 = [p.psum(f"pt{i}", [P, 512], BF16) for i in range(2)]

    p.on("vector", lambda e: e.memset(cst[:, 0:1], EPS_LN), inc=("vprep", 1))
    p.on("vector", lambda e: e.memset(cst[:, 1:2], EPS_RMS), inc=("vprep", 1))
    vprep_all = p.cnt["vprep"]
    p.wait("scalar", "vprep", vprep_all)
    p.dma("sync", ident[:], win["ident"], sname="cload")
    for (dst, src) in [(wv_bf, win["wv"]), (wo_bf, win["wo"]), (wg_bf, win["w_gate"]),
                       (wu_bf, win["w_up"]), (wd_bf, win["w_down"])]:
        p.dma("gpsimd", dst, src, sname="wcast")
    wcast_done = p.cnt["wcast"]

    p.dma("gpsimd", n1_row[:], win["norm1_w"].partition_broadcast(P), sname="wcast")
    p.dma("gpsimd", n2_row[:], win["norm2_w"].partition_broadcast(P), sname="wcast")
    p.dma("gpsimd", bv_row[:], win["bv"].partition_broadcast(P), sname="wcast")
    p.dma("sync", wr_row[:], win["w_router"].partition_broadcast(P), sname="cload")
    p.dma("sync", br_col[:], win["b_router"].partition_broadcast(P), sname="cload")
    wcast_done = p.cnt["wcast"]
    p.wait("vector", "cload")
    p.wait("vector", "wcast", wcast_done)
    p.wait("tensor", "cload")
    wrv = p.on("vector", lambda e: e.reduce_sum(wr1[:], wr_row[:], axis=AX.X), inc=("stv", 1))

    def dbg_tap(slot, src_ap, waits, barrier=True):
        if dbg is None:
            return
        for sn, v in waits:
            p.wait("gpsimd", sn, v)
        dst = dbg[slot]
        if tuple(dst.shape) != tuple(src_ap.shape):
            dst = dst[:, 0:src_ap.shape[1]]
        p.dma("gpsimd", dst, src_ap, sname=f"dbgst{slot}")
        if barrier:
            for eng in ("vector", "scalar", "tensor", "sync"):
                p.wait(eng, f"dbgst{slot}")

    def stream_mm(wap_fn, ngroups, nk, lhsT_fn, evac_fn, etag):
        for g in range(ngroups):
            for k in range(nk):
                if p.cnt.get("wru", 0) >= 4:
                    p.wait("gpsimd", "wru", p.cnt["wru"] - 3)
                slot = p.cnt.get("wru", 0) % 4
                ld = p.dma("gpsimd", wring[slot][:], wap_fn(g, k), sname=f"wrl{slot}")
                p.wait("tensor", f"wrl{slot}", ld)
                for i in range(4):
                    if k == 0 and p.cnt.get(etag, 0) >= 4 - i:
                        p.wait("tensor", etag, p.cnt[etag] - (4 - i) + 1)
                    p.on("tensor", lambda e, g=g, k=k, i=i, slot=slot, nk=nk: e.matmul(
                        pacc[i][:], lhsT_fn(g, k, i), wring[slot][:],
                        start=(k == 0), stop=(k == nk - 1)),
                        inc=("wru", 1) if i == 3 else None)
            p.wait("vector", "wru")
            evac_fn(g, etag)
        p.wait("tensor", etag)

    b3_vv = {}
    for seq in range(nseq):
        xs = x_in[seq]
        xo = x_out[seq]

        pv_t = {}
        ps_t = {}
        for tt in range(NT):
            if tt >= 2:
                p.wait("sync", "pv", pv_t[tt - 2])
                p.wait("sync", "ps", ps_t[tt - 2])
            ld = p.dma("sync", xb[tt % 2][:], xs[tt * P:(tt + 1) * P, :],
                       sname=f"xld{tt % 2}")
            p.wait("vector", f"xld{tt % 2}", ld)
            p.on("vector", lambda e, tt=tt: e.reduce_sum(
                s_sum[:, tt:tt + 1], xb[tt % 2][:], axis=AX.X), inc=("pv", 1))
            p.wait("vector", "pv")
            pv_t[tt] = p.on("vector", lambda e, tt=tt: e.scalar_tensor_tensor(
                junkv[:], xb[tt % 2][:], 1.0, wr_row[:], ALU.mult, ALU.mult,
                accum_out=s_rsc[:, tt:tt + 1]), inc=("pv", 1))
            p.wait("scalar", f"xld{tt % 2}", ld)
            p.wait("scalar", "ps")
            ps_t[tt] = p.on("scalar", lambda e, tt=tt: e.activation(
                junkb[:], xb[tt % 2][:], AF.Square, accum_out=s_sq[:, tt:tt + 1]),
                inc=("ps", 1))
        p.wait("vector", "ps")
        vln, sln = emit_ln_stats_math(p, s_sum[:], s_sq[:], s_m[:], s_var[:],
                                      s_rstd[:], s_nmr[:], D, cst[:, 0:1])
        p.wait("vector", "stv", wrv)
        p.wait("vector", "sts", sln)
        p.wait("vector", "pv")
        p.on("vector", lambda e: e.tensor_scalar(s_mask[:], s_m[:], wr1[:, 0:1], None,
                                                 ALU.mult, ALU.bypass), inc=("stv", 1))
        p.wait("vector", "stv")
        p.on("vector", lambda e: e.tensor_sub(s_var[:], s_rsc[:], s_mask[:]),
             inc=("stv", 1))
        p.wait("vector", "stv")
        p.on("vector", lambda e: e.tensor_mul(s_mask[:], s_var[:], s_rstd[:]),
             inc=("stv", 1))
        p.wait("vector", "stv")
        p.on("vector", lambda e: e.tensor_scalar(s_var[:], s_mask[:], br_col[:, 0:1], None,
                                                 ALU.add, ALU.bypass), inc=("stv", 1))
        p.wait("vector", "stv")
        vmask = p.on("vector", lambda e: e.tensor_single_scalar(
            s_mask[:], s_var[:], 0.0, ALU.is_gt), inc=("stv", 1))
        p.wait("tensor", "wcast", wcast_done)
        p.wait("gpsimd", "wcast", wcast_done)

        for tg in range(NTG):
            tts = [tg * 4 + i for i in range(4)]

            for i, tt in enumerate(tts):
                if i >= 2:
                    p.wait("sync", "ps", ps_t[("a0", tg, i - 2)])
                if tg == 0 and i == 0:
                    p.wait("sync", "ps", ps_t[NT - 1])
                    p.wait("sync", "pv", pv_t[NT - 1])
                ld = p.dma("sync", xb[i % 2][:], xs[tt * P:(tt + 1) * P, :],
                           sname=f"xld{i % 2}")
                p.wait("scalar", f"xld{i % 2}", ld)
                if i == 0 and tg == 0:
                    p.wait("scalar", "stv", vln)
                    p.wait("scalar", "sts", sln)
                p.on("scalar", lambda e, i=i, tt=tt: e.activation(
                    yb[i][:], xb[i % 2][:], AF.Identity,
                    bias=s_nmr[:, tt:tt + 1], scale=s_rstd[:, tt:tt + 1]),
                    inc=("ps", 1))
                p.wait("scalar", "ps")
                ps_t[("a0", tg, i)] = p.on("scalar", lambda e, i=i: e.activation(
                    junkb[:], yb[i][:], AF.Square, accum_out=g_sq[:, i:i + 1]),
                    inc=("ps", 1))
            if seq == 0 and tg == 0:
                dbg_tap(0, yb[0][:], [("ps", p.cnt["ps"])])
            r1t = emit_rms_stats_math(p, g_sq[:], g_r1[:], D, cst[:, 1:2])
            p.wait("vector", "sts", r1t)
            p.wait("vector", "pv")
            for i in range(4):
                p.on("vector", lambda e, i=i: e.scalar_tensor_tensor(
                    hb[i][:], yb[i][:], g_r1[:, i:i + 1], n1_row[:], ALU.mult, ALU.mult),
                    inc=("pv", 1))
            p.wait("tensor", "pv")
            if seq == 0 and tg == 0:
                dbg_tap(1, hb[0][:], [("pv", p.cnt["pv"])])
            emit_transpose_tiles(p, [(lambda c, i=i: hb[i][:, c * P:(c + 1) * P])
                                     for i in range(4)], h1f, pt2, ident, list(range(NC)), 512)

            def lhs_h1(g, k, i):
                return h1f[:, k * 512 + i * P: k * 512 + (i + 1) * P]

            def evac_v(g, etag):
                for i in range(4):
                    p.on("vector", lambda e, g=g, i=i: e.tensor_tensor(
                        db[i][:, g * 512:(g + 1) * 512], pacc[i][:],
                        bv_row[:, g * 512:(g + 1) * 512], ALU.add), inc=(etag, 1))

            stream_mm(lambda g, k: wv_bf[k * P:(k + 1) * P, g * 512:(g + 1) * 512],
                      4, NC, lhs_h1, evac_v, "wve")

            if seq == 0 and tg == 0:
                dbg_tap(2, db[0][:], [("wve", p.cnt["wve"])])
            emit_transpose_tiles(p, [(lambda c, i=i: db[i][:, c * P:(c + 1) * P])
                                     for i in range(4)], h1f, pt2, ident, list(range(NC)), 512)

            def evac_h(g, etag):
                for i in range(4):
                    p.on("vector", lambda e, g=g, i=i: e.tensor_tensor(
                        hb[i][:, g * 512:(g + 1) * 512], pacc[i][:],
                        yb[i][:, g * 512:(g + 1) * 512], ALU.add), inc=(etag, 1))

            stream_mm(lambda g, k: wo_bf[k * P:(k + 1) * P, g * 512:(g + 1) * 512],
                      4, NC, lhs_h1, evac_h, "woe")

            if seq == 0 and tg == 0:
                dbg_tap(3, hb[0][:], [("woe", p.cnt["woe"])])
            p.wait("scalar", "woe")
            for i in range(4):
                p.wait("scalar", "ps")
                p.on("scalar", lambda e, i=i: e.activation(
                    junkb[:], hb[i][:], AF.Square, accum_out=g_sq2[:, i:i + 1]),
                    inc=("ps", 1))
            r2t = emit_rms_stats_math(p, g_sq2[:], g_r2[:], D, cst[:, 1:2])
            p.wait("vector", "sts", r2t)
            for i in range(4):
                p.on("vector", lambda e, i=i: e.scalar_tensor_tensor(
                    db[i][:], hb[i][:], g_r2[:, i:i + 1], n2_row[:], ALU.mult, ALU.mult),
                    inc=("pv", 1) if i == 3 else None)
            p.wait("tensor", "pv")
            emit_transpose_tiles(p, [(lambda c, i=i: db[i][:, c * P:(c + 1) * P])
                                     for i in range(4)], h2f, pt2, ident, list(range(NC)), 512)

            if seq == 0 and tg == 0:
                dbg_tap(4, db[0][:], [("pv", p.cnt["pv"])])
            for fc in range(NFC):
                for j, (wsrc, ps_) in enumerate([(wg_bf, pg), (wu_bf, pu)]):
                    if p.cnt.get("gau", 0) >= 2:
                        p.wait("gpsimd", "gau", p.cnt["gau"] - 1)
                    gslot = p.cnt.get("gau", 0) % 2
                    ld = p.dma("gpsimd", gring[gslot][:], wsrc[fc], sname=f"gal{gslot}")
                    p.wait("tensor", f"gal{gslot}", ld)
                    if j == 0:
                        p.wait("tensor", "ctv")
                    for k in range(NC):
                        p.on("tensor", lambda e, k=k, gslot=gslot, ps_=ps_: e.matmul(
                            ps_[:], gring[gslot][:, k * P:(k + 1) * P],
                            h2f[:, k * 512:(k + 1) * 512],
                            start=(k == 0), stop=(k == NC - 1)),
                            inc=("gau", 1) if k == NC - 1 else None)
                p.wait("scalar", "gau")
                sc = p.on("scalar", lambda e: e.activation(sgb[:], pg[:], AF.Silu),
                          inc=("ps", 1))
                p.wait("vector", "ps", sc)
                p.on("vector", lambda e, fc=fc: e.tensor_tensor(
                    ufm[:, fc * 512:(fc + 1) * 512], sgb[:], pu[:], ALU.mult),
                    inc=("ctv", 1))
            p.wait("tensor", "ctv")

            if seq == 0 and tg == 0:
                dbg_tap(5, ufm[:, 0:512], [("ctv", p.cnt["ctv"])])
            def lhs_u(g, k, i):
                return ufm[:, k * 512 + i * P: k * 512 + (i + 1) * P]

            def evac_d(g, etag):
                for i in range(4):
                    p.on("vector", lambda e, g=g, i=i: e.tensor_tensor(
                        db[i][:, g * 512:(g + 1) * 512], pacc[i][:],
                        hb[i][:, g * 512:(g + 1) * 512], ALU.add), inc=(etag, 1))

            stream_mm(lambda g, k: wd_bf[k * P:(k + 1) * P, g * 512:(g + 1) * 512],
                      4, NFC, lhs_u, evac_d, "wde")

            if seq == 0 and tg == 0:
                dbg_tap(6, db[0][:], [("wde", p.cnt["wde"])])
            p.wait("vector", "wde")
            p.wait("vector", "ps")
            if tg == 0:
                p.wait("vector", "stv", vmask)
            for i, tt in enumerate(tts):
                p.wait("vector", "pv")
                p.on("vector", lambda e, i=i: e.tensor_sub(junkv[:], db[i][:], yb[i][:]),
                     inc=("pv", 1))
                p.wait("vector", "pv")
                vv = p.on("vector", lambda e, i=i, tt=tt: e.scalar_tensor_tensor(
                    db[i][:], junkv[:], s_mask[:, tt:tt + 1], yb[i][:], ALU.mult, ALU.add),
                    inc=("pv", 1))
                p.on("vector", lambda e, i=i: e.reduce_sum(
                    o_sum[:, i:i + 1], db[i][:], axis=AX.X), inc=("pv", 1))
                p.wait("scalar", "pv", vv)
                p.wait("scalar", "ps")
                p.on("scalar", lambda e, i=i: e.activation(
                    junkb[:], db[i][:], AF.Square, accum_out=o_sq[:, i:i + 1]),
                    inc=("ps", 1))
            p.wait("vector", "ps")
            if seq == 0 and tg == 0:
                dbg_tap(7, db[0][:], [("ps", p.cnt["ps"])])
                dbg_tap(8, s_mask[:, 0:NT], [("stv", p.cnt["stv"])])
                dbg_tap(9, s_rstd[:, 0:NT], [])
                dbg_tap(10, s_m[:, 0:NT], [])
                dbg_tap(11, s_rsc[:, 0:NT], [])
            vo, so = emit_ln_stats_math(p, o_sum[:], o_sq[:], o_m[:], o_var[:],
                                        o_rstd[:], o_nmr[:], D, cst[:, 0:1])
            for i, tt in enumerate(tts):
                if i < 2:
                    p.wait("sync", "ps", ps_t[("a0", tg, 2 + i)])
                else:
                    p.wait("sync", "pv", b3_vv[i - 2])
                ld = p.dma("sync", xb[i % 2][:], xs[tt * P:(tt + 1) * P, :],
                           sname=f"xld{i % 2}")
                if i == 0:
                    p.wait("scalar", "stv", vo)
                    p.wait("scalar", "sts", so)
                sc = p.on("scalar", lambda e, i=i: e.activation(
                    yb[i][:], db[i][:], AF.Identity, bias=o_nmr[:, i:i + 1],
                    scale=o_rstd[:, i:i + 1]), inc=("ps", 1))
                p.wait("vector", "ps", sc)
                p.wait("vector", f"xld{i % 2}", ld)
                vv = p.on("vector", lambda e, i=i: e.tensor_add(
                    xob[i % 2][:], xb[i % 2][:], yb[i][:]), inc=("pv", 1))
                b3_vv[i] = vv
                p.wait("sync", "pv", vv)
                st = p.dma("sync", xo[tt * P:(tt + 1) * P, :], xob[i % 2][:],
                           sname=f"xst{i % 2}")
                p.wait("sync", f"xst{i % 2}", st)

    p.wait("sync", "xst0", p.cnt.get("xst0", 0))
    p.wait("sync", "xst1", p.cnt.get("xst1", 0))
    return p



def l1ffn_host_inputs(w):
    D, FF, NC, NFC = 2048, 5632, 16, 44
    slab = lambda m: np.ascontiguousarray(
        np.asarray(m, np.float32).reshape(NC, P, NFC, P).transpose(2, 1, 0, 3)
        .reshape(NFC, P, NC * P))
    return {
        "wv": np.ascontiguousarray(np.asarray(w["wv"], np.float32)),
        "wo": np.ascontiguousarray(np.asarray(w["wo"], np.float32)),
        "w_gate": slab(w["w_gate"]),
        "w_up": slab(w["w_up"]),
        "w_down": np.ascontiguousarray(np.asarray(w["w_down"], np.float32)),
        "bv": np.asarray(w["bv"], np.float32).reshape(1, D),
        "w_router": np.asarray(w["w_router"], np.float32).reshape(1, D),
        "b_router": np.asarray(w["b_router"], np.float32).reshape(1, 1),
        "norm1_w": np.asarray(w["norm1_w"], np.float32).reshape(1, D),
        "norm2_w": np.asarray(w["norm2_w"], np.float32).reshape(1, D),
        "ident": np.eye(P, dtype=ml_dtypes.bfloat16),
    }


def declare_l1ffn_inputs(nc, pfx="i_"):
    D, FF, NC, NFC = 2048, 5632, 16, 44
    names = {
        "wv": [D, D], "wo": [D, D], "w_gate": [NFC, P, NC * P], "w_up": [NFC, P, NC * P],
        "w_down": [FF, D], "bv": [1, D], "w_router": [1, D], "b_router": [1, 1],
        "norm1_w": [1, D], "norm2_w": [1, D],
    }
    win = {k: nc.dram_tensor(pfx + k, sh, F32, kind="ExternalInput").ap()
           for k, sh in names.items()}
    win["ident"] = nc.dram_tensor(pfx + "ident", [P, P], BF16, kind="ExternalInput").ap()
    return win



def build_l0ffn(p, T, x_in, x_out, win, nseq=2, dbg=None):
    nc = p.nc
    D, FF = 2048, 8192
    NT = T // 128
    assert NT % 4 == 0
    NTG = NT // 4
    NC, NFC = 16, 64
    EPS_LN = 1e-5
    SQ2_64 = 64.0 * (2.0 ** 0.5)
    ISQ8192 = 1.0 / 90.50966799187808

    wu_bf = nc.dram_tensor("f_wu_bf", [NFC, P, NC * P], BF16, kind="Internal").ap()
    wg_bf = nc.dram_tensor("f_wg_bf", [NFC, P, NC * P], BF16, kind="Internal").ap()
    wd_bf = nc.dram_tensor("f_wd_bf", [D, FF], BF16, kind="Internal").ap()
    wd2a = nc.dram_tensor("f_wd2a", [NFC, D, P], BF16, kind="Internal").ap()
    wd2b = nc.dram_tensor("f_wd2b", [NFC, P, D], BF16, kind="Internal").ap()

    ident = p.sbuf("f_ident", [P, P], BF16)
    cst = p.sbuf("f_cst", [P, 1], F32)
    h64 = p.sbuf("f_h64", [64, 64], BF16)
    h128 = p.sbuf("f_h128", [P, P], BF16)
    su_row = p.sbuf("f_su_row", [P, D], BF16)
    rowtmp = p.sbuf("f_rowtmp", [P, D], F32)
    rd_row = p.sbuf("f_rd_row", [P, D], F32)
    st_a = p.sbuf("f_st_a", [P, NFC], F32)
    st_b = p.sbuf("f_st_b", [P, NFC], F32)
    st_c = p.sbuf("f_st_c", [P, NFC], F32)
    r_up = p.sbuf("f_r_up", [P, NFC], F32)
    r_g = p.sbuf("f_r_g", [P, NFC], F32)

    xb = [p.sbuf(f"f_xb{i}", [P, D], F32) for i in range(2)]
    xob = [p.sbuf(f"f_xob{i}", [P, D], F32) for i in range(2)]
    yb = [p.sbuf(f"f_yb{i}", [P, D], BF16) for i in range(2)]
    db = [p.sbuf(f"f_db{i}", [P, D], BF16) for i in range(4)]
    z0f = p.sbuf("f_z0f", [P, NC * 512], BF16)
    x1f = p.sbuf("f_x1f", [P, NC * 512], BF16)
    zf = p.sbuf("f_zf", [P, NFC * 512], BF16)
    junkb = p.sbuf("f_junkb", [P, D], BF16)
    sgb = p.sbuf("f_sgb", [P, 512], BF16)
    ub = [p.sbuf(f"f_ub{i}", [P, 512], BF16) for i in range(2)]
    gring = [p.sbuf(f"f_gring{i}", [P, NC * P], BF16) for i in range(2)]
    wring = [p.sbuf(f"f_wring{i}", [P, 512], BF16) for i in range(4)]
    tring = [p.sbuf(f"f_tring{i}", [P, P], BF16) for i in range(2)]
    wdacc = p.sbuf("f_wdacc", [P, D], BF16)

    s_sum = p.sbuf("f_s_sum", [P, NT], F32)
    s_sq = p.sbuf("f_s_sq", [P, NT], F32)
    s_m = p.sbuf("f_s_m", [P, NT], F32)
    s_var = p.sbuf("f_s_var", [P, NT], F32)
    s_rstd = p.sbuf("f_s_rstd", [P, NT], F32)
    s_nmr = p.sbuf("f_s_nmr", [P, NT], F32)
    o_sum = p.sbuf("f_o_sum", [P, 4], F32)
    o_sq = p.sbuf("f_o_sq", [P, 4], F32)
    o_m = p.sbuf("f_o_m", [P, 4], F32)
    o_var = p.sbuf("f_o_var", [P, 4], F32)
    o_rstd = p.sbuf("f_o_rstd", [P, 4], F32)
    o_nmr = p.sbuf("f_o_nmr", [P, 4], F32)

    pp = [p.psum(f"f_pp{i}", [P, 512], F32) for i in range(6)]
    pt2 = [p.psum(f"f_pt{i}", [P, 512], BF16) for i in range(2)]

    p.on("vector", lambda e: e.memset(cst[:, 0:1], EPS_LN), inc=("vprep", 1))
    p.wait("scalar", "vprep")
    p.dma("sync", ident[:], win["ident"], sname="cload")
    p.dma("sync", h64[:], win["h64"], sname="cload")
    p.dma("sync", h128[:], win["h128"], sname="cload")
    p.dma("sync", st_a[:], win["wsu_up"], sname="cload")
    p.dma("sync", st_b[:], win["wsu_g"], sname="cload")
    p.dma("sync", st_c[:], win["su_dp_fm"], sname="cload")
    p.dma("sync", rowtmp[:], win["sdp_row"].partition_broadcast(P), sname="cload")
    p.dma("gpsimd", su_row[:], win["su_row"].partition_broadcast(P), sname="wcast")
    for (dst, src) in [(wu_bf, win["w_up"]), (wg_bf, win["w_gate"]), (wd_bf, win["w_down"])]:
        p.dma("gpsimd", dst, src, sname="wcast")
    wcast_done = p.cnt["wcast"]
    cload_done = p.cnt["cload"]
    p.wait("vector", "cload", cload_done)
    p.wait("vector", "wcast", wcast_done)
    p.on("vector", lambda e: e.scalar_tensor_tensor(r_up[:], st_a[:], SQ2_64, st_c[:],
                                                    ALU.mult, ALU.mult), inc=("stv", 1))
    p.on("vector", lambda e: e.tensor_single_scalar(r_g[:], st_b[:], SQ2_64, ALU.mult),
         inc=("stv", 1))
    p.on("vector", lambda e: e.tensor_single_scalar(rd_row[:], rowtmp[:], ISQ8192, ALU.mult),
         inc=("stv", 1))
    rows_done = p.cnt["stv"]

    p.wait("tensor", "cload", cload_done)
    p.wait("gpsimd", "wcast", wcast_done)
    DBLK = 16
    NJ = DBLK * P // 512
    for di in range(D // DBLK):
        slot = di % 2
        if di >= 2:
            p.wait("gpsimd", "knm", (di - 1) * NJ)
            p.wait("gpsimd", f"qms{slot}", p.cnt.get(f"qms{slot}", 0))
        src = wd_bf[di * DBLK:(di + 1) * DBLK, :].rearrange("d (c p) -> c d p", p=P)
        dst3 = gring[slot][0:64, 0:DBLK * P].rearrange("c (d p) -> c d p", p=P)
        ld = p.dma("gpsimd", dst3, src, sname=f"gtl{slot}")
        p.wait("tensor", f"gtl{slot}", ld)
        for j in range(NJ):
            if di * NJ + j >= 2:
                p.wait("tensor", "knv", di * NJ + j - 1)
            p.on("tensor", lambda e, slot=slot, j=j: e.matmul(
                pp[4 + j % 2][0:64, :], h64[:], gring[slot][0:64, j * 512:(j + 1) * 512],
                start=True, stop=True), inc=("knm", 1))
            p.wait("vector", "knm")
            if di >= 2 and j == 0:
                p.wait("vector", f"qms{slot}", p.cnt.get(f"qms{slot}", 0))
            p.on("vector", lambda e, slot=slot, j=j: e.tensor_copy(
                gring[slot][64:128, j * 512:(j + 1) * 512], pp[4 + j % 2][0:64, :]),
                inc=("knv", 1))
        p.wait("gpsimd", "knv")
        p.dma("gpsimd", wd2a[:, di * DBLK:(di + 1) * DBLK, :],
              gring[slot][64:128, 0:DBLK * P].rearrange("c (d p) -> c d p", p=P),
              sname=f"qms{slot}")

    p.wait("gpsimd", "qms0", p.cnt.get("qms0", 0))
    p.wait("gpsimd", "qms1", p.cnt.get("qms1", 0))
    for c in range(NFC):
        for dc in range(NC):
            it = c * NC + dc
            slot = it % 2
            if it >= 2:
                p.wait("gpsimd", "vhm", it - 1)
            ld = p.dma("gpsimd", tring[slot][:], wd2a[c, dc * P:(dc + 1) * P, :],
                       sname=f"pql{slot}")
            p.wait("tensor", f"pql{slot}", ld)
            if it >= 2:
                p.wait("tensor", "vhv", it - 1)
            p.on("tensor", lambda e, slot=slot: e.transpose(
                pt2[slot][:, 0:128], tring[slot][:], ident[:]), inc=("vhm", 1))
            p.wait("vector", "vhm")
            if dc == 0 and c >= 1:
                p.wait("vector", "cxs", p.cnt["cxs"])
            p.on("vector", lambda e, slot=slot, dc=dc: e.tensor_copy(
                wdacc[:, dc * P:(dc + 1) * P], pt2[slot][:, 0:128]), inc=("vhv", 1))
        p.wait("gpsimd", "vhv")
        p.dma("gpsimd", wd2b[c], wdacc[:], sname="cxs")
    wd2_done = p.cnt["cxs"]

    def dbg_tap(slot, src_ap, waits, barrier=True):
        if dbg is None:
            return
        for sn, v in waits:
            p.wait("gpsimd", sn, v)
        dst = dbg[slot]
        if tuple(dst.shape) != tuple(src_ap.shape):
            dst = dst[:, 0:src_ap.shape[1]]
        p.dma("gpsimd", dst, src_ap, sname=f"dbgst{slot}")
        if barrier:
            for eng in ("vector", "scalar", "tensor", "sync"):
                p.wait(eng, f"dbgst{slot}")

    b3_vv = {}
    ps_t = {}
    pv_t = {}
    for seq in range(nseq):
        xs = x_in[seq]
        xo = x_out[seq]

        for tt in range(NT):
            if tt >= 2:
                p.wait("sync", "pv", pv_t[tt - 2])
                p.wait("sync", "ps", ps_t[tt - 2])
            ld = p.dma("sync", xb[tt % 2][:], xs[tt * P:(tt + 1) * P, :],
                       sname=f"xld{tt % 2}")
            p.wait("vector", f"xld{tt % 2}", ld)
            p.wait("vector", "pv")
            pv_t[tt] = p.on("vector", lambda e, tt=tt: e.reduce_sum(
                s_sum[:, tt:tt + 1], xb[tt % 2][:], axis=AX.X), inc=("pv", 1))
            p.wait("scalar", f"xld{tt % 2}", ld)
            p.wait("scalar", "ps")
            ps_t[tt] = p.on("scalar", lambda e, tt=tt: e.activation(
                junkb[:], xb[tt % 2][:], AF.Square, accum_out=s_sq[:, tt:tt + 1]),
                inc=("ps", 1))
        p.wait("vector", "ps")
        vln, sln = emit_ln_stats_math(p, s_sum[:], s_sq[:], s_m[:], s_var[:],
                                      s_rstd[:], s_nmr[:], D, cst[:, 0:1])
        if seq == 0:
            p.wait("gpsimd", "cxs", wd2_done)

        a0_pv = {}
        for tg in range(NTG):
            tts = [tg * 4 + i for i in range(4)]
            for i, tt in enumerate(tts):
                if i >= 2:
                    p.wait("sync", "ps", ps_t[("a0", tg, i - 2)])
                if tg == 0 and i == 0:
                    p.wait("sync", "ps", ps_t[NT - 1])
                    p.wait("sync", "pv", pv_t[NT - 1])
                ld = p.dma("sync", xb[i % 2][:], xs[tt * P:(tt + 1) * P, :],
                           sname=f"xld{i % 2}")
                p.wait("scalar", f"xld{i % 2}", ld)
                if i == 0 and tg == 0:
                    p.wait("scalar", "stv", vln)
                    p.wait("scalar", "sts", sln)
                    if seq == 0:
                        p.wait("scalar", "stv", rows_done)
                if i >= 2:
                    p.wait("scalar", "pv", a0_pv[i - 2])
                ps_t[("a0", tg, i)] = p.on("scalar", lambda e, i=i, tt=tt: e.activation(
                    yb[i % 2][:], xb[i % 2][:], AF.Identity,
                    bias=s_nmr[:, tt:tt + 1], scale=s_rstd[:, tt:tt + 1]), inc=("ps", 1))
                p.wait("vector", "ps", ps_t[("a0", tg, i)])
                p.wait("vector", "pv")
                a0_pv[i] = p.on("vector", lambda e, i=i: e.tensor_mul(
                    db[i][:], yb[i % 2][:], su_row[:]), inc=("pv", 1))
            p.wait("tensor", "pv")
            if seq == 0 and tg == 0:
                dbg_tap(0, db[0][:], [("pv", p.cnt["pv"])])
            emit_transpose_tiles(p, [(lambda c, i=i: db[i][:, c * P:(c + 1) * P])
                                     for i in range(4)], z0f, pt2, ident, list(range(NC)), 512)

            for m in range(NC):
                slot = p.cnt.get("kvau", 0) // NC % 2
                if m < 2:
                    p.wait("gpsimd", "qau", p.cnt.get("qau", 0))
                if p.cnt.get("kvau", 0) >= 2 * NC:
                    p.wait("gpsimd", "kvau", p.cnt["kvau"] - NC)
                ld = p.dma("gpsimd", gring[slot][:], win["hs_slab"][m], sname=f"gal{slot}")
                p.wait("tensor", f"gal{slot}", ld)
                if p.cnt.get("kvav", 0) >= 2:
                    p.wait("tensor", "kvav", p.cnt["kvav"] - 1)
                for k in range(NC):
                    p.on("tensor", lambda e, slot=slot, k=k, m=m: e.matmul(
                        pp[m % 2][:], gring[slot][:, k * P:(k + 1) * P],
                        z0f[:, k * 512:(k + 1) * 512], start=(k == 0), stop=(k == NC - 1)),
                        inc=("kvau", 1))
                p.wait("vector", "kvau")
                p.on("vector", lambda e, m=m: e.tensor_copy(
                    x1f[:, m * 512:(m + 1) * 512], pp[m % 2][:]), inc=("kvav", 1))
            p.wait("tensor", "kvav")

            sil_t = {}
            for ic in range(NFC):
                gbank = (ic % 2) * 4
                for wsrc, ppj in ((wg_bf, 0), (wu_bf, 1)):
                    gu = p.cnt.get("qau", 0)
                    slot = gu % 2
                    if ic == 0:
                        p.wait("gpsimd", "kvau", p.cnt["kvau"])
                    if gu >= 2:
                        p.wait("gpsimd", "qau", gu - 1)
                    ld = p.dma("gpsimd", gring[slot][:], wsrc[ic], sname=f"hbl{slot}")
                    p.wait("tensor", f"hbl{slot}", ld)
                    if ic >= 2 and ppj == 0:
                        p.wait("tensor", "qav", p.cnt["qav"] - 1)
                    for k in range(NC):
                        p.on("tensor", lambda e, slot=slot, k=k, ppj=ppj, gbank=gbank:
                             e.matmul(pp[gbank + ppj][:], gring[slot][:, k * P:(k + 1) * P],
                                      x1f[:, k * 512:(k + 1) * 512],
                                      start=(k == 0), stop=(k == NC - 1)),
                             inc=("qau", 1) if k == NC - 1 else None)
                p.wait("scalar", "qau")
                p.wait("scalar", "ps")
                p.wait("scalar", "qav")
                sc = p.on("scalar", lambda e, ic=ic, gbank=gbank: e.activation(
                    sgb[:], pp[gbank][:], AF.Silu, scale=r_g[:, ic:ic + 1]), inc=("ps", 1))
                p.wait("vector", "ps", sc)
                if ic >= 2:
                    p.wait("vector", "qsm", p.cnt["qsm"] - 1)
                p.on("vector", lambda e, ic=ic, gbank=gbank: e.scalar_tensor_tensor(
                    ub[ic % 2][:], pp[gbank + 1][:], r_up[:, ic:ic + 1], sgb[:],
                    ALU.mult, ALU.mult), inc=("qav", 1))
                p.wait("tensor", "qav")
                if ic >= 2:
                    p.wait("tensor", "qsv", p.cnt["qsv"] - 1)
                p.on("tensor", lambda e, ic=ic: e.matmul(
                    pp[2 + ic % 2][:], h128[:], ub[ic % 2][:], start=True, stop=True),
                    inc=("qsm", 1))
                p.wait("vector", "qsm")
                p.on("vector", lambda e, ic=ic: e.tensor_copy(
                    zf[:, ic * 512:(ic + 1) * 512], pp[2 + ic % 2][:]), inc=("qsv", 1))
            p.wait("tensor", "qsv")
            p.wait("tensor", "qav")
            if seq == 0 and tg == 0:
                dbg_tap(1, x1f[:, 0:512], [("kvav", p.cnt["kvav"])])
                dbg_tap(2, zf[:, 0:512], [("qsv", p.cnt["qsv"])])

            for dg in range(4):
                for c in range(NFC):
                    wu = p.cnt.get("wru", 0)
                    slot = wu % 4
                    if wu >= 4:
                        p.wait("gpsimd", "wru", wu - 3)
                    ld = p.dma("gpsimd", wring[slot][:],
                               wd2b[c, :, dg * 512:(dg + 1) * 512], sname=f"wrl{slot}")
                    p.wait("tensor", f"wrl{slot}", ld)
                    for i in range(4):
                        if c == 0 and p.cnt.get("wde", 0) >= 4 - i:
                            p.wait("tensor", "wde", p.cnt["wde"] - (4 - i) + 1)
                        p.on("tensor", lambda e, c=c, i=i, slot=slot: e.matmul(
                            pp[i][:], zf[:, c * 512 + i * P: c * 512 + (i + 1) * P],
                            wring[slot][:], start=(c == 0), stop=(c == NFC - 1)),
                            inc=("wru", 1) if i == 3 else None)
                p.wait("vector", "wru")
                for i in range(4):
                    p.on("vector", lambda e, dg=dg, i=i: e.tensor_tensor(
                        db[i][:, dg * 512:(dg + 1) * 512], pp[i][:],
                        rd_row[:, dg * 512:(dg + 1) * 512], ALU.mult), inc=("wde", 1))
            p.wait("tensor", "wde")
            if seq == 0 and tg == 0:
                dbg_tap(3, db[0][:], [("wde", p.cnt["wde"])])

            p.wait("vector", "wde")
            for i, tt in enumerate(tts):
                p.wait("vector", "pv")
                p.on("vector", lambda e, i=i: e.reduce_sum(
                    o_sum[:, i:i + 1], db[i][:], axis=AX.X), inc=("pv", 1))
                p.wait("scalar", "wde")
                p.wait("scalar", "ps")
                p.on("scalar", lambda e, i=i: e.activation(
                    junkb[:], db[i][:], AF.Square, accum_out=o_sq[:, i:i + 1]),
                    inc=("ps", 1))
            p.wait("vector", "ps")
            vo, so = emit_ln_stats_math(p, o_sum[:], o_sq[:], o_m[:], o_var[:],
                                        o_rstd[:], o_nmr[:], D, cst[:, 0:1])
            for i, tt in enumerate(tts):
                if i < 2:
                    p.wait("sync", "ps", ps_t[("a0", tg, 2 + i)])
                else:
                    p.wait("sync", "pv", b3_vv[i - 2])
                ld = p.dma("sync", xb[i % 2][:], xs[tt * P:(tt + 1) * P, :],
                           sname=f"xld{i % 2}")
                if i == 0:
                    p.wait("scalar", "stv", vo)
                    p.wait("scalar", "sts", so)
                if i >= 2:
                    p.wait("scalar", "pv", b3_vv[i - 2])
                sc = p.on("scalar", lambda e, i=i: e.activation(
                    yb[i % 2][:], db[i][:], AF.Identity, bias=o_nmr[:, i:i + 1],
                    scale=o_rstd[:, i:i + 1]), inc=("ps", 1))
                p.wait("vector", "ps", sc)
                p.wait("vector", f"xld{i % 2}", ld)
                vv = p.on("vector", lambda e, i=i: e.tensor_add(
                    xob[i % 2][:], xb[i % 2][:], yb[i % 2][:]), inc=("pv", 1))
                b3_vv[i] = vv
                p.wait("sync", "pv", vv)
                st = p.dma("sync", xo[tt * P:(tt + 1) * P, :], xob[i % 2][:],
                           sname=f"xst{i % 2}")
                p.wait("sync", f"xst{i % 2}", st)

    p.wait("sync", "xst0", p.cnt.get("xst0", 0))
    p.wait("sync", "xst1", p.cnt.get("xst1", 0))
    return p


def _hadamard(n):
    i = np.arange(n)
    return np.where(np.bitwise_count(i[:, None] & i[None, :]) % 2 == 0, 1.0, -1.0)


def l0ffn_host_inputs(w):
    D, FF, NC, NFC = 2048, 8192, 16, 64
    H = _hadamard(D) * (2.0 ** -12)
    hs_slab = np.ascontiguousarray(
        H.reshape(NC, P, NC, P).transpose(2, 1, 0, 3).reshape(NC, P, NC * P)
    ).astype(ml_dtypes.bfloat16)
    wslab = lambda m: np.ascontiguousarray(
        np.asarray(m, np.float32).reshape(NFC, P, NC, P).transpose(0, 3, 2, 1)
        .reshape(NFC, P, NC * P))
    wsu = np.asarray(w["Wscale_ug"], np.float32)
    return {
        "hs_slab": hs_slab,
        "h128": _hadamard(P).astype(ml_dtypes.bfloat16),
        "h64": _hadamard(64).astype(ml_dtypes.bfloat16),
        "su_row": np.asarray(w["SU_ug"], np.float32).reshape(1, D),
        "su_dp_fm": np.ascontiguousarray(
            np.asarray(w["SU_dp"], np.float32).reshape(NFC, P).T),
        "wsu_up": np.ascontiguousarray(wsu[:FF].reshape(NFC, P).T),
        "wsu_g": np.ascontiguousarray(wsu[FF:].reshape(NFC, P).T),
        "sdp_row": np.asarray(w["Wscale_dp"], np.float32).reshape(1, D),
        "w_up": wslab(w["W_up"]),
        "w_gate": wslab(w["W_gate"]),
        "w_down": np.ascontiguousarray(np.asarray(w["W_down"], np.float32)),
        "ident": np.eye(P, dtype=ml_dtypes.bfloat16),
    }


def declare_l0ffn_inputs(nc, pfx="i_"):
    D, FF, NC, NFC = 2048, 8192, 16, 64
    f32 = {
        "su_row": [1, D], "su_dp_fm": [P, NFC], "wsu_up": [P, NFC], "wsu_g": [P, NFC],
        "sdp_row": [1, D], "w_up": [NFC, P, NC * P], "w_gate": [NFC, P, NC * P],
        "w_down": [D, FF],
    }
    bf = {"hs_slab": [NC, P, NC * P], "h128": [P, P], "h64": [64, 64], "ident": [P, P]}
    win = {k: nc.dram_tensor(pfx + k, sh, F32, kind="ExternalInput").ap()
           for k, sh in f32.items()}
    win.update({k: nc.dram_tensor(pfx + k, sh, BF16, kind="ExternalInput").ap()
                for k, sh in bf.items()})
    return win



def build_l0attn(p, T, x_in, x_out, win, nseq=2, dbg=None):
    nc = p.nc
    D = 2048
    NT = T // 128
    assert NT % 4 == 0
    NC = 16
    NH, NKV, DH = 16, 4, 128
    EPS_LN = 1e-5
    SCL = 1.0 / (DH ** 0.5)
    NGLM = 2 * (NT - 3) if NT >= 3 else 0

    wq_bf = nc.dram_tensor("a_wq_bf", [NH, P, NC * P], BF16, kind="Internal").ap()
    wkvd_bf = nc.dram_tensor("a_wkvd_bf", [NKV, P, NC * P], BF16, kind="Internal").ap()
    wku_bf = nc.dram_tensor("a_wku_bf", [NKV, P, NKV * P], BF16, kind="Internal").ap()
    wvu_bf = nc.dram_tensor("a_wvu_bf", [NKV * P, NKV * P], BF16, kind="Internal").ap()
    wo_bf = nc.dram_tensor("a_wo_bf", [D, D], BF16, kind="Internal").ap()
    ctx_dram = nc.dram_tensor("a_ctx", [T, D], BF16, kind="Internal").ap()

    ident = p.sbuf("a_ident", [P, P], BF16)
    cst = p.sbuf("a_cst", [P, 1], F32)
    prot = p.sbuf("a_prot", [P, P], BF16)
    cosf = p.sbuf("a_cosf", [P, T], BF16)
    sinf = p.sbuf("a_sinf", [P, T], BF16)
    bb = p.sbuf("a_bb", [P, 384], F32)
    gt_sb = p.sbuf("a_gt", [P, NT * 32], BF16)
    wvu_sb = p.sbuf("a_wvu", [P, NKV * 512], BF16)

    lnf = p.sbuf("a_lnf", [P, NC * T], BF16)
    cf = p.sbuf("a_cf", [P, NKV * T], BF16)
    kro = p.sbuf("a_kro", [P, NKV * T], BF16)
    vtm = p.sbuf("a_vtm", [P, max(NT, 16) * 512], BF16)
    kglob = p.sbuf("a_kglob", [P, NKV * 32], BF16)
    vglob = p.sbuf("a_vglob", [32, 512], BF16)
    qraw = p.sbuf("a_qraw", [P, T], BF16)
    kraw = qraw
    qro = p.sbuf("a_qro", [P, T], BF16)
    tmb = [p.sbuf(f"a_tmb{i}", [P, 512], BF16) for i in range(2)]
    sc1 = p.sbuf("a_sc1", [P, 416], F32)
    probs = p.sbuf("a_probs", [P, 416], BF16)
    pbT = [p.sbuf(f"a_pbT{i}", [P, P], BF16) for i in range(2)]
    ctxh = p.sbuf("a_ctxh", [P, NT * P], BF16)
    nmc = p.sbuf("a_nmc", [P, 4], F32)

    xb = [p.sbuf(f"a_xb{i}", [P, D], F32) for i in range(2)]
    junkb = None
    xob = [p.sbuf(f"a_xob{i}", [P, D], F32) for i in range(2)]
    _yb0 = p.sbuf("a_yb0", [P, D], BF16)
    yb = [_yb0, _yb0]
    db = [p.sbuf(f"a_db{i}", [P, D], BF16) for i in range(4)]
    garing = [p.sbuf(f"a_garing{i}", [P, NC * P], BF16) for i in range(2)]
    wring = [p.sbuf(f"a_wring{i}", [P, 512], BF16) for i in range(2)]

    s_sum = p.sbuf("a_s_sum", [P, NT], F32)
    s_sq = p.sbuf("a_s_sq", [P, NT], F32)
    s_m = p.sbuf("a_s_m", [P, NT], F32)
    s_var = p.sbuf("a_s_var", [P, NT], F32)
    s_rstd = p.sbuf("a_s_rstd", [P, NT], F32)
    s_nmr = p.sbuf("a_s_nmr", [P, NT], F32)
    o_sum = p.sbuf("a_o_sum", [P, 4], F32)
    o_sq = p.sbuf("a_o_sq", [P, 4], F32)
    o_m = p.sbuf("a_o_m", [P, 4], F32)
    o_var = p.sbuf("a_o_var", [P, 4], F32)
    o_rstd = p.sbuf("a_o_rstd", [P, 4], F32)
    o_nmr = p.sbuf("a_o_nmr", [P, 4], F32)

    pp = [p.psum(f"a_pp{i}", [P, 512], F32) for i in range(6)]
    pt2 = [p.psum(f"a_pt{i}", [P, 512], BF16) for i in range(2)]

    p.on("vector", lambda e: e.memset(cst[:, 0:1], EPS_LN), inc=("vprep", 1))
    p.wait("scalar", "vprep")
    for dst, key in [(ident, "ident"), (prot, "prot"), (cosf, "cos_fm"), (sinf, "sin_fm"),
                     (bb, "band_bias")]:
        p.dma("sync", dst[:], win[key], sname="cload")
    for kt in range(NT):
        p.dma("sync", gt_sb[:, kt * 32:(kt + 1) * 32],
              win["gt"][kt * P:(kt + 1) * P, :], sname="cload")
    for k in range(NKV):
        p.dma("gpsimd", wvu_sb[:, k * 512:(k + 1) * 512],
              win["wv_up"][k * P:(k + 1) * P, :], sname="wcast")
    for (dst, src) in [(wq_bf, win["wq"]), (wkvd_bf, win["wkv_down"]),
                       (wku_bf, win["wk_up"]), (wo_bf, win["wo"])]:
        p.dma("gpsimd", dst, src, sname="wcast")
    wcast_done = p.cnt["wcast"]
    cload_done = p.cnt["cload"]
    p.wait("tensor", "cload", cload_done)
    p.wait("tensor", "wcast", wcast_done)
    p.wait("gpsimd", "wcast", wcast_done)

    def dbg_tap(slot, src_ap, waits, barrier=True):
        if dbg is None:
            return
        for sn, v in waits:
            p.wait("gpsimd", sn, v)
        dst = dbg[slot]
        if tuple(dst.shape) != tuple(src_ap.shape):
            dst = dst[:, 0:src_ap.shape[1]]
        p.dma("gpsimd", dst, src_ap, sname=f"dbgst{slot}")
        if barrier:
            for eng in ("vector", "scalar", "tensor", "sync"):
                p.wait(eng, f"dbgst{slot}")

    b3_vv = {}
    b3_sc = {}
    ps_t = {}
    pv_t = {}
    for seq in range(nseq):
        xs = x_in[seq]
        xo = x_out[seq]

        for tt in range(NT):
            if tt >= 2:
                p.wait("sync", "pv", pv_t[tt - 2])
                p.wait("sync", "ps", ps_t[tt - 2])
            ld = p.dma("sync", xb[tt % 2][:], xs[tt * P:(tt + 1) * P, :],
                       sname=f"xld{tt % 2}")
            p.wait("vector", f"xld{tt % 2}", ld)
            p.wait("vector", "pv")
            pv_t[tt] = p.on("vector", lambda e, tt=tt: e.reduce_sum(
                s_sum[:, tt:tt + 1], xb[tt % 2][:], axis=AX.X), inc=("pv", 1))
            p.wait("scalar", f"xld{tt % 2}", ld)
            p.wait("scalar", "ps")
            ps_t[tt] = p.on("scalar", lambda e, tt=tt: e.activation(
                yb[0][:], xb[tt % 2][:], AF.Square, accum_out=s_sq[:, tt:tt + 1]),
                inc=("ps", 1))
        p.wait("vector", "ps")
        vln, sln = emit_ln_stats_math(p, s_sum[:], s_sq[:], s_m[:], s_var[:],
                                      s_rstd[:], s_nmr[:], D, cst[:, 0:1])

        a0_sc = {}
        a0_tr = {}
        for tt in range(NT):
            i = tt % 2
            if tt >= 2:
                p.wait("sync", "ps", a0_sc[tt - 2])
            if tt == 0:
                p.wait("sync", "ps", ps_t[NT - 1])
                p.wait("sync", "pv", pv_t[NT - 1])
            ld = p.dma("sync", xb[i][:], xs[tt * P:(tt + 1) * P, :], sname=f"xld{i}")
            p.wait("scalar", f"xld{i}", ld)
            if tt == 0:
                p.wait("scalar", "stv", vln)
                p.wait("scalar", "sts", sln)
            if tt >= 1:
                p.wait("scalar", "trT", a0_tr[tt - 1])
            a0_sc[tt] = p.on("scalar", lambda e, i=i, tt=tt: e.activation(
                yb[i][:], xb[i][:], AF.Identity,
                bias=s_nmr[:, tt:tt + 1], scale=s_rstd[:, tt:tt + 1]), inc=("ps", 1))
            p.wait("tensor", "ps", a0_sc[tt])
            emit_transpose_tiles(p, [(lambda c, i=i: yb[i][:, c * P:(c + 1) * P])],
                                 lnf, pt2, ident, list(range(NC)), T, coloff=tt * P)
            a0_tr[tt] = p.cnt["trT"]

        p.wait("vector", "a3m", p.cnt.get("a3m", 0))
        p.wait("vector", "sxm", p.cnt.get("sxm", 0))
        for m in range(NKV):
            slot = m % 2
            if m < 2:
                p.wait("gpsimd", "gau", p.cnt.get("gau", 0))
                p.wait("gpsimd", "qmu", p.cnt.get("qmu", 0))
            else:
                p.wait("gpsimd", "gau", p.cnt["gau"] - NT // 4)
            ld = p.dma("gpsimd", garing[slot][:], wkvd_bf[m], sname=f"gal{slot}")
            p.wait("tensor", f"gal{slot}", ld)
            for ng in range(T // 512):
                gidx = p.cnt.get("a1v", 0)
                bank = gidx % 2
                if gidx >= 2:
                    p.wait("tensor", "a1v", gidx - 1)
                for k in range(NC):
                    p.on("tensor", lambda e, slot=slot, k=k, ng=ng, bank=bank: e.matmul(
                        pp[bank][:], garing[slot][:, k * P:(k + 1) * P],
                        lnf[:, k * T + ng * 512: k * T + (ng + 1) * 512],
                        start=(k == 0), stop=(k == NC - 1)),
                        inc=("gau", 1) if k == NC - 1 else None)
                p.wait("vector", "gau")
                p.on("vector", lambda e, m=m, ng=ng, bank=bank: e.tensor_copy(
                    cf[:, m * T + ng * 512: m * T + (ng + 1) * 512], pp[bank][:]),
                    inc=("a1v", 1))
        p.wait("tensor", "a1v")

        p.wait("vector", "sxm", p.cnt.get("sxm", 0))
        p.wait("vector", "qrv", p.cnt.get("qrv", 0))
        a2slab = {}
        for m in range(NKV):
            slot = m % 2
            if m < 2:
                p.wait("gpsimd", "gau", p.cnt.get("gau", 0))
            else:
                p.wait("gpsimd", "a2m", a2slab[m - 2])
            ld = p.dma("gpsimd", garing[slot][:, 0:NKV * P], wku_bf[m], sname=f"gal{slot}")
            p.wait("tensor", f"gal{slot}", ld)
            for ng in range(T // 512):
                if p.cnt.get("a2v", 0) >= 1:
                    p.wait("tensor", "a2v", p.cnt["a2v"])
                for k in range(NKV):
                    t_ = p.on("tensor", lambda e, slot=slot, k=k, ng=ng: e.matmul(
                        pp[0][:], garing[slot][:, k * P:(k + 1) * P],
                        cf[:, k * T + ng * 512: k * T + (ng + 1) * 512],
                        start=(k == 0), stop=(k == NKV - 1)),
                        inc=("a2m", 1) if k == NKV - 1 else None)
                    if k == NKV - 1:
                        a2slab[m] = t_
                p.wait("vector", "a2m")
                p.on("vector", lambda e, ng=ng: e.tensor_copy(
                    kraw[:, ng * 512:(ng + 1) * 512], pp[0][:]), inc=("a2v", 1))
                p.wait("tensor", "a2v")
                p.on("tensor", lambda e, ng=ng: e.matmul(
                    pp[1][:], prot[:], kraw[:, ng * 512:(ng + 1) * 512],
                    start=True, stop=True), inc=("a2m", 1))
                p.wait("vector", "a2m")
                p.wait("vector", "rpv")
                p.on("vector", lambda e, ng=ng: e.tensor_mul(
                    tmb[0][:], kraw[:, ng * 512:(ng + 1) * 512],
                    cosf[:, ng * 512:(ng + 1) * 512]), inc=("rpv", 1))
                p.on("vector", lambda e, ng=ng: e.tensor_mul(
                    tmb[1][:], pp[1][:], sinf[:, ng * 512:(ng + 1) * 512]),
                    inc=("rpv", 1))
                p.wait("vector", "rpv")
                p.on("vector", lambda e, m=m, ng=ng: e.tensor_add(
                    kro[:, m * T + ng * 512: m * T + (ng + 1) * 512],
                    tmb[0][:], tmb[1][:]), inc=("rpv", 1))
        p.wait("tensor", "rpv")

        p.wait("vector", "pvm", p.cnt.get("pvm", 0))
        p.wait("vector", "wou", p.cnt.get("wou", 0))
        for tt in range(NT):
            if p.cnt.get("a3v", 0) >= 2:
                p.wait("tensor", "a3v", p.cnt["a3v"] - 1)
            for k in range(NKV):
                p.on("tensor", lambda e, k=k, tt=tt: e.matmul(
                    pp[2 + tt % 2][:], cf[:, k * T + tt * P: k * T + (tt + 1) * P],
                    wvu_sb[:, k * 512:(k + 1) * 512],
                    start=(k == 0), stop=(k == NKV - 1)),
                    inc=("a3m", 1) if k == NKV - 1 else None)
            p.wait("vector", "a3m")
            p.on("vector", lambda e, tt=tt: e.tensor_copy(
                vtm[:, tt * 512:(tt + 1) * 512], pp[2 + tt % 2][:]), inc=("a3v", 1))
        p.wait("tensor", "a3v")

        NGL_ALL = T // 64
        for m in range(NKV):
            p.wait("vector", "rpv")
            p.on("vector", lambda e, m=m: e.tensor_copy(
                kglob[:, m * 32:m * 32 + NGL_ALL],
                kro[:, m * T:(m + 1) * T].rearrange("p (g s) -> p g s", s=64)[:, :, 0]),
                inc=("a4v", 1))
        for kt in range(NT):
            p.on("tensor", lambda e, kt=kt: e.matmul(
                pp[4][0:32, :], gt_sb[:, kt * 32:kt * 32 + 32],
                vtm[:, kt * 512:(kt + 1) * 512],
                start=(kt == 0), stop=(kt == NT - 1)), inc=("a4m", 1) if kt == NT - 1 else None)
        p.wait("vector", "a4m")
        p.on("vector", lambda e: e.tensor_copy(vglob[:, :], pp[4][0:32, :]), inc=("a4v", 1))
        a4_done = p.cnt["a4v"]

        for h in range(NH):
            g = h // 4
            slot = h % 2
            if h < 2:
                p.wait("gpsimd", "a2m", p.cnt.get("a2m", 0))
                if seq > 0:
                    for _ci in range(4):
                        p.wait("gpsimd", f"cxl{_ci}", p.cnt.get(f"cxl{_ci}", 0))
            if h >= 2:
                p.wait("gpsimd", "qmu", p.cnt["qmu"] - 1)
            ld = p.dma("gpsimd", garing[slot][:], wq_bf[h], sname=f"gal{slot}")
            p.wait("tensor", f"gal{slot}", ld)
            if h == 0:
                p.wait("tensor", "a4v", a4_done)
            for ng in range(T // 512):
                if h > 0 or ng > 0:
                    p.wait("tensor", "qrv", p.cnt.get("qrv", 0))
                for k in range(NC):
                    p.on("tensor", lambda e, slot=slot, k=k, ng=ng: e.matmul(
                        pp[0][:], garing[slot][:, k * P:(k + 1) * P],
                        lnf[:, k * T + ng * 512: k * T + (ng + 1) * 512],
                        start=(k == 0), stop=(k == NC - 1)),
                        inc=("qmu", 1) if k == NC - 1 else None)
                p.wait("vector", "qmu")
                p.on("vector", lambda e, ng=ng: e.tensor_copy(
                    qraw[:, ng * 512:(ng + 1) * 512], pp[0][:]), inc=("qmv", 1))
                p.wait("tensor", "qmv")
                p.on("tensor", lambda e, ng=ng: e.matmul(
                    pp[1][:], prot[:], qraw[:, ng * 512:(ng + 1) * 512],
                    start=True, stop=True), inc=("qrm", 1))
                p.wait("vector", "qrm")
                p.wait("vector", "rpv")
                if ng == 0:
                    p.wait("vector", "sxm", p.cnt.get("sxm", 0))
                p.on("vector", lambda e, ng=ng: e.tensor_mul(
                    tmb[0][:], qraw[:, ng * 512:(ng + 1) * 512],
                    cosf[:, ng * 512:(ng + 1) * 512]), inc=("rpv", 1))
                p.on("vector", lambda e, ng=ng: e.tensor_mul(
                    tmb[1][:], pp[1][:], sinf[:, ng * 512:(ng + 1) * 512]), inc=("rpv", 1))
                p.wait("vector", "rpv")
                p.on("vector", lambda e, ng=ng: e.tensor_add(
                    qro[:, ng * 512:(ng + 1) * 512], tmb[0][:], tmb[1][:]),
                    inc=("qrv", 1))
            p.wait("tensor", "qrv", p.cnt.get("qrv", 0))
            for qt in range(NT):
                bc0 = max(0, qt - 2)
                nb = qt - bc0 + 1
                ngl = max(0, 2 * (qt - 2))
                nw = nb * P + ngl
                p.wait("tensor", "sxv", p.cnt.get("sxv", 0))
                p.on("tensor", lambda e, qt=qt, g=g, bc0=bc0, nb=nb, ngl=ngl: e.matmul(
                    pp[2][:, 0:nb * P], qro[:, qt * P:(qt + 1) * P],
                    kro[:, g * T + bc0 * P: g * T + (qt + 1) * P],
                    start=True, stop=(ngl == 0)), inc=("sxm", 1))
                if ngl > 0:
                    p.on("tensor", lambda e, qt=qt, g=g, nb=nb, ngl=ngl: e.matmul(
                        pp[2][:, nb * P:nb * P + ngl], qro[:, qt * P:(qt + 1) * P],
                        kglob[:, g * 32: g * 32 + ngl],
                        start=False, stop=True), inc=("sxm", 1))
                p.wait("vector", "sxm")
                p.wait("vector", "expd", p.cnt.get("expd", 0))
                p.on("vector", lambda e, nb=nb: e.tensor_add(
                    sc1[:, 0:nb * P], pp[2][:, 0:nb * P], bb[:, (3 - nb) * P:384]),
                    inc=("sxv", 1))
                if ngl > 0:
                    p.on("vector", lambda e, nb=nb, ngl=ngl: e.tensor_copy(
                        sc1[:, nb * P:nb * P + ngl], pp[2][:, nb * P:nb * P + ngl]),
                        inc=("sxv", 1))
                sx = p.cnt["sxv"]
                p.wait("scalar", "sxv", sx)
                p.wait("scalar", "ptm", p.cnt.get("ptm", 0))
                p.wait("scalar", "ps")
                se = p.on("scalar", lambda e, nw=nw: e.activation(
                    probs[:, 0:nw], sc1[:, 0:nw], AF.Exp, scale=SCL, bias=0.0,
                    accum_out=nmc[:, 2:3]), inc=("ps", 1))
                p.wait("vector", "ps", se)
                p.on("vector", lambda e: e.reciprocal(nmc[:, 3:4], nmc[:, 2:3]),
                     inc=("expd", 1))
                p.wait("tensor", "ps", se)
                for b in range(nb + (1 if ngl > 0 else 0)):
                    j = p.cnt.get("ptm", 0) % 2
                    isg = b == nb
                    wdt = ngl if isg else P
                    if p.cnt.get("ptm", 0) >= 2:
                        p.wait("tensor", "ptv", p.cnt["ptv"] - 1)
                    p.on("tensor", lambda e, b=b, j=j, wdt=wdt: e.transpose(
                        pt2[j][0:wdt, 0:P], probs[:, b * P: b * P + wdt], ident[:]),
                        inc=("ptm", 1))
                    p.wait("vector", "ptm")
                    if p.cnt.get("pvm", 0) >= 2:
                        p.wait("vector", "pvm", p.cnt["pvm"] - 1)
                    p.on("vector", lambda e, j=j, wdt=wdt: e.tensor_copy(
                        pbT[j][0:wdt, :], pt2[j][0:wdt, 0:P]), inc=("ptv", 1))
                    p.wait("tensor", "ptv")
                    if isg:
                        p.on("tensor", lambda e, j=j, g=g, wdt=wdt, b=b: e.matmul(
                            pp[3][:, 0:P], pbT[j][0:wdt, :], vglob[0:wdt, g * P:(g + 1) * P],
                            start=(b == 0), stop=True), inc=("pvm", 1))
                    else:
                        p.on("tensor", lambda e, j=j, g=g, b=b, bc0=bc0, nb=nb, ngl=ngl: e.matmul(
                            pp[3][:, 0:P], pbT[j][:, :],
                            vtm[:, (bc0 + b) * 512 + g * P: (bc0 + b) * 512 + (g + 1) * P],
                            start=(b == 0), stop=(b == nb - 1 and ngl == 0)),
                            inc=("pvm", 1))
                p.wait("vector", "pvm")
                p.wait("vector", "expd")
                p.on("vector", lambda e, qt=qt, h=h: e.tensor_scalar(
                    ctxh[:, qt * P:(qt + 1) * P], pp[3][:, 0:P], nmc[:, 3:4], None,
                    ALU.mult, ALU.bypass), inc=("ctv", 1))
                p.wait("tensor", "ctv")
            p.wait("gpsimd", "ctv", p.cnt["ctv"])
            p.dma("gpsimd",
                  ctx_dram[:, h * P:(h + 1) * P].rearrange("(q p) c -> p q c", p=P),
                  ctxh[:].rearrange("p (q c) -> p q c", c=P), sname="cxs")
            p.wait("vector", "cxs", p.cnt["cxs"])

        p.wait("sync", "cxs", p.cnt["cxs"])
        for tgc in range(NT // 4):
            tts = [tgc * 4 + i for i in range(4)]
            for i, tt in enumerate(tts):
                if b3_sc.get(i) is not None:
                    p.wait("sync", "ps", b3_sc[i])
                ld = p.dma("sync", db[i][:], ctx_dram[tt * P:(tt + 1) * P, :],
                           sname=f"cxl{i}")
                p.wait("tensor", f"cxl{i}", ld)
            emit_transpose_tiles(p, [(lambda c, i=i: db[i][:, c * P:(c + 1) * P])
                                     for i in range(4)], vtm, pt2, ident,
                                 list(range(NC)), 512)
            for ng in range(4):
                for k in range(NC):
                    wu = p.cnt.get("wou", 0)
                    slot = wu % 2
                    if wu >= 2:
                        p.wait("gpsimd", "wou", wu - 1)
                    ld = p.dma("gpsimd", wring[slot][:],
                               wo_bf[k * P:(k + 1) * P, ng * 512:(ng + 1) * 512],
                               sname=f"wol{slot}")
                    p.wait("tensor", f"wol{slot}", ld)
                    if k == 0:
                        p.wait("tensor", "woe")
                    for i in range(4):
                        p.on("tensor", lambda e, k=k, i=i, slot=slot: e.matmul(
                            pp[i][:], vtm[:, k * 512 + i * P: k * 512 + (i + 1) * P],
                            wring[slot][:], start=(k == 0), stop=(k == NC - 1)),
                            inc=("wou", 1) if i == 3 else None)
                p.wait("vector", "wou")
                for i in range(4):
                    p.on("vector", lambda e, ng=ng, i=i: e.tensor_copy(
                        db[i][:, ng * 512:(ng + 1) * 512], pp[i][:]),
                        inc=("woe", 1) if i == 3 else None)
            p.wait("tensor", "woe")
            p.wait("vector", "woe")
            for i, tt in enumerate(tts):
                p.wait("vector", "pv")
                p.on("vector", lambda e, i=i: e.reduce_sum(
                    o_sum[:, i:i + 1], db[i][:], axis=AX.X), inc=("pv", 1))
                p.wait("scalar", "woe")
                p.wait("scalar", "ps")
                p.on("scalar", lambda e, i=i: e.activation(
                    yb[0][:], db[i][:], AF.Square, accum_out=o_sq[:, i:i + 1]),
                    inc=("ps", 1))
            p.wait("vector", "ps")
            vo, so = emit_ln_stats_math(p, o_sum[:], o_sq[:], o_m[:], o_var[:],
                                        o_rstd[:], o_nmr[:], D, cst[:, 0:1])
            for i, tt in enumerate(tts):
                if i >= 2:
                    p.wait("sync", "pv", b3_vv[i - 2])
                ld = p.dma("sync", xb[i % 2][:], xs[tt * P:(tt + 1) * P, :],
                           sname=f"xld{i % 2}")
                if i == 0:
                    p.wait("scalar", "stv", vo)
                    p.wait("scalar", "sts", so)
                if i >= 1:
                    p.wait("scalar", "pv", b3_vv[i - 1])
                sc = p.on("scalar", lambda e, i=i: e.activation(
                    yb[i % 2][:], db[i][:], AF.Identity, bias=o_nmr[:, i:i + 1],
                    scale=o_rstd[:, i:i + 1]), inc=("ps", 1))
                b3_sc[i] = sc
                p.wait("vector", "ps", sc)
                p.wait("vector", f"xld{i % 2}", ld)
                vv = p.on("vector", lambda e, i=i: e.tensor_add(
                    xob[i % 2][:], xb[i % 2][:], yb[i % 2][:]), inc=("pv", 1))
                b3_vv[i] = vv
                p.wait("sync", "pv", vv)
                st = p.dma("sync", xo[tt * P:(tt + 1) * P, :], xob[i % 2][:],
                           sname=f"xst{i % 2}")
                p.wait("sync", f"xst{i % 2}", st)

    p.wait("sync", "xst0", p.cnt.get("xst0", 0))
    p.wait("sync", "xst1", p.cnt.get("xst1", 0))
    return p


def l0attn_host_inputs(w, T):
    D, NC, NH, NKV, DH = 2048, 16, 16, 4, 128
    THETA = 10000.0

    def slab(m, nm):
        m = np.asarray(m, np.float32)
        din, dout = m.shape
        nk = din // P
        return np.ascontiguousarray(
            m.reshape(nk, P, nm, P).transpose(2, 1, 0, 3).reshape(nm, P, nk * P))

    freqs = 1.0 / THETA ** (np.arange(0, DH, 2) / DH)
    emb = np.arange(T)[:, None] * freqs[None, :]
    cos = np.concatenate([np.cos(emb), np.cos(emb)], -1).astype(np.float32)
    sin = np.concatenate([np.sin(emb), np.sin(emb)], -1).astype(np.float32)
    prot_P = np.zeros((P, P), np.float32)
    for pp_ in range(64):
        prot_P[pp_, pp_ + 64] = -1.0
        prot_P[pp_ + 64, pp_] = 1.0
    ir = np.arange(128)[:, None]
    jb = np.arange(384)[None, :]
    masked = (np.abs(ir + 256 - jb) > 256) & (jb % 64 != 0)
    bias = np.where(masked, -1e30, 0.0).astype(np.float32)
    gt = np.zeros((T, 32), np.float32)
    for gg in range(T // 64):
        gt[64 * gg, gg] = 1.0
    return {
        "wq": slab(w["Wq"], NH),
        "wkv_down": slab(w["Wkv_down"], NKV),
        "wk_up": slab(w["Wk_up"], NKV),
        "wv_up": np.ascontiguousarray(np.asarray(w["Wv_up"], np.float32)),
        "wo": np.ascontiguousarray(np.asarray(w["Wo"], np.float32)),
        "cos_fm": np.ascontiguousarray(cos.T).astype(ml_dtypes.bfloat16),
        "sin_fm": np.ascontiguousarray(sin.T).astype(ml_dtypes.bfloat16),
        "prot": prot_P.T.astype(ml_dtypes.bfloat16),
        "band_bias": bias,
        "gt": gt.astype(ml_dtypes.bfloat16),
        "ident": np.eye(P, dtype=ml_dtypes.bfloat16),
    }


def declare_l0attn_inputs(nc, T, pfx="i_"):
    D, NC, NH, NKV = 2048, 16, 16, 4
    f32 = {"wq": [NH, P, NC * P], "wkv_down": [NKV, P, NC * P], "wk_up": [NKV, P, NKV * P],
           "wv_up": [NKV * P, NKV * P], "wo": [D, D], "band_bias": [P, 384]}
    bf = {"cos_fm": [P, T], "sin_fm": [P, T], "prot": [P, P], "gt": [T, 32],
          "ident": [P, P]}
    win = {k: nc.dram_tensor(pfx + k, sh, F32, kind="ExternalInput").ap()
           for k, sh in f32.items()}
    win.update({k: nc.dram_tensor(pfx + k, sh, BF16, kind="ExternalInput").ap()
                for k, sh in bf.items()})
    return win



def emit_attn_qt(p, NT, NG, scale, pp, pt2, ident, nm4, probs, pbT, vh, ctxh,
                 mm_first, mm_last):
    sesum_prev = {}
    ptm_bank = {ng: p.cnt.get("ptm", 0) for ng in range(NG)}
    pv_tgt = p.cnt.get("pv", 0)
    CPB = 512 // P
    for qt in range(NT):
        smt = {}
        for ng in range(NG):
            if sesum_prev.get(ng) is not None:
                p.wait("tensor", "ps", sesum_prev[ng])
            if mm_first is not None:
                mm_first(qt, ng)
            smt[ng] = mm_last(qt, ng)
        p.wait("scalar", "pv", pv_tgt)
        sesum = {}
        for ng in range(NG):
            p.wait("scalar", "sxm", smt[ng])
            p.wait("scalar", "ptm", ptm_bank[ng])
            p.wait("scalar", "ps")
            sesum[ng] = p.on("scalar", lambda e, ng=ng, scale=scale: e.activation(
                probs[:, ng * 512:(ng + 1) * 512], pp[ng][:], AF.Exp,
                scale=scale, bias=0.0, accum_out=nm4[:, ng:ng + 1]),
                inc=("ps", 1))
            for bb in range(CPB):
                b = ng * CPB + bb
                j = p.cnt.get("ptm", 0) % 2
                if bb == 0:
                    p.wait("tensor", "ps", sesum[ng])
                if p.cnt.get("ptm", 0) >= 2:
                    p.wait("tensor", "ptv", p.cnt["ptv"] - 1)
                p.on("tensor", lambda e, b=b, j=j: e.transpose(
                    pt2[j][:, 0:P], probs[:, b * P:(b + 1) * P], ident[:]),
                    inc=("ptm", 1))
                p.wait("vector", "ptm")
                if p.cnt.get("pvm", 0) >= 2:
                    p.wait("vector", "pvm", p.cnt["pvm"] - 1)
                p.on("vector", lambda e, j=j: e.tensor_copy(
                    pbT[j][:], pt2[j][:, 0:P]), inc=("ptv", 1))
                p.wait("tensor", "ptv")
                if b == 0:
                    p.wait("tensor", "ctv", p.cnt.get("ctv", 0))
                p.on("tensor", lambda e, j=j, b=b: e.matmul(
                    pp[5][:, 0:P], pbT[j][:], vh[:, b * P:(b + 1) * P],
                    start=(b == 0), stop=(b == NT - 1)), inc=("pvm", 1))
            ptm_bank[ng] = p.cnt["ptm"]
        sesum_prev = dict(sesum)
        p.wait("vector", "ps", sesum[NG - 1])
        p.on("vector", lambda e, NG=NG: e.reduce_sum(
            nm4[:, 6:7], nm4[:, 0:NG], axis=AX.X), inc=("pv", 1))
        pv_tgt = p.cnt["pv"]
        p.wait("vector", "pv")
        p.on("vector", lambda e: e.reciprocal(nm4[:, 7:8], nm4[:, 6:7]), inc=("expd", 1))
        p.wait("vector", "pvm")
        p.wait("vector", "expd")
        p.on("vector", lambda e, qt=qt: e.tensor_scalar(
            ctxh[:, qt * P:(qt + 1) * P], pp[5][:, 0:P], nm4[:, 7:8], None,
            ALU.mult, ALU.bypass), inc=("ctv", 1))



def build_l1attn(p, T, x_in, x_out, win, nseq=2, dbg=None):
    nc = p.nc
    D = 2048
    NT = T // 128
    assert NT % 4 == 0
    NC = 16
    H = 16
    QR, KVR = 1536, 512
    NQ = QR // 128
    NKVC = KVR // 128
    EPS_LN, EPS_RMS = 1e-5, 1e-6
    SCLM = 1.0 / (192.0 ** 0.5)
    SCLP = 1.0 / (128.0 ** 0.5)

    wqa_bf = nc.dram_tensor("m_wqa", [NQ, P, NC * P], BF16, kind="Internal").ap()
    wkva_bf = nc.dram_tensor("m_wkva", [NQ, P, NC * P], BF16, kind="Internal").ap()
    qbn_bf = nc.dram_tensor("m_qbn", [H, P, NQ * P], BF16, kind="Internal").ap()
    qbp_bf = nc.dram_tensor("m_qbp", [H, P, NQ * 64], BF16, kind="Internal").ap()
    kvbk_bf = nc.dram_tensor("m_kvbk", [H, P, NKVC * P], BF16, kind="Internal").ap()
    kvbv_bf = nc.dram_tensor("m_kvbv", [H, P, NKVC * P], BF16, kind="Internal").ap()
    ow_bf = nc.dram_tensor("m_ow", [D, D], BF16, kind="Internal").ap()
    spq_bf = nc.dram_tensor("m_spq", [H, P, NC * P], BF16, kind="Internal").ap()
    spk_bf = nc.dram_tensor("m_spk", [H, P, NC * P], BF16, kind="Internal").ap()
    spv_bf = nc.dram_tensor("m_spv", [H, P, NC * P], BF16, kind="Internal").ap()
    spo_bf = nc.dram_tensor("m_spo", [D, D], BF16, kind="Internal").ap()
    gw_bf = nc.dram_tensor("m_gw", [D, 2], BF16, kind="Internal").ap()
    ln_dram = nc.dram_tensor("m_ln", [P, NC * T], BF16, kind="Internal").ap()
    qm_dram = nc.dram_tensor("m_qm", [NQ, P, T], BF16, kind="Internal").ap()
    ctx_dram = nc.dram_tensor("m_ctx", [T, D], BF16, kind="Internal").ap()
    mo_dram = nc.dram_tensor("m_mo", [T, D], BF16, kind="Internal").ap()

    ident = p.sbuf("m_ident", [P, P], BF16)
    cst = p.sbuf("m_cst", [P, 2], F32)
    ones1b = p.sbuf("m_ones1b", [1, P], BF16)
    onescol = p.sbuf("m_onescol", [P, 1], BF16)
    prot1 = p.sbuf("m_prot1", [64, 64], BF16)
    sup = p.sbuf("m_sup", [P, 64], BF16)
    sdn = p.sbuf("m_sdn", [P, 64], BF16)
    cos64 = p.sbuf("m_cos64", [P, T], BF16)
    sin64 = p.sbuf("m_sin64", [P, T], BF16)
    lnw_q = p.sbuf("m_lnw_q", [P, NQ], F32)
    lnw_kv = p.sbuf("m_lnw_kv", [P, NKVC], F32)
    gb = p.sbuf("m_gb", [P, 2], F32)

    big0 = p.sbuf("m_big0", [P, NC * T], BF16)
    kvaf = p.sbuf("m_kvaf", [P, NQ * T], BF16)
    cbuf = p.sbuf("m_cbuf", [P, T], BF16)
    rb_fm = p.sbuf("m_rb", [P, max(T, 2048)], BF16)
    r2b = p.sbuf("m_r2b", [1, 512], BF16)
    kn = p.sbuf("m_kn", [P, D], BF16)
    mob = [kn, kn]
    vh = p.sbuf("m_vh", [P, NT * P], BF16)
    qn = p.sbuf("m_qn", [P, T], BF16)
    qpb = p.sbuf("m_qpb", [64, T], BF16)
    ts1 = p.sbuf("m_ts1", [P, D], BF16)
    kpe_h = ts1[0:64, 0:T] if False else ts1
    probs = p.sbuf("m_probs", [P, T], BF16)
    kpraw = probs
    sqs = probs[:, 0:512]
    pbT = [p.sbuf(f"m_pbT{i}", [P, P], BF16) for i in range(2)]
    rp2 = [p.sbuf(f"m_rp{i}", [64, 512], BF16) for i in range(2)]
    ctxh = p.sbuf("m_ctxh", [P, NT * P], BF16)
    nm4 = p.sbuf("m_nm4", [P, 8], F32)
    g0s = p.sbuf("m_g0s", [P, NT], F32)
    g1s = p.sbuf("m_g1s", [P, NT], F32)
    gds = p.sbuf("m_gds", [P, NT], F32)
    s_var = gds

    garing = [p.sbuf(f"m_garing{i}", [P, NC * P], BF16) for i in range(2)]
    wring = [p.sbuf(f"m_wring{i}", [P, 512], BF16) for i in range(2)]
    qmring = wring

    xb = [p.sbuf(f"m_xb{i}", [P, D], F32) for i in range(2)]
    xob = p.sbuf("m_xob", [P, D], F32)
    rstg = xob
    _yb = p.sbuf("m_yb", [P, D], BF16)
    yb = [_yb, _yb]
    db = [rb_fm, kvaf[:, 3 * T:3 * T + D]] + \
        [p.sbuf(f"m_db{i}", [P, D], BF16) for i in range(2)]

    s_sum = p.sbuf("m_s_sum", [P, NT], F32)
    s_sq = p.sbuf("m_s_sq", [P, NT], F32)
    s_m = p.sbuf("m_s_m", [P, NT], F32)
    s_rstd = p.sbuf("m_s_rstd", [P, NT], F32)
    s_nmr = p.sbuf("m_s_nmr", [P, NT], F32)
    o_sum = s_sum
    o_sq = s_sq
    o_m = nm4
    o_var = p.sbuf("m_o_var", [P, 4], F32)
    o_rstd = p.sbuf("m_o_rstd", [P, 4], F32)
    o_nmr = p.sbuf("m_o_nmr", [P, 4], F32)

    pp = [p.psum(f"m_pp{i}", [P, 512], F32) for i in range(6)]
    pt2 = [p.psum(f"m_pt{i}", [P, 512], BF16) for i in range(2)]

    p.on("vector", lambda e: e.memset(cst[:, 0:1], EPS_LN), inc=("vprep", 1))
    p.on("vector", lambda e: e.memset(cst[:, 1:2], EPS_RMS), inc=("vprep", 1))
    p.on("vector", lambda e: e.memset(ones1b[:], 1.0), inc=("vprep", 1))
    p.on("vector", lambda e: e.memset(onescol[:], 1.0), inc=("vprep", 1))
    p.wait("scalar", "vprep")
    p.wait("tensor", "vprep")
    for dst, key in [(ident, "ident"), (prot1, "prot1"), (sup, "sup"), (sdn, "sdn"),
                     (cos64, "cos64"), (sin64, "sin64")]:
        p.dma("sync", dst[:], win[key], sname="cload")
    p.dma("sync", lnw_q[:], win["lnw_q"], sname="cload")
    p.dma("sync", lnw_kv[:], win["lnw_kv"], sname="cload")
    p.dma("sync", gb[:], win["gate_b"].partition_broadcast(P), sname="cload")
    for (dst, src) in [(wqa_bf, win["q_a"]), (wkva_bf, win["kv_a"]), (qbn_bf, win["qbn"]),
                       (qbp_bf, win["qbp"]), (kvbk_bf, win["kvbk"]), (kvbv_bf, win["kvbv"]),
                       (ow_bf, win["ow"]), (spq_bf, win["spq"]), (spk_bf, win["spk"]),
                       (spv_bf, win["spv"]), (spo_bf, win["spo"]), (gw_bf, win["gw"])]:
        p.dma("gpsimd", dst, src, sname="wcast")
    wcast_done = p.cnt["wcast"]
    cload_done = p.cnt["cload"]
    p.wait("tensor", "cload", cload_done)
    p.wait("tensor", "wcast", wcast_done)
    p.wait("gpsimd", "wcast", wcast_done)
    p.wait("vector", "cload", cload_done)

    def dbg_tap(slot, src_ap, waits, barrier=True):
        if dbg is None:
            return
        for sn, v in waits:
            p.wait("gpsimd", sn, v)
        dst = dbg[slot]
        if tuple(dst.shape) != tuple(src_ap.shape):
            dst = dst[0:src_ap.shape[0], 0:src_ap.shape[1]]
        p.dma("gpsimd", dst, src_ap, sname=f"dbgst{slot}")
        if barrier:
            for eng in ("vector", "scalar", "tensor", "sync"):
                p.wait(eng, f"dbgst{slot}")

    NG = T // 512
    b3_vv = {}
    b3_sc = {}
    b3_mo = {}
    ps_t = {}
    pv_t = {}

    def fm_rms(src_fn, nchunk, denom, lnw_col_fn, dst_fn, eps_col, tag):
        for ng in range(NG):
            for j in range(nchunk):
                p.wait("scalar", "kvav", p.cnt.get("kvav", 0))
                p.wait("scalar", "ps")
                pst = p.on("scalar", lambda e, j=j, ng=ng: e.activation(
                    sqs[:], src_fn(j)[:, ng * 512:(ng + 1) * 512], AF.Square),
                    inc=("ps", 1))
                p.wait("tensor", "ps", pst)
                if j == 0 and p.cnt.get(tag + "r", 0) >= 1:
                    p.wait("tensor", tag + "r", p.cnt[tag + "r"])
                p.on("tensor", lambda e, j=j, nchunk=nchunk: e.matmul(
                    pp[2][0:1, :], onescol[:], sqs[:],
                    start=(j == 0), stop=(j == nchunk - 1)), inc=("ssm", 1))
                p.wait("scalar", "ssm")
                p.wait("scalar", "ps")
            p.wait("scalar", tag + "r", p.cnt.get(tag + "r", 0))
            s1 = p.on("scalar", lambda e, denom=denom, eps_col=eps_col: e.activation(
                rstg[0:1, 0:512], pp[2][0:1, :], AF.Ln, scale=1.0 / denom,
                bias=eps_col[0:1, :]), inc=("sts", 1))
            p.wait("scalar", "sts", s1)
            p.wait("scalar", "ssb", p.cnt.get("ssb", 0))
            s2 = p.on("scalar", lambda e, ng=ng: e.activation(
                r2b[0:1, 0:512], rstg[0:1, 0:512], AF.Exp, scale=-0.5),
                inc=(tag + "r", 1))
            p.wait("tensor", tag + "r", p.cnt[tag + "r"])
            if p.cnt.get("ssv", 0) >= 1:
                p.wait("tensor", "ssv", p.cnt["ssv"])
            p.on("tensor", lambda e, ng=ng: e.matmul(
                pp[3][:], ones1b[:], r2b[0:1, 0:512],
                start=True, stop=True), inc=("ssb", 1))
            p.wait("vector", "ssb")
            p.on("vector", lambda e, ng=ng: e.tensor_copy(
                rb_fm[:, ng * 512:(ng + 1) * 512], pp[3][:]), inc=("ssv", 1))
        p.wait("vector", "ssv")
        p.wait("tensor", "ssv")
        for j in range(nchunk):
            for ng in range(NG):
                p.wait("vector", tag + "a")
                p.on("vector", lambda e, j=j, ng=ng: e.scalar_tensor_tensor(
                    dst_fn(j)[:, ng * 512:(ng + 1) * 512],
                    src_fn(j)[:, ng * 512:(ng + 1) * 512],
                    lnw_col_fn(j), rb_fm[:, ng * 512:(ng + 1) * 512],
                    ALU.mult, ALU.mult), inc=(tag + "a", 1))

    for seq in range(nseq):
        xs = x_in[seq]
        xo = x_out[seq]

        for tt in range(NT):
            if tt >= 2:
                p.wait("sync", "pv", pv_t[tt - 2])
                p.wait("sync", "ps", ps_t[tt - 2])
            ld = p.dma("sync", xb[tt % 2][:], xs[tt * P:(tt + 1) * P, :],
                       sname=f"xld{tt % 2}")
            p.wait("vector", f"xld{tt % 2}", ld)
            p.wait("vector", "pv")
            pv_t[tt] = p.on("vector", lambda e, tt=tt: e.reduce_sum(
                s_sum[:, tt:tt + 1], xb[tt % 2][:], axis=AX.X), inc=("pv", 1))
            p.wait("scalar", f"xld{tt % 2}", ld)
            p.wait("scalar", "ps")
            ps_t[tt] = p.on("scalar", lambda e, tt=tt: e.activation(
                yb[0][:], xb[tt % 2][:], AF.Square, accum_out=s_sq[:, tt:tt + 1]),
                inc=("ps", 1))
        p.wait("vector", "ps")
        vln, sln = emit_ln_stats_math(p, s_sum[:], s_sq[:], s_m[:], s_var[:],
                                      s_rstd[:], s_nmr[:], D, cst[:, 0:1])

        a0_sc = {}
        a0_tr = {}
        for tt in range(NT):
            i = tt % 2
            if tt >= 2:
                p.wait("sync", "ps", a0_sc[tt - 2])
            if tt == 0:
                p.wait("sync", "ps", ps_t[NT - 1])
                p.wait("sync", "pv", pv_t[NT - 1])
                p.wait("scalar", "q2a", p.cnt.get("q2a", 0))
                p.wait("scalar", "spm", p.cnt.get("spm", 0))
            ld = p.dma("sync", xb[i][:], xs[tt * P:(tt + 1) * P, :], sname=f"xld{i}")
            p.wait("scalar", f"xld{i}", ld)
            if tt == 0:
                p.wait("scalar", "stv", vln)
                p.wait("scalar", "sts", sln)
            if tt >= 1:
                p.wait("scalar", "trT", a0_tr[tt - 1])
            a0_sc[tt] = p.on("scalar", lambda e, i=i, tt=tt: e.activation(
                yb[0][:], xb[i][:], AF.Identity,
                bias=s_nmr[:, tt:tt + 1], scale=s_rstd[:, tt:tt + 1]), inc=("ps", 1))
            p.wait("tensor", "ps", a0_sc[tt])
            if tt == 0:
                p.wait("tensor", "q2a", p.cnt.get("q2a", 0))
                p.wait("tensor", "spm", p.cnt.get("spm", 0))
            emit_transpose_tiles(p, [(lambda c: yb[0][:, c * P:(c + 1) * P])],
                                 big0, pt2, ident, list(range(NC)), T, coloff=tt * P)
            a0_tr[tt] = p.cnt["trT"]

        if True:
            gwt = garing[0]
            p.wait("gpsimd", "qau", p.cnt.get("qau", 0))
            p.wait("gpsimd", "qmu", p.cnt.get("qmu", 0))
            p.wait("gpsimd", "a1m", p.cnt.get("a1m", 0))
            p.wait("gpsimd", "a3m", p.cnt.get("a3m", 0))
            ld = p.dma("gpsimd", gwt[:, 0:2 * NC].rearrange("p (k c) -> p k c", c=2),
                       gw_bf.rearrange("(k p) c -> p k c", p=P), sname="gtl")
            gtl_done = p.cnt["gtl"]
            p.wait("tensor", "gtl", gtl_done)
            for tt in range(NT):
                gi = p.cnt.get("gtv", 0)
                if gi >= 2:
                    p.wait("tensor", "gtv", gi - 1)
                for k in range(NC):
                    p.on("tensor", lambda e, tt=tt, k=k, gi=gi: e.matmul(
                        pp[4 + gi % 2][:, 0:2], big0[:, k * T + tt * P:k * T + (tt + 1) * P],
                        gwt[:, 2 * k:2 * k + 2], start=(k == 0), stop=(k == NC - 1)),
                        inc=("gtm2", 1) if k == NC - 1 else None)
                p.wait("vector", "gtm2")
                p.wait("vector", "gtv")
                p.on("vector", lambda e, tt=tt, gi=gi: e.tensor_tensor(
                    nm4[:, 4:6], pp[4 + gi % 2][:, 0:2], gb[:, 0:2], ALU.add),
                    inc=("pv", 1))
                p.wait("vector", "pv")
                p.on("vector", lambda e, tt=tt: e.tensor_sub(
                    gds[:, tt:tt + 1], nm4[:, 4:5], nm4[:, 5:6]), inc=("gtv", 1))
            gv = p.cnt["gtv"]
            p.wait("scalar", "gtv", gv)
            sg1 = p.on("scalar", lambda e: e.activation(g0s[:], gds[:], AF.Sigmoid),
                       inc=("sts", 1))
            p.wait("vector", "sts", sg1)
            p.on("vector", lambda e: e.tensor_scalar(g1s[:], g0s[:], -1.0, 1.0,
                                                     ALU.mult, ALU.add), inc=("gsv", 1))

        for m in range(NQ):
            slot = m % 2
            if m < 2:
                p.wait("gpsimd", "kvau", p.cnt.get("kvau", 0))
                p.wait("gpsimd", "qau", p.cnt.get("qau", 0))
                p.wait("gpsimd", "gtm2", p.cnt.get("gtm2", 0))
            else:
                p.wait("gpsimd", "kvau", p.cnt["kvau"] - NG)
            ld = p.dma("gpsimd", garing[slot][:], wkva_bf[m], sname=f"gal{slot}")
            p.wait("tensor", f"gal{slot}", ld)
            for ng in range(NG):
                gidx = p.cnt.get("kvav", 0)
                bank = gidx % 2
                if gidx >= 2:
                    p.wait("tensor", "kvav", gidx - 1)
                for k in range(NC):
                    p.on("tensor", lambda e, slot=slot, k=k, ng=ng, bank=bank: e.matmul(
                        pp[bank][:], garing[slot][:, k * P:(k + 1) * P],
                        big0[:, k * T + ng * 512:k * T + (ng + 1) * 512],
                        start=(k == 0), stop=(k == NC - 1)),
                        inc=("kvau", 1) if k == NC - 1 else None)
                p.wait("vector", "kvau")
                p.on("vector", lambda e, m=m, ng=ng, bank=bank: e.tensor_copy(
                    kvaf[:, m * T + ng * 512:m * T + (ng + 1) * 512], pp[bank][:]),
                    inc=("kvav", 1))
        p.wait("tensor", "kvav")

        cdst = [cbuf] + [None, None, None]

        def csrc(j):
            return kvaf[:, j * T:(j + 1) * T]

        def cget(j):
            if j == 0:
                return cbuf[:]
            return kvaf[:, (j - 1) * T:j * T]

        fm_rms(lambda j: kvaf[:, j * T:(j + 1) * T], NKVC, KVR,
               lambda j: lnw_kv[:, j:j + 1], lambda j: cget(j), cst[:, 1:2], "ckv")
        ckva = p.cnt["ckva"]
        p.wait("tensor", "ckva", ckva)

        for m in range(NQ):
            slot = m % 2
            if m < 2:
                p.wait("gpsimd", "kvau", p.cnt["kvau"])
                p.wait("gpsimd", "ssm", p.cnt.get("ssm", 0))
            else:
                p.wait("gpsimd", "qau", p.cnt["qau"] - NG)
            ld = p.dma("gpsimd", garing[slot][:], wqa_bf[m], sname=f"gal{slot}")
            p.wait("tensor", f"gal{slot}", ld)
            for ng in range(NG):
                gidx = p.cnt.get("qav", 0)
                bank = gidx % 2
                if gidx >= 2:
                    p.wait("tensor", "qav", gidx - 1)
                    p.wait("tensor", "ps")
                for k in range(NC):
                    p.on("tensor", lambda e, slot=slot, k=k, ng=ng, bank=bank: e.matmul(
                        pp[bank][:], garing[slot][:, k * P:(k + 1) * P],
                        big0[:, k * T + ng * 512:k * T + (ng + 1) * 512],
                        start=(k == 0), stop=(k == NC - 1)),
                        inc=("qau", 1) if k == NC - 1 else None)
                p.wait("vector", "qau")
                if p.cnt.get(f"qms{bank}", 0) >= 16:
                    p.wait("vector", f"qms{bank}", p.cnt[f"qms{bank}"])
                p.on("vector", lambda e, ng=ng, bank=bank: e.tensor_copy(
                    qmring[bank][:], pp[bank][:]), inc=("qav", 1))
                p.wait("gpsimd", "qav", p.cnt["qav"])
                p.dma("gpsimd", qm_dram[m, :, ng * 512:(ng + 1) * 512], qmring[bank][:],
                      sname=f"qms{bank}")
                p.wait("scalar", "qav", p.cnt["qav"])
                p.wait("scalar", "ps")
                pst = p.on("scalar", lambda e, bank=bank: e.activation(
                    sqs[:], pp[bank][:], AF.Square), inc=("ps", 1))
                p.wait("tensor", "ps", pst)
                p.on("tensor", lambda e, m=m, ng=ng: e.matmul(
                    pp[2 + ng][0:1, :], onescol[:], sqs[:],
                    start=(m == 0), stop=(m == NQ - 1)), inc=("ssm", 1))
                p.wait("scalar", "ssm")

        for ng in range(NG):
            p.wait("scalar", "ckvr", p.cnt.get("ckvr", 0))
            s1 = p.on("scalar", lambda e, ng=ng: e.activation(
                rstg[0:1, 0:512], pp[2 + ng][0:1, :], AF.Ln, scale=1.0 / QR,
                bias=cst[0:1, 1:2]), inc=("sts", 1))
            p.wait("scalar", "sts", s1)
            p.wait("scalar", "ssb", p.cnt.get("ssb", 0))
            s2 = p.on("scalar", lambda e, ng=ng: e.activation(
                r2b[0:1, 0:512], rstg[0:1, 0:512], AF.Exp, scale=-0.5),
                inc=("ckvr", 1))
            p.wait("tensor", "ckvr", p.cnt["ckvr"])
            p.wait("tensor", "ckva", ckva)
            p.wait("tensor", "qav", p.cnt["qav"])
            if p.cnt.get("ssv", 0) >= 1:
                p.wait("tensor", "ssv", p.cnt["ssv"])
            p.on("tensor", lambda e, ng=ng: e.matmul(
                pp[0][:], ones1b[:], r2b[0:1, 0:512],
                start=True, stop=True), inc=("ssb", 1))
            p.wait("vector", "ssb")
            p.wait("vector", "ckva", ckva)
            p.on("vector", lambda e, ng=ng: e.tensor_copy(
                rb_fm[:, ng * 512:(ng + 1) * 512], pp[0][:]), inc=("ssv", 1))
        p.wait("vector", "ssv")

        p.wait("gpsimd", "qau", p.cnt["qau"])
        p.wait("gpsimd", "gtm2", p.cnt.get("gtm2", 0))
        p.dma("gpsimd", ln_dram, big0[:], sname="lnsp")
        lnsp = p.cnt["lnsp"]
        p.wait("sync", "qms0", p.cnt.get("qms0", 0))
        p.wait("sync", "qms1", p.cnt.get("qms1", 0))
        p.wait("sync", "lnsp", lnsp)
        for m in range(NQ):
            for ng in range(NG):
                qi = p.cnt.get("q2a", 0)
                slot = qi % 2
                if qi >= 2:
                    p.wait("sync", "q2a", qi - 1)
                ld = p.dma("sync", qmring[slot][:], qm_dram[m, :, ng * 512:(ng + 1) * 512],
                           sname=f"q2l{slot}")
                p.wait("vector", f"q2l{slot}", ld)
                p.wait("vector", "ssv")
                p.on("vector", lambda e, m=m, ng=ng, slot=slot: e.scalar_tensor_tensor(
                    big0[:, m * T + ng * 512:m * T + (ng + 1) * 512], qmring[slot][:],
                    lnw_q[:, m:m + 1], rb_fm[:, ng * 512:(ng + 1) * 512],
                    ALU.mult, ALU.mult), inc=("q2a", 1))
        q2a_done = p.cnt["q2a"]

        for h in range(H):
            hs = 0
            if h == 0:
                p.wait("gpsimd", "kvav", p.cnt["kvav"])
                p.wait("gpsimd", "qau", p.cnt["qau"])
            else:
                p.wait("gpsimd", "qmu", p.cnt["qmu"])
                p.wait("gpsimd", "knm", p.cnt["knm"])
                p.wait("gpsimd", "vhm", p.cnt["vhm"])
                p.wait("gpsimd", "a2m", p.cnt["a2m"])
            ld1 = p.dma("gpsimd", garing[0][:, 0:NQ * P], qbn_bf[h], sname="hbl0")
            ld2 = p.dma("gpsimd", garing[1][:, 0:NQ * 64], qbp_bf[h], sname="hbl1")
            ld3 = p.dma("gpsimd", garing[1][:, NQ * 64:NQ * 64 + NKVC * P],
                        kvbk_bf[h], sname="hbl1")
            ld4 = p.dma("gpsimd", garing[1][:, NQ * 64 + NKVC * P:NQ * 64 + 2 * NKVC * P],
                        kvbv_bf[h], sname="hbl1")
            p.wait("tensor", "hbl0", p.cnt["hbl0"])
            p.wait("tensor", "hbl1", p.cnt["hbl1"])
            kboff, vboff = NQ * 64, NQ * 64 + NKVC * P
            if h == 0:
                p.wait("tensor", "q2a", q2a_done)

            for ng in range(NG):
                if p.cnt.get("knv", 0) >= 1:
                    p.wait("tensor", "knv", p.cnt["knv"])
                for k in range(NKVC):
                    p.on("tensor", lambda e, k=k, ng=ng, kboff=kboff: e.matmul(
                        pp[0][:], garing[1][:, kboff + k * P:kboff + (k + 1) * P],
                        cget(k)[:, ng * 512:(ng + 1) * 512],
                        start=(k == 0), stop=(k == NKVC - 1)),
                        inc=("knm", 1) if k == NKVC - 1 else None)
                p.wait("vector", "knm")
                p.wait("vector", "ps")
                p.on("vector", lambda e, ng=ng: e.tensor_copy(
                    kn[:, ng * 512:(ng + 1) * 512], pp[0][:]), inc=("knv", 1))
            for tt in range(NT):
                if p.cnt.get("vhv", 0) >= 1:
                    p.wait("tensor", "vhv", p.cnt["vhv"])
                for k in range(NKVC):
                    p.on("tensor", lambda e, k=k, tt=tt, vboff=vboff: e.matmul(
                        pp[1][:, 0:P], cget(k)[:, tt * P:(tt + 1) * P],
                        garing[1][:, vboff + k * P:vboff + (k + 1) * P],
                        start=(k == 0), stop=(k == NKVC - 1)),
                        inc=("vhm", 1) if k == NKVC - 1 else None)
                p.wait("vector", "vhm")
                p.on("vector", lambda e, tt=tt: e.tensor_copy(
                    vh[:, tt * P:(tt + 1) * P], pp[1][:, 0:P]), inc=("vhv", 1))
            kchunk = 4 + h // 2
            shmat = sup if h % 2 == 0 else sdn
            p.wait("tensor", "knv", p.cnt.get("knv", 0))
            p.wait("tensor", "vhv", p.cnt.get("vhv", 0))
            for ng in range(NG):
                if p.cnt.get("a2v", 0) >= 1:
                    p.wait("tensor", "a2v", p.cnt["a2v"])
                p.on("tensor", lambda e, kchunk=kchunk, ng=ng, shmat=shmat: e.matmul(
                    pp[0][0:64, :], shmat[:],
                    kvaf[:, kchunk * T + ng * 512:kchunk * T + (ng + 1) * 512],
                    start=True, stop=True), inc=("a2m", 1))
                p.wait("vector", "a2m")
                p.on("vector", lambda e, ng=ng: e.tensor_copy(
                    kpraw[0:64, ng * 512:(ng + 1) * 512], pp[0][0:64, :]),
                    inc=("a2v", 1))
                p.wait("tensor", "a2v")
                p.on("tensor", lambda e, ng=ng: e.matmul(
                    pp[1][0:64, :], prot1[:], kpraw[0:64, ng * 512:(ng + 1) * 512],
                    start=True, stop=True), inc=("a2m", 1))
                p.wait("vector", "a2m")
                p.wait("vector", "rpv")
                p.on("vector", lambda e, ng=ng: e.tensor_mul(
                    rp2[0][:], kpraw[0:64, ng * 512:(ng + 1) * 512],
                    cos64[0:64, ng * 512:(ng + 1) * 512]), inc=("rpv", 1))
                p.on("vector", lambda e, ng=ng: e.tensor_mul(
                    rp2[1][:], pp[1][0:64, :],
                    sin64[0:64, ng * 512:(ng + 1) * 512]), inc=("rpv", 1))
                p.wait("vector", "rpv")
                p.on("vector", lambda e, ng=ng: e.tensor_add(
                    kpe_h[0:64, ng * 512:(ng + 1) * 512], rp2[0][:],
                    rp2[1][:]), inc=("a2v", 1))
            p.wait("tensor", "a2v", p.cnt.get("a2v", 0))
            for ng in range(NG):
                if p.cnt.get("qmv", 0) >= 1:
                    p.wait("tensor", "qmv", p.cnt["qmv"])
                for k in range(NQ):
                    p.on("tensor", lambda e, k=k, ng=ng: e.matmul(
                        pp[0][:], garing[0][:, k * P:(k + 1) * P],
                        big0[:, k * T + ng * 512:k * T + (ng + 1) * 512],
                        start=(k == 0), stop=(k == NQ - 1)),
                        inc=("qmu", 1) if k == NQ - 1 else None)
                p.wait("vector", "qmu")
                p.on("vector", lambda e, ng=ng: e.tensor_copy(
                    qn[:, ng * 512:(ng + 1) * 512], pp[0][:]), inc=("qmv", 1))
                for k in range(NQ):
                    p.on("tensor", lambda e, k=k, ng=ng: e.matmul(
                        pp[1][0:64, :], garing[1][:, k * 64:(k + 1) * 64],
                        big0[:, k * T + ng * 512:k * T + (ng + 1) * 512],
                        start=(k == 0), stop=(k == NQ - 1)),
                        inc=("qmu", 1) if k == NQ - 1 else None)
                p.wait("vector", "qmu")
                p.on("vector", lambda e, ng=ng: e.tensor_copy(
                    kpraw[0:64, ng * 512:(ng + 1) * 512], pp[1][0:64, :]), inc=("qmv", 1))
                p.wait("tensor", "qmv")
                p.on("tensor", lambda e, ng=ng: e.matmul(
                    pp[1][0:64, :], prot1[:], kpraw[0:64, ng * 512:(ng + 1) * 512],
                    start=True, stop=True), inc=("qpm", 1))
                p.wait("vector", "qpm")
                p.wait("vector", "rpv")
                p.on("vector", lambda e, ng=ng: e.tensor_mul(
                    rp2[0][:], kpraw[0:64, ng * 512:(ng + 1) * 512],
                    cos64[0:64, ng * 512:(ng + 1) * 512]), inc=("rpv", 1))
                p.on("vector", lambda e, ng=ng: e.tensor_mul(
                    rp2[1][:], pp[1][0:64, :],
                    sin64[0:64, ng * 512:(ng + 1) * 512]), inc=("rpv", 1))
                p.wait("vector", "rpv")
                p.on("vector", lambda e, ng=ng: e.tensor_add(
                    qpb[0:64, ng * 512:(ng + 1) * 512], rp2[0][:],
                    rp2[1][:]), inc=("qmv", 1))
            p.wait("tensor", "qmv")
            p.wait("tensor", "a2v")

            emit_attn_qt(p, NT, NG, SCLM, pp, pt2, ident, nm4, probs, pbT, vh, ctxh,
                         lambda qt, ng: p.on("tensor",
                             lambda e, qt=qt, ng=ng: e.matmul(
                                 pp[ng][:], qn[:, qt * P:(qt + 1) * P],
                                 kn[:, ng * 512:(ng + 1) * 512], start=True, stop=False)),
                         lambda qt, ng: p.on("tensor",
                             lambda e, qt=qt, ng=ng: e.matmul(
                                 pp[ng][:], qpb[0:64, qt * P:(qt + 1) * P],
                                 kpe_h[0:64, ng * 512:(ng + 1) * 512],
                                 start=False, stop=True), inc=("sxm", 1)))
            p.wait("gpsimd", "qmu", p.cnt["qmu"])
            p.on("gpsimd", lambda e: e.engine_nop(), inc=("hwu", 1))
            p.wait("gpsimd", "ctv", p.cnt["ctv"])
            p.dma("gpsimd",
                  ctx_dram[:, h * P:(h + 1) * P].rearrange("(q p) c -> p q c", p=P),
                  ctxh[:].rearrange("p (q c) -> p q c", c=P), sname="cxs")
            p.wait("vector", "cxs", p.cnt["cxs"])

        cxf = big0
        for tgc in range(NT // 4):
            tts = [tgc * 4 + i for i in range(4)]
            if tgc == 0:
                p.wait("sync", "cxs", p.cnt["cxs"])
                p.wait("sync", "knm", p.cnt["knm"])
                p.wait("sync", "sxm", p.cnt["sxm"])
                p.wait("sync", "q2a", q2a_done)
            for i, tt in enumerate(tts):
                if b3_sc.get(("c", i)) is not None:
                    p.wait("sync", f"mov{i}", b3_sc[("c", i)])
                ld = p.dma("sync", db[i][:], ctx_dram[tt * P:(tt + 1) * P, :],
                           sname=f"cxl{i}")
                p.wait("tensor", f"cxl{i}", ld)
            emit_transpose_tiles(p, [(lambda c, i=i: db[i][:, c * P:(c + 1) * P])
                                     for i in range(4)], cxf, pt2, ident,
                                 list(range(NC)), 512)
            for ng in range(4):
                for k in range(NC):
                    wu = p.cnt.get("wou", 0)
                    slot = wu % 2
                    if wu >= 2:
                        p.wait("gpsimd", "wou", wu - 1)
                    ld = p.dma("gpsimd", wring[slot][:],
                               ow_bf[k * P:(k + 1) * P, ng * 512:(ng + 1) * 512],
                               sname=f"wol{slot}")
                    p.wait("tensor", f"wol{slot}", ld)
                    if k == 0:
                        p.wait("tensor", "woe")
                    for i in range(4):
                        p.on("tensor", lambda e, k=k, i=i, slot=slot: e.matmul(
                            pp[i][:], cxf[:, k * 512 + i * P:k * 512 + (i + 1) * P],
                            wring[slot][:], start=(k == 0), stop=(k == NC - 1)),
                            inc=("wou", 1) if i == 3 else None)
                p.wait("vector", "wou")
                for i in range(4):
                    p.on("vector", lambda e, ng=ng, i=i: e.tensor_copy(
                        db[i][:, ng * 512:(ng + 1) * 512], pp[i][:]),
                        inc=("woe", 1) if i == 3 else None)
            p.wait("tensor", "woe")
            p.wait("vector", "woe")
            for i, tt in enumerate(tts):
                p.wait("gpsimd", "woe", p.cnt["woe"])
                st = p.dma("gpsimd", mo_dram[tt * P:(tt + 1) * P, :], db[i][:],
                           sname=f"mov{i}")
                b3_sc[("c", i)] = st
            for i in range(4):
                p.wait("sync", f"mov{i}", p.cnt.get(f"mov{i}", 0))

        p.wait("sync", "qmu", p.cnt["qmu"])
        p.wait("sync", "wou", p.cnt.get("wou", 0))
        p.wait("sync", "lnsp", p.cnt["lnsp"])
        p.wait("sync", "gtm2", p.cnt.get("gtm2", 0))
        ld = p.dma("sync", big0[:], ln_dram, sname="lnrl")
        lnrl = p.cnt["lnrl"]
        for h in range(H):
            hs = h % 2
            if h >= 1:
                p.wait("gpsimd", "a1m", p.cnt["a1m"])
            ldq = p.dma("gpsimd", garing[hs][:], spq_bf[h], sname=f"pql{hs}")
            p.wait("tensor", f"pql{hs}", ldq)
            if h == 0:
                p.wait("tensor", "lnrl", lnrl)
                for _mi in range(4):
                    p.wait("tensor", f"mov{_mi}", p.cnt.get(f"mov{_mi}", 0))
            for ng in range(NG):
                if p.cnt.get("a3v", 0) >= 1:
                    p.wait("tensor", "a3v", p.cnt["a3v"])
                for k in range(NC):
                    p.on("tensor", lambda e, hs=hs, k=k, ng=ng: e.matmul(
                        pp[4][:], garing[hs][:, k * P:(k + 1) * P],
                        big0[:, k * T + ng * 512:k * T + (ng + 1) * 512],
                        start=(k == 0), stop=(k == NC - 1)),
                        inc=("a3m", 1) if k == NC - 1 else None)
                p.wait("vector", "a3m")
                p.on("vector", lambda e, ng=ng: e.tensor_copy(
                    qn[:, ng * 512:(ng + 1) * 512], pp[4][:]), inc=("a3v", 1))
            p.wait("gpsimd", "a3m", p.cnt["a3m"])
            ldk = p.dma("gpsimd", garing[hs][:], spk_bf[h], sname=f"pql{hs}")
            p.wait("tensor", f"pql{hs}", ldk)
            p.wait("tensor", "a3v", p.cnt.get("a3v", 0))
            for ng in range(NG):
                if p.cnt.get("a4v", 0) >= 1:
                    p.wait("tensor", "a4v", p.cnt["a4v"])
                for k in range(NC):
                    p.on("tensor", lambda e, hs=hs, k=k, ng=ng: e.matmul(
                        pp[4][:], garing[hs][:, k * P:(k + 1) * P],
                        big0[:, k * T + ng * 512:k * T + (ng + 1) * 512],
                        start=(k == 0), stop=(k == NC - 1)),
                        inc=("a4m", 1) if k == NC - 1 else None)
                p.wait("vector", "a4m")
                p.on("vector", lambda e, ng=ng: e.tensor_copy(
                    kn[:, ng * 512:(ng + 1) * 512], pp[4][:]), inc=("a4v", 1))
            p.wait("gpsimd", "a4m", p.cnt["a4m"])
            ldv = p.dma("gpsimd", garing[hs][:], spv_bf[h], sname=f"pql{hs}")
            p.wait("tensor", f"pql{hs}", ldv)
            for tt in range(NT):
                if p.cnt.get("a1v", 0) >= 1:
                    p.wait("tensor", "a1v", p.cnt["a1v"])
                for k in range(NC):
                    p.on("tensor", lambda e, hs=hs, k=k, tt=tt: e.matmul(
                        pp[5][:, 0:P], big0[:, k * T + tt * P:k * T + (tt + 1) * P],
                        garing[hs][:, k * P:(k + 1) * P],
                        start=(k == 0), stop=(k == NC - 1)),
                        inc=("a1m", 1) if k == NC - 1 else None)
                p.wait("vector", "a1m")
                p.on("vector", lambda e, tt=tt: e.tensor_copy(
                    vh[:, tt * P:(tt + 1) * P], pp[5][:, 0:P]), inc=("a1v", 1))
            p.wait("tensor", "a1v")
            p.wait("tensor", "a4v")
            p.wait("tensor", "a3v")
            emit_attn_qt(p, NT, NG, SCLP, pp, pt2, ident, nm4, probs, pbT, vh, ctxh,
                         None,
                         lambda qt, ng: p.on("tensor",
                             lambda e, qt=qt, ng=ng: e.matmul(
                                 pp[ng][:], qn[:, qt * P:(qt + 1) * P],
                                 kn[:, ng * 512:(ng + 1) * 512], start=True, stop=True),
                             inc=("sxm", 1)))
            p.wait("gpsimd", "ctv", p.cnt["ctv"])
            if seq == 0 and h == 0:
                pass
            p.wait("gpsimd", "cxl0", p.cnt.get("cxl0", 0))
            p.wait("gpsimd", "cxl1", p.cnt.get("cxl1", 0))
            p.wait("gpsimd", "cxl2", p.cnt.get("cxl2", 0))
            p.wait("gpsimd", "cxl3", p.cnt.get("cxl3", 0))
            p.dma("gpsimd",
                  ctx_dram[:, h * P:(h + 1) * P].rearrange("(q p) c -> p q c", p=P),
                  ctxh[:].rearrange("p (q c) -> p q c", c=P), sname="cxs")
            p.wait("vector", "cxs", p.cnt["cxs"])

        for tgc in range(NT // 4):
            tts = [tgc * 4 + i for i in range(4)]
            if tgc == 0:
                p.wait("sync", "cxs", p.cnt["cxs"])
                p.wait("sync", "a4m", p.cnt["a4m"])
                for _mi in range(4):
                    p.wait("sync", f"mov{_mi}", p.cnt.get(f"mov{_mi}", 0))
            for i, tt in enumerate(tts):
                if b3_sc.get(("e", i)) is not None:
                    p.wait("sync", "pv", b3_sc[("e", i)])
                ld = p.dma("sync", db[i][:], ctx_dram[tt * P:(tt + 1) * P, :],
                           sname=f"cxl{i}")
                p.wait("tensor", f"cxl{i}", ld)
            emit_transpose_tiles(p, [(lambda c, i=i: db[i][:, c * P:(c + 1) * P])
                                     for i in range(4)], cxf, pt2, ident,
                                 list(range(NC)), 512)
            for ng in range(4):
                for k in range(NC):
                    wu = p.cnt.get("wou", 0)
                    slot = wu % 2
                    if wu >= 2:
                        p.wait("gpsimd", "wou", wu - 1)
                    ld = p.dma("gpsimd", wring[slot][:],
                               spo_bf[k * P:(k + 1) * P, ng * 512:(ng + 1) * 512],
                               sname=f"wol{slot}")
                    p.wait("tensor", f"wol{slot}", ld)
                    if k == 0:
                        p.wait("tensor", "woe")
                    for i in range(4):
                        p.on("tensor", lambda e, k=k, i=i, slot=slot: e.matmul(
                            pp[i][:], cxf[:, k * 512 + i * P:k * 512 + (i + 1) * P],
                            wring[slot][:], start=(k == 0), stop=(k == NC - 1)),
                            inc=("wou", 1) if i == 3 else None)
                p.wait("vector", "wou")
                for i in range(4):
                    p.on("vector", lambda e, ng=ng, i=i: e.tensor_copy(
                        db[i][:, ng * 512:(ng + 1) * 512], pp[i][:]),
                        inc=("woe", 1) if i == 3 else None)
            p.wait("tensor", "woe")
            p.wait("vector", "woe")
            for i, tt in enumerate(tts):
                mslot = i % 2
                if b3_mo.get(mslot) is not None:
                    p.wait("sync", "ps", b3_mo[mslot])
                ldm = p.dma("sync", mob[mslot][:], mo_dram[tt * P:(tt + 1) * P, :],
                            sname=f"q2l{mslot}")
                p.wait("vector", f"q2l{mslot}", ldm)
                if i == 0 and tgc == 0:
                    p.wait("vector", "gsv", p.cnt["gsv"])
                p.wait("vector", "gtv", p.cnt.get("gtv", 0))
                p.on("vector", lambda e, tt=tt, mslot=mslot: e.tensor_scalar(
                    ts1[:], mob[mslot][:], g0s[:, tt:tt + 1], None, ALU.mult,
                    ALU.bypass), inc=("gtv", 1))
                p.wait("vector", "gtv")
                vv0 = p.on("vector", lambda e, i=i, tt=tt, mslot=mslot: e.scalar_tensor_tensor(
                    mob[mslot][:], db[i][:], g1s[:, tt:tt + 1], ts1[:],
                    ALU.mult, ALU.add), inc=("gtv", 1))
                b3_sc[("e", i)] = vv0
                p.wait("vector", "gtv")
                p.on("vector", lambda e, i=i, mslot=mslot: e.reduce_sum(
                    o_sum[:, i:i + 1], mob[mslot][:], axis=AX.X), inc=("pv", 1))
                p.wait("scalar", "gtv", vv0)
                p.wait("scalar", "ps")
                p.on("scalar", lambda e, i=i, mslot=mslot: e.activation(
                    yb[0][:], mob[mslot][:], AF.Square, accum_out=o_sq[:, i:i + 1]),
                    inc=("ps", 1))
                p.wait("vector", "ps")
                vo, so = emit_ln_stats_math(p, o_sum[:, i:i + 1], o_sq[:, i:i + 1],
                                            o_m[:, i:i + 1], o_var[:, i:i + 1],
                                            o_rstd[:, i:i + 1], o_nmr[:, i:i + 1],
                                            D, cst[:, 0:1])
                ld = p.dma("sync", xb[i % 2][:], xs[tt * P:(tt + 1) * P, :],
                           sname=f"xld{i % 2}")
                p.wait("scalar", "stv", vo)
                p.wait("scalar", "sts", so)
                if ("er", i - 1) in b3_vv:
                    p.wait("scalar", "pv", b3_vv[("er", i - 1)])
                sc = p.on("scalar", lambda e, i=i, mslot=mslot: e.activation(
                    yb[0][:], mob[mslot][:], AF.Identity, bias=o_nmr[:, i:i + 1],
                    scale=o_rstd[:, i:i + 1]), inc=("ps", 1))
                b3_mo[mslot] = sc
                p.wait("vector", "ps", sc)
                p.wait("vector", f"xld{i % 2}", ld)
                vv = p.on("vector", lambda e, i=i: e.tensor_add(
                    xob[:], xb[i % 2][:], yb[0][:]), inc=("pv", 1))
                b3_vv[("er", i)] = vv
                p.wait("sync", "pv", vv)
                st = p.dma("sync", xo[tt * P:(tt + 1) * P, :], xob[:],
                           sname=f"xst{i % 2}")
                p.wait("sync", f"xst{i % 2}", st)

    p.wait("sync", "xst0", p.cnt.get("xst0", 0))
    p.wait("sync", "xst1", p.cnt.get("xst1", 0))
    return p


def l1attn_host_inputs(w, T):
    D, NC, H, NQ, NKVC = 2048, 16, 16, 12, 4
    THETA = 10000.0

    def slabB(m, nm):
        m = np.asarray(m, np.float32)
        dout, din = m.shape
        nk = din // P
        mt = np.ascontiguousarray(m.T)
        return np.ascontiguousarray(
            mt.reshape(nk, P, nm, dout // nm).transpose(2, 1, 0, 3)
            .reshape(nm, P, nk * (dout // nm)))

    qa = slabB(w["q_a_w"], NQ)
    kva = slabB(w["kv_a_w"], NQ)
    qb = np.asarray(w["q_b_w"], np.float32)
    qbt = qb.T
    qbn = np.zeros((H, P, NQ * P), np.float32)
    qbp = np.zeros((H, P, NQ * 64), np.float32)
    for h in range(H):
        blkq = qbt[:, h * 192:(h + 1) * 192]
        qbn[h] = blkq[:, 0:128].reshape(NQ, P, P).transpose(1, 0, 2).reshape(P, NQ * P)
        qbp[h] = blkq[:, 128:192].reshape(NQ, P, 64).transpose(1, 0, 2).reshape(P, NQ * 64)
    kvb = np.asarray(w["kv_b_w"], np.float32)
    kvbt = kvb.T
    kvbk = np.zeros((H, P, NKVC * P), np.float32)
    kvbv = np.zeros((H, P, NKVC * P), np.float32)
    for h in range(H):
        blkk = kvbt[:, h * 256:h * 256 + 128]
        blkv = kvbt[:, h * 256 + 128:(h + 1) * 256]
        kvbk[h] = blkk.reshape(NKVC, P, P).transpose(1, 0, 2).reshape(P, NKVC * P)
        kvbv[h] = blkv.reshape(NKVC, P, P).transpose(1, 0, 2).reshape(P, NKVC * P)

    def slabH(m):
        mt = np.ascontiguousarray(np.asarray(m, np.float32).T)
        return np.ascontiguousarray(
            mt.reshape(NC, P, H, P).transpose(2, 1, 0, 3).reshape(H, P, NC * P))

    inv = 1.0 / THETA ** (np.arange(0, 64, 2) / 64.0)
    t = np.arange(T)
    fr = np.outer(t, inv)
    emb = np.concatenate([fr, fr], -1)
    cos = np.cos(emb)
    sin = np.sin(emb)
    cos128 = np.tile(cos.T, (2, 1))
    sin128 = np.tile(sin.T, (2, 1))
    p1 = np.zeros((64, 64), np.float32)
    for j in range(32):
        p1[j, j + 32] = -1.0
        p1[j + 32, j] = 1.0
    supm = np.zeros((P, 64), np.float32)
    sdnm = np.zeros((P, 64), np.float32)
    for j in range(64):
        supm[j, j] = 1.0
        sdnm[64 + j, j] = 1.0
    return {
        "q_a": qa, "kv_a": kva, "qbn": qbn, "qbp": qbp, "kvbk": kvbk, "kvbv": kvbv,
        "ow": np.ascontiguousarray(np.asarray(w["o_w"], np.float32).T),
        "spq": slabH(w["sp_q_w"]), "spk": slabH(w["sp_k_w"]), "spv": slabH(w["sp_v_w"]),
        "spo": np.ascontiguousarray(np.asarray(w["sp_o_w"], np.float32).T),
        "gw": np.ascontiguousarray(np.asarray(w["gate_w"], np.float32).T),
        "gate_b": np.asarray(w["gate_b"], np.float32).reshape(1, 2),
        "lnw_q": np.ascontiguousarray(
            np.asarray(w["q_a_ln_w"], np.float32).reshape(NQ, P).T),
        "lnw_kv": np.ascontiguousarray(
            np.asarray(w["kv_a_ln_w"], np.float32).reshape(NKVC, P).T),
        "cos64": cos128.astype(ml_dtypes.bfloat16),
        "sin64": sin128.astype(ml_dtypes.bfloat16),
        "prot1": p1.T.astype(ml_dtypes.bfloat16),
        "sup": supm.astype(ml_dtypes.bfloat16),
        "sdn": sdnm.astype(ml_dtypes.bfloat16),
        "ident": np.eye(P, dtype=ml_dtypes.bfloat16),
    }


def declare_l1attn_inputs(nc, T, pfx="i_"):
    D, NC, H, NQ, NKVC = 2048, 16, 16, 12, 4
    f32 = {"q_a": [NQ, P, NC * P], "kv_a": [NQ, P, NC * P], "qbn": [H, P, NQ * P],
           "qbp": [H, P, NQ * 64], "kvbk": [H, P, NKVC * P], "kvbv": [H, P, NKVC * P],
           "ow": [D, D], "spq": [H, P, NC * P], "spk": [H, P, NC * P],
           "spv": [H, P, NC * P], "spo": [D, D], "gw": [D, 2], "gate_b": [1, 2],
           "lnw_q": [P, NQ], "lnw_kv": [P, NKVC]}
    bf = {"cos64": [P, T], "sin64": [P, T], "prot1": [64, 64], "sup": [P, 64],
          "sdn": [P, 64], "ident": [P, P]}
    win = {k: nc.dram_tensor(pfx + k, sh, F32, kind="ExternalInput").ap()
           for k, sh in f32.items()}
    win.update({k: nc.dram_tensor(pfx + k, sh, BF16, kind="ExternalInput").ap()
                for k, sh in bf.items()})
    return win

def _split_weights(inputs):
    w_l0a = {k: inputs["L0_attn_" + k] for k in
             ["Wq", "Wkv_down", "Wk_up", "Wv_up", "Wo"]}
    w_l0f = {k: inputs["L0_ffn_" + k] for k in
             ["SU_ug", "SU_dp", "Wscale_ug", "Wscale_dp", "W_up", "W_gate", "W_down"]}
    w_l1a = {k: inputs["L1_attn_" + k] for k in
             ["q_a_w", "q_a_ln_w", "q_b_w", "kv_a_w", "kv_a_ln_w", "kv_b_w", "o_w",
              "sp_q_w", "sp_k_w", "sp_v_w", "sp_o_w", "gate_w", "gate_b"]}
    w_l1f = {k: inputs["L1_ffn_" + k] for k in
             ["w_router", "b_router", "norm1_w", "wv", "bv", "wo", "norm2_w",
              "w_gate", "w_up", "w_down"]}
    return w_l0a, w_l0f, w_l1a, w_l1f


def build_fused(T, nseq):
    nc = make_nc()
    x_in = nc.dram_tensor("x_in", [nseq, T, 2048], F32, kind="ExternalInput").ap()
    x_out = nc.dram_tensor("x_out", [nseq, T, 2048], F32, kind="ExternalOutput").ap()
    h1 = nc.dram_tensor("hand1", [nseq, T, 2048], F32, kind="Internal").ap()
    h2 = nc.dram_tensor("hand2", [nseq, T, 2048], F32, kind="Internal").ap()
    h3 = nc.dram_tensor("hand3", [nseq, T, 2048], F32, kind="Internal").ap()
    win0a = declare_l0attn_inputs(nc, T, pfx="i0a_")
    win0f = declare_l0ffn_inputs(nc, pfx="i0f_")
    win1a = declare_l1attn_inputs(nc, T, pfx="i1a_")
    win1f = declare_l1ffn_inputs(nc, pfx="i1f_")
    p = Prog(nc)
    p.push_scope()
    build_l0attn(p, T, x_in, h1, win0a, nseq=nseq)
    p.pop_scope()
    p.barrier()
    p.push_scope()
    build_l0ffn(p, T, h1, h2, win0f, nseq=nseq)
    p.pop_scope()
    p.barrier()
    p.push_scope()
    build_l1attn(p, T, h2, h3, win1a, nseq=nseq)
    p.pop_scope()
    p.barrier()
    p.push_scope()
    build_l1ffn(p, T, h3, x_out, win1f, nseq=nseq)
    p.pop_scope()
    p.finalize()
    return nc, p


def kernel(**inputs):
    NCORES = 8
    T = 2048
    NSEQ = 2
    x = np.ascontiguousarray(np.asarray(inputs["L0_attn_x"], np.float32))
    assert x.shape[0] == NCORES * NSEQ
    w_l0a, w_l0f, w_l1a, w_l1f = _split_weights(inputs)
    nc, _ = build_fused(T, NSEQ)
    host = {}
    host.update({"i0a_" + k: v for k, v in l0attn_host_inputs(w_l0a, T).items()})
    host.update({"i0f_" + k: v for k, v in l0ffn_host_inputs(w_l0f).items()})
    host.update({"i1a_" + k: v for k, v in l1attn_host_inputs(w_l1a, T).items()})
    host.update({"i1f_" + k: v for k, v in l1ffn_host_inputs(w_l1f).items()})
    in_maps = []
    for c in range(NCORES):
        m = {"x_in": np.ascontiguousarray(x[c * NSEQ:(c + 1) * NSEQ])}
        m.update(host)
        in_maps.append(m)
    res = run_bass_kernel_spmd(nc, in_maps, core_ids=list(range(NCORES)))
    out = [np.asarray(res.results[c]["x_out"]) for c in range(NCORES)]
    return np.concatenate(out, axis=0).astype(np.float32)


# version 08
# speedup vs baseline: 1.7762x; 1.3747x over previous
import numpy as np
import ml_dtypes
from contextlib import ExitStack

import concourse.bass as bass
import concourse.mybir as mybir
from concourse.bass_utils import run_bass_kernel_spmd

F32 = mybir.dt.float32
BF16 = mybir.dt.bfloat16
AF = mybir.ActivationFunctionType
ALU = mybir.AluOpType
AX = mybir.AxisListType

P = 128


class Prog:
    ENGINES = ("sync", "tensor", "vector", "scalar", "gpsimd")

    def __init__(self, nc):
        self.nc = nc
        self.ctx = ExitStack()
        self.q = {e: [] for e in self.ENGINES}
        self._sems = {}
        self.cnt = {}

    def sem(self, name):
        if name not in self._sems:
            self._sems[name] = self.ctx.enter_context(self.nc.semaphore(name))
            self.cnt[name] = 0
        return self._sems[name]

    def sbuf(self, name, shape, dtype):
        ctx = getattr(self, "scope_ctx", None) or self.ctx
        return ctx.enter_context(self.nc.sbuf_tensor(name, shape, dtype))

    def psum(self, name, shape, dtype=F32):
        ctx = getattr(self, "scope_ctx", None) or self.ctx
        return ctx.enter_context(self.nc.psum_tensor(name, shape, dtype))

    def push_scope(self):
        self.scope_ctx = ExitStack()

    def pop_scope(self):
        if getattr(self, "scope_ctx", None) is not None:
            self.scope_ctx.close()
            self.scope_ctx = None

    def barrier(self):
        snap = [(k, v) for k, v in self.cnt.items() if v > 0]
        for eng in self.ENGINES:
            for k, v in snap:
                self.wait(eng, k, v)

    def on(self, eng, fn, inc=None):
        tgt = None
        if inc is not None:
            sname, n = inc
            self.sem(sname)
            self.cnt[sname] += n
            tgt = self.cnt[sname]
            sh = self._sems[sname]

            def run(e, fn=fn, sh=sh, n=n):
                fn(e).then_inc(sh, n)
        else:
            def run(e, fn=fn):
                fn(e)
        self.q[eng].append(run)
        return tgt

    def wait(self, eng, sname, val=None):
        self.sem(sname)
        if val is None:
            val = self.cnt[sname]
        if val <= 0:
            return
        sh = self._sems[sname]
        self.q[eng].append(lambda e, sh=sh, v=val: e.wait_ge(sh, v))

    def dma(self, eng, out, in_, sname=None, **kw):
        if sname is None:
            sname = "dq_" + eng
        return self.on(eng, lambda e, o=out, i=in_, kw=kw: e.dma_start(out=o, in_=i, **kw),
                       inc=(sname, 16))

    def finalize(self):
        blk = self.ctx.enter_context(self.nc.Block())
        for e in self.ENGINES:
            lst = self.q[e]
            if not lst:
                continue
            getattr(blk, e)(lambda eng, lst=lst: [f(eng) for f in lst])
        self.ctx.close()


def make_nc():
    return bass.Bass("TRN2", target_bir_lowering=False, debug=False)



def emit_bcast_row(p, pb, ones1f, src_row, dst, width):
    ng = (width + 511) // 512
    for g in range(ng):
        n = min(512, width - g * 512)
        t = p.on("tensor", lambda e, g=g, n=n: e.matmul(
            pb[:, 0:n], ones1f[:], src_row[:, g * 512:g * 512 + n], start=True, stop=True),
            inc=("bcmm", 1))
        p.wait("vector", "bcmm", t)
        ev = p.on("vector", lambda e, dst=dst, g=g, n=n: e.tensor_copy(
            dst[:, g * 512:g * 512 + n], pb[:, 0:n]), inc=("bcev", 1))
        p.wait("tensor", "bcev", ev)


def emit_ln_stats_math(p, s_sum, s_sq, s_m, s_var, s_rstd, s_nmr, D, eps_col):
    p.wait("vector", "pv")
    p.on("vector", lambda e: e.tensor_single_scalar(s_m, s_sum, 1.0 / D, ALU.mult),
         inc=("stv", 1))
    p.on("vector", lambda e: e.tensor_single_scalar(s_var, s_sq, 1.0 / D, ALU.mult),
         inc=("stv", 1))
    p.wait("vector", "stv")
    p.on("vector", lambda e: e.tensor_mul(s_nmr, s_m, s_m), inc=("stv", 1))
    p.wait("vector", "stv")
    v = p.on("vector", lambda e: e.tensor_sub(s_sq, s_var, s_nmr), inc=("stv", 1))
    p.wait("scalar", "stv", v)
    s1 = p.on("scalar", lambda e: e.activation(s_var, s_sq, AF.Ln, bias=eps_col),
              inc=("sts", 1))
    p.wait("scalar", "sts", s1)
    s2 = p.on("scalar", lambda e: e.activation(s_rstd, s_var, AF.Exp, scale=-0.5),
              inc=("sts", 1))
    p.wait("vector", "sts", s2)
    p.wait("vector", "stv")
    v = p.on("vector", lambda e: e.scalar_tensor_tensor(s_nmr, s_m, -1.0, s_rstd,
                                                        ALU.mult, ALU.mult),
             inc=("stv", 1))
    return v, s2


def emit_rms_stats_math(p, s_sq, s_rstd, D, eps_col):
    p.wait("vector", "ps")
    p.wait("vector", "stv")
    v = p.on("vector", lambda e: e.tensor_single_scalar(s_rstd, s_sq, 1.0 / D, ALU.mult),
             inc=("stv", 1))
    p.wait("scalar", "stv", v)
    s1 = p.on("scalar", lambda e: e.activation(s_sq, s_rstd, AF.Ln, bias=eps_col),
              inc=("sts", 1))
    p.wait("scalar", "sts", s1)
    s2 = p.on("scalar", lambda e: e.activation(s_rstd, s_sq, AF.Exp, scale=-0.5),
              inc=("sts", 1))
    return s2


def emit_transpose_tiles(p, src_tile_fns, dst_slab, pt2, ident, chunks, W, coloff=0):
    ntt = len(src_tile_fns)
    for ci, c in enumerate(chunks):
        j = p.cnt.get("trT", 0) % 2
        if p.cnt.get("trT", 0) >= 2:
            p.wait("tensor", "trV", p.cnt["trV"] - 1)
        for tt in range(ntt):
            t = p.on("tensor", lambda e, c=c, tt=tt, j=j, ntt=ntt:
                     e.matmul(pt2[j][:, tt * 128:(tt + 1) * 128],
                              src_tile_fns[tt](c), ident[:],
                              start=(tt == 0), stop=(tt == ntt - 1),
                              is_transpose=True),
                     inc=("trT", 1) if tt == ntt - 1 else None)
        p.wait("vector", "trT", t)
        p.on("vector", lambda e, ci=ci, j=j, ntt=ntt:
             e.tensor_copy(dst_slab[:, ci * W + coloff:
                                    ci * W + coloff + ntt * 128],
                           pt2[j][:, 0:ntt * 128]),
             inc=("trV", 1))
    p.wait("tensor", "trV")



def build_l1ffn(p, T, x_in, x_out, win, nseq=2, dbg=None):
    nc = p.nc
    D, FF = 2048, 5632
    NT = T // 128
    assert NT % 4 == 0
    NTG = NT // 4
    NC, NFC = 16, 44
    EPS_LN, EPS_RMS = 1e-5, 1e-6

    wv_bf = nc.dram_tensor("s_wv_bf", [D, D], BF16, kind="Internal").ap()
    wo_bf = nc.dram_tensor("s_wo_bf", [D, D], BF16, kind="Internal").ap()
    wg_bf = nc.dram_tensor("s_wg_bf", [NFC, P, NC * P], BF16, kind="Internal").ap()
    wu_bf = nc.dram_tensor("s_wu_bf", [NFC, P, NC * P], BF16, kind="Internal").ap()
    wd_bf = nc.dram_tensor("s_wd_bf", [FF, D], BF16, kind="Internal").ap()

    cst = p.sbuf("cst", [P, 2], F32)
    ident = p.sbuf("ident", [P, P], BF16)
    n1_row = p.sbuf("n1_row", [P, D], BF16)
    n2_row = p.sbuf("n2_row", [P, D], BF16)
    bv_row = p.sbuf("bv_row", [P, D], BF16)
    wr_row = p.sbuf("wr_row", [P, D], F32)
    br_col = p.sbuf("br_col", [P, 1], F32)
    wr1 = p.sbuf("wr1", [P, 1], F32)

    xb = [p.sbuf(f"xb{i}", [P, D], F32) for i in range(2)]
    xob = [p.sbuf(f"xob{i}", [P, D], F32) for i in range(2)]
    yb = [p.sbuf(f"yb{i}", [P, D], BF16) for i in range(4)]
    hb = [p.sbuf(f"hb{i}", [P, D], BF16) for i in range(4)]
    db = [p.sbuf(f"db{i}", [P, D], BF16) for i in range(4)]
    h1f = p.sbuf("h1f", [P, NC * 512], BF16)
    h2f = p.sbuf("h2f", [P, NC * 512], BF16)
    ufm = p.sbuf("ufm", [P, NFC * 512], BF16)
    junkb = p.sbuf("junkb", [P, D], BF16)
    junkv = p.sbuf("junkv", [P, D], BF16)
    wring = [p.sbuf(f"wring{i}", [P, 512], BF16) for i in range(4)]
    gring = [p.sbuf(f"gring{i}", [P, NC * P], BF16) for i in range(2)]
    sgb = p.sbuf("sgb", [P, 512], BF16)

    s_sum = p.sbuf("s_sum", [P, NT], F32)
    s_sq = p.sbuf("s_sq", [P, NT], F32)
    s_rsc = p.sbuf("s_rsc", [P, NT], F32)
    s_m = p.sbuf("s_m", [P, NT], F32)
    s_var = p.sbuf("s_var", [P, NT], F32)
    s_rstd = p.sbuf("s_rstd", [P, NT], F32)
    s_nmr = p.sbuf("s_nmr", [P, NT], F32)
    s_mask = p.sbuf("s_mask", [P, NT], F32)
    g_sq = p.sbuf("g_sq", [P, 4], F32)
    g_r1 = p.sbuf("g_r1", [P, 4], F32)
    g_sq2 = p.sbuf("g_sq2", [P, 4], F32)
    g_r2 = p.sbuf("g_r2", [P, 4], F32)
    o_sum = p.sbuf("o_sum", [P, 4], F32)
    o_sq = p.sbuf("o_sq", [P, 4], F32)
    o_m = p.sbuf("o_m", [P, 4], F32)
    o_var = p.sbuf("o_var", [P, 4], F32)
    o_rstd = p.sbuf("o_rstd", [P, 4], F32)
    o_nmr = p.sbuf("o_nmr", [P, 4], F32)

    pacc = [p.psum(f"pacc{i}", [P, 512], F32) for i in range(4)]
    pg = p.psum("pg", [P, 512], F32)
    pu = p.psum("pu", [P, 512], F32)
    pb = pg
    pt2 = [p.psum(f"pt{i}", [P, 512], BF16) for i in range(2)]

    p.on("vector", lambda e: e.memset(cst[:, 0:1], EPS_LN), inc=("vprep", 1))
    p.on("vector", lambda e: e.memset(cst[:, 1:2], EPS_RMS), inc=("vprep", 1))
    vprep_all = p.cnt["vprep"]
    p.wait("scalar", "vprep", vprep_all)
    p.dma("sync", ident[:], win["ident"], sname="cload")
    for (dst, src) in [(wv_bf, win["wv"]), (wo_bf, win["wo"]), (wg_bf, win["w_gate"]),
                       (wu_bf, win["w_up"]), (wd_bf, win["w_down"])]:
        p.dma("gpsimd", dst, src, sname="wcast")
    wcast_done = p.cnt["wcast"]

    p.dma("gpsimd", n1_row[:], win["norm1_w"].partition_broadcast(P), sname="wcast")
    p.dma("gpsimd", n2_row[:], win["norm2_w"].partition_broadcast(P), sname="wcast")
    p.dma("gpsimd", bv_row[:], win["bv"].partition_broadcast(P), sname="wcast")
    p.dma("sync", wr_row[:], win["w_router"].partition_broadcast(P), sname="cload")
    p.dma("sync", br_col[:], win["b_router"].partition_broadcast(P), sname="cload")
    wcast_done = p.cnt["wcast"]
    p.wait("vector", "cload")
    p.wait("vector", "wcast", wcast_done)
    p.wait("tensor", "cload")
    wrv = p.on("vector", lambda e: e.reduce_sum(wr1[:], wr_row[:], axis=AX.X), inc=("stv", 1))

    def dbg_tap(slot, src_ap, waits, barrier=True):
        if dbg is None:
            return
        for sn, v in waits:
            p.wait("gpsimd", sn, v)
        dst = dbg[slot]
        if tuple(dst.shape) != tuple(src_ap.shape):
            dst = dst[:, 0:src_ap.shape[1]]
        p.dma("gpsimd", dst, src_ap, sname=f"dbgst{slot}")
        if barrier:
            for eng in ("vector", "scalar", "tensor", "sync"):
                p.wait(eng, f"dbgst{slot}")

    def stream_mm(wap_fn, ngroups, nk, lhsT_fn, evac_fn, etag):
        for g in range(ngroups):
            for k in range(nk):
                if p.cnt.get("wru", 0) >= 4:
                    p.wait("gpsimd", "wru", p.cnt["wru"] - 3)
                slot = p.cnt.get("wru", 0) % 4
                ld = p.dma("gpsimd", wring[slot][:], wap_fn(g, k), sname=f"wrl{slot}")
                p.wait("tensor", f"wrl{slot}", ld)
                for i in range(4):
                    if k == 0 and p.cnt.get(etag, 0) >= 4 - i:
                        p.wait("tensor", etag, p.cnt[etag] - (4 - i) + 1)
                    p.on("tensor", lambda e, g=g, k=k, i=i, slot=slot, nk=nk: e.matmul(
                        pacc[i][:], lhsT_fn(g, k, i), wring[slot][:],
                        start=(k == 0), stop=(k == nk - 1)),
                        inc=("wru", 1) if i == 3 else None)
            p.wait("vector", "wru")
            evac_fn(g, etag)
        p.wait("tensor", etag)

    b3_vv = {}
    for seq in range(nseq):
        xs = x_in[seq]
        xo = x_out[seq]

        pv_t = {}
        ps_t = {}
        for tt in range(NT):
            if tt >= 2:
                p.wait("sync", "pv", pv_t[tt - 2])
                p.wait("sync", "ps", ps_t[tt - 2])
            ld = p.dma("sync", xb[tt % 2][:], xs[tt * P:(tt + 1) * P, :],
                       sname=f"xld{tt % 2}")
            p.wait("vector", f"xld{tt % 2}", ld)
            p.on("vector", lambda e, tt=tt: e.reduce_sum(
                s_sum[:, tt:tt + 1], xb[tt % 2][:], axis=AX.X), inc=("pv", 1))
            p.wait("vector", "pv")
            pv_t[tt] = p.on("vector", lambda e, tt=tt: e.scalar_tensor_tensor(
                junkv[:], xb[tt % 2][:], 1.0, wr_row[:], ALU.mult, ALU.mult,
                accum_out=s_rsc[:, tt:tt + 1]), inc=("pv", 1))
            p.wait("scalar", f"xld{tt % 2}", ld)
            p.wait("scalar", "ps")
            ps_t[tt] = p.on("scalar", lambda e, tt=tt: e.activation(
                junkb[:], xb[tt % 2][:], AF.Square, accum_out=s_sq[:, tt:tt + 1]),
                inc=("ps", 1))
        p.wait("vector", "ps")
        vln, sln = emit_ln_stats_math(p, s_sum[:], s_sq[:], s_m[:], s_var[:],
                                      s_rstd[:], s_nmr[:], D, cst[:, 0:1])
        p.wait("vector", "stv", wrv)
        p.wait("vector", "sts", sln)
        p.wait("vector", "pv")
        p.on("vector", lambda e: e.tensor_scalar(s_mask[:], s_m[:], wr1[:, 0:1], None,
                                                 ALU.mult, ALU.bypass), inc=("stv", 1))
        p.wait("vector", "stv")
        p.on("vector", lambda e: e.tensor_sub(s_var[:], s_rsc[:], s_mask[:]),
             inc=("stv", 1))
        p.wait("vector", "stv")
        p.on("vector", lambda e: e.tensor_mul(s_mask[:], s_var[:], s_rstd[:]),
             inc=("stv", 1))
        p.wait("vector", "stv")
        p.on("vector", lambda e: e.tensor_scalar(s_var[:], s_mask[:], br_col[:, 0:1], None,
                                                 ALU.add, ALU.bypass), inc=("stv", 1))
        p.wait("vector", "stv")
        vmask = p.on("vector", lambda e: e.tensor_single_scalar(
            s_mask[:], s_var[:], 0.0, ALU.is_gt), inc=("stv", 1))
        p.wait("tensor", "wcast", wcast_done)
        p.wait("gpsimd", "wcast", wcast_done)

        for tg in range(NTG):
            tts = [tg * 4 + i for i in range(4)]

            for i, tt in enumerate(tts):
                if i >= 2:
                    p.wait("sync", "ps", ps_t[("a0", tg, i - 2)])
                if tg == 0 and i == 0:
                    p.wait("sync", "ps", ps_t[NT - 1])
                    p.wait("sync", "pv", pv_t[NT - 1])
                ld = p.dma("sync", xb[i % 2][:], xs[tt * P:(tt + 1) * P, :],
                           sname=f"xld{i % 2}")
                p.wait("scalar", f"xld{i % 2}", ld)
                if i == 0 and tg == 0:
                    p.wait("scalar", "stv", vln)
                    p.wait("scalar", "sts", sln)
                p.on("scalar", lambda e, i=i, tt=tt: e.activation(
                    yb[i][:], xb[i % 2][:], AF.Identity,
                    bias=s_nmr[:, tt:tt + 1], scale=s_rstd[:, tt:tt + 1]),
                    inc=("ps", 1))
                p.wait("scalar", "ps")
                ps_t[("a0", tg, i)] = p.on("scalar", lambda e, i=i: e.activation(
                    junkb[:], yb[i][:], AF.Square, accum_out=g_sq[:, i:i + 1]),
                    inc=("ps", 1))
            if seq == 0 and tg == 0:
                dbg_tap(0, yb[0][:], [("ps", p.cnt["ps"])])
            r1t = emit_rms_stats_math(p, g_sq[:], g_r1[:], D, cst[:, 1:2])
            p.wait("vector", "sts", r1t)
            p.wait("vector", "pv")
            for i in range(4):
                p.on("vector", lambda e, i=i: e.scalar_tensor_tensor(
                    hb[i][:], yb[i][:], g_r1[:, i:i + 1], n1_row[:], ALU.mult, ALU.mult),
                    inc=("pv", 1))
            p.wait("tensor", "pv")
            if seq == 0 and tg == 0:
                dbg_tap(1, hb[0][:], [("pv", p.cnt["pv"])])
            emit_transpose_tiles(p, [(lambda c, i=i: hb[i][:, c * P:(c + 1) * P])
                                     for i in range(4)], h1f, pt2, ident, list(range(NC)), 512)

            def lhs_h1(g, k, i):
                return h1f[:, k * 512 + i * P: k * 512 + (i + 1) * P]

            def evac_v(g, etag):
                for i in range(4):
                    p.on("vector", lambda e, g=g, i=i: e.tensor_tensor(
                        db[i][:, g * 512:(g + 1) * 512], pacc[i][:],
                        bv_row[:, g * 512:(g + 1) * 512], ALU.add), inc=(etag, 1))

            stream_mm(lambda g, k: wv_bf[k * P:(k + 1) * P, g * 512:(g + 1) * 512],
                      4, NC, lhs_h1, evac_v, "wve")

            if seq == 0 and tg == 0:
                dbg_tap(2, db[0][:], [("wve", p.cnt["wve"])])
            emit_transpose_tiles(p, [(lambda c, i=i: db[i][:, c * P:(c + 1) * P])
                                     for i in range(4)], h1f, pt2, ident, list(range(NC)), 512)

            def evac_h(g, etag):
                for i in range(4):
                    p.on("vector", lambda e, g=g, i=i: e.tensor_tensor(
                        hb[i][:, g * 512:(g + 1) * 512], pacc[i][:],
                        yb[i][:, g * 512:(g + 1) * 512], ALU.add), inc=(etag, 1))

            stream_mm(lambda g, k: wo_bf[k * P:(k + 1) * P, g * 512:(g + 1) * 512],
                      4, NC, lhs_h1, evac_h, "woe")

            if seq == 0 and tg == 0:
                dbg_tap(3, hb[0][:], [("woe", p.cnt["woe"])])
            p.wait("scalar", "woe")
            for i in range(4):
                p.wait("scalar", "ps")
                p.on("scalar", lambda e, i=i: e.activation(
                    junkb[:], hb[i][:], AF.Square, accum_out=g_sq2[:, i:i + 1]),
                    inc=("ps", 1))
            r2t = emit_rms_stats_math(p, g_sq2[:], g_r2[:], D, cst[:, 1:2])
            p.wait("vector", "sts", r2t)
            for i in range(4):
                p.on("vector", lambda e, i=i: e.scalar_tensor_tensor(
                    db[i][:], hb[i][:], g_r2[:, i:i + 1], n2_row[:], ALU.mult, ALU.mult),
                    inc=("pv", 1) if i == 3 else None)
            p.wait("tensor", "pv")
            emit_transpose_tiles(p, [(lambda c, i=i: db[i][:, c * P:(c + 1) * P])
                                     for i in range(4)], h2f, pt2, ident, list(range(NC)), 512)

            if seq == 0 and tg == 0:
                dbg_tap(4, db[0][:], [("pv", p.cnt["pv"])])
            for fc in range(NFC):
                for j, (wsrc, ps_) in enumerate([(wg_bf, pg), (wu_bf, pu)]):
                    if p.cnt.get("gau", 0) >= 2:
                        p.wait("gpsimd", "gau", p.cnt["gau"] - 1)
                    gslot = p.cnt.get("gau", 0) % 2
                    ld = p.dma("gpsimd", gring[gslot][:], wsrc[fc], sname=f"gal{gslot}")
                    p.wait("tensor", f"gal{gslot}", ld)
                    if j == 0:
                        p.wait("tensor", "ctv")
                    for k in range(NC):
                        p.on("tensor", lambda e, k=k, gslot=gslot, ps_=ps_: e.matmul(
                            ps_[:], gring[gslot][:, k * P:(k + 1) * P],
                            h2f[:, k * 512:(k + 1) * 512],
                            start=(k == 0), stop=(k == NC - 1)),
                            inc=("gau", 1) if k == NC - 1 else None)
                p.wait("scalar", "gau")
                sc = p.on("scalar", lambda e: e.activation(sgb[:], pg[:], AF.Silu),
                          inc=("ps", 1))
                p.wait("vector", "ps", sc)
                p.on("vector", lambda e, fc=fc: e.tensor_tensor(
                    ufm[:, fc * 512:(fc + 1) * 512], sgb[:], pu[:], ALU.mult),
                    inc=("ctv", 1))
            p.wait("tensor", "ctv")

            if seq == 0 and tg == 0:
                dbg_tap(5, ufm[:, 0:512], [("ctv", p.cnt["ctv"])])
            def lhs_u(g, k, i):
                return ufm[:, k * 512 + i * P: k * 512 + (i + 1) * P]

            def evac_d(g, etag):
                for i in range(4):
                    p.on("vector", lambda e, g=g, i=i: e.tensor_tensor(
                        db[i][:, g * 512:(g + 1) * 512], pacc[i][:],
                        hb[i][:, g * 512:(g + 1) * 512], ALU.add), inc=(etag, 1))

            stream_mm(lambda g, k: wd_bf[k * P:(k + 1) * P, g * 512:(g + 1) * 512],
                      4, NFC, lhs_u, evac_d, "wde")

            if seq == 0 and tg == 0:
                dbg_tap(6, db[0][:], [("wde", p.cnt["wde"])])
            p.wait("vector", "wde")
            p.wait("vector", "ps")
            if tg == 0:
                p.wait("vector", "stv", vmask)
            for i, tt in enumerate(tts):
                p.wait("vector", "pv")
                p.on("vector", lambda e, i=i: e.tensor_sub(junkv[:], db[i][:], yb[i][:]),
                     inc=("pv", 1))
                p.wait("vector", "pv")
                vv = p.on("vector", lambda e, i=i, tt=tt: e.scalar_tensor_tensor(
                    db[i][:], junkv[:], s_mask[:, tt:tt + 1], yb[i][:], ALU.mult, ALU.add),
                    inc=("pv", 1))
                p.on("vector", lambda e, i=i: e.reduce_sum(
                    o_sum[:, i:i + 1], db[i][:], axis=AX.X), inc=("pv", 1))
                p.wait("scalar", "pv", vv)
                p.wait("scalar", "ps")
                p.on("scalar", lambda e, i=i: e.activation(
                    junkb[:], db[i][:], AF.Square, accum_out=o_sq[:, i:i + 1]),
                    inc=("ps", 1))
            p.wait("vector", "ps")
            if seq == 0 and tg == 0:
                dbg_tap(7, db[0][:], [("ps", p.cnt["ps"])])
                dbg_tap(8, s_mask[:, 0:NT], [("stv", p.cnt["stv"])])
                dbg_tap(9, s_rstd[:, 0:NT], [])
                dbg_tap(10, s_m[:, 0:NT], [])
                dbg_tap(11, s_rsc[:, 0:NT], [])
            vo, so = emit_ln_stats_math(p, o_sum[:], o_sq[:], o_m[:], o_var[:],
                                        o_rstd[:], o_nmr[:], D, cst[:, 0:1])
            for i, tt in enumerate(tts):
                if i < 2:
                    p.wait("sync", "ps", ps_t[("a0", tg, 2 + i)])
                else:
                    p.wait("sync", "pv", b3_vv[i - 2])
                ld = p.dma("sync", xb[i % 2][:], xs[tt * P:(tt + 1) * P, :],
                           sname=f"xld{i % 2}")
                if i == 0:
                    p.wait("scalar", "stv", vo)
                    p.wait("scalar", "sts", so)
                sc = p.on("scalar", lambda e, i=i: e.activation(
                    yb[i][:], db[i][:], AF.Identity, bias=o_nmr[:, i:i + 1],
                    scale=o_rstd[:, i:i + 1]), inc=("ps", 1))
                p.wait("vector", "ps", sc)
                p.wait("vector", f"xld{i % 2}", ld)
                vv = p.on("vector", lambda e, i=i: e.tensor_add(
                    xob[i % 2][:], xb[i % 2][:], yb[i][:]), inc=("pv", 1))
                b3_vv[i] = vv
                p.wait("sync", "pv", vv)
                st = p.dma("sync", xo[tt * P:(tt + 1) * P, :], xob[i % 2][:],
                           sname=f"xst{i % 2}")
                p.wait("sync", f"xst{i % 2}", st)

    p.wait("sync", "xst0", p.cnt.get("xst0", 0))
    p.wait("sync", "xst1", p.cnt.get("xst1", 0))
    return p



def l1ffn_host_inputs(w):
    D, FF, NC, NFC = 2048, 5632, 16, 44
    slab = lambda m: np.ascontiguousarray(
        np.asarray(m, np.float32).reshape(NC, P, NFC, P).transpose(2, 1, 0, 3)
        .reshape(NFC, P, NC * P))
    return {
        "wv": np.ascontiguousarray(np.asarray(w["wv"], np.float32)),
        "wo": np.ascontiguousarray(np.asarray(w["wo"], np.float32)),
        "w_gate": slab(w["w_gate"]),
        "w_up": slab(w["w_up"]),
        "w_down": np.ascontiguousarray(np.asarray(w["w_down"], np.float32)),
        "bv": np.asarray(w["bv"], np.float32).reshape(1, D),
        "w_router": np.asarray(w["w_router"], np.float32).reshape(1, D),
        "b_router": np.asarray(w["b_router"], np.float32).reshape(1, 1),
        "norm1_w": np.asarray(w["norm1_w"], np.float32).reshape(1, D),
        "norm2_w": np.asarray(w["norm2_w"], np.float32).reshape(1, D),
        "ident": np.eye(P, dtype=ml_dtypes.bfloat16),
    }


def declare_l1ffn_inputs(nc, pfx="i_"):
    D, FF, NC, NFC = 2048, 5632, 16, 44
    names = {
        "wv": [D, D], "wo": [D, D], "w_gate": [NFC, P, NC * P], "w_up": [NFC, P, NC * P],
        "w_down": [FF, D], "bv": [1, D], "w_router": [1, D], "b_router": [1, 1],
        "norm1_w": [1, D], "norm2_w": [1, D],
    }
    win = {k: nc.dram_tensor(pfx + k, sh, F32, kind="ExternalInput").ap()
           for k, sh in names.items()}
    win["ident"] = nc.dram_tensor(pfx + "ident", [P, P], BF16, kind="ExternalInput").ap()
    return win



def build_l0ffn(p, T, x_in, x_out, win, nseq=2, dbg=None):
    nc = p.nc
    D, FF = 2048, 8192
    NT = T // 128
    assert NT % 4 == 0
    NTG = NT // 4
    NC, NFC = 16, 64
    EPS_LN = 1e-5
    SQ2_64 = 64.0 * (2.0 ** 0.5)
    ISQ8192 = 1.0 / 90.50966799187808

    wu_bf = nc.dram_tensor("f_wu_bf", [NFC, P, NC * P], BF16, kind="Internal").ap()
    wg_bf = nc.dram_tensor("f_wg_bf", [NFC, P, NC * P], BF16, kind="Internal").ap()
    wd_bf = nc.dram_tensor("f_wd_bf", [D, FF], BF16, kind="Internal").ap()
    wd2a = nc.dram_tensor("f_wd2a", [NFC, D, P], BF16, kind="Internal").ap()
    wd2b = nc.dram_tensor("f_wd2b", [NFC, P, D], BF16, kind="Internal").ap()

    ident = p.sbuf("f_ident", [P, P], BF16)
    cst = p.sbuf("f_cst", [P, 1], F32)
    h64 = p.sbuf("f_h64", [64, 64], BF16)
    h128 = p.sbuf("f_h128", [P, P], BF16)
    su_row = p.sbuf("f_su_row", [P, D], BF16)
    rowtmp = p.sbuf("f_rowtmp", [P, D], F32)
    rd_row = p.sbuf("f_rd_row", [P, D], F32)
    st_a = p.sbuf("f_st_a", [P, NFC], F32)
    st_b = p.sbuf("f_st_b", [P, NFC], F32)
    st_c = p.sbuf("f_st_c", [P, NFC], F32)
    r_up = p.sbuf("f_r_up", [P, NFC], F32)
    r_g = p.sbuf("f_r_g", [P, NFC], F32)

    xb = [p.sbuf(f"f_xb{i}", [P, D], F32) for i in range(2)]
    xob = [p.sbuf(f"f_xob{i}", [P, D], F32) for i in range(2)]
    yb = [p.sbuf(f"f_yb{i}", [P, D], BF16) for i in range(2)]
    db = [p.sbuf(f"f_db{i}", [P, D], BF16) for i in range(4)]
    z0f = p.sbuf("f_z0f", [P, NC * 512], BF16)
    x1f = p.sbuf("f_x1f", [P, NC * 512], BF16)
    zf = p.sbuf("f_zf", [P, NFC * 512], BF16)
    junkb = p.sbuf("f_junkb", [P, D], BF16)
    sgb = p.sbuf("f_sgb", [P, 512], BF16)
    ub = [p.sbuf(f"f_ub{i}", [P, 512], BF16) for i in range(2)]
    gring = [p.sbuf(f"f_gring{i}", [P, NC * P], BF16) for i in range(2)]
    wring = [p.sbuf(f"f_wring{i}", [P, 512], BF16) for i in range(4)]
    tring = [p.sbuf(f"f_tring{i}", [P, P], BF16) for i in range(2)]
    wdacc = p.sbuf("f_wdacc", [P, D], BF16)

    s_sum = p.sbuf("f_s_sum", [P, NT], F32)
    s_sq = p.sbuf("f_s_sq", [P, NT], F32)
    s_m = p.sbuf("f_s_m", [P, NT], F32)
    s_var = p.sbuf("f_s_var", [P, NT], F32)
    s_rstd = p.sbuf("f_s_rstd", [P, NT], F32)
    s_nmr = p.sbuf("f_s_nmr", [P, NT], F32)
    o_sum = p.sbuf("f_o_sum", [P, 4], F32)
    o_sq = p.sbuf("f_o_sq", [P, 4], F32)
    o_m = p.sbuf("f_o_m", [P, 4], F32)
    o_var = p.sbuf("f_o_var", [P, 4], F32)
    o_rstd = p.sbuf("f_o_rstd", [P, 4], F32)
    o_nmr = p.sbuf("f_o_nmr", [P, 4], F32)

    pp = [p.psum(f"f_pp{i}", [P, 512], F32) for i in range(6)]
    pt2 = [p.psum(f"f_pt{i}", [P, 512], BF16) for i in range(2)]

    p.on("vector", lambda e: e.memset(cst[:, 0:1], EPS_LN), inc=("vprep", 1))
    p.wait("scalar", "vprep")
    p.dma("sync", ident[:], win["ident"], sname="cload")
    p.dma("sync", h64[:], win["h64"], sname="cload")
    p.dma("sync", h128[:], win["h128"], sname="cload")
    p.dma("sync", st_a[:], win["wsu_up"], sname="cload")
    p.dma("sync", st_b[:], win["wsu_g"], sname="cload")
    p.dma("sync", st_c[:], win["su_dp_fm"], sname="cload")
    p.dma("sync", rowtmp[:], win["sdp_row"].partition_broadcast(P), sname="cload")
    p.dma("gpsimd", su_row[:], win["su_row"].partition_broadcast(P), sname="wcast")
    for (dst, src) in [(wu_bf, win["w_up"]), (wg_bf, win["w_gate"]), (wd_bf, win["w_down"])]:
        p.dma("gpsimd", dst, src, sname="wcast")
    wcast_done = p.cnt["wcast"]
    cload_done = p.cnt["cload"]
    p.wait("vector", "cload", cload_done)
    p.wait("vector", "wcast", wcast_done)
    p.on("vector", lambda e: e.scalar_tensor_tensor(r_up[:], st_a[:], SQ2_64, st_c[:],
                                                    ALU.mult, ALU.mult), inc=("stv", 1))
    p.on("vector", lambda e: e.tensor_single_scalar(r_g[:], st_b[:], SQ2_64, ALU.mult),
         inc=("stv", 1))
    p.on("vector", lambda e: e.tensor_single_scalar(rd_row[:], rowtmp[:], ISQ8192, ALU.mult),
         inc=("stv", 1))
    rows_done = p.cnt["stv"]

    p.wait("tensor", "cload", cload_done)
    p.wait("gpsimd", "wcast", wcast_done)
    DBLK = 16
    NJ = DBLK * P // 512
    for di in range(D // DBLK):
        slot = di % 2
        if di >= 2:
            p.wait("gpsimd", "knm", (di - 1) * NJ)
            p.wait("gpsimd", f"qms{slot}", p.cnt.get(f"qms{slot}", 0))
        src = wd_bf[di * DBLK:(di + 1) * DBLK, :].rearrange("d (c p) -> c d p", p=P)
        dst3 = gring[slot][0:64, 0:DBLK * P].rearrange("c (d p) -> c d p", p=P)
        ld = p.dma("gpsimd", dst3, src, sname=f"gtl{slot}")
        p.wait("tensor", f"gtl{slot}", ld)
        for j in range(NJ):
            if di * NJ + j >= 2:
                p.wait("tensor", "knv", di * NJ + j - 1)
            p.on("tensor", lambda e, slot=slot, j=j: e.matmul(
                pp[4 + j % 2][0:64, :], h64[:], gring[slot][0:64, j * 512:(j + 1) * 512],
                start=True, stop=True), inc=("knm", 1))
            p.wait("vector", "knm")
            if di >= 2 and j == 0:
                p.wait("vector", f"qms{slot}", p.cnt.get(f"qms{slot}", 0))
            p.on("vector", lambda e, slot=slot, j=j: e.tensor_copy(
                gring[slot][64:128, j * 512:(j + 1) * 512], pp[4 + j % 2][0:64, :]),
                inc=("knv", 1))
        p.wait("gpsimd", "knv")
        p.dma("gpsimd", wd2a[:, di * DBLK:(di + 1) * DBLK, :],
              gring[slot][64:128, 0:DBLK * P].rearrange("c (d p) -> c d p", p=P),
              sname=f"qms{slot}")

    p.wait("gpsimd", "qms0", p.cnt.get("qms0", 0))
    p.wait("gpsimd", "qms1", p.cnt.get("qms1", 0))
    for c in range(NFC):
        for dc in range(NC):
            it = c * NC + dc
            slot = it % 2
            if it >= 2:
                p.wait("gpsimd", "vhm", it - 1)
            ld = p.dma("gpsimd", tring[slot][:], wd2a[c, dc * P:(dc + 1) * P, :],
                       sname=f"pql{slot}")
            p.wait("tensor", f"pql{slot}", ld)
            if it >= 2:
                p.wait("tensor", "vhv", it - 1)
            p.on("tensor", lambda e, slot=slot: e.transpose(
                pt2[slot][:, 0:128], tring[slot][:], ident[:]), inc=("vhm", 1))
            p.wait("vector", "vhm")
            if dc == 0 and c >= 1:
                p.wait("vector", "cxs", p.cnt["cxs"])
            p.on("vector", lambda e, slot=slot, dc=dc: e.tensor_copy(
                wdacc[:, dc * P:(dc + 1) * P], pt2[slot][:, 0:128]), inc=("vhv", 1))
        p.wait("gpsimd", "vhv")
        p.dma("gpsimd", wd2b[c], wdacc[:], sname="cxs")
    wd2_done = p.cnt["cxs"]

    def dbg_tap(slot, src_ap, waits, barrier=True):
        if dbg is None:
            return
        for sn, v in waits:
            p.wait("gpsimd", sn, v)
        dst = dbg[slot]
        if tuple(dst.shape) != tuple(src_ap.shape):
            dst = dst[:, 0:src_ap.shape[1]]
        p.dma("gpsimd", dst, src_ap, sname=f"dbgst{slot}")
        if barrier:
            for eng in ("vector", "scalar", "tensor", "sync"):
                p.wait(eng, f"dbgst{slot}")

    b3_vv = {}
    ps_t = {}
    pv_t = {}
    for seq in range(nseq):
        xs = x_in[seq]
        xo = x_out[seq]

        for tt in range(NT):
            if tt >= 2:
                p.wait("sync", "pv", pv_t[tt - 2])
                p.wait("sync", "ps", ps_t[tt - 2])
            ld = p.dma("sync", xb[tt % 2][:], xs[tt * P:(tt + 1) * P, :],
                       sname=f"xld{tt % 2}")
            p.wait("vector", f"xld{tt % 2}", ld)
            p.wait("vector", "pv")
            pv_t[tt] = p.on("vector", lambda e, tt=tt: e.reduce_sum(
                s_sum[:, tt:tt + 1], xb[tt % 2][:], axis=AX.X), inc=("pv", 1))
            p.wait("scalar", f"xld{tt % 2}", ld)
            p.wait("scalar", "ps")
            ps_t[tt] = p.on("scalar", lambda e, tt=tt: e.activation(
                junkb[:], xb[tt % 2][:], AF.Square, accum_out=s_sq[:, tt:tt + 1]),
                inc=("ps", 1))
        p.wait("vector", "ps")
        vln, sln = emit_ln_stats_math(p, s_sum[:], s_sq[:], s_m[:], s_var[:],
                                      s_rstd[:], s_nmr[:], D, cst[:, 0:1])
        if seq == 0:
            p.wait("gpsimd", "cxs", wd2_done)

        a0_pv = {}
        for tg in range(NTG):
            tts = [tg * 4 + i for i in range(4)]
            for i, tt in enumerate(tts):
                if i >= 2:
                    p.wait("sync", "ps", ps_t[("a0", tg, i - 2)])
                if tg == 0 and i == 0:
                    p.wait("sync", "ps", ps_t[NT - 1])
                    p.wait("sync", "pv", pv_t[NT - 1])
                ld = p.dma("sync", xb[i % 2][:], xs[tt * P:(tt + 1) * P, :],
                           sname=f"xld{i % 2}")
                p.wait("scalar", f"xld{i % 2}", ld)
                if i == 0 and tg == 0:
                    p.wait("scalar", "stv", vln)
                    p.wait("scalar", "sts", sln)
                    if seq == 0:
                        p.wait("scalar", "stv", rows_done)
                if i >= 2:
                    p.wait("scalar", "pv", a0_pv[i - 2])
                ps_t[("a0", tg, i)] = p.on("scalar", lambda e, i=i, tt=tt: e.activation(
                    yb[i % 2][:], xb[i % 2][:], AF.Identity,
                    bias=s_nmr[:, tt:tt + 1], scale=s_rstd[:, tt:tt + 1]), inc=("ps", 1))
                p.wait("vector", "ps", ps_t[("a0", tg, i)])
                p.wait("vector", "pv")
                a0_pv[i] = p.on("vector", lambda e, i=i: e.tensor_mul(
                    db[i][:], yb[i % 2][:], su_row[:]), inc=("pv", 1))
            p.wait("tensor", "pv")
            if seq == 0 and tg == 0:
                dbg_tap(0, db[0][:], [("pv", p.cnt["pv"])])
            emit_transpose_tiles(p, [(lambda c, i=i: db[i][:, c * P:(c + 1) * P])
                                     for i in range(4)], z0f, pt2, ident, list(range(NC)), 512)

            for m in range(NC):
                slot = p.cnt.get("kvau", 0) // NC % 2
                if m < 2:
                    p.wait("gpsimd", "qau", p.cnt.get("qau", 0))
                if p.cnt.get("kvau", 0) >= 2 * NC:
                    p.wait("gpsimd", "kvau", p.cnt["kvau"] - NC)
                ld = p.dma("gpsimd", gring[slot][:], win["hs_slab"][m], sname=f"gal{slot}")
                p.wait("tensor", f"gal{slot}", ld)
                if p.cnt.get("kvav", 0) >= 2:
                    p.wait("tensor", "kvav", p.cnt["kvav"] - 1)
                for k in range(NC):
                    p.on("tensor", lambda e, slot=slot, k=k, m=m: e.matmul(
                        pp[m % 2][:], gring[slot][:, k * P:(k + 1) * P],
                        z0f[:, k * 512:(k + 1) * 512], start=(k == 0), stop=(k == NC - 1)),
                        inc=("kvau", 1))
                p.wait("vector", "kvau")
                p.on("vector", lambda e, m=m: e.tensor_copy(
                    x1f[:, m * 512:(m + 1) * 512], pp[m % 2][:]), inc=("kvav", 1))
            p.wait("tensor", "kvav")

            sil_t = {}
            for ic in range(NFC):
                gbank = (ic % 2) * 4
                for wsrc, ppj in ((wg_bf, 0), (wu_bf, 1)):
                    gu = p.cnt.get("qau", 0)
                    slot = gu % 2
                    if ic == 0:
                        p.wait("gpsimd", "kvau", p.cnt["kvau"])
                    if gu >= 2:
                        p.wait("gpsimd", "qau", gu - 1)
                    ld = p.dma("gpsimd", gring[slot][:], wsrc[ic], sname=f"hbl{slot}")
                    p.wait("tensor", f"hbl{slot}", ld)
                    if ic >= 2 and ppj == 0:
                        p.wait("tensor", "qav", p.cnt["qav"] - 1)
                    for k in range(NC):
                        p.on("tensor", lambda e, slot=slot, k=k, ppj=ppj, gbank=gbank:
                             e.matmul(pp[gbank + ppj][:], gring[slot][:, k * P:(k + 1) * P],
                                      x1f[:, k * 512:(k + 1) * 512],
                                      start=(k == 0), stop=(k == NC - 1)),
                             inc=("qau", 1) if k == NC - 1 else None)
                p.wait("scalar", "qau")
                p.wait("scalar", "ps")
                p.wait("scalar", "qav")
                sc = p.on("scalar", lambda e, ic=ic, gbank=gbank: e.activation(
                    sgb[:], pp[gbank][:], AF.Silu, scale=r_g[:, ic:ic + 1]), inc=("ps", 1))
                p.wait("vector", "ps", sc)
                if ic >= 2:
                    p.wait("vector", "qsm", p.cnt["qsm"] - 1)
                p.on("vector", lambda e, ic=ic, gbank=gbank: e.scalar_tensor_tensor(
                    ub[ic % 2][:], pp[gbank + 1][:], r_up[:, ic:ic + 1], sgb[:],
                    ALU.mult, ALU.mult), inc=("qav", 1))
                p.wait("tensor", "qav")
                if ic >= 2:
                    p.wait("tensor", "qsv", p.cnt["qsv"] - 1)
                p.on("tensor", lambda e, ic=ic: e.matmul(
                    pp[2 + ic % 2][:], h128[:], ub[ic % 2][:], start=True, stop=True),
                    inc=("qsm", 1))
                p.wait("vector", "qsm")
                p.on("vector", lambda e, ic=ic: e.tensor_copy(
                    zf[:, ic * 512:(ic + 1) * 512], pp[2 + ic % 2][:]), inc=("qsv", 1))
            p.wait("tensor", "qsv")
            p.wait("tensor", "qav")
            if seq == 0 and tg == 0:
                dbg_tap(1, x1f[:, 0:512], [("kvav", p.cnt["kvav"])])
                dbg_tap(2, zf[:, 0:512], [("qsv", p.cnt["qsv"])])

            for dg in range(4):
                for c in range(NFC):
                    wu = p.cnt.get("wru", 0)
                    slot = wu % 4
                    if wu >= 4:
                        p.wait("gpsimd", "wru", wu - 3)
                    ld = p.dma("gpsimd", wring[slot][:],
                               wd2b[c, :, dg * 512:(dg + 1) * 512], sname=f"wrl{slot}")
                    p.wait("tensor", f"wrl{slot}", ld)
                    for i in range(4):
                        if c == 0 and p.cnt.get("wde", 0) >= 4 - i:
                            p.wait("tensor", "wde", p.cnt["wde"] - (4 - i) + 1)
                        p.on("tensor", lambda e, c=c, i=i, slot=slot: e.matmul(
                            pp[i][:], zf[:, c * 512 + i * P: c * 512 + (i + 1) * P],
                            wring[slot][:], start=(c == 0), stop=(c == NFC - 1)),
                            inc=("wru", 1) if i == 3 else None)
                p.wait("vector", "wru")
                for i in range(4):
                    p.on("vector", lambda e, dg=dg, i=i: e.tensor_tensor(
                        db[i][:, dg * 512:(dg + 1) * 512], pp[i][:],
                        rd_row[:, dg * 512:(dg + 1) * 512], ALU.mult), inc=("wde", 1))
            p.wait("tensor", "wde")
            if seq == 0 and tg == 0:
                dbg_tap(3, db[0][:], [("wde", p.cnt["wde"])])

            p.wait("vector", "wde")
            for i, tt in enumerate(tts):
                p.wait("vector", "pv")
                p.on("vector", lambda e, i=i: e.reduce_sum(
                    o_sum[:, i:i + 1], db[i][:], axis=AX.X), inc=("pv", 1))
                p.wait("scalar", "wde")
                p.wait("scalar", "ps")
                p.on("scalar", lambda e, i=i: e.activation(
                    junkb[:], db[i][:], AF.Square, accum_out=o_sq[:, i:i + 1]),
                    inc=("ps", 1))
            p.wait("vector", "ps")
            vo, so = emit_ln_stats_math(p, o_sum[:], o_sq[:], o_m[:], o_var[:],
                                        o_rstd[:], o_nmr[:], D, cst[:, 0:1])
            for i, tt in enumerate(tts):
                if i < 2:
                    p.wait("sync", "ps", ps_t[("a0", tg, 2 + i)])
                else:
                    p.wait("sync", "pv", b3_vv[i - 2])
                ld = p.dma("sync", xb[i % 2][:], xs[tt * P:(tt + 1) * P, :],
                           sname=f"xld{i % 2}")
                if i == 0:
                    p.wait("scalar", "stv", vo)
                    p.wait("scalar", "sts", so)
                if i >= 2:
                    p.wait("scalar", "pv", b3_vv[i - 2])
                sc = p.on("scalar", lambda e, i=i: e.activation(
                    yb[i % 2][:], db[i][:], AF.Identity, bias=o_nmr[:, i:i + 1],
                    scale=o_rstd[:, i:i + 1]), inc=("ps", 1))
                p.wait("vector", "ps", sc)
                p.wait("vector", f"xld{i % 2}", ld)
                vv = p.on("vector", lambda e, i=i: e.tensor_add(
                    xob[i % 2][:], xb[i % 2][:], yb[i % 2][:]), inc=("pv", 1))
                b3_vv[i] = vv
                p.wait("sync", "pv", vv)
                st = p.dma("sync", xo[tt * P:(tt + 1) * P, :], xob[i % 2][:],
                           sname=f"xst{i % 2}")
                p.wait("sync", f"xst{i % 2}", st)

    p.wait("sync", "xst0", p.cnt.get("xst0", 0))
    p.wait("sync", "xst1", p.cnt.get("xst1", 0))
    return p


def _hadamard(n):
    i = np.arange(n)
    return np.where(np.bitwise_count(i[:, None] & i[None, :]) % 2 == 0, 1.0, -1.0)


def l0ffn_host_inputs(w):
    D, FF, NC, NFC = 2048, 8192, 16, 64
    H = _hadamard(D) * (2.0 ** -12)
    hs_slab = np.ascontiguousarray(
        H.reshape(NC, P, NC, P).transpose(2, 1, 0, 3).reshape(NC, P, NC * P)
    ).astype(ml_dtypes.bfloat16)
    wslab = lambda m: np.ascontiguousarray(
        np.asarray(m, np.float32).reshape(NFC, P, NC, P).transpose(0, 3, 2, 1)
        .reshape(NFC, P, NC * P))
    wsu = np.asarray(w["Wscale_ug"], np.float32)
    return {
        "hs_slab": hs_slab,
        "h128": _hadamard(P).astype(ml_dtypes.bfloat16),
        "h64": _hadamard(64).astype(ml_dtypes.bfloat16),
        "su_row": np.asarray(w["SU_ug"], np.float32).reshape(1, D),
        "su_dp_fm": np.ascontiguousarray(
            np.asarray(w["SU_dp"], np.float32).reshape(NFC, P).T),
        "wsu_up": np.ascontiguousarray(wsu[:FF].reshape(NFC, P).T),
        "wsu_g": np.ascontiguousarray(wsu[FF:].reshape(NFC, P).T),
        "sdp_row": np.asarray(w["Wscale_dp"], np.float32).reshape(1, D),
        "w_up": wslab(w["W_up"]),
        "w_gate": wslab(w["W_gate"]),
        "w_down": np.ascontiguousarray(np.asarray(w["W_down"], np.float32)),
        "ident": np.eye(P, dtype=ml_dtypes.bfloat16),
    }


def declare_l0ffn_inputs(nc, pfx="i_"):
    D, FF, NC, NFC = 2048, 8192, 16, 64
    f32 = {
        "su_row": [1, D], "su_dp_fm": [P, NFC], "wsu_up": [P, NFC], "wsu_g": [P, NFC],
        "sdp_row": [1, D], "w_up": [NFC, P, NC * P], "w_gate": [NFC, P, NC * P],
        "w_down": [D, FF],
    }
    bf = {"hs_slab": [NC, P, NC * P], "h128": [P, P], "h64": [64, 64], "ident": [P, P]}
    win = {k: nc.dram_tensor(pfx + k, sh, F32, kind="ExternalInput").ap()
           for k, sh in f32.items()}
    win.update({k: nc.dram_tensor(pfx + k, sh, BF16, kind="ExternalInput").ap()
                for k, sh in bf.items()})
    return win



def build_l0attn(p, T, x_in, x_out, win, nseq=2, dbg=None):
    nc = p.nc
    D = 2048
    NT = T // 128
    assert NT % 4 == 0
    NC = 16
    NH, NKV, DH = 16, 4, 128
    EPS_LN = 1e-5
    SCL = 1.0 / (DH ** 0.5)
    NGLM = 2 * (NT - 3) if NT >= 3 else 0

    wq_bf = nc.dram_tensor("a_wq_bf", [NH, P, NC * P], BF16, kind="Internal").ap()
    wkvd_bf = nc.dram_tensor("a_wkvd_bf", [NKV, P, NC * P], BF16, kind="Internal").ap()
    wku_bf = nc.dram_tensor("a_wku_bf", [NKV, P, NKV * P], BF16, kind="Internal").ap()
    wvu_bf = nc.dram_tensor("a_wvu_bf", [NKV * P, NKV * P], BF16, kind="Internal").ap()
    wo_bf = nc.dram_tensor("a_wo_bf", [D, D], BF16, kind="Internal").ap()
    ctx_dram = nc.dram_tensor("a_ctx", [T, D], BF16, kind="Internal").ap()

    ident = p.sbuf("a_ident", [P, P], BF16)
    cst = p.sbuf("a_cst", [P, 1], F32)
    prot = p.sbuf("a_prot", [P, P], BF16)
    cosf = p.sbuf("a_cosf", [P, T], BF16)
    sinf = p.sbuf("a_sinf", [P, T], BF16)
    bb = p.sbuf("a_bb", [P, 384], BF16)
    gt_sb = p.sbuf("a_gt", [P, NT * 32], BF16)
    wvu_sb = p.sbuf("a_wvu", [P, NKV * 512], BF16)

    lnf = p.sbuf("a_lnf", [P, NC * T], BF16)
    cf = p.sbuf("a_cf", [P, NKV * T], BF16)
    kro = p.sbuf("a_kro", [P, NKV * T], BF16)
    vtm = p.sbuf("a_vtm", [P, max(NT, 16) * 512], BF16)
    kglob = p.sbuf("a_kglob", [P, NKV * 32], BF16)
    vglob = p.sbuf("a_vglob", [32, 512], BF16)
    qraw = p.sbuf("a_qraw", [P, T], BF16)
    kraw = qraw
    qro = p.sbuf("a_qro", [P, T], BF16)
    tmb = [p.sbuf(f"a_tmb{i}", [P, 512], BF16) for i in range(2)]
    sc1 = p.sbuf("a_sc1", [P, 416], F32)
    probs = p.sbuf("a_probs", [P, 416], BF16)
    pbT = [p.sbuf("a_pbT0", [P, 512], BF16)]
    ctxh = p.sbuf("a_ctxh", [P, NT * P], BF16)
    nmc = p.sbuf("a_nmc", [P, 4], F32)

    xb = [p.sbuf(f"a_xb{i}", [P, D], F32) for i in range(2)]
    junkb = None
    xob = [p.sbuf(f"a_xob{i}", [P, D], F32) for i in range(2)]
    _yb0 = p.sbuf("a_yb0", [P, D], BF16)
    yb = [_yb0, _yb0]
    db = [p.sbuf(f"a_db{i}", [P, D], BF16) for i in range(4)]
    garing = [p.sbuf(f"a_garing{i}", [P, NC * P], BF16) for i in range(2)]
    wring = [p.sbuf(f"a_wring{i}", [P, 512], BF16) for i in range(2)]

    s_sum = p.sbuf("a_s_sum", [P, NT], F32)
    s_sq = p.sbuf("a_s_sq", [P, NT], F32)
    s_m = p.sbuf("a_s_m", [P, NT], F32)
    s_var = p.sbuf("a_s_var", [P, NT], F32)
    s_rstd = p.sbuf("a_s_rstd", [P, NT], F32)
    s_nmr = p.sbuf("a_s_nmr", [P, NT], F32)
    o_sum = p.sbuf("a_o_sum", [P, 4], F32)
    o_sq = p.sbuf("a_o_sq", [P, 4], F32)
    o_m = p.sbuf("a_o_m", [P, 4], F32)
    o_var = p.sbuf("a_o_var", [P, 4], F32)
    o_rstd = p.sbuf("a_o_rstd", [P, 4], F32)
    o_nmr = p.sbuf("a_o_nmr", [P, 4], F32)

    pp = [p.psum(f"a_pp{i}", [P, 512], F32) for i in range(6)]
    pt2 = [p.psum(f"a_pt{i}", [P, 512], BF16) for i in range(2)]

    p.on("vector", lambda e: e.memset(cst[:, 0:1], EPS_LN), inc=("vprep", 1))
    p.wait("scalar", "vprep")
    for dst, key in [(ident, "ident"), (prot, "prot"), (cosf, "cos_fm"), (sinf, "sin_fm"),
                     (bb, "band_bias")]:
        p.dma("sync", dst[:], win[key], sname="cload")
    for kt in range(NT):
        p.dma("sync", gt_sb[:, kt * 32:(kt + 1) * 32],
              win["gt"][kt * P:(kt + 1) * P, :], sname="cload")
    for k in range(NKV):
        p.dma("gpsimd", wvu_sb[:, k * 512:(k + 1) * 512],
              win["wv_up"][k * P:(k + 1) * P, :], sname="wcast")
    for (dst, src) in [(wq_bf, win["wq"]), (wkvd_bf, win["wkv_down"]),
                       (wku_bf, win["wk_up"]), (wo_bf, win["wo"])]:
        p.dma("gpsimd", dst, src, sname="wcast")
    wcast_done = p.cnt["wcast"]
    cload_done = p.cnt["cload"]
    p.wait("tensor", "cload", cload_done)
    p.wait("tensor", "wcast", wcast_done)
    p.wait("gpsimd", "wcast", wcast_done)

    def dbg_tap(slot, src_ap, waits, barrier=True):
        if dbg is None:
            return
        for sn, v in waits:
            p.wait("gpsimd", sn, v)
        dst = dbg[slot]
        if tuple(dst.shape) != tuple(src_ap.shape):
            dst = dst[:, 0:src_ap.shape[1]]
        p.dma("gpsimd", dst, src_ap, sname=f"dbgst{slot}")
        if barrier:
            for eng in ("vector", "scalar", "tensor", "sync"):
                p.wait(eng, f"dbgst{slot}")

    b3_vv = {}
    b3_sc = {}
    ps_t = {}
    pv_t = {}
    for seq in range(nseq):
        xs = x_in[seq]
        xo = x_out[seq]

        for tt in range(NT):
            if tt >= 2:
                p.wait("sync", "pv", pv_t[tt - 2])
                p.wait("sync", "ps", ps_t[tt - 2])
            ld = p.dma("sync", xb[tt % 2][:], xs[tt * P:(tt + 1) * P, :],
                       sname=f"xld{tt % 2}")
            p.wait("vector", f"xld{tt % 2}", ld)
            p.wait("vector", "pv")
            pv_t[tt] = p.on("vector", lambda e, tt=tt: e.reduce_sum(
                s_sum[:, tt:tt + 1], xb[tt % 2][:], axis=AX.X), inc=("pv", 1))
            p.wait("scalar", f"xld{tt % 2}", ld)
            p.wait("scalar", "ps")
            ps_t[tt] = p.on("scalar", lambda e, tt=tt: e.activation(
                yb[0][:], xb[tt % 2][:], AF.Square, accum_out=s_sq[:, tt:tt + 1]),
                inc=("ps", 1))
        p.wait("vector", "ps")
        vln, sln = emit_ln_stats_math(p, s_sum[:], s_sq[:], s_m[:], s_var[:],
                                      s_rstd[:], s_nmr[:], D, cst[:, 0:1])

        a0_sc = {}
        a0_tr = {}
        for tt in range(NT):
            i = tt % 2
            if tt >= 2:
                p.wait("sync", "ps", a0_sc[tt - 2])
            if tt == 0:
                p.wait("sync", "ps", ps_t[NT - 1])
                p.wait("sync", "pv", pv_t[NT - 1])
            ld = p.dma("sync", xb[i][:], xs[tt * P:(tt + 1) * P, :], sname=f"xld{i}")
            p.wait("scalar", f"xld{i}", ld)
            if tt == 0:
                p.wait("scalar", "stv", vln)
                p.wait("scalar", "sts", sln)
            if tt >= 1:
                p.wait("scalar", "trT", a0_tr[tt - 1])
            a0_sc[tt] = p.on("scalar", lambda e, i=i, tt=tt: e.activation(
                yb[i][:], xb[i][:], AF.Identity,
                bias=s_nmr[:, tt:tt + 1], scale=s_rstd[:, tt:tt + 1]), inc=("ps", 1))
            p.wait("tensor", "ps", a0_sc[tt])
            emit_transpose_tiles(p, [(lambda c, i=i: yb[i][:, c * P:(c + 1) * P])],
                                 lnf, pt2, ident, list(range(NC)), T, coloff=tt * P)
            a0_tr[tt] = p.cnt["trT"]

        p.wait("vector", "a3m", p.cnt.get("a3m", 0))
        p.wait("vector", "sxm", p.cnt.get("sxm", 0))
        for m in range(NKV):
            slot = m % 2
            if m < 2:
                p.wait("gpsimd", "gau", p.cnt.get("gau", 0))
                p.wait("gpsimd", "qmu", p.cnt.get("qmu", 0))
            else:
                p.wait("gpsimd", "gau", p.cnt["gau"] - NT // 4)
            ld = p.dma("gpsimd", garing[slot][:], wkvd_bf[m], sname=f"gal{slot}")
            p.wait("tensor", f"gal{slot}", ld)
            for ng in range(T // 512):
                gidx = p.cnt.get("a1v", 0)
                bank = gidx % 2
                if gidx >= 2:
                    p.wait("tensor", "a1v", gidx - 1)
                for k in range(NC):
                    p.on("tensor", lambda e, slot=slot, k=k, ng=ng, bank=bank: e.matmul(
                        pp[bank][:], garing[slot][:, k * P:(k + 1) * P],
                        lnf[:, k * T + ng * 512: k * T + (ng + 1) * 512],
                        start=(k == 0), stop=(k == NC - 1)),
                        inc=("gau", 1) if k == NC - 1 else None)
                p.wait("vector", "gau")
                p.on("vector", lambda e, m=m, ng=ng, bank=bank: e.tensor_copy(
                    cf[:, m * T + ng * 512: m * T + (ng + 1) * 512], pp[bank][:]),
                    inc=("a1v", 1))
        p.wait("tensor", "a1v")

        p.wait("vector", "sxm", p.cnt.get("sxm", 0))
        p.wait("vector", "qrv", p.cnt.get("qrv", 0))
        a2slab = {}
        for m in range(NKV):
            slot = m % 2
            if m < 2:
                p.wait("gpsimd", "gau", p.cnt.get("gau", 0))
            else:
                p.wait("gpsimd", "a2m", a2slab[m - 2])
            ld = p.dma("gpsimd", garing[slot][:, 0:NKV * P], wku_bf[m], sname=f"gal{slot}")
            p.wait("tensor", f"gal{slot}", ld)
            for ng in range(T // 512):
                if p.cnt.get("a2v", 0) >= 1:
                    p.wait("tensor", "a2v", p.cnt["a2v"])
                for k in range(NKV):
                    t_ = p.on("tensor", lambda e, slot=slot, k=k, ng=ng: e.matmul(
                        pp[0][:], garing[slot][:, k * P:(k + 1) * P],
                        cf[:, k * T + ng * 512: k * T + (ng + 1) * 512],
                        start=(k == 0), stop=(k == NKV - 1)),
                        inc=("a2m", 1) if k == NKV - 1 else None)
                    if k == NKV - 1:
                        a2slab[m] = t_
                p.wait("vector", "a2m")
                p.on("vector", lambda e, ng=ng: e.tensor_copy(
                    kraw[:, ng * 512:(ng + 1) * 512], pp[0][:]), inc=("a2v", 1))
                p.wait("tensor", "a2v")
                p.on("tensor", lambda e, ng=ng: e.matmul(
                    pp[1][:], prot[:], kraw[:, ng * 512:(ng + 1) * 512],
                    start=True, stop=True), inc=("a2m", 1))
                p.wait("vector", "a2m")
                p.wait("vector", "rpv")
                p.on("vector", lambda e, ng=ng: e.tensor_mul(
                    tmb[0][:], kraw[:, ng * 512:(ng + 1) * 512],
                    cosf[:, ng * 512:(ng + 1) * 512]), inc=("rpv", 1))
                p.on("vector", lambda e, ng=ng: e.tensor_mul(
                    tmb[1][:], pp[1][:], sinf[:, ng * 512:(ng + 1) * 512]),
                    inc=("rpv", 1))
                p.wait("vector", "rpv")
                p.on("vector", lambda e, m=m, ng=ng: e.tensor_add(
                    kro[:, m * T + ng * 512: m * T + (ng + 1) * 512],
                    tmb[0][:], tmb[1][:]), inc=("rpv", 1))
        p.wait("tensor", "rpv")

        p.wait("vector", "pvm", p.cnt.get("pvm", 0))
        p.wait("vector", "wou", p.cnt.get("wou", 0))
        for tt in range(NT):
            if p.cnt.get("a3v", 0) >= 2:
                p.wait("tensor", "a3v", p.cnt["a3v"] - 1)
            for k in range(NKV):
                p.on("tensor", lambda e, k=k, tt=tt: e.matmul(
                    pp[2 + tt % 2][:], cf[:, k * T + tt * P: k * T + (tt + 1) * P],
                    wvu_sb[:, k * 512:(k + 1) * 512],
                    start=(k == 0), stop=(k == NKV - 1)),
                    inc=("a3m", 1) if k == NKV - 1 else None)
            p.wait("vector", "a3m")
            p.on("vector", lambda e, tt=tt: e.tensor_copy(
                vtm[:, tt * 512:(tt + 1) * 512], pp[2 + tt % 2][:]), inc=("a3v", 1))
        p.wait("tensor", "a3v")

        NGL_ALL = T // 64
        for m in range(NKV):
            p.wait("vector", "rpv")
            p.on("vector", lambda e, m=m: e.tensor_copy(
                kglob[:, m * 32:m * 32 + NGL_ALL],
                kro[:, m * T:(m + 1) * T].rearrange("p (g s) -> p g s", s=64)[:, :, 0]),
                inc=("a4v", 1))
        for kt in range(NT):
            p.on("tensor", lambda e, kt=kt: e.matmul(
                pp[4][0:32, :], gt_sb[:, kt * 32:kt * 32 + 32],
                vtm[:, kt * 512:(kt + 1) * 512],
                start=(kt == 0), stop=(kt == NT - 1)), inc=("a4m", 1) if kt == NT - 1 else None)
        p.wait("vector", "a4m")
        p.on("vector", lambda e: e.tensor_copy(vglob[:, :], pp[4][0:32, :]), inc=("a4v", 1))
        a4_done = p.cnt["a4v"]

        for h in range(NH):
            g = h // 4
            slot = h % 2
            if h < 2:
                p.wait("gpsimd", "a2m", p.cnt.get("a2m", 0))
                if seq > 0:
                    for _ci in range(4):
                        p.wait("gpsimd", f"cxl{_ci}", p.cnt.get(f"cxl{_ci}", 0))
            if h >= 2:
                p.wait("gpsimd", "qmu", p.cnt["qmu"] - 1)
            ld = p.dma("gpsimd", garing[slot][:], wq_bf[h], sname=f"gal{slot}")
            p.wait("tensor", f"gal{slot}", ld)
            if h == 0:
                p.wait("tensor", "a4v", a4_done)
            for ng in range(T // 512):
                if h > 0 or ng > 0:
                    p.wait("tensor", "qrv", p.cnt.get("qrv", 0))
                for k in range(NC):
                    p.on("tensor", lambda e, slot=slot, k=k, ng=ng: e.matmul(
                        pp[0][:], garing[slot][:, k * P:(k + 1) * P],
                        lnf[:, k * T + ng * 512: k * T + (ng + 1) * 512],
                        start=(k == 0), stop=(k == NC - 1)),
                        inc=("qmu", 1) if k == NC - 1 else None)
                p.wait("vector", "qmu")
                p.on("vector", lambda e, ng=ng: e.tensor_copy(
                    qraw[:, ng * 512:(ng + 1) * 512], pp[0][:]), inc=("qmv", 1))
                p.wait("tensor", "qmv")
                p.on("tensor", lambda e, ng=ng: e.matmul(
                    pp[1][:], prot[:], qraw[:, ng * 512:(ng + 1) * 512],
                    start=True, stop=True), inc=("qrm", 1))
                p.wait("vector", "qrm")
                p.wait("vector", "rpv")
                if ng == 0:
                    p.wait("vector", "sxm", p.cnt.get("sxm", 0))
                p.on("vector", lambda e, ng=ng: e.tensor_mul(
                    tmb[0][:], qraw[:, ng * 512:(ng + 1) * 512],
                    cosf[:, ng * 512:(ng + 1) * 512]), inc=("rpv", 1))
                p.on("vector", lambda e, ng=ng: e.tensor_mul(
                    tmb[1][:], pp[1][:], sinf[:, ng * 512:(ng + 1) * 512]), inc=("rpv", 1))
                p.wait("vector", "rpv")
                p.on("vector", lambda e, ng=ng: e.tensor_add(
                    qro[:, ng * 512:(ng + 1) * 512], tmb[0][:], tmb[1][:]),
                    inc=("qrv", 1))
            p.wait("tensor", "qrv", p.cnt.get("qrv", 0))
            pbread = None
            bandcp = None
            globcp = None
            for qt in range(NT):
                bc0 = max(0, qt - 2)
                nb = qt - bc0 + 1
                ngl = max(0, 2 * (qt - 2))
                nw = nb * P + ngl
                p.wait("tensor", "sxv", p.cnt.get("sxv", 0))
                p.on("tensor", lambda e, qt=qt, g=g, bc0=bc0, nb=nb, ngl=ngl: e.matmul(
                    pp[2][:, 0:nb * P], qro[:, qt * P:(qt + 1) * P],
                    kro[:, g * T + bc0 * P: g * T + (qt + 1) * P],
                    start=True, stop=(ngl == 0)), inc=("sxm", 1))
                if ngl > 0:
                    p.on("tensor", lambda e, qt=qt, g=g, nb=nb, ngl=ngl: e.matmul(
                        pp[2][:, nb * P:nb * P + ngl], qro[:, qt * P:(qt + 1) * P],
                        kglob[:, g * 32: g * 32 + ngl],
                        start=False, stop=True), inc=("sxm", 1))
                p.wait("vector", "sxm")
                p.wait("vector", "expd", p.cnt.get("expd", 0))
                p.on("vector", lambda e, nb=nb: e.tensor_add(
                    sc1[:, 0:nb * P], pp[2][:, 0:nb * P], bb[:, (3 - nb) * P:384]),
                    inc=("sxv", 1))
                if ngl > 0:
                    p.on("vector", lambda e, nb=nb, ngl=ngl: e.tensor_copy(
                        sc1[:, nb * P:nb * P + ngl], pp[2][:, nb * P:nb * P + ngl]),
                        inc=("sxv", 1))
                sx = p.cnt["sxv"]
                p.wait("scalar", "sxv", sx)
                p.wait("scalar", "ptm", p.cnt.get("ptm", 0))
                p.wait("scalar", "ps")
                se = p.on("scalar", lambda e, nw=nw: e.activation(
                    probs[:, 0:nw], sc1[:, 0:nw], AF.Exp, scale=SCL, bias=0.0,
                    accum_out=nmc[:, 2:3]), inc=("ps", 1))
                p.wait("vector", "ps", se)
                p.on("vector", lambda e: e.reciprocal(nmc[:, 3:4], nmc[:, 2:3]),
                     inc=("expd", 1))
                nch = nb + (1 if ngl > 0 else 0)
                p.wait("tensor", "ps", se)
                if bandcp is not None:
                    p.wait("tensor", "ptv", bandcp)
                for b in range(nb):
                    p.on("tensor", lambda e, b=b, nb=nb: e.matmul(
                        pt2[0][:, b * P:(b + 1) * P],
                        probs[:, b * P:(b + 1) * P], ident[:],
                        start=(b == 0), stop=(b == nb - 1), is_transpose=True),
                        inc=("ptm", 1) if b == nb - 1 else None)
                if ngl > 0:
                    if globcp is not None:
                        p.wait("tensor", "ptv", globcp)
                    p.on("tensor", lambda e, nb=nb, ngl=ngl: e.matmul(
                        pt2[1][0:ngl, 0:P], probs[:, nb * P: nb * P + ngl], ident[:],
                        start=True, stop=True, is_transpose=True), inc=("ptm", 1))
                p.wait("vector", "ptm", p.cnt["ptm"])
                if pbread is not None:
                    p.wait("vector", "pvm", pbread)
                p.on("vector", lambda e, nb=nb: e.tensor_copy(
                    pbT[0][:, 0:nb * P], pt2[0][:, 0:nb * P]), inc=("ptv", 1))
                bandcp = p.cnt["ptv"]
                if ngl > 0:
                    p.on("vector", lambda e, nb=nb, ngl=ngl: e.tensor_copy(
                        pbT[0][0:ngl, nb * P:(nb + 1) * P],
                        pt2[1][0:ngl, 0:P]), inc=("ptv", 1))
                    globcp = p.cnt["ptv"]
                p.wait("tensor", "ptv", p.cnt["ptv"])
                for b in range(nch):
                    isg = b == nb
                    wdt = ngl if isg else P
                    if isg:
                        p.on("tensor", lambda e, g=g, wdt=wdt, b=b: e.matmul(
                            pp[3][:, 0:P], pbT[0][0:wdt, b * P:(b + 1) * P],
                            vglob[0:wdt, g * P:(g + 1) * P],
                            start=(b == 0), stop=True), inc=("pvm", 1))
                    else:
                        p.on("tensor", lambda e, g=g, b=b, bc0=bc0, nb=nb, ngl=ngl: e.matmul(
                            pp[3][:, 0:P], pbT[0][:, b * P:(b + 1) * P],
                            vtm[:, (bc0 + b) * 512 + g * P: (bc0 + b) * 512 + (g + 1) * P],
                            start=(b == 0), stop=(b == nb - 1 and ngl == 0)),
                            inc=("pvm", 1))
                pbread = p.cnt["pvm"]
                p.wait("vector", "pvm")
                p.wait("vector", "expd")
                p.on("vector", lambda e, qt=qt, h=h: e.tensor_scalar(
                    ctxh[:, qt * P:(qt + 1) * P], pp[3][:, 0:P], nmc[:, 3:4], None,
                    ALU.mult, ALU.bypass), inc=("ctv", 1))
                p.wait("tensor", "ctv")
            p.wait("gpsimd", "ctv", p.cnt["ctv"])
            p.dma("gpsimd",
                  ctx_dram[:, h * P:(h + 1) * P].rearrange("(q p) c -> p q c", p=P),
                  ctxh[:].rearrange("p (q c) -> p q c", c=P), sname="cxs")
            p.wait("vector", "cxs", p.cnt["cxs"])

        p.wait("sync", "cxs", p.cnt["cxs"])
        for tgc in range(NT // 4):
            tts = [tgc * 4 + i for i in range(4)]
            for i, tt in enumerate(tts):
                if b3_sc.get(i) is not None:
                    p.wait("sync", "ps", b3_sc[i])
                ld = p.dma("sync", db[i][:], ctx_dram[tt * P:(tt + 1) * P, :],
                           sname=f"cxl{i}")
                p.wait("tensor", f"cxl{i}", ld)
            emit_transpose_tiles(p, [(lambda c, i=i: db[i][:, c * P:(c + 1) * P])
                                     for i in range(4)], vtm, pt2, ident,
                                 list(range(NC)), 512)
            for ng in range(4):
                for k in range(NC):
                    wu = p.cnt.get("wou", 0)
                    slot = wu % 2
                    if wu >= 2:
                        p.wait("gpsimd", "wou", wu - 1)
                    ld = p.dma("gpsimd", wring[slot][:],
                               wo_bf[k * P:(k + 1) * P, ng * 512:(ng + 1) * 512],
                               sname=f"wol{slot}")
                    p.wait("tensor", f"wol{slot}", ld)
                    if k == 0:
                        p.wait("tensor", "woe")
                    for i in range(4):
                        p.on("tensor", lambda e, k=k, i=i, slot=slot: e.matmul(
                            pp[i][:], vtm[:, k * 512 + i * P: k * 512 + (i + 1) * P],
                            wring[slot][:], start=(k == 0), stop=(k == NC - 1)),
                            inc=("wou", 1) if i == 3 else None)
                p.wait("vector", "wou")
                for i in range(4):
                    p.on("vector", lambda e, ng=ng, i=i: e.tensor_copy(
                        db[i][:, ng * 512:(ng + 1) * 512], pp[i][:]),
                        inc=("woe", 1) if i == 3 else None)
            p.wait("tensor", "woe")
            p.wait("vector", "woe")
            for i, tt in enumerate(tts):
                p.wait("vector", "pv")
                p.on("vector", lambda e, i=i: e.reduce_sum(
                    o_sum[:, i:i + 1], db[i][:], axis=AX.X), inc=("pv", 1))
                p.wait("scalar", "woe")
                p.wait("scalar", "ps")
                p.on("scalar", lambda e, i=i: e.activation(
                    yb[0][:], db[i][:], AF.Square, accum_out=o_sq[:, i:i + 1]),
                    inc=("ps", 1))
            p.wait("vector", "ps")
            vo, so = emit_ln_stats_math(p, o_sum[:], o_sq[:], o_m[:], o_var[:],
                                        o_rstd[:], o_nmr[:], D, cst[:, 0:1])
            for i, tt in enumerate(tts):
                if i >= 2:
                    p.wait("sync", "pv", b3_vv[i - 2])
                ld = p.dma("sync", xb[i % 2][:], xs[tt * P:(tt + 1) * P, :],
                           sname=f"xld{i % 2}")
                if i == 0:
                    p.wait("scalar", "stv", vo)
                    p.wait("scalar", "sts", so)
                if i >= 1:
                    p.wait("scalar", "pv", b3_vv[i - 1])
                sc = p.on("scalar", lambda e, i=i: e.activation(
                    yb[i % 2][:], db[i][:], AF.Identity, bias=o_nmr[:, i:i + 1],
                    scale=o_rstd[:, i:i + 1]), inc=("ps", 1))
                b3_sc[i] = sc
                p.wait("vector", "ps", sc)
                p.wait("vector", f"xld{i % 2}", ld)
                vv = p.on("vector", lambda e, i=i: e.tensor_add(
                    xob[i % 2][:], xb[i % 2][:], yb[i % 2][:]), inc=("pv", 1))
                b3_vv[i] = vv
                p.wait("sync", "pv", vv)
                st = p.dma("sync", xo[tt * P:(tt + 1) * P, :], xob[i % 2][:],
                           sname=f"xst{i % 2}")
                p.wait("sync", f"xst{i % 2}", st)

    p.wait("sync", "xst0", p.cnt.get("xst0", 0))
    p.wait("sync", "xst1", p.cnt.get("xst1", 0))
    return p


def l0attn_host_inputs(w, T):
    D, NC, NH, NKV, DH = 2048, 16, 16, 4, 128
    THETA = 10000.0

    def slab(m, nm):
        m = np.asarray(m, np.float32)
        din, dout = m.shape
        nk = din // P
        return np.ascontiguousarray(
            m.reshape(nk, P, nm, P).transpose(2, 1, 0, 3).reshape(nm, P, nk * P))

    freqs = 1.0 / THETA ** (np.arange(0, DH, 2) / DH)
    emb = np.arange(T)[:, None] * freqs[None, :]
    cos = np.concatenate([np.cos(emb), np.cos(emb)], -1).astype(np.float32)
    sin = np.concatenate([np.sin(emb), np.sin(emb)], -1).astype(np.float32)
    prot_P = np.zeros((P, P), np.float32)
    for pp_ in range(64):
        prot_P[pp_, pp_ + 64] = -1.0
        prot_P[pp_ + 64, pp_] = 1.0
    ir = np.arange(128)[:, None]
    jb = np.arange(384)[None, :]
    masked = (np.abs(ir + 256 - jb) > 256) & (jb % 64 != 0)
    bias = np.where(masked, -1e30, 0.0).astype(np.float32)
    gt = np.zeros((T, 32), np.float32)
    for gg in range(T // 64):
        gt[64 * gg, gg] = 1.0
    return {
        "wq": slab(w["Wq"], NH),
        "wkv_down": slab(w["Wkv_down"], NKV),
        "wk_up": slab(w["Wk_up"], NKV),
        "wv_up": np.ascontiguousarray(np.asarray(w["Wv_up"], np.float32)),
        "wo": np.ascontiguousarray(np.asarray(w["Wo"], np.float32)),
        "cos_fm": np.ascontiguousarray(cos.T).astype(ml_dtypes.bfloat16),
        "sin_fm": np.ascontiguousarray(sin.T).astype(ml_dtypes.bfloat16),
        "prot": prot_P.T.astype(ml_dtypes.bfloat16),
        "band_bias": bias.astype(ml_dtypes.bfloat16),
        "gt": gt.astype(ml_dtypes.bfloat16),
        "ident": np.eye(P, dtype=ml_dtypes.bfloat16),
    }


def declare_l0attn_inputs(nc, T, pfx="i_"):
    D, NC, NH, NKV = 2048, 16, 16, 4
    f32 = {"wq": [NH, P, NC * P], "wkv_down": [NKV, P, NC * P], "wk_up": [NKV, P, NKV * P],
           "wv_up": [NKV * P, NKV * P], "wo": [D, D]}
    bf = {"cos_fm": [P, T], "sin_fm": [P, T], "prot": [P, P], "gt": [T, 32],
          "ident": [P, P], "band_bias": [P, 384]}
    win = {k: nc.dram_tensor(pfx + k, sh, F32, kind="ExternalInput").ap()
           for k, sh in f32.items()}
    win.update({k: nc.dram_tensor(pfx + k, sh, BF16, kind="ExternalInput").ap()
                for k, sh in bf.items()})
    return win



def emit_attn_qt(p, NT, NG, scale, pp, pt2, ident, nm4, probs, pbT, vh, ctxh,
                 mm_first, mm_last):
    sesum_prev = {}
    ptm_bank = {ng: p.cnt.get("ptm", 0) for ng in range(NG)}
    pv_tgt = p.cnt.get("pv", 0)
    pvm_last = {}
    CPB = 512 // P

    def emit_pv(ng, jb, copy_tgt):
        for bb in range(CPB):
            b = ng * CPB + bb
            if bb == 0:
                p.wait("tensor", "ptv", copy_tgt)
                if b == 0:
                    p.wait("tensor", "ctv", p.cnt.get("ctv", 0))
            p.on("tensor", lambda e, jb=jb, bb=bb, b=b: e.matmul(
                pp[5][:, 0:P], pbT[jb][:, bb * P:(bb + 1) * P],
                vh[:, b * P:(b + 1) * P],
                start=(b == 0), stop=(b == NT - 1)), inc=("pvm", 1))
        pvm_last[jb] = p.cnt["pvm"]

    for qt in range(NT):
        smt = {}
        for ng in range(NG):
            if sesum_prev.get(ng) is not None:
                p.wait("tensor", "ps", sesum_prev[ng])
            if mm_first is not None:
                mm_first(qt, ng)
            smt[ng] = mm_last(qt, ng)
        p.wait("scalar", "pv", pv_tgt)
        sesum = {}
        prev = None
        for ng in range(NG):
            jb = ng % 2
            p.wait("scalar", "sxm", smt[ng])
            p.wait("scalar", "ptm", ptm_bank[ng])
            p.wait("scalar", "ps")
            sesum[ng] = p.on("scalar", lambda e, ng=ng, scale=scale: e.activation(
                probs[:, ng * 512:(ng + 1) * 512], pp[ng][:], AF.Exp,
                scale=scale, bias=0.0, accum_out=nm4[:, ng:ng + 1]),
                inc=("ps", 1))
            p.wait("tensor", "ps", sesum[ng])
            if p.cnt.get("ptm", 0) >= 2:
                p.wait("tensor", "ptv", p.cnt["ptv"] - 1)
            for bb in range(CPB):
                b = ng * CPB + bb
                p.on("tensor", lambda e, b=b, bb=bb, jb=jb: e.matmul(
                    pt2[jb][:, bb * P:(bb + 1) * P], probs[:, b * P:(b + 1) * P],
                    ident[:], start=(bb == 0), stop=(bb == CPB - 1),
                    is_transpose=True),
                    inc=("ptm", 1) if bb == CPB - 1 else None)
            ptm_bank[ng] = p.cnt["ptm"]
            p.wait("vector", "ptm", ptm_bank[ng])
            if pvm_last.get(jb) is not None:
                p.wait("vector", "pvm", pvm_last[jb])
            p.on("vector", lambda e, jb=jb: e.tensor_copy(
                pbT[jb][:], pt2[jb][:]), inc=("ptv", 1))
            cp_tgt = p.cnt["ptv"]
            if prev is not None:
                emit_pv(*prev)
            prev = (ng, jb, cp_tgt)
        emit_pv(*prev)
        sesum_prev = dict(sesum)
        p.wait("vector", "ps", sesum[NG - 1])
        p.on("vector", lambda e, NG=NG: e.reduce_sum(
            nm4[:, 6:7], nm4[:, 0:NG], axis=AX.X), inc=("pv", 1))
        pv_tgt = p.cnt["pv"]
        p.wait("vector", "pv")
        p.on("vector", lambda e: e.reciprocal(nm4[:, 7:8], nm4[:, 6:7]), inc=("expd", 1))
        p.wait("vector", "pvm")
        p.wait("vector", "expd")
        p.on("vector", lambda e, qt=qt: e.tensor_scalar(
            ctxh[:, qt * P:(qt + 1) * P], pp[5][:, 0:P], nm4[:, 7:8], None,
            ALU.mult, ALU.bypass), inc=("ctv", 1))



def build_l1attn(p, T, x_in, x_out, win, nseq=2, dbg=None):
    nc = p.nc
    D = 2048
    NT = T // 128
    assert NT % 4 == 0
    NC = 16
    H = 16
    QR, KVR = 1536, 512
    NQ = QR // 128
    NKVC = KVR // 128
    EPS_LN, EPS_RMS = 1e-5, 1e-6
    SCLM = 1.0 / (192.0 ** 0.5)
    SCLP = 1.0 / (128.0 ** 0.5)

    wqa_bf = nc.dram_tensor("m_wqa", [NQ, P, NC * P], BF16, kind="Internal").ap()
    wkva_bf = nc.dram_tensor("m_wkva", [NQ, P, NC * P], BF16, kind="Internal").ap()
    qbn_bf = nc.dram_tensor("m_qbn", [H, P, NQ * P], BF16, kind="Internal").ap()
    qbp_bf = nc.dram_tensor("m_qbp", [H, P, NQ * 64], BF16, kind="Internal").ap()
    kvbk_bf = nc.dram_tensor("m_kvbk", [H, P, NKVC * P], BF16, kind="Internal").ap()
    kvbv_bf = nc.dram_tensor("m_kvbv", [H, P, NKVC * P], BF16, kind="Internal").ap()
    ow_bf = nc.dram_tensor("m_ow", [D, D], BF16, kind="Internal").ap()
    spq_bf = nc.dram_tensor("m_spq", [H, P, NC * P], BF16, kind="Internal").ap()
    spk_bf = nc.dram_tensor("m_spk", [H, P, NC * P], BF16, kind="Internal").ap()
    spv_bf = nc.dram_tensor("m_spv", [H, P, NC * P], BF16, kind="Internal").ap()
    spo_bf = nc.dram_tensor("m_spo", [D, D], BF16, kind="Internal").ap()
    gw_bf = nc.dram_tensor("m_gw", [D, 2], BF16, kind="Internal").ap()
    ln_dram = nc.dram_tensor("m_ln", [P, NC * T], BF16, kind="Internal").ap()
    qm_dram = nc.dram_tensor("m_qm", [NQ, P, T], BF16, kind="Internal").ap()
    ctx_dram = nc.dram_tensor("m_ctx", [T, D], BF16, kind="Internal").ap()
    mo_dram = nc.dram_tensor("m_mo", [T, D], BF16, kind="Internal").ap()

    ident = p.sbuf("m_ident", [P, P], BF16)
    cst = p.sbuf("m_cst", [P, 2], F32)
    ones1b = p.sbuf("m_ones1b", [1, P], BF16)
    onescol = p.sbuf("m_onescol", [P, 1], BF16)
    prot1 = p.sbuf("m_prot1", [64, 64], BF16)
    sup = p.sbuf("m_sup", [P, 64], BF16)
    sdn = p.sbuf("m_sdn", [P, 64], BF16)
    cos64 = p.sbuf("m_cos64", [P, T], BF16)
    sin64 = p.sbuf("m_sin64", [P, T], BF16)
    lnw_q = p.sbuf("m_lnw_q", [P, NQ], F32)
    lnw_kv = p.sbuf("m_lnw_kv", [P, NKVC], F32)
    gb = p.sbuf("m_gb", [P, 2], F32)

    big0 = p.sbuf("m_big0", [P, NC * T], BF16)
    kvaf = p.sbuf("m_kvaf", [P, NQ * T], BF16)
    cbuf = p.sbuf("m_cbuf", [P, T], BF16)
    rb_fm = p.sbuf("m_rb", [P, max(T, 2048)], BF16)
    r2b = p.sbuf("m_r2b", [1, 512], BF16)
    kn = p.sbuf("m_kn", [P, D], BF16)
    mob = [kn, kn]
    vh = p.sbuf("m_vh", [P, NT * P], BF16)
    qn = p.sbuf("m_qn", [P, T], BF16)
    qpb = p.sbuf("m_qpb", [64, T], BF16)
    ts1 = p.sbuf("m_ts1", [P, D], BF16)
    kpe_h = ts1[0:64, 0:T] if False else ts1
    probs = p.sbuf("m_probs", [P, T], BF16)
    kpraw = probs
    sqs = probs[:, 0:512]
    pbT = [p.sbuf(f"m_pbT{i}", [P, 512], BF16) for i in range(2)]
    rp2 = [p.sbuf(f"m_rp{i}", [64, 512], BF16) for i in range(2)]
    ctxh = p.sbuf("m_ctxh", [P, NT * P], BF16)
    nm4 = p.sbuf("m_nm4", [P, 8], F32)
    g0s = p.sbuf("m_g0s", [P, NT], F32)
    g1s = p.sbuf("m_g1s", [P, NT], F32)
    gds = p.sbuf("m_gds", [P, NT], F32)
    s_var = gds

    garing = [p.sbuf(f"m_garing{i}", [P, NC * P], BF16) for i in range(2)]
    wring = [p.sbuf(f"m_wring{i}", [P, 512], BF16) for i in range(2)]
    qmring = wring

    xb = [p.sbuf(f"m_xb{i}", [P, D], F32) for i in range(2)]
    xob = p.sbuf("m_xob", [P, D], F32)
    rstg = xob
    _yb = p.sbuf("m_yb", [P, D], BF16)
    yb = [_yb, _yb]
    if 3 * T + 2 * D <= NQ * T:
        db = [rb_fm, kvaf[:, 3 * T:3 * T + D], kvaf[:, 3 * T + D:3 * T + 2 * D],
              p.sbuf("m_db0", [P, D], BF16)]
    else:
        db = [rb_fm, kvaf[:, 3 * T:3 * T + D]] + \
            [p.sbuf(f"m_db{i}", [P, D], BF16) for i in range(2)]

    s_sum = p.sbuf("m_s_sum", [P, NT], F32)
    s_sq = p.sbuf("m_s_sq", [P, NT], F32)
    s_m = p.sbuf("m_s_m", [P, NT], F32)
    s_rstd = p.sbuf("m_s_rstd", [P, NT], F32)
    s_nmr = p.sbuf("m_s_nmr", [P, NT], F32)
    o_sum = s_sum
    o_sq = s_sq
    o_m = nm4
    o_var = p.sbuf("m_o_var", [P, 4], F32)
    o_rstd = p.sbuf("m_o_rstd", [P, 4], F32)
    o_nmr = p.sbuf("m_o_nmr", [P, 4], F32)

    pp = [p.psum(f"m_pp{i}", [P, 512], F32) for i in range(6)]
    pt2 = [p.psum(f"m_pt{i}", [P, 512], BF16) for i in range(2)]

    p.on("vector", lambda e: e.memset(cst[:, 0:1], EPS_LN), inc=("vprep", 1))
    p.on("vector", lambda e: e.memset(cst[:, 1:2], EPS_RMS), inc=("vprep", 1))
    p.on("vector", lambda e: e.memset(ones1b[:], 1.0), inc=("vprep", 1))
    p.on("vector", lambda e: e.memset(onescol[:], 1.0), inc=("vprep", 1))
    p.wait("scalar", "vprep")
    p.wait("tensor", "vprep")
    for dst, key in [(ident, "ident"), (prot1, "prot1"), (sup, "sup"), (sdn, "sdn"),
                     (cos64, "cos64"), (sin64, "sin64")]:
        p.dma("sync", dst[:], win[key], sname="cload")
    p.dma("sync", lnw_q[:], win["lnw_q"], sname="cload")
    p.dma("sync", lnw_kv[:], win["lnw_kv"], sname="cload")
    p.dma("sync", gb[:], win["gate_b"].partition_broadcast(P), sname="cload")
    for (dst, src) in [(wqa_bf, win["q_a"]), (wkva_bf, win["kv_a"]), (qbn_bf, win["qbn"]),
                       (qbp_bf, win["qbp"]), (kvbk_bf, win["kvbk"]), (kvbv_bf, win["kvbv"]),
                       (ow_bf, win["ow"]), (spq_bf, win["spq"]), (spk_bf, win["spk"]),
                       (spv_bf, win["spv"]), (spo_bf, win["spo"]), (gw_bf, win["gw"])]:
        p.dma("gpsimd", dst, src, sname="wcast")
    wcast_done = p.cnt["wcast"]
    cload_done = p.cnt["cload"]
    p.wait("tensor", "cload", cload_done)
    p.wait("tensor", "wcast", wcast_done)
    p.wait("gpsimd", "wcast", wcast_done)
    p.wait("vector", "cload", cload_done)

    def dbg_tap(slot, src_ap, waits, barrier=True):
        if dbg is None:
            return
        for sn, v in waits:
            p.wait("gpsimd", sn, v)
        dst = dbg[slot]
        if tuple(dst.shape) != tuple(src_ap.shape):
            dst = dst[0:src_ap.shape[0], 0:src_ap.shape[1]]
        p.dma("gpsimd", dst, src_ap, sname=f"dbgst{slot}")
        if barrier:
            for eng in ("vector", "scalar", "tensor", "sync"):
                p.wait(eng, f"dbgst{slot}")

    NG = T // 512
    b3_vv = {}
    b3_sc = {}
    b3_mo = {}
    ps_t = {}
    pv_t = {}

    def fm_rms(src_fn, nchunk, denom, lnw_col_fn, dst_fn, eps_col, tag):
        for ng in range(NG):
            for j in range(nchunk):
                p.wait("scalar", "kvav", p.cnt.get("kvav", 0))
                p.wait("scalar", "ps")
                pst = p.on("scalar", lambda e, j=j, ng=ng: e.activation(
                    sqs[:], src_fn(j)[:, ng * 512:(ng + 1) * 512], AF.Square),
                    inc=("ps", 1))
                p.wait("tensor", "ps", pst)
                if j == 0 and p.cnt.get(tag + "r", 0) >= 1:
                    p.wait("tensor", tag + "r", p.cnt[tag + "r"])
                p.on("tensor", lambda e, j=j, nchunk=nchunk: e.matmul(
                    pp[2][0:1, :], onescol[:], sqs[:],
                    start=(j == 0), stop=(j == nchunk - 1)), inc=("ssm", 1))
                p.wait("scalar", "ssm")
                p.wait("scalar", "ps")
            p.wait("scalar", tag + "r", p.cnt.get(tag + "r", 0))
            s1 = p.on("scalar", lambda e, denom=denom, eps_col=eps_col: e.activation(
                rstg[0:1, 0:512], pp[2][0:1, :], AF.Ln, scale=1.0 / denom,
                bias=eps_col[0:1, :]), inc=("sts", 1))
            p.wait("scalar", "sts", s1)
            p.wait("scalar", "ssb", p.cnt.get("ssb", 0))
            s2 = p.on("scalar", lambda e, ng=ng: e.activation(
                r2b[0:1, 0:512], rstg[0:1, 0:512], AF.Exp, scale=-0.5),
                inc=(tag + "r", 1))
            p.wait("tensor", tag + "r", p.cnt[tag + "r"])
            if p.cnt.get("ssv", 0) >= 1:
                p.wait("tensor", "ssv", p.cnt["ssv"])
            p.on("tensor", lambda e, ng=ng: e.matmul(
                pp[3][:], ones1b[:], r2b[0:1, 0:512],
                start=True, stop=True), inc=("ssb", 1))
            p.wait("vector", "ssb")
            p.on("vector", lambda e, ng=ng: e.tensor_copy(
                rb_fm[:, ng * 512:(ng + 1) * 512], pp[3][:]), inc=("ssv", 1))
        p.wait("vector", "ssv")
        p.wait("tensor", "ssv")
        for j in range(nchunk):
            for ng in range(NG):
                p.wait("vector", tag + "a")
                p.on("vector", lambda e, j=j, ng=ng: e.scalar_tensor_tensor(
                    dst_fn(j)[:, ng * 512:(ng + 1) * 512],
                    src_fn(j)[:, ng * 512:(ng + 1) * 512],
                    lnw_col_fn(j), rb_fm[:, ng * 512:(ng + 1) * 512],
                    ALU.mult, ALU.mult), inc=(tag + "a", 1))

    for seq in range(nseq):
        xs = x_in[seq]
        xo = x_out[seq]

        for tt in range(NT):
            if tt >= 2:
                p.wait("sync", "pv", pv_t[tt - 2])
                p.wait("sync", "ps", ps_t[tt - 2])
            ld = p.dma("sync", xb[tt % 2][:], xs[tt * P:(tt + 1) * P, :],
                       sname=f"xld{tt % 2}")
            p.wait("vector", f"xld{tt % 2}", ld)
            p.wait("vector", "pv")
            pv_t[tt] = p.on("vector", lambda e, tt=tt: e.reduce_sum(
                s_sum[:, tt:tt + 1], xb[tt % 2][:], axis=AX.X), inc=("pv", 1))
            p.wait("scalar", f"xld{tt % 2}", ld)
            p.wait("scalar", "ps")
            ps_t[tt] = p.on("scalar", lambda e, tt=tt: e.activation(
                yb[0][:], xb[tt % 2][:], AF.Square, accum_out=s_sq[:, tt:tt + 1]),
                inc=("ps", 1))
        p.wait("vector", "ps")
        vln, sln = emit_ln_stats_math(p, s_sum[:], s_sq[:], s_m[:], s_var[:],
                                      s_rstd[:], s_nmr[:], D, cst[:, 0:1])

        a0_sc = {}
        a0_tr = {}
        for tt in range(NT):
            i = tt % 2
            if tt >= 2:
                p.wait("sync", "ps", a0_sc[tt - 2])
            if tt == 0:
                p.wait("sync", "ps", ps_t[NT - 1])
                p.wait("sync", "pv", pv_t[NT - 1])
                p.wait("scalar", "q2a", p.cnt.get("q2a", 0))
                p.wait("scalar", "spm", p.cnt.get("spm", 0))
            ld = p.dma("sync", xb[i][:], xs[tt * P:(tt + 1) * P, :], sname=f"xld{i}")
            p.wait("scalar", f"xld{i}", ld)
            if tt == 0:
                p.wait("scalar", "stv", vln)
                p.wait("scalar", "sts", sln)
            if tt >= 1:
                p.wait("scalar", "trT", a0_tr[tt - 1])
            a0_sc[tt] = p.on("scalar", lambda e, i=i, tt=tt: e.activation(
                yb[0][:], xb[i][:], AF.Identity,
                bias=s_nmr[:, tt:tt + 1], scale=s_rstd[:, tt:tt + 1]), inc=("ps", 1))
            p.wait("tensor", "ps", a0_sc[tt])
            if tt == 0:
                p.wait("tensor", "q2a", p.cnt.get("q2a", 0))
                p.wait("tensor", "spm", p.cnt.get("spm", 0))
            emit_transpose_tiles(p, [(lambda c: yb[0][:, c * P:(c + 1) * P])],
                                 big0, pt2, ident, list(range(NC)), T, coloff=tt * P)
            a0_tr[tt] = p.cnt["trT"]

        if True:
            gwt = garing[0]
            p.wait("gpsimd", "qau", p.cnt.get("qau", 0))
            p.wait("gpsimd", "qmu", p.cnt.get("qmu", 0))
            p.wait("gpsimd", "a1m", p.cnt.get("a1m", 0))
            p.wait("gpsimd", "a3m", p.cnt.get("a3m", 0))
            ld = p.dma("gpsimd", gwt[:, 0:2 * NC].rearrange("p (k c) -> p k c", c=2),
                       gw_bf.rearrange("(k p) c -> p k c", p=P), sname="gtl")
            gtl_done = p.cnt["gtl"]
            p.wait("tensor", "gtl", gtl_done)
            for tt in range(NT):
                gi = p.cnt.get("gtv", 0)
                if gi >= 2:
                    p.wait("tensor", "gtv", gi - 1)
                for k in range(NC):
                    p.on("tensor", lambda e, tt=tt, k=k, gi=gi: e.matmul(
                        pp[4 + gi % 2][:, 0:2], big0[:, k * T + tt * P:k * T + (tt + 1) * P],
                        gwt[:, 2 * k:2 * k + 2], start=(k == 0), stop=(k == NC - 1)),
                        inc=("gtm2", 1) if k == NC - 1 else None)
                p.wait("vector", "gtm2")
                p.wait("vector", "gtv")
                p.on("vector", lambda e, tt=tt, gi=gi: e.tensor_tensor(
                    nm4[:, 4:6], pp[4 + gi % 2][:, 0:2], gb[:, 0:2], ALU.add),
                    inc=("pv", 1))
                p.wait("vector", "pv")
                p.on("vector", lambda e, tt=tt: e.tensor_sub(
                    gds[:, tt:tt + 1], nm4[:, 4:5], nm4[:, 5:6]), inc=("gtv", 1))
            gv = p.cnt["gtv"]
            p.wait("scalar", "gtv", gv)
            sg1 = p.on("scalar", lambda e: e.activation(g0s[:], gds[:], AF.Sigmoid),
                       inc=("sts", 1))
            p.wait("vector", "sts", sg1)
            p.on("vector", lambda e: e.tensor_scalar(g1s[:], g0s[:], -1.0, 1.0,
                                                     ALU.mult, ALU.add), inc=("gsv", 1))

        for m in range(NQ):
            slot = m % 2
            if m < 2:
                p.wait("gpsimd", "kvau", p.cnt.get("kvau", 0))
                p.wait("gpsimd", "qau", p.cnt.get("qau", 0))
                p.wait("gpsimd", "gtm2", p.cnt.get("gtm2", 0))
            else:
                p.wait("gpsimd", "kvau", p.cnt["kvau"] - NG)
            ld = p.dma("gpsimd", garing[slot][:], wkva_bf[m], sname=f"gal{slot}")
            p.wait("tensor", f"gal{slot}", ld)
            for ng in range(NG):
                gidx = p.cnt.get("kvav", 0)
                bank = gidx % 2
                if gidx >= 2:
                    p.wait("tensor", "kvav", gidx - 1)
                for k in range(NC):
                    p.on("tensor", lambda e, slot=slot, k=k, ng=ng, bank=bank: e.matmul(
                        pp[bank][:], garing[slot][:, k * P:(k + 1) * P],
                        big0[:, k * T + ng * 512:k * T + (ng + 1) * 512],
                        start=(k == 0), stop=(k == NC - 1)),
                        inc=("kvau", 1) if k == NC - 1 else None)
                p.wait("vector", "kvau")
                p.on("vector", lambda e, m=m, ng=ng, bank=bank: e.tensor_copy(
                    kvaf[:, m * T + ng * 512:m * T + (ng + 1) * 512], pp[bank][:]),
                    inc=("kvav", 1))
        p.wait("tensor", "kvav")

        cdst = [cbuf] + [None, None, None]

        def csrc(j):
            return kvaf[:, j * T:(j + 1) * T]

        def cget(j):
            if j == 0:
                return cbuf[:]
            return kvaf[:, (j - 1) * T:j * T]

        fm_rms(lambda j: kvaf[:, j * T:(j + 1) * T], NKVC, KVR,
               lambda j: lnw_kv[:, j:j + 1], lambda j: cget(j), cst[:, 1:2], "ckv")
        ckva = p.cnt["ckva"]
        p.wait("tensor", "ckva", ckva)

        for m in range(NQ):
            slot = m % 2
            if m < 2:
                p.wait("gpsimd", "kvau", p.cnt["kvau"])
                p.wait("gpsimd", "ssm", p.cnt.get("ssm", 0))
            else:
                p.wait("gpsimd", "qau", p.cnt["qau"] - NG)
            ld = p.dma("gpsimd", garing[slot][:], wqa_bf[m], sname=f"gal{slot}")
            p.wait("tensor", f"gal{slot}", ld)
            for ng in range(NG):
                gidx = p.cnt.get("qav", 0)
                bank = gidx % 2
                if gidx >= 2:
                    p.wait("tensor", "qav", gidx - 1)
                    p.wait("tensor", "ps")
                for k in range(NC):
                    p.on("tensor", lambda e, slot=slot, k=k, ng=ng, bank=bank: e.matmul(
                        pp[bank][:], garing[slot][:, k * P:(k + 1) * P],
                        big0[:, k * T + ng * 512:k * T + (ng + 1) * 512],
                        start=(k == 0), stop=(k == NC - 1)),
                        inc=("qau", 1) if k == NC - 1 else None)
                p.wait("vector", "qau")
                if p.cnt.get(f"qms{bank}", 0) >= 16:
                    p.wait("vector", f"qms{bank}", p.cnt[f"qms{bank}"])
                p.on("vector", lambda e, ng=ng, bank=bank: e.tensor_copy(
                    qmring[bank][:], pp[bank][:]), inc=("qav", 1))
                p.wait("gpsimd", "qav", p.cnt["qav"])
                p.dma("gpsimd", qm_dram[m, :, ng * 512:(ng + 1) * 512], qmring[bank][:],
                      sname=f"qms{bank}")
                p.wait("scalar", "qav", p.cnt["qav"])
                p.wait("scalar", "ps")
                pst = p.on("scalar", lambda e, bank=bank: e.activation(
                    sqs[:], pp[bank][:], AF.Square), inc=("ps", 1))
                p.wait("tensor", "ps", pst)
                p.on("tensor", lambda e, m=m, ng=ng: e.matmul(
                    pp[2 + ng][0:1, :], onescol[:], sqs[:],
                    start=(m == 0), stop=(m == NQ - 1)), inc=("ssm", 1))
                p.wait("scalar", "ssm")

        for ng in range(NG):
            p.wait("scalar", "ckvr", p.cnt.get("ckvr", 0))
            s1 = p.on("scalar", lambda e, ng=ng: e.activation(
                rstg[0:1, 0:512], pp[2 + ng][0:1, :], AF.Ln, scale=1.0 / QR,
                bias=cst[0:1, 1:2]), inc=("sts", 1))
            p.wait("scalar", "sts", s1)
            p.wait("scalar", "ssb", p.cnt.get("ssb", 0))
            s2 = p.on("scalar", lambda e, ng=ng: e.activation(
                r2b[0:1, 0:512], rstg[0:1, 0:512], AF.Exp, scale=-0.5),
                inc=("ckvr", 1))
            p.wait("tensor", "ckvr", p.cnt["ckvr"])
            p.wait("tensor", "ckva", ckva)
            p.wait("tensor", "qav", p.cnt["qav"])
            if p.cnt.get("ssv", 0) >= 1:
                p.wait("tensor", "ssv", p.cnt["ssv"])
            p.on("tensor", lambda e, ng=ng: e.matmul(
                pp[0][:], ones1b[:], r2b[0:1, 0:512],
                start=True, stop=True), inc=("ssb", 1))
            p.wait("vector", "ssb")
            p.wait("vector", "ckva", ckva)
            p.on("vector", lambda e, ng=ng: e.tensor_copy(
                rb_fm[:, ng * 512:(ng + 1) * 512], pp[0][:]), inc=("ssv", 1))
        p.wait("vector", "ssv")

        p.wait("gpsimd", "qau", p.cnt["qau"])
        p.wait("gpsimd", "gtm2", p.cnt.get("gtm2", 0))
        p.dma("gpsimd", ln_dram, big0[:], sname="lnsp")
        lnsp = p.cnt["lnsp"]
        p.wait("sync", "qms0", p.cnt.get("qms0", 0))
        p.wait("sync", "qms1", p.cnt.get("qms1", 0))
        p.wait("sync", "lnsp", lnsp)
        for m in range(NQ):
            for ng in range(NG):
                qi = p.cnt.get("q2a", 0)
                slot = qi % 2
                if qi >= 2:
                    p.wait("sync", "q2a", qi - 1)
                ld = p.dma("sync", qmring[slot][:], qm_dram[m, :, ng * 512:(ng + 1) * 512],
                           sname=f"q2l{slot}")
                p.wait("vector", f"q2l{slot}", ld)
                p.wait("vector", "ssv")
                p.on("vector", lambda e, m=m, ng=ng, slot=slot: e.scalar_tensor_tensor(
                    big0[:, m * T + ng * 512:m * T + (ng + 1) * 512], qmring[slot][:],
                    lnw_q[:, m:m + 1], rb_fm[:, ng * 512:(ng + 1) * 512],
                    ALU.mult, ALU.mult), inc=("q2a", 1))
        q2a_done = p.cnt["q2a"]

        for h in range(H):
            hs = 0
            if h == 0:
                p.wait("gpsimd", "kvav", p.cnt["kvav"])
                p.wait("gpsimd", "qau", p.cnt["qau"])
            else:
                p.wait("gpsimd", "qmu", p.cnt["qmu"])
                p.wait("gpsimd", "knm", p.cnt["knm"])
                p.wait("gpsimd", "vhm", p.cnt["vhm"])
                p.wait("gpsimd", "a2m", p.cnt["a2m"])
            ld1 = p.dma("gpsimd", garing[0][:, 0:NQ * P], qbn_bf[h], sname="hbl0")
            ld2 = p.dma("gpsimd", garing[1][:, 0:NQ * 64], qbp_bf[h], sname="hbl1")
            ld3 = p.dma("gpsimd", garing[1][:, NQ * 64:NQ * 64 + NKVC * P],
                        kvbk_bf[h], sname="hbl1")
            ld4 = p.dma("gpsimd", garing[1][:, NQ * 64 + NKVC * P:NQ * 64 + 2 * NKVC * P],
                        kvbv_bf[h], sname="hbl1")
            p.wait("tensor", "hbl0", p.cnt["hbl0"])
            p.wait("tensor", "hbl1", p.cnt["hbl1"])
            kboff, vboff = NQ * 64, NQ * 64 + NKVC * P
            if h == 0:
                p.wait("tensor", "q2a", q2a_done)

            for ng in range(NG):
                if p.cnt.get("knv", 0) >= 1:
                    p.wait("tensor", "knv", p.cnt["knv"])
                for k in range(NKVC):
                    p.on("tensor", lambda e, k=k, ng=ng, kboff=kboff: e.matmul(
                        pp[0][:], garing[1][:, kboff + k * P:kboff + (k + 1) * P],
                        cget(k)[:, ng * 512:(ng + 1) * 512],
                        start=(k == 0), stop=(k == NKVC - 1)),
                        inc=("knm", 1) if k == NKVC - 1 else None)
                p.wait("vector", "knm")
                p.wait("vector", "ps")
                p.on("vector", lambda e, ng=ng: e.tensor_copy(
                    kn[:, ng * 512:(ng + 1) * 512], pp[0][:]), inc=("knv", 1))
            for tt in range(NT):
                if p.cnt.get("vhv", 0) >= 1:
                    p.wait("tensor", "vhv", p.cnt["vhv"])
                for k in range(NKVC):
                    p.on("tensor", lambda e, k=k, tt=tt, vboff=vboff: e.matmul(
                        pp[1][:, 0:P], cget(k)[:, tt * P:(tt + 1) * P],
                        garing[1][:, vboff + k * P:vboff + (k + 1) * P],
                        start=(k == 0), stop=(k == NKVC - 1)),
                        inc=("vhm", 1) if k == NKVC - 1 else None)
                p.wait("vector", "vhm")
                p.on("vector", lambda e, tt=tt: e.tensor_copy(
                    vh[:, tt * P:(tt + 1) * P], pp[1][:, 0:P]), inc=("vhv", 1))
            kchunk = 4 + h // 2
            shmat = sup if h % 2 == 0 else sdn
            p.wait("tensor", "knv", p.cnt.get("knv", 0))
            p.wait("tensor", "vhv", p.cnt.get("vhv", 0))
            for ng in range(NG):
                if p.cnt.get("a2v", 0) >= 1:
                    p.wait("tensor", "a2v", p.cnt["a2v"])
                p.on("tensor", lambda e, kchunk=kchunk, ng=ng, shmat=shmat: e.matmul(
                    pp[0][0:64, :], shmat[:],
                    kvaf[:, kchunk * T + ng * 512:kchunk * T + (ng + 1) * 512],
                    start=True, stop=True), inc=("a2m", 1))
                p.wait("vector", "a2m")
                p.on("vector", lambda e, ng=ng: e.tensor_copy(
                    kpraw[0:64, ng * 512:(ng + 1) * 512], pp[0][0:64, :]),
                    inc=("a2v", 1))
                p.wait("tensor", "a2v")
                p.on("tensor", lambda e, ng=ng: e.matmul(
                    pp[1][0:64, :], prot1[:], kpraw[0:64, ng * 512:(ng + 1) * 512],
                    start=True, stop=True), inc=("a2m", 1))
                p.wait("vector", "a2m")
                p.wait("vector", "rpv")
                p.on("vector", lambda e, ng=ng: e.tensor_mul(
                    rp2[0][:], kpraw[0:64, ng * 512:(ng + 1) * 512],
                    cos64[0:64, ng * 512:(ng + 1) * 512]), inc=("rpv", 1))
                p.on("vector", lambda e, ng=ng: e.tensor_mul(
                    rp2[1][:], pp[1][0:64, :],
                    sin64[0:64, ng * 512:(ng + 1) * 512]), inc=("rpv", 1))
                p.wait("vector", "rpv")
                p.on("vector", lambda e, ng=ng: e.tensor_add(
                    kpe_h[0:64, ng * 512:(ng + 1) * 512], rp2[0][:],
                    rp2[1][:]), inc=("a2v", 1))
            p.wait("tensor", "a2v", p.cnt.get("a2v", 0))
            for ng in range(NG):
                if p.cnt.get("qmv", 0) >= 1:
                    p.wait("tensor", "qmv", p.cnt["qmv"])
                for k in range(NQ):
                    p.on("tensor", lambda e, k=k, ng=ng: e.matmul(
                        pp[0][:], garing[0][:, k * P:(k + 1) * P],
                        big0[:, k * T + ng * 512:k * T + (ng + 1) * 512],
                        start=(k == 0), stop=(k == NQ - 1)),
                        inc=("qmu", 1) if k == NQ - 1 else None)
                p.wait("vector", "qmu")
                p.on("vector", lambda e, ng=ng: e.tensor_copy(
                    qn[:, ng * 512:(ng + 1) * 512], pp[0][:]), inc=("qmv", 1))
                for k in range(NQ):
                    p.on("tensor", lambda e, k=k, ng=ng: e.matmul(
                        pp[1][0:64, :], garing[1][:, k * 64:(k + 1) * 64],
                        big0[:, k * T + ng * 512:k * T + (ng + 1) * 512],
                        start=(k == 0), stop=(k == NQ - 1)),
                        inc=("qmu", 1) if k == NQ - 1 else None)
                p.wait("vector", "qmu")
                p.on("vector", lambda e, ng=ng: e.tensor_copy(
                    kpraw[0:64, ng * 512:(ng + 1) * 512], pp[1][0:64, :]), inc=("qmv", 1))
                p.wait("tensor", "qmv")
                p.on("tensor", lambda e, ng=ng: e.matmul(
                    pp[1][0:64, :], prot1[:], kpraw[0:64, ng * 512:(ng + 1) * 512],
                    start=True, stop=True), inc=("qpm", 1))
                p.wait("vector", "qpm")
                p.wait("vector", "rpv")
                p.on("vector", lambda e, ng=ng: e.tensor_mul(
                    rp2[0][:], kpraw[0:64, ng * 512:(ng + 1) * 512],
                    cos64[0:64, ng * 512:(ng + 1) * 512]), inc=("rpv", 1))
                p.on("vector", lambda e, ng=ng: e.tensor_mul(
                    rp2[1][:], pp[1][0:64, :],
                    sin64[0:64, ng * 512:(ng + 1) * 512]), inc=("rpv", 1))
                p.wait("vector", "rpv")
                p.on("vector", lambda e, ng=ng: e.tensor_add(
                    qpb[0:64, ng * 512:(ng + 1) * 512], rp2[0][:],
                    rp2[1][:]), inc=("qmv", 1))
            p.wait("tensor", "qmv")
            p.wait("tensor", "a2v")

            emit_attn_qt(p, NT, NG, SCLM, pp, pt2, ident, nm4, probs, pbT, vh, ctxh,
                         lambda qt, ng: p.on("tensor",
                             lambda e, qt=qt, ng=ng: e.matmul(
                                 pp[ng][:], qn[:, qt * P:(qt + 1) * P],
                                 kn[:, ng * 512:(ng + 1) * 512], start=True, stop=False)),
                         lambda qt, ng: p.on("tensor",
                             lambda e, qt=qt, ng=ng: e.matmul(
                                 pp[ng][:], qpb[0:64, qt * P:(qt + 1) * P],
                                 kpe_h[0:64, ng * 512:(ng + 1) * 512],
                                 start=False, stop=True), inc=("sxm", 1)))
            p.wait("gpsimd", "qmu", p.cnt["qmu"])
            p.on("gpsimd", lambda e: e.engine_nop(), inc=("hwu", 1))
            p.wait("gpsimd", "ctv", p.cnt["ctv"])
            p.dma("gpsimd",
                  ctx_dram[:, h * P:(h + 1) * P].rearrange("(q p) c -> p q c", p=P),
                  ctxh[:].rearrange("p (q c) -> p q c", c=P), sname="cxs")
            p.wait("vector", "cxs", p.cnt["cxs"])

        cxf = big0
        for tgc in range(NT // 4):
            tts = [tgc * 4 + i for i in range(4)]
            if tgc == 0:
                p.wait("sync", "cxs", p.cnt["cxs"])
                p.wait("sync", "knm", p.cnt["knm"])
                p.wait("sync", "sxm", p.cnt["sxm"])
                p.wait("sync", "q2a", q2a_done)
            for i, tt in enumerate(tts):
                if b3_sc.get(("c", i)) is not None:
                    p.wait("sync", f"mov{i}", b3_sc[("c", i)])
                ld = p.dma("sync", db[i][:], ctx_dram[tt * P:(tt + 1) * P, :],
                           sname=f"cxl{i}")
                p.wait("tensor", f"cxl{i}", ld)
            emit_transpose_tiles(p, [(lambda c, i=i: db[i][:, c * P:(c + 1) * P])
                                     for i in range(4)], cxf, pt2, ident,
                                 list(range(NC)), 512)
            for ng in range(4):
                for k in range(NC):
                    wu = p.cnt.get("wou", 0)
                    slot = wu % 2
                    if wu >= 2:
                        p.wait("gpsimd", "wou", wu - 1)
                    ld = p.dma("gpsimd", wring[slot][:],
                               ow_bf[k * P:(k + 1) * P, ng * 512:(ng + 1) * 512],
                               sname=f"wol{slot}")
                    p.wait("tensor", f"wol{slot}", ld)
                    if k == 0:
                        p.wait("tensor", "woe")
                    for i in range(4):
                        p.on("tensor", lambda e, k=k, i=i, slot=slot: e.matmul(
                            pp[i][:], cxf[:, k * 512 + i * P:k * 512 + (i + 1) * P],
                            wring[slot][:], start=(k == 0), stop=(k == NC - 1)),
                            inc=("wou", 1) if i == 3 else None)
                p.wait("vector", "wou")
                for i in range(4):
                    p.on("vector", lambda e, ng=ng, i=i: e.tensor_copy(
                        db[i][:, ng * 512:(ng + 1) * 512], pp[i][:]),
                        inc=("woe", 1) if i == 3 else None)
            p.wait("tensor", "woe")
            p.wait("vector", "woe")
            for i, tt in enumerate(tts):
                p.wait("gpsimd", "woe", p.cnt["woe"])
                st = p.dma("gpsimd", mo_dram[tt * P:(tt + 1) * P, :], db[i][:],
                           sname=f"mov{i}")
                b3_sc[("c", i)] = st
            for i in range(4):
                p.wait("sync", f"mov{i}", p.cnt.get(f"mov{i}", 0))

        p.wait("sync", "qmu", p.cnt["qmu"])
        p.wait("sync", "wou", p.cnt.get("wou", 0))
        p.wait("sync", "lnsp", p.cnt["lnsp"])
        p.wait("sync", "gtm2", p.cnt.get("gtm2", 0))
        ld = p.dma("sync", big0[:], ln_dram, sname="lnrl")
        lnrl = p.cnt["lnrl"]
        for h in range(H):
            hs = h % 2
            if h >= 1:
                p.wait("gpsimd", "a1m", p.cnt["a1m"])
            ldq = p.dma("gpsimd", garing[hs][:], spq_bf[h], sname=f"pql{hs}")
            p.wait("tensor", f"pql{hs}", ldq)
            if h == 0:
                p.wait("tensor", "lnrl", lnrl)
                for _mi in range(4):
                    p.wait("tensor", f"mov{_mi}", p.cnt.get(f"mov{_mi}", 0))
            for ng in range(NG):
                if p.cnt.get("a3v", 0) >= 1:
                    p.wait("tensor", "a3v", p.cnt["a3v"])
                for k in range(NC):
                    p.on("tensor", lambda e, hs=hs, k=k, ng=ng: e.matmul(
                        pp[4][:], garing[hs][:, k * P:(k + 1) * P],
                        big0[:, k * T + ng * 512:k * T + (ng + 1) * 512],
                        start=(k == 0), stop=(k == NC - 1)),
                        inc=("a3m", 1) if k == NC - 1 else None)
                p.wait("vector", "a3m")
                p.on("vector", lambda e, ng=ng: e.tensor_copy(
                    qn[:, ng * 512:(ng + 1) * 512], pp[4][:]), inc=("a3v", 1))
            p.wait("gpsimd", "a3m", p.cnt["a3m"])
            ldk = p.dma("gpsimd", garing[hs][:], spk_bf[h], sname=f"pql{hs}")
            p.wait("tensor", f"pql{hs}", ldk)
            p.wait("tensor", "a3v", p.cnt.get("a3v", 0))
            for ng in range(NG):
                if p.cnt.get("a4v", 0) >= 1:
                    p.wait("tensor", "a4v", p.cnt["a4v"])
                for k in range(NC):
                    p.on("tensor", lambda e, hs=hs, k=k, ng=ng: e.matmul(
                        pp[4][:], garing[hs][:, k * P:(k + 1) * P],
                        big0[:, k * T + ng * 512:k * T + (ng + 1) * 512],
                        start=(k == 0), stop=(k == NC - 1)),
                        inc=("a4m", 1) if k == NC - 1 else None)
                p.wait("vector", "a4m")
                p.on("vector", lambda e, ng=ng: e.tensor_copy(
                    kn[:, ng * 512:(ng + 1) * 512], pp[4][:]), inc=("a4v", 1))
            p.wait("gpsimd", "a4m", p.cnt["a4m"])
            ldv = p.dma("gpsimd", garing[hs][:], spv_bf[h], sname=f"pql{hs}")
            p.wait("tensor", f"pql{hs}", ldv)
            for tt in range(NT):
                if p.cnt.get("a1v", 0) >= 1:
                    p.wait("tensor", "a1v", p.cnt["a1v"])
                for k in range(NC):
                    p.on("tensor", lambda e, hs=hs, k=k, tt=tt: e.matmul(
                        pp[5][:, 0:P], big0[:, k * T + tt * P:k * T + (tt + 1) * P],
                        garing[hs][:, k * P:(k + 1) * P],
                        start=(k == 0), stop=(k == NC - 1)),
                        inc=("a1m", 1) if k == NC - 1 else None)
                p.wait("vector", "a1m")
                p.on("vector", lambda e, tt=tt: e.tensor_copy(
                    vh[:, tt * P:(tt + 1) * P], pp[5][:, 0:P]), inc=("a1v", 1))
            p.wait("tensor", "a1v")
            p.wait("tensor", "a4v")
            p.wait("tensor", "a3v")
            emit_attn_qt(p, NT, NG, SCLP, pp, pt2, ident, nm4, probs, pbT, vh, ctxh,
                         None,
                         lambda qt, ng: p.on("tensor",
                             lambda e, qt=qt, ng=ng: e.matmul(
                                 pp[ng][:], qn[:, qt * P:(qt + 1) * P],
                                 kn[:, ng * 512:(ng + 1) * 512], start=True, stop=True),
                             inc=("sxm", 1)))
            p.wait("gpsimd", "ctv", p.cnt["ctv"])
            if seq == 0 and h == 0:
                pass
            p.wait("gpsimd", "cxl0", p.cnt.get("cxl0", 0))
            p.wait("gpsimd", "cxl1", p.cnt.get("cxl1", 0))
            p.wait("gpsimd", "cxl2", p.cnt.get("cxl2", 0))
            p.wait("gpsimd", "cxl3", p.cnt.get("cxl3", 0))
            p.dma("gpsimd",
                  ctx_dram[:, h * P:(h + 1) * P].rearrange("(q p) c -> p q c", p=P),
                  ctxh[:].rearrange("p (q c) -> p q c", c=P), sname="cxs")
            p.wait("vector", "cxs", p.cnt["cxs"])

        for tgc in range(NT // 4):
            tts = [tgc * 4 + i for i in range(4)]
            if tgc == 0:
                p.wait("sync", "cxs", p.cnt["cxs"])
                p.wait("sync", "a4m", p.cnt["a4m"])
                for _mi in range(4):
                    p.wait("sync", f"mov{_mi}", p.cnt.get(f"mov{_mi}", 0))
            for i, tt in enumerate(tts):
                if b3_sc.get(("e", i)) is not None:
                    p.wait("sync", "pv", b3_sc[("e", i)])
                ld = p.dma("sync", db[i][:], ctx_dram[tt * P:(tt + 1) * P, :],
                           sname=f"cxl{i}")
                p.wait("tensor", f"cxl{i}", ld)
            emit_transpose_tiles(p, [(lambda c, i=i: db[i][:, c * P:(c + 1) * P])
                                     for i in range(4)], cxf, pt2, ident,
                                 list(range(NC)), 512)
            for ng in range(4):
                for k in range(NC):
                    wu = p.cnt.get("wou", 0)
                    slot = wu % 2
                    if wu >= 2:
                        p.wait("gpsimd", "wou", wu - 1)
                    ld = p.dma("gpsimd", wring[slot][:],
                               spo_bf[k * P:(k + 1) * P, ng * 512:(ng + 1) * 512],
                               sname=f"wol{slot}")
                    p.wait("tensor", f"wol{slot}", ld)
                    if k == 0:
                        p.wait("tensor", "woe")
                    for i in range(4):
                        p.on("tensor", lambda e, k=k, i=i, slot=slot: e.matmul(
                            pp[i][:], cxf[:, k * 512 + i * P:k * 512 + (i + 1) * P],
                            wring[slot][:], start=(k == 0), stop=(k == NC - 1)),
                            inc=("wou", 1) if i == 3 else None)
                p.wait("vector", "wou")
                for i in range(4):
                    p.on("vector", lambda e, ng=ng, i=i: e.tensor_copy(
                        db[i][:, ng * 512:(ng + 1) * 512], pp[i][:]),
                        inc=("woe", 1) if i == 3 else None)
            p.wait("tensor", "woe")
            p.wait("vector", "woe")
            for i, tt in enumerate(tts):
                mslot = i % 2
                if b3_mo.get(mslot) is not None:
                    p.wait("sync", "ps", b3_mo[mslot])
                ldm = p.dma("sync", mob[mslot][:], mo_dram[tt * P:(tt + 1) * P, :],
                            sname=f"q2l{mslot}")
                p.wait("vector", f"q2l{mslot}", ldm)
                if i == 0 and tgc == 0:
                    p.wait("vector", "gsv", p.cnt["gsv"])
                p.wait("vector", "gtv", p.cnt.get("gtv", 0))
                p.on("vector", lambda e, tt=tt, mslot=mslot: e.tensor_scalar(
                    ts1[:], mob[mslot][:], g0s[:, tt:tt + 1], None, ALU.mult,
                    ALU.bypass), inc=("gtv", 1))
                p.wait("vector", "gtv")
                vv0 = p.on("vector", lambda e, i=i, tt=tt, mslot=mslot: e.scalar_tensor_tensor(
                    mob[mslot][:], db[i][:], g1s[:, tt:tt + 1], ts1[:],
                    ALU.mult, ALU.add), inc=("gtv", 1))
                b3_sc[("e", i)] = vv0
                p.wait("vector", "gtv")
                p.on("vector", lambda e, i=i, mslot=mslot: e.reduce_sum(
                    o_sum[:, i:i + 1], mob[mslot][:], axis=AX.X), inc=("pv", 1))
                p.wait("scalar", "gtv", vv0)
                p.wait("scalar", "ps")
                p.on("scalar", lambda e, i=i, mslot=mslot: e.activation(
                    yb[0][:], mob[mslot][:], AF.Square, accum_out=o_sq[:, i:i + 1]),
                    inc=("ps", 1))
                p.wait("vector", "ps")
                vo, so = emit_ln_stats_math(p, o_sum[:, i:i + 1], o_sq[:, i:i + 1],
                                            o_m[:, i:i + 1], o_var[:, i:i + 1],
                                            o_rstd[:, i:i + 1], o_nmr[:, i:i + 1],
                                            D, cst[:, 0:1])
                ld = p.dma("sync", xb[i % 2][:], xs[tt * P:(tt + 1) * P, :],
                           sname=f"xld{i % 2}")
                p.wait("scalar", "stv", vo)
                p.wait("scalar", "sts", so)
                if ("er", i - 1) in b3_vv:
                    p.wait("scalar", "pv", b3_vv[("er", i - 1)])
                sc = p.on("scalar", lambda e, i=i, mslot=mslot: e.activation(
                    yb[0][:], mob[mslot][:], AF.Identity, bias=o_nmr[:, i:i + 1],
                    scale=o_rstd[:, i:i + 1]), inc=("ps", 1))
                b3_mo[mslot] = sc
                p.wait("vector", "ps", sc)
                p.wait("vector", f"xld{i % 2}", ld)
                vv = p.on("vector", lambda e, i=i: e.tensor_add(
                    xob[:], xb[i % 2][:], yb[0][:]), inc=("pv", 1))
                b3_vv[("er", i)] = vv
                p.wait("sync", "pv", vv)
                st = p.dma("sync", xo[tt * P:(tt + 1) * P, :], xob[:],
                           sname=f"xst{i % 2}")
                p.wait("sync", f"xst{i % 2}", st)

    p.wait("sync", "xst0", p.cnt.get("xst0", 0))
    p.wait("sync", "xst1", p.cnt.get("xst1", 0))
    return p


def l1attn_host_inputs(w, T):
    D, NC, H, NQ, NKVC = 2048, 16, 16, 12, 4
    THETA = 10000.0

    def slabB(m, nm):
        m = np.asarray(m, np.float32)
        dout, din = m.shape
        nk = din // P
        mt = np.ascontiguousarray(m.T)
        return np.ascontiguousarray(
            mt.reshape(nk, P, nm, dout // nm).transpose(2, 1, 0, 3)
            .reshape(nm, P, nk * (dout // nm)))

    qa = slabB(w["q_a_w"], NQ)
    kva = slabB(w["kv_a_w"], NQ)
    qb = np.asarray(w["q_b_w"], np.float32)
    qbt = qb.T
    qbn = np.zeros((H, P, NQ * P), np.float32)
    qbp = np.zeros((H, P, NQ * 64), np.float32)
    for h in range(H):
        blkq = qbt[:, h * 192:(h + 1) * 192]
        qbn[h] = blkq[:, 0:128].reshape(NQ, P, P).transpose(1, 0, 2).reshape(P, NQ * P)
        qbp[h] = blkq[:, 128:192].reshape(NQ, P, 64).transpose(1, 0, 2).reshape(P, NQ * 64)
    kvb = np.asarray(w["kv_b_w"], np.float32)
    kvbt = kvb.T
    kvbk = np.zeros((H, P, NKVC * P), np.float32)
    kvbv = np.zeros((H, P, NKVC * P), np.float32)
    for h in range(H):
        blkk = kvbt[:, h * 256:h * 256 + 128]
        blkv = kvbt[:, h * 256 + 128:(h + 1) * 256]
        kvbk[h] = blkk.reshape(NKVC, P, P).transpose(1, 0, 2).reshape(P, NKVC * P)
        kvbv[h] = blkv.reshape(NKVC, P, P).transpose(1, 0, 2).reshape(P, NKVC * P)

    def slabH(m):
        mt = np.ascontiguousarray(np.asarray(m, np.float32).T)
        return np.ascontiguousarray(
            mt.reshape(NC, P, H, P).transpose(2, 1, 0, 3).reshape(H, P, NC * P))

    inv = 1.0 / THETA ** (np.arange(0, 64, 2) / 64.0)
    t = np.arange(T)
    fr = np.outer(t, inv)
    emb = np.concatenate([fr, fr], -1)
    cos = np.cos(emb)
    sin = np.sin(emb)
    cos128 = np.tile(cos.T, (2, 1))
    sin128 = np.tile(sin.T, (2, 1))
    p1 = np.zeros((64, 64), np.float32)
    for j in range(32):
        p1[j, j + 32] = -1.0
        p1[j + 32, j] = 1.0
    supm = np.zeros((P, 64), np.float32)
    sdnm = np.zeros((P, 64), np.float32)
    for j in range(64):
        supm[j, j] = 1.0
        sdnm[64 + j, j] = 1.0
    return {
        "q_a": qa, "kv_a": kva, "qbn": qbn, "qbp": qbp, "kvbk": kvbk, "kvbv": kvbv,
        "ow": np.ascontiguousarray(np.asarray(w["o_w"], np.float32).T),
        "spq": slabH(w["sp_q_w"]), "spk": slabH(w["sp_k_w"]), "spv": slabH(w["sp_v_w"]),
        "spo": np.ascontiguousarray(np.asarray(w["sp_o_w"], np.float32).T),
        "gw": np.ascontiguousarray(np.asarray(w["gate_w"], np.float32).T),
        "gate_b": np.asarray(w["gate_b"], np.float32).reshape(1, 2),
        "lnw_q": np.ascontiguousarray(
            np.asarray(w["q_a_ln_w"], np.float32).reshape(NQ, P).T),
        "lnw_kv": np.ascontiguousarray(
            np.asarray(w["kv_a_ln_w"], np.float32).reshape(NKVC, P).T),
        "cos64": cos128.astype(ml_dtypes.bfloat16),
        "sin64": sin128.astype(ml_dtypes.bfloat16),
        "prot1": p1.T.astype(ml_dtypes.bfloat16),
        "sup": supm.astype(ml_dtypes.bfloat16),
        "sdn": sdnm.astype(ml_dtypes.bfloat16),
        "ident": np.eye(P, dtype=ml_dtypes.bfloat16),
    }


def declare_l1attn_inputs(nc, T, pfx="i_"):
    D, NC, H, NQ, NKVC = 2048, 16, 16, 12, 4
    f32 = {"q_a": [NQ, P, NC * P], "kv_a": [NQ, P, NC * P], "qbn": [H, P, NQ * P],
           "qbp": [H, P, NQ * 64], "kvbk": [H, P, NKVC * P], "kvbv": [H, P, NKVC * P],
           "ow": [D, D], "spq": [H, P, NC * P], "spk": [H, P, NC * P],
           "spv": [H, P, NC * P], "spo": [D, D], "gw": [D, 2], "gate_b": [1, 2],
           "lnw_q": [P, NQ], "lnw_kv": [P, NKVC]}
    bf = {"cos64": [P, T], "sin64": [P, T], "prot1": [64, 64], "sup": [P, 64],
          "sdn": [P, 64], "ident": [P, P]}
    win = {k: nc.dram_tensor(pfx + k, sh, F32, kind="ExternalInput").ap()
           for k, sh in f32.items()}
    win.update({k: nc.dram_tensor(pfx + k, sh, BF16, kind="ExternalInput").ap()
                for k, sh in bf.items()})
    return win

def _split_weights(inputs):
    w_l0a = {k: inputs["L0_attn_" + k] for k in
             ["Wq", "Wkv_down", "Wk_up", "Wv_up", "Wo"]}
    w_l0f = {k: inputs["L0_ffn_" + k] for k in
             ["SU_ug", "SU_dp", "Wscale_ug", "Wscale_dp", "W_up", "W_gate", "W_down"]}
    w_l1a = {k: inputs["L1_attn_" + k] for k in
             ["q_a_w", "q_a_ln_w", "q_b_w", "kv_a_w", "kv_a_ln_w", "kv_b_w", "o_w",
              "sp_q_w", "sp_k_w", "sp_v_w", "sp_o_w", "gate_w", "gate_b"]}
    w_l1f = {k: inputs["L1_ffn_" + k] for k in
             ["w_router", "b_router", "norm1_w", "wv", "bv", "wo", "norm2_w",
              "w_gate", "w_up", "w_down"]}
    return w_l0a, w_l0f, w_l1a, w_l1f


def build_fused(T, nseq):
    nc = make_nc()
    x_in = nc.dram_tensor("x_in", [nseq, T, 2048], F32, kind="ExternalInput").ap()
    x_out = nc.dram_tensor("x_out", [nseq, T, 2048], F32, kind="ExternalOutput").ap()
    h1 = nc.dram_tensor("hand1", [nseq, T, 2048], F32, kind="Internal").ap()
    h2 = nc.dram_tensor("hand2", [nseq, T, 2048], F32, kind="Internal").ap()
    h3 = nc.dram_tensor("hand3", [nseq, T, 2048], F32, kind="Internal").ap()
    win0a = declare_l0attn_inputs(nc, T, pfx="i0a_")
    win0f = declare_l0ffn_inputs(nc, pfx="i0f_")
    win1a = declare_l1attn_inputs(nc, T, pfx="i1a_")
    win1f = declare_l1ffn_inputs(nc, pfx="i1f_")
    p = Prog(nc)
    p.push_scope()
    build_l0attn(p, T, x_in, h1, win0a, nseq=nseq)
    p.pop_scope()
    p.barrier()
    p.push_scope()
    build_l0ffn(p, T, h1, h2, win0f, nseq=nseq)
    p.pop_scope()
    p.barrier()
    p.push_scope()
    build_l1attn(p, T, h2, h3, win1a, nseq=nseq)
    p.pop_scope()
    p.barrier()
    p.push_scope()
    build_l1ffn(p, T, h3, x_out, win1f, nseq=nseq)
    p.pop_scope()
    p.finalize()
    return nc, p


def kernel(**inputs):
    NCORES = 8
    T = 2048
    NSEQ = 2
    x = np.ascontiguousarray(np.asarray(inputs["L0_attn_x"], np.float32))
    assert x.shape[0] == NCORES * NSEQ
    w_l0a, w_l0f, w_l1a, w_l1f = _split_weights(inputs)
    nc, _ = build_fused(T, NSEQ)
    host = {}
    host.update({"i0a_" + k: v for k, v in l0attn_host_inputs(w_l0a, T).items()})
    host.update({"i0f_" + k: v for k, v in l0ffn_host_inputs(w_l0f).items()})
    host.update({"i1a_" + k: v for k, v in l1attn_host_inputs(w_l1a, T).items()})
    host.update({"i1f_" + k: v for k, v in l1ffn_host_inputs(w_l1f).items()})
    in_maps = []
    for c in range(NCORES):
        m = {"x_in": np.ascontiguousarray(x[c * NSEQ:(c + 1) * NSEQ])}
        m.update(host)
        in_maps.append(m)
    res = run_bass_kernel_spmd(nc, in_maps, core_ids=list(range(NCORES)))
    out = [np.asarray(res.results[c]["x_out"]) for c in range(NCORES)]
    return np.concatenate(out, axis=0).astype(np.float32)


# version 09 (final)
# speedup vs baseline: 1.7902x; 1.0079x over previous
import numpy as np
import ml_dtypes
from contextlib import ExitStack

import concourse.bass as bass
import concourse.mybir as mybir
from concourse.bass_utils import run_bass_kernel_spmd

F32 = mybir.dt.float32
BF16 = mybir.dt.bfloat16
AF = mybir.ActivationFunctionType
ALU = mybir.AluOpType
AX = mybir.AxisListType

P = 128


class Prog:
    ENGINES = ("sync", "tensor", "vector", "scalar", "gpsimd")

    def __init__(self, nc):
        self.nc = nc
        self.ctx = ExitStack()
        self.q = {e: [] for e in self.ENGINES}
        self._sems = {}
        self.cnt = {}

    def sem(self, name):
        if name not in self._sems:
            self._sems[name] = self.ctx.enter_context(self.nc.semaphore(name))
            self.cnt[name] = 0
        return self._sems[name]

    def sbuf(self, name, shape, dtype):
        ctx = getattr(self, "scope_ctx", None) or self.ctx
        return ctx.enter_context(self.nc.sbuf_tensor(name, shape, dtype))

    def psum(self, name, shape, dtype=F32):
        ctx = getattr(self, "scope_ctx", None) or self.ctx
        return ctx.enter_context(self.nc.psum_tensor(name, shape, dtype))

    def push_scope(self):
        self.scope_ctx = ExitStack()

    def pop_scope(self):
        if getattr(self, "scope_ctx", None) is not None:
            self.scope_ctx.close()
            self.scope_ctx = None

    def barrier(self):
        snap = [(k, v) for k, v in self.cnt.items() if v > 0]
        for eng in self.ENGINES:
            for k, v in snap:
                self.wait(eng, k, v)

    def on(self, eng, fn, inc=None):
        tgt = None
        if inc is not None:
            sname, n = inc
            self.sem(sname)
            self.cnt[sname] += n
            tgt = self.cnt[sname]
            sh = self._sems[sname]

            def run(e, fn=fn, sh=sh, n=n):
                fn(e).then_inc(sh, n)
        else:
            def run(e, fn=fn):
                fn(e)
        self.q[eng].append(run)
        return tgt

    def wait(self, eng, sname, val=None):
        self.sem(sname)
        if val is None:
            val = self.cnt[sname]
        if val <= 0:
            return
        sh = self._sems[sname]
        self.q[eng].append(lambda e, sh=sh, v=val: e.wait_ge(sh, v))

    def dma(self, eng, out, in_, sname=None, **kw):
        if sname is None:
            sname = "dq_" + eng
        return self.on(eng, lambda e, o=out, i=in_, kw=kw: e.dma_start(out=o, in_=i, **kw),
                       inc=(sname, 16))

    def finalize(self):
        blk = self.ctx.enter_context(self.nc.Block())
        for e in self.ENGINES:
            lst = self.q[e]
            if not lst:
                continue
            getattr(blk, e)(lambda eng, lst=lst: [f(eng) for f in lst])
        self.ctx.close()


def make_nc():
    return bass.Bass("TRN2", target_bir_lowering=False, debug=False)



def emit_bcast_row(p, pb, ones1f, src_row, dst, width):
    ng = (width + 511) // 512
    for g in range(ng):
        n = min(512, width - g * 512)
        t = p.on("tensor", lambda e, g=g, n=n: e.matmul(
            pb[:, 0:n], ones1f[:], src_row[:, g * 512:g * 512 + n], start=True, stop=True),
            inc=("bcmm", 1))
        p.wait("vector", "bcmm", t)
        ev = p.on("vector", lambda e, dst=dst, g=g, n=n: e.tensor_copy(
            dst[:, g * 512:g * 512 + n], pb[:, 0:n]), inc=("bcev", 1))
        p.wait("tensor", "bcev", ev)


def emit_ln_stats_math(p, s_sum, s_sq, s_m, s_var, s_rstd, s_nmr, D, eps_col):
    p.wait("vector", "pv")
    p.on("vector", lambda e: e.tensor_single_scalar(s_m, s_sum, 1.0 / D, ALU.mult),
         inc=("stv", 1))
    p.on("vector", lambda e: e.tensor_single_scalar(s_var, s_sq, 1.0 / D, ALU.mult),
         inc=("stv", 1))
    p.wait("vector", "stv")
    p.on("vector", lambda e: e.tensor_mul(s_nmr, s_m, s_m), inc=("stv", 1))
    p.wait("vector", "stv")
    v = p.on("vector", lambda e: e.tensor_sub(s_sq, s_var, s_nmr), inc=("stv", 1))
    p.wait("scalar", "stv", v)
    s1 = p.on("scalar", lambda e: e.activation(s_var, s_sq, AF.Ln, bias=eps_col),
              inc=("sts", 1))
    p.wait("scalar", "sts", s1)
    s2 = p.on("scalar", lambda e: e.activation(s_rstd, s_var, AF.Exp, scale=-0.5),
              inc=("sts", 1))
    p.wait("vector", "sts", s2)
    p.wait("vector", "stv")
    v = p.on("vector", lambda e: e.scalar_tensor_tensor(s_nmr, s_m, -1.0, s_rstd,
                                                        ALU.mult, ALU.mult),
             inc=("stv", 1))
    return v, s2


def emit_rms_stats_math(p, s_sq, s_rstd, D, eps_col):
    p.wait("vector", "ps")
    p.wait("vector", "stv")
    v = p.on("vector", lambda e: e.tensor_single_scalar(s_rstd, s_sq, 1.0 / D, ALU.mult),
             inc=("stv", 1))
    p.wait("scalar", "stv", v)
    s1 = p.on("scalar", lambda e: e.activation(s_sq, s_rstd, AF.Ln, bias=eps_col),
              inc=("sts", 1))
    p.wait("scalar", "sts", s1)
    s2 = p.on("scalar", lambda e: e.activation(s_rstd, s_sq, AF.Exp, scale=-0.5),
              inc=("sts", 1))
    return s2


def emit_transpose_tiles(p, src_tile_fns, dst_slab, pt2, ident, chunks, W, coloff=0):
    ntt = len(src_tile_fns)
    for ci, c in enumerate(chunks):
        j = p.cnt.get("trT", 0) % 2
        if p.cnt.get("trT", 0) >= 2:
            p.wait("tensor", "trV", p.cnt["trV"] - 1)
        for tt in range(ntt):
            t = p.on("tensor", lambda e, c=c, tt=tt, j=j, ntt=ntt:
                     e.matmul(pt2[j][:, tt * 128:(tt + 1) * 128],
                              src_tile_fns[tt](c), ident[:],
                              start=(tt == 0), stop=(tt == ntt - 1),
                              is_transpose=True),
                     inc=("trT", 1) if tt == ntt - 1 else None)
        p.wait("vector", "trT", t)
        p.on("vector", lambda e, ci=ci, j=j, ntt=ntt:
             e.tensor_copy(dst_slab[:, ci * W + coloff:
                                    ci * W + coloff + ntt * 128],
                           pt2[j][:, 0:ntt * 128]),
             inc=("trV", 1))
    p.wait("tensor", "trV")



def build_l1ffn(p, T, x_in, x_out, win, nseq=2, dbg=None):
    nc = p.nc
    D, FF = 2048, 5632
    NT = T // 128
    assert NT % 4 == 0
    NTG = NT // 4
    NC, NFC = 16, 44
    EPS_LN, EPS_RMS = 1e-5, 1e-6

    wv_bf = nc.dram_tensor("s_wv_bf", [D, D], BF16, kind="Internal").ap()
    wo_bf = nc.dram_tensor("s_wo_bf", [D, D], BF16, kind="Internal").ap()
    wg_bf = nc.dram_tensor("s_wg_bf", [NFC, P, NC * P], BF16, kind="Internal").ap()
    wu_bf = nc.dram_tensor("s_wu_bf", [NFC, P, NC * P], BF16, kind="Internal").ap()
    wd_bf = nc.dram_tensor("s_wd_bf", [FF, D], BF16, kind="Internal").ap()

    cst = p.sbuf("cst", [P, 2], F32)
    ident = p.sbuf("ident", [P, P], BF16)
    n1_row = p.sbuf("n1_row", [P, D], BF16)
    n2_row = p.sbuf("n2_row", [P, D], BF16)
    bv_row = p.sbuf("bv_row", [P, D], BF16)
    wr_row = p.sbuf("wr_row", [P, D], F32)
    br_col = p.sbuf("br_col", [P, 1], F32)
    wr1 = p.sbuf("wr1", [P, 1], F32)

    xb = [p.sbuf(f"xb{i}", [P, D], F32) for i in range(2)]
    xob = [p.sbuf(f"xob{i}", [P, D], F32) for i in range(2)]
    yb = [p.sbuf(f"yb{i}", [P, D], BF16) for i in range(4)]
    hb = [p.sbuf(f"hb{i}", [P, D], BF16) for i in range(4)]
    db = [p.sbuf(f"db{i}", [P, D], BF16) for i in range(4)]
    h1f = p.sbuf("h1f", [P, NC * 512], BF16)
    h2f = p.sbuf("h2f", [P, NC * 512], BF16)
    ufm = p.sbuf("ufm", [P, NFC * 512], BF16)
    junkb = p.sbuf("junkb", [P, D], BF16)
    junkv = p.sbuf("junkv", [P, D], BF16)
    wring = [p.sbuf(f"wring{i}", [P, 512], BF16) for i in range(4)]
    gring = [p.sbuf(f"gring{i}", [P, NC * P], BF16) for i in range(2)]
    sgb = p.sbuf("sgb", [P, 512], BF16)

    s_sum = p.sbuf("s_sum", [P, NT], F32)
    s_sq = p.sbuf("s_sq", [P, NT], F32)
    s_rsc = p.sbuf("s_rsc", [P, NT], F32)
    s_m = p.sbuf("s_m", [P, NT], F32)
    s_var = p.sbuf("s_var", [P, NT], F32)
    s_rstd = p.sbuf("s_rstd", [P, NT], F32)
    s_nmr = p.sbuf("s_nmr", [P, NT], F32)
    s_mask = p.sbuf("s_mask", [P, NT], F32)
    g_sq = p.sbuf("g_sq", [P, 4], F32)
    g_r1 = p.sbuf("g_r1", [P, 4], F32)
    g_sq2 = p.sbuf("g_sq2", [P, 4], F32)
    g_r2 = p.sbuf("g_r2", [P, 4], F32)
    o_sum = p.sbuf("o_sum", [P, 4], F32)
    o_sq = p.sbuf("o_sq", [P, 4], F32)
    o_m = p.sbuf("o_m", [P, 4], F32)
    o_var = p.sbuf("o_var", [P, 4], F32)
    o_rstd = p.sbuf("o_rstd", [P, 4], F32)
    o_nmr = p.sbuf("o_nmr", [P, 4], F32)

    pacc = [p.psum(f"pacc{i}", [P, 512], F32) for i in range(4)]
    pg = p.psum("pg", [P, 512], F32)
    pu = p.psum("pu", [P, 512], F32)
    pb = pg
    pt2 = [p.psum(f"pt{i}", [P, 512], BF16) for i in range(2)]

    p.on("vector", lambda e: e.memset(cst[:, 0:1], EPS_LN), inc=("vprep", 1))
    p.on("vector", lambda e: e.memset(cst[:, 1:2], EPS_RMS), inc=("vprep", 1))
    vprep_all = p.cnt["vprep"]
    p.wait("scalar", "vprep", vprep_all)
    p.dma("sync", ident[:], win["ident"], sname="cload")
    for (dst, src) in [(wv_bf, win["wv"]), (wo_bf, win["wo"]), (wg_bf, win["w_gate"]),
                       (wu_bf, win["w_up"]), (wd_bf, win["w_down"])]:
        p.dma("gpsimd", dst, src, sname="wcast")
    wcast_done = p.cnt["wcast"]

    p.dma("gpsimd", n1_row[:], win["norm1_w"].partition_broadcast(P), sname="wcast")
    p.dma("gpsimd", n2_row[:], win["norm2_w"].partition_broadcast(P), sname="wcast")
    p.dma("gpsimd", bv_row[:], win["bv"].partition_broadcast(P), sname="wcast")
    p.dma("sync", wr_row[:], win["w_router"].partition_broadcast(P), sname="cload")
    p.dma("sync", br_col[:], win["b_router"].partition_broadcast(P), sname="cload")
    wcast_done = p.cnt["wcast"]
    p.wait("vector", "cload")
    p.wait("vector", "wcast", wcast_done)
    p.wait("tensor", "cload")
    wrv = p.on("vector", lambda e: e.reduce_sum(wr1[:], wr_row[:], axis=AX.X), inc=("stv", 1))

    def dbg_tap(slot, src_ap, waits, barrier=True):
        if dbg is None:
            return
        for sn, v in waits:
            p.wait("gpsimd", sn, v)
        dst = dbg[slot]
        if tuple(dst.shape) != tuple(src_ap.shape):
            dst = dst[:, 0:src_ap.shape[1]]
        p.dma("gpsimd", dst, src_ap, sname=f"dbgst{slot}")
        if barrier:
            for eng in ("vector", "scalar", "tensor", "sync"):
                p.wait(eng, f"dbgst{slot}")

    def stream_mm(wap_fn, ngroups, nk, lhsT_fn, evac_fn, etag):
        for g in range(ngroups):
            for k in range(nk):
                if p.cnt.get("wru", 0) >= 4:
                    p.wait("gpsimd", "wru", p.cnt["wru"] - 3)
                slot = p.cnt.get("wru", 0) % 4
                ld = p.dma("gpsimd", wring[slot][:], wap_fn(g, k), sname=f"wrl{slot}")
                p.wait("tensor", f"wrl{slot}", ld)
                for i in range(4):
                    if k == 0 and p.cnt.get(etag, 0) >= 4 - i:
                        p.wait("tensor", etag, p.cnt[etag] - (4 - i) + 1)
                    p.on("tensor", lambda e, g=g, k=k, i=i, slot=slot, nk=nk: e.matmul(
                        pacc[i][:], lhsT_fn(g, k, i), wring[slot][:],
                        start=(k == 0), stop=(k == nk - 1)),
                        inc=("wru", 1) if i == 3 else None)
            p.wait("vector", "wru")
            evac_fn(g, etag)
        p.wait("tensor", etag)

    b3_vv = {}
    for seq in range(nseq):
        xs = x_in[seq]
        xo = x_out[seq]

        pv_t = {}
        ps_t = {}
        for tt in range(NT):
            if tt >= 2:
                p.wait("sync", "pv", pv_t[tt - 2])
                p.wait("sync", "ps", ps_t[tt - 2])
            ld = p.dma("sync", xb[tt % 2][:], xs[tt * P:(tt + 1) * P, :],
                       sname=f"xld{tt % 2}")
            p.wait("vector", f"xld{tt % 2}", ld)
            p.on("vector", lambda e, tt=tt: e.reduce_sum(
                s_sum[:, tt:tt + 1], xb[tt % 2][:], axis=AX.X), inc=("pv", 1))
            p.wait("vector", "pv")
            pv_t[tt] = p.on("vector", lambda e, tt=tt: e.scalar_tensor_tensor(
                junkv[:], xb[tt % 2][:], 1.0, wr_row[:], ALU.mult, ALU.mult,
                accum_out=s_rsc[:, tt:tt + 1]), inc=("pv", 1))
            p.wait("scalar", f"xld{tt % 2}", ld)
            p.wait("scalar", "ps")
            ps_t[tt] = p.on("scalar", lambda e, tt=tt: e.activation(
                junkb[:], xb[tt % 2][:], AF.Square, accum_out=s_sq[:, tt:tt + 1]),
                inc=("ps", 1))
        p.wait("vector", "ps")
        vln, sln = emit_ln_stats_math(p, s_sum[:], s_sq[:], s_m[:], s_var[:],
                                      s_rstd[:], s_nmr[:], D, cst[:, 0:1])
        p.wait("vector", "stv", wrv)
        p.wait("vector", "sts", sln)
        p.wait("vector", "pv")
        p.on("vector", lambda e: e.tensor_scalar(s_mask[:], s_m[:], wr1[:, 0:1], None,
                                                 ALU.mult, ALU.bypass), inc=("stv", 1))
        p.wait("vector", "stv")
        p.on("vector", lambda e: e.tensor_sub(s_var[:], s_rsc[:], s_mask[:]),
             inc=("stv", 1))
        p.wait("vector", "stv")
        p.on("vector", lambda e: e.tensor_mul(s_mask[:], s_var[:], s_rstd[:]),
             inc=("stv", 1))
        p.wait("vector", "stv")
        p.on("vector", lambda e: e.tensor_scalar(s_var[:], s_mask[:], br_col[:, 0:1], None,
                                                 ALU.add, ALU.bypass), inc=("stv", 1))
        p.wait("vector", "stv")
        vmask = p.on("vector", lambda e: e.tensor_single_scalar(
            s_mask[:], s_var[:], 0.0, ALU.is_gt), inc=("stv", 1))
        p.wait("tensor", "wcast", wcast_done)
        p.wait("gpsimd", "wcast", wcast_done)

        for tg in range(NTG):
            tts = [tg * 4 + i for i in range(4)]

            for i, tt in enumerate(tts):
                if i >= 2:
                    p.wait("sync", "ps", ps_t[("a0", tg, i - 2)])
                if tg == 0 and i == 0:
                    p.wait("sync", "ps", ps_t[NT - 1])
                    p.wait("sync", "pv", pv_t[NT - 1])
                ld = p.dma("sync", xb[i % 2][:], xs[tt * P:(tt + 1) * P, :],
                           sname=f"xld{i % 2}")
                p.wait("scalar", f"xld{i % 2}", ld)
                if i == 0 and tg == 0:
                    p.wait("scalar", "stv", vln)
                    p.wait("scalar", "sts", sln)
                p.on("scalar", lambda e, i=i, tt=tt: e.activation(
                    yb[i][:], xb[i % 2][:], AF.Identity,
                    bias=s_nmr[:, tt:tt + 1], scale=s_rstd[:, tt:tt + 1]),
                    inc=("ps", 1))
                p.wait("scalar", "ps")
                ps_t[("a0", tg, i)] = p.on("scalar", lambda e, i=i: e.activation(
                    junkb[:], yb[i][:], AF.Square, accum_out=g_sq[:, i:i + 1]),
                    inc=("ps", 1))
            if seq == 0 and tg == 0:
                dbg_tap(0, yb[0][:], [("ps", p.cnt["ps"])])
            r1t = emit_rms_stats_math(p, g_sq[:], g_r1[:], D, cst[:, 1:2])
            p.wait("vector", "sts", r1t)
            p.wait("vector", "pv")
            for i in range(4):
                p.on("vector", lambda e, i=i: e.scalar_tensor_tensor(
                    hb[i][:], yb[i][:], g_r1[:, i:i + 1], n1_row[:], ALU.mult, ALU.mult),
                    inc=("pv", 1))
            p.wait("tensor", "pv")
            if seq == 0 and tg == 0:
                dbg_tap(1, hb[0][:], [("pv", p.cnt["pv"])])
            emit_transpose_tiles(p, [(lambda c, i=i: hb[i][:, c * P:(c + 1) * P])
                                     for i in range(4)], h1f, pt2, ident, list(range(NC)), 512)

            def lhs_h1(g, k, i):
                return h1f[:, k * 512 + i * P: k * 512 + (i + 1) * P]

            def evac_v(g, etag):
                for i in range(4):
                    p.on("vector", lambda e, g=g, i=i: e.tensor_tensor(
                        db[i][:, g * 512:(g + 1) * 512], pacc[i][:],
                        bv_row[:, g * 512:(g + 1) * 512], ALU.add), inc=(etag, 1))

            stream_mm(lambda g, k: wv_bf[k * P:(k + 1) * P, g * 512:(g + 1) * 512],
                      4, NC, lhs_h1, evac_v, "wve")

            if seq == 0 and tg == 0:
                dbg_tap(2, db[0][:], [("wve", p.cnt["wve"])])
            emit_transpose_tiles(p, [(lambda c, i=i: db[i][:, c * P:(c + 1) * P])
                                     for i in range(4)], h1f, pt2, ident, list(range(NC)), 512)

            def evac_h(g, etag):
                for i in range(4):
                    p.on("vector", lambda e, g=g, i=i: e.tensor_tensor(
                        hb[i][:, g * 512:(g + 1) * 512], pacc[i][:],
                        yb[i][:, g * 512:(g + 1) * 512], ALU.add), inc=(etag, 1))

            stream_mm(lambda g, k: wo_bf[k * P:(k + 1) * P, g * 512:(g + 1) * 512],
                      4, NC, lhs_h1, evac_h, "woe")

            if seq == 0 and tg == 0:
                dbg_tap(3, hb[0][:], [("woe", p.cnt["woe"])])
            p.wait("scalar", "woe")
            for i in range(4):
                p.wait("scalar", "ps")
                p.on("scalar", lambda e, i=i: e.activation(
                    junkb[:], hb[i][:], AF.Square, accum_out=g_sq2[:, i:i + 1]),
                    inc=("ps", 1))
            r2t = emit_rms_stats_math(p, g_sq2[:], g_r2[:], D, cst[:, 1:2])
            p.wait("vector", "sts", r2t)
            for i in range(4):
                p.on("vector", lambda e, i=i: e.scalar_tensor_tensor(
                    db[i][:], hb[i][:], g_r2[:, i:i + 1], n2_row[:], ALU.mult, ALU.mult),
                    inc=("pv", 1) if i == 3 else None)
            p.wait("tensor", "pv")
            emit_transpose_tiles(p, [(lambda c, i=i: db[i][:, c * P:(c + 1) * P])
                                     for i in range(4)], h2f, pt2, ident, list(range(NC)), 512)

            if seq == 0 and tg == 0:
                dbg_tap(4, db[0][:], [("pv", p.cnt["pv"])])
            for fc in range(NFC):
                for j, (wsrc, ps_) in enumerate([(wg_bf, pg), (wu_bf, pu)]):
                    if p.cnt.get("gau", 0) >= 2:
                        p.wait("gpsimd", "gau", p.cnt["gau"] - 1)
                    gslot = p.cnt.get("gau", 0) % 2
                    ld = p.dma("gpsimd", gring[gslot][:], wsrc[fc], sname=f"gal{gslot}")
                    p.wait("tensor", f"gal{gslot}", ld)
                    if j == 0:
                        p.wait("tensor", "ctv")
                    for k in range(NC):
                        p.on("tensor", lambda e, k=k, gslot=gslot, ps_=ps_: e.matmul(
                            ps_[:], gring[gslot][:, k * P:(k + 1) * P],
                            h2f[:, k * 512:(k + 1) * 512],
                            start=(k == 0), stop=(k == NC - 1)),
                            inc=("gau", 1) if k == NC - 1 else None)
                p.wait("scalar", "gau")
                sc = p.on("scalar", lambda e: e.activation(sgb[:], pg[:], AF.Silu),
                          inc=("ps", 1))
                p.wait("vector", "ps", sc)
                p.on("vector", lambda e, fc=fc: e.tensor_tensor(
                    ufm[:, fc * 512:(fc + 1) * 512], sgb[:], pu[:], ALU.mult),
                    inc=("ctv", 1))
            p.wait("tensor", "ctv")

            if seq == 0 and tg == 0:
                dbg_tap(5, ufm[:, 0:512], [("ctv", p.cnt["ctv"])])
            def lhs_u(g, k, i):
                return ufm[:, k * 512 + i * P: k * 512 + (i + 1) * P]

            def evac_d(g, etag):
                for i in range(4):
                    p.on("vector", lambda e, g=g, i=i: e.tensor_tensor(
                        db[i][:, g * 512:(g + 1) * 512], pacc[i][:],
                        hb[i][:, g * 512:(g + 1) * 512], ALU.add), inc=(etag, 1))

            stream_mm(lambda g, k: wd_bf[k * P:(k + 1) * P, g * 512:(g + 1) * 512],
                      4, NFC, lhs_u, evac_d, "wde")

            if seq == 0 and tg == 0:
                dbg_tap(6, db[0][:], [("wde", p.cnt["wde"])])
            p.wait("vector", "wde")
            p.wait("vector", "ps")
            if tg == 0:
                p.wait("vector", "stv", vmask)
            for i, tt in enumerate(tts):
                p.wait("vector", "pv")
                p.on("vector", lambda e, i=i: e.tensor_sub(junkv[:], db[i][:], yb[i][:]),
                     inc=("pv", 1))
                p.wait("vector", "pv")
                vv = p.on("vector", lambda e, i=i, tt=tt: e.scalar_tensor_tensor(
                    db[i][:], junkv[:], s_mask[:, tt:tt + 1], yb[i][:], ALU.mult, ALU.add),
                    inc=("pv", 1))
                p.on("vector", lambda e, i=i: e.reduce_sum(
                    o_sum[:, i:i + 1], db[i][:], axis=AX.X), inc=("pv", 1))
                p.wait("scalar", "pv", vv)
                p.wait("scalar", "ps")
                p.on("scalar", lambda e, i=i: e.activation(
                    junkb[:], db[i][:], AF.Square, accum_out=o_sq[:, i:i + 1]),
                    inc=("ps", 1))
            p.wait("vector", "ps")
            if seq == 0 and tg == 0:
                dbg_tap(7, db[0][:], [("ps", p.cnt["ps"])])
                dbg_tap(8, s_mask[:, 0:NT], [("stv", p.cnt["stv"])])
                dbg_tap(9, s_rstd[:, 0:NT], [])
                dbg_tap(10, s_m[:, 0:NT], [])
                dbg_tap(11, s_rsc[:, 0:NT], [])
            vo, so = emit_ln_stats_math(p, o_sum[:], o_sq[:], o_m[:], o_var[:],
                                        o_rstd[:], o_nmr[:], D, cst[:, 0:1])
            for i, tt in enumerate(tts):
                if i < 2:
                    p.wait("sync", "ps", ps_t[("a0", tg, 2 + i)])
                else:
                    p.wait("sync", "pv", b3_vv[i - 2])
                ld = p.dma("sync", xb[i % 2][:], xs[tt * P:(tt + 1) * P, :],
                           sname=f"xld{i % 2}")
                if i == 0:
                    p.wait("scalar", "stv", vo)
                    p.wait("scalar", "sts", so)
                sc = p.on("scalar", lambda e, i=i: e.activation(
                    yb[i][:], db[i][:], AF.Identity, bias=o_nmr[:, i:i + 1],
                    scale=o_rstd[:, i:i + 1]), inc=("ps", 1))
                p.wait("vector", "ps", sc)
                p.wait("vector", f"xld{i % 2}", ld)
                vv = p.on("vector", lambda e, i=i: e.tensor_add(
                    xob[i % 2][:], xb[i % 2][:], yb[i][:]), inc=("pv", 1))
                b3_vv[i] = vv
                p.wait("sync", "pv", vv)
                st = p.dma("sync", xo[tt * P:(tt + 1) * P, :], xob[i % 2][:],
                           sname=f"xst{i % 2}")
                p.wait("sync", f"xst{i % 2}", st)

    p.wait("sync", "xst0", p.cnt.get("xst0", 0))
    p.wait("sync", "xst1", p.cnt.get("xst1", 0))
    return p



def l1ffn_host_inputs(w):
    D, FF, NC, NFC = 2048, 5632, 16, 44
    slab = lambda m: np.ascontiguousarray(
        np.asarray(m, np.float32).reshape(NC, P, NFC, P).transpose(2, 1, 0, 3)
        .reshape(NFC, P, NC * P))
    return {
        "wv": np.ascontiguousarray(np.asarray(w["wv"], np.float32)),
        "wo": np.ascontiguousarray(np.asarray(w["wo"], np.float32)),
        "w_gate": slab(w["w_gate"]),
        "w_up": slab(w["w_up"]),
        "w_down": np.ascontiguousarray(np.asarray(w["w_down"], np.float32)),
        "bv": np.asarray(w["bv"], np.float32).reshape(1, D),
        "w_router": np.asarray(w["w_router"], np.float32).reshape(1, D),
        "b_router": np.asarray(w["b_router"], np.float32).reshape(1, 1),
        "norm1_w": np.asarray(w["norm1_w"], np.float32).reshape(1, D),
        "norm2_w": np.asarray(w["norm2_w"], np.float32).reshape(1, D),
        "ident": np.eye(P, dtype=ml_dtypes.bfloat16),
    }


def declare_l1ffn_inputs(nc, pfx="i_"):
    D, FF, NC, NFC = 2048, 5632, 16, 44
    names = {
        "wv": [D, D], "wo": [D, D], "w_gate": [NFC, P, NC * P], "w_up": [NFC, P, NC * P],
        "w_down": [FF, D], "bv": [1, D], "w_router": [1, D], "b_router": [1, 1],
        "norm1_w": [1, D], "norm2_w": [1, D],
    }
    win = {k: nc.dram_tensor(pfx + k, sh, F32, kind="ExternalInput").ap()
           for k, sh in names.items()}
    win["ident"] = nc.dram_tensor(pfx + "ident", [P, P], BF16, kind="ExternalInput").ap()
    return win



def build_l0ffn(p, T, x_in, x_out, win, nseq=2, dbg=None):
    nc = p.nc
    D, FF = 2048, 8192
    NT = T // 128
    assert NT % 4 == 0
    NTG = NT // 4
    NC, NFC = 16, 64
    EPS_LN = 1e-5
    SQ2_64 = 64.0 * (2.0 ** 0.5)
    ISQ8192 = 1.0 / 90.50966799187808

    wu_bf = nc.dram_tensor("f_wu_bf", [NFC, P, NC * P], BF16, kind="Internal").ap()
    wg_bf = nc.dram_tensor("f_wg_bf", [NFC, P, NC * P], BF16, kind="Internal").ap()
    wd_bf = nc.dram_tensor("f_wd_bf", [D, FF], BF16, kind="Internal").ap()
    wd2a = nc.dram_tensor("f_wd2a", [NFC, D, P], BF16, kind="Internal").ap()
    wd2b = nc.dram_tensor("f_wd2b", [NFC, P, D], BF16, kind="Internal").ap()

    ident = p.sbuf("f_ident", [P, P], BF16)
    cst = p.sbuf("f_cst", [P, 1], F32)
    h64 = p.sbuf("f_h64", [64, 64], BF16)
    h128 = p.sbuf("f_h128", [P, P], BF16)
    su_row = p.sbuf("f_su_row", [P, D], BF16)
    rowtmp = p.sbuf("f_rowtmp", [P, D], F32)
    rd_row = p.sbuf("f_rd_row", [P, D], F32)
    st_a = p.sbuf("f_st_a", [P, NFC], F32)
    st_b = p.sbuf("f_st_b", [P, NFC], F32)
    st_c = p.sbuf("f_st_c", [P, NFC], F32)
    r_up = p.sbuf("f_r_up", [P, NFC], F32)
    r_g = p.sbuf("f_r_g", [P, NFC], F32)

    xb = [p.sbuf(f"f_xb{i}", [P, D], F32) for i in range(2)]
    xob = [p.sbuf(f"f_xob{i}", [P, D], F32) for i in range(2)]
    yb = [p.sbuf(f"f_yb{i}", [P, D], BF16) for i in range(2)]
    db = [p.sbuf(f"f_db{i}", [P, D], BF16) for i in range(4)]
    z0f = p.sbuf("f_z0f", [P, NC * 512], BF16)
    x1f = p.sbuf("f_x1f", [P, NC * 512], BF16)
    zf = p.sbuf("f_zf", [P, NFC * 512], BF16)
    junkb = p.sbuf("f_junkb", [P, D], BF16)
    sgb = p.sbuf("f_sgb", [P, 512], BF16)
    ub = [p.sbuf(f"f_ub{i}", [P, 512], BF16) for i in range(2)]
    gring = [p.sbuf(f"f_gring{i}", [P, NC * P], BF16) for i in range(2)]
    wring = [p.sbuf(f"f_wring{i}", [P, 512], BF16) for i in range(4)]
    tring = [p.sbuf(f"f_tring{i}", [P, P], BF16) for i in range(2)]
    wdacc = p.sbuf("f_wdacc", [P, D], BF16)

    s_sum = p.sbuf("f_s_sum", [P, NT], F32)
    s_sq = p.sbuf("f_s_sq", [P, NT], F32)
    s_m = p.sbuf("f_s_m", [P, NT], F32)
    s_var = p.sbuf("f_s_var", [P, NT], F32)
    s_rstd = p.sbuf("f_s_rstd", [P, NT], F32)
    s_nmr = p.sbuf("f_s_nmr", [P, NT], F32)
    o_sum = p.sbuf("f_o_sum", [P, 4], F32)
    o_sq = p.sbuf("f_o_sq", [P, 4], F32)
    o_m = p.sbuf("f_o_m", [P, 4], F32)
    o_var = p.sbuf("f_o_var", [P, 4], F32)
    o_rstd = p.sbuf("f_o_rstd", [P, 4], F32)
    o_nmr = p.sbuf("f_o_nmr", [P, 4], F32)

    pp = [p.psum(f"f_pp{i}", [P, 512], F32) for i in range(6)]
    pt2 = [p.psum(f"f_pt{i}", [P, 512], BF16) for i in range(2)]

    p.on("vector", lambda e: e.memset(cst[:, 0:1], EPS_LN), inc=("vprep", 1))
    p.wait("scalar", "vprep")
    p.dma("sync", ident[:], win["ident"], sname="cload")
    p.dma("sync", h64[:], win["h64"], sname="cload")
    p.dma("sync", h128[:], win["h128"], sname="cload")
    p.dma("sync", st_a[:], win["wsu_up"], sname="cload")
    p.dma("sync", st_b[:], win["wsu_g"], sname="cload")
    p.dma("sync", st_c[:], win["su_dp_fm"], sname="cload")
    p.dma("sync", rowtmp[:], win["sdp_row"].partition_broadcast(P), sname="cload")
    p.dma("gpsimd", su_row[:], win["su_row"].partition_broadcast(P), sname="wcast")
    for (dst, src) in [(wu_bf, win["w_up"]), (wg_bf, win["w_gate"]), (wd_bf, win["w_down"])]:
        p.dma("gpsimd", dst, src, sname="wcast")
    wcast_done = p.cnt["wcast"]
    cload_done = p.cnt["cload"]
    p.wait("vector", "cload", cload_done)
    p.wait("vector", "wcast", wcast_done)
    p.on("vector", lambda e: e.scalar_tensor_tensor(r_up[:], st_a[:], SQ2_64, st_c[:],
                                                    ALU.mult, ALU.mult), inc=("stv", 1))
    p.on("vector", lambda e: e.tensor_single_scalar(r_g[:], st_b[:], SQ2_64, ALU.mult),
         inc=("stv", 1))
    p.on("vector", lambda e: e.tensor_single_scalar(rd_row[:], rowtmp[:], ISQ8192, ALU.mult),
         inc=("stv", 1))
    rows_done = p.cnt["stv"]

    p.wait("tensor", "cload", cload_done)
    p.wait("gpsimd", "wcast", wcast_done)
    DBLK = 16
    NJ = DBLK * P // 512
    for di in range(D // DBLK):
        slot = di % 2
        if di >= 2:
            p.wait("gpsimd", "knm", (di - 1) * NJ)
            p.wait("gpsimd", f"qms{slot}", p.cnt.get(f"qms{slot}", 0))
        src = wd_bf[di * DBLK:(di + 1) * DBLK, :].rearrange("d (c p) -> c d p", p=P)
        dst3 = gring[slot][0:64, 0:DBLK * P].rearrange("c (d p) -> c d p", p=P)
        ld = p.dma("gpsimd", dst3, src, sname=f"gtl{slot}")
        p.wait("tensor", f"gtl{slot}", ld)
        for j in range(NJ):
            if di * NJ + j >= 2:
                p.wait("tensor", "knv", di * NJ + j - 1)
            p.on("tensor", lambda e, slot=slot, j=j: e.matmul(
                pp[4 + j % 2][0:64, :], h64[:], gring[slot][0:64, j * 512:(j + 1) * 512],
                start=True, stop=True), inc=("knm", 1))
            p.wait("vector", "knm")
            if di >= 2 and j == 0:
                p.wait("vector", f"qms{slot}", p.cnt.get(f"qms{slot}", 0))
            p.on("vector", lambda e, slot=slot, j=j: e.tensor_copy(
                gring[slot][64:128, j * 512:(j + 1) * 512], pp[4 + j % 2][0:64, :]),
                inc=("knv", 1))
        p.wait("gpsimd", "knv")
        p.dma("gpsimd", wd2a[:, di * DBLK:(di + 1) * DBLK, :],
              gring[slot][64:128, 0:DBLK * P].rearrange("c (d p) -> c d p", p=P),
              sname=f"qms{slot}")

    p.wait("gpsimd", "qms0", p.cnt.get("qms0", 0))
    p.wait("gpsimd", "qms1", p.cnt.get("qms1", 0))
    for c in range(NFC):
        for dc in range(NC):
            it = c * NC + dc
            slot = it % 2
            if it >= 2:
                p.wait("gpsimd", "vhm", it - 1)
            ld = p.dma("gpsimd", tring[slot][:], wd2a[c, dc * P:(dc + 1) * P, :],
                       sname=f"pql{slot}")
            p.wait("tensor", f"pql{slot}", ld)
            if it >= 2:
                p.wait("tensor", "vhv", it - 1)
            p.on("tensor", lambda e, slot=slot: e.transpose(
                pt2[slot][:, 0:128], tring[slot][:], ident[:]), inc=("vhm", 1))
            p.wait("vector", "vhm")
            if dc == 0 and c >= 1:
                p.wait("vector", "cxs", p.cnt["cxs"])
            p.on("vector", lambda e, slot=slot, dc=dc: e.tensor_copy(
                wdacc[:, dc * P:(dc + 1) * P], pt2[slot][:, 0:128]), inc=("vhv", 1))
        p.wait("gpsimd", "vhv")
        p.dma("gpsimd", wd2b[c], wdacc[:], sname="cxs")
    wd2_done = p.cnt["cxs"]

    def dbg_tap(slot, src_ap, waits, barrier=True):
        if dbg is None:
            return
        for sn, v in waits:
            p.wait("gpsimd", sn, v)
        dst = dbg[slot]
        if tuple(dst.shape) != tuple(src_ap.shape):
            dst = dst[:, 0:src_ap.shape[1]]
        p.dma("gpsimd", dst, src_ap, sname=f"dbgst{slot}")
        if barrier:
            for eng in ("vector", "scalar", "tensor", "sync"):
                p.wait(eng, f"dbgst{slot}")

    b3_vv = {}
    ps_t = {}
    pv_t = {}
    for seq in range(nseq):
        xs = x_in[seq]
        xo = x_out[seq]

        for tt in range(NT):
            if tt >= 2:
                p.wait("sync", "pv", pv_t[tt - 2])
                p.wait("sync", "ps", ps_t[tt - 2])
            ld = p.dma("sync", xb[tt % 2][:], xs[tt * P:(tt + 1) * P, :],
                       sname=f"xld{tt % 2}")
            p.wait("vector", f"xld{tt % 2}", ld)
            p.wait("vector", "pv")
            pv_t[tt] = p.on("vector", lambda e, tt=tt: e.reduce_sum(
                s_sum[:, tt:tt + 1], xb[tt % 2][:], axis=AX.X), inc=("pv", 1))
            p.wait("scalar", f"xld{tt % 2}", ld)
            p.wait("scalar", "ps")
            ps_t[tt] = p.on("scalar", lambda e, tt=tt: e.activation(
                junkb[:], xb[tt % 2][:], AF.Square, accum_out=s_sq[:, tt:tt + 1]),
                inc=("ps", 1))
        p.wait("vector", "ps")
        vln, sln = emit_ln_stats_math(p, s_sum[:], s_sq[:], s_m[:], s_var[:],
                                      s_rstd[:], s_nmr[:], D, cst[:, 0:1])
        if seq == 0:
            p.wait("gpsimd", "cxs", wd2_done)

        a0_pv = {}
        for tg in range(NTG):
            tts = [tg * 4 + i for i in range(4)]
            for i, tt in enumerate(tts):
                if i >= 2:
                    p.wait("sync", "ps", ps_t[("a0", tg, i - 2)])
                if tg == 0 and i == 0:
                    p.wait("sync", "ps", ps_t[NT - 1])
                    p.wait("sync", "pv", pv_t[NT - 1])
                ld = p.dma("sync", xb[i % 2][:], xs[tt * P:(tt + 1) * P, :],
                           sname=f"xld{i % 2}")
                p.wait("scalar", f"xld{i % 2}", ld)
                if i == 0 and tg == 0:
                    p.wait("scalar", "stv", vln)
                    p.wait("scalar", "sts", sln)
                    if seq == 0:
                        p.wait("scalar", "stv", rows_done)
                if i >= 2:
                    p.wait("scalar", "pv", a0_pv[i - 2])
                ps_t[("a0", tg, i)] = p.on("scalar", lambda e, i=i, tt=tt: e.activation(
                    yb[i % 2][:], xb[i % 2][:], AF.Identity,
                    bias=s_nmr[:, tt:tt + 1], scale=s_rstd[:, tt:tt + 1]), inc=("ps", 1))
                p.wait("vector", "ps", ps_t[("a0", tg, i)])
                p.wait("vector", "pv")
                a0_pv[i] = p.on("vector", lambda e, i=i: e.tensor_mul(
                    db[i][:], yb[i % 2][:], su_row[:]), inc=("pv", 1))
            p.wait("tensor", "pv")
            if seq == 0 and tg == 0:
                dbg_tap(0, db[0][:], [("pv", p.cnt["pv"])])
            emit_transpose_tiles(p, [(lambda c, i=i: db[i][:, c * P:(c + 1) * P])
                                     for i in range(4)], z0f, pt2, ident, list(range(NC)), 512)

            for m in range(NC):
                slot = p.cnt.get("kvau", 0) // NC % 2
                if m < 2:
                    p.wait("gpsimd", "qau", p.cnt.get("qau", 0))
                if p.cnt.get("kvau", 0) >= 2 * NC:
                    p.wait("gpsimd", "kvau", p.cnt["kvau"] - NC)
                ld = p.dma("gpsimd", gring[slot][:], win["hs_slab"][m], sname=f"gal{slot}")
                p.wait("tensor", f"gal{slot}", ld)
                if p.cnt.get("kvav", 0) >= 2:
                    p.wait("tensor", "kvav", p.cnt["kvav"] - 1)
                for k in range(NC):
                    p.on("tensor", lambda e, slot=slot, k=k, m=m: e.matmul(
                        pp[m % 2][:], gring[slot][:, k * P:(k + 1) * P],
                        z0f[:, k * 512:(k + 1) * 512], start=(k == 0), stop=(k == NC - 1)),
                        inc=("kvau", 1))
                p.wait("vector", "kvau")
                p.on("vector", lambda e, m=m: e.tensor_copy(
                    x1f[:, m * 512:(m + 1) * 512], pp[m % 2][:]), inc=("kvav", 1))
            p.wait("tensor", "kvav")

            sil_t = {}
            for ic in range(NFC):
                gbank = (ic % 2) * 4
                for wsrc, ppj in ((wg_bf, 0), (wu_bf, 1)):
                    gu = p.cnt.get("qau", 0)
                    slot = gu % 2
                    if ic == 0:
                        p.wait("gpsimd", "kvau", p.cnt["kvau"])
                    if gu >= 2:
                        p.wait("gpsimd", "qau", gu - 1)
                    ld = p.dma("gpsimd", gring[slot][:], wsrc[ic], sname=f"hbl{slot}")
                    p.wait("tensor", f"hbl{slot}", ld)
                    if ic >= 2 and ppj == 0:
                        p.wait("tensor", "qav", p.cnt["qav"] - 1)
                    for k in range(NC):
                        p.on("tensor", lambda e, slot=slot, k=k, ppj=ppj, gbank=gbank:
                             e.matmul(pp[gbank + ppj][:], gring[slot][:, k * P:(k + 1) * P],
                                      x1f[:, k * 512:(k + 1) * 512],
                                      start=(k == 0), stop=(k == NC - 1)),
                             inc=("qau", 1) if k == NC - 1 else None)
                p.wait("scalar", "qau")
                p.wait("scalar", "ps")
                p.wait("scalar", "qav")
                sc = p.on("scalar", lambda e, ic=ic, gbank=gbank: e.activation(
                    sgb[:], pp[gbank][:], AF.Silu, scale=r_g[:, ic:ic + 1]), inc=("ps", 1))
                p.wait("vector", "ps", sc)
                if ic >= 2:
                    p.wait("vector", "qsm", p.cnt["qsm"] - 1)
                p.on("vector", lambda e, ic=ic, gbank=gbank: e.scalar_tensor_tensor(
                    ub[ic % 2][:], pp[gbank + 1][:], r_up[:, ic:ic + 1], sgb[:],
                    ALU.mult, ALU.mult), inc=("qav", 1))
                p.wait("tensor", "qav")
                if ic >= 2:
                    p.wait("tensor", "qsv", p.cnt["qsv"] - 1)
                p.on("tensor", lambda e, ic=ic: e.matmul(
                    pp[2 + ic % 2][:], h128[:], ub[ic % 2][:], start=True, stop=True),
                    inc=("qsm", 1))
                p.wait("vector", "qsm")
                p.on("vector", lambda e, ic=ic: e.tensor_copy(
                    zf[:, ic * 512:(ic + 1) * 512], pp[2 + ic % 2][:]), inc=("qsv", 1))
            p.wait("tensor", "qsv")
            p.wait("tensor", "qav")
            if seq == 0 and tg == 0:
                dbg_tap(1, x1f[:, 0:512], [("kvav", p.cnt["kvav"])])
                dbg_tap(2, zf[:, 0:512], [("qsv", p.cnt["qsv"])])

            for dg in range(4):
                for c in range(NFC):
                    wu = p.cnt.get("wru", 0)
                    slot = wu % 4
                    if wu >= 4:
                        p.wait("gpsimd", "wru", wu - 3)
                    ld = p.dma("gpsimd", wring[slot][:],
                               wd2b[c, :, dg * 512:(dg + 1) * 512], sname=f"wrl{slot}")
                    p.wait("tensor", f"wrl{slot}", ld)
                    for i in range(4):
                        if c == 0 and p.cnt.get("wde", 0) >= 4 - i:
                            p.wait("tensor", "wde", p.cnt["wde"] - (4 - i) + 1)
                        p.on("tensor", lambda e, c=c, i=i, slot=slot: e.matmul(
                            pp[i][:], zf[:, c * 512 + i * P: c * 512 + (i + 1) * P],
                            wring[slot][:], start=(c == 0), stop=(c == NFC - 1)),
                            inc=("wru", 1) if i == 3 else None)
                p.wait("vector", "wru")
                for i in range(4):
                    p.on("vector", lambda e, dg=dg, i=i: e.tensor_tensor(
                        db[i][:, dg * 512:(dg + 1) * 512], pp[i][:],
                        rd_row[:, dg * 512:(dg + 1) * 512], ALU.mult), inc=("wde", 1))
            p.wait("tensor", "wde")
            if seq == 0 and tg == 0:
                dbg_tap(3, db[0][:], [("wde", p.cnt["wde"])])

            p.wait("vector", "wde")
            for i, tt in enumerate(tts):
                p.wait("vector", "pv")
                p.on("vector", lambda e, i=i: e.reduce_sum(
                    o_sum[:, i:i + 1], db[i][:], axis=AX.X), inc=("pv", 1))
                p.wait("scalar", "wde")
                p.wait("scalar", "ps")
                p.on("scalar", lambda e, i=i: e.activation(
                    junkb[:], db[i][:], AF.Square, accum_out=o_sq[:, i:i + 1]),
                    inc=("ps", 1))
            p.wait("vector", "ps")
            vo, so = emit_ln_stats_math(p, o_sum[:], o_sq[:], o_m[:], o_var[:],
                                        o_rstd[:], o_nmr[:], D, cst[:, 0:1])
            for i, tt in enumerate(tts):
                if i < 2:
                    p.wait("sync", "ps", ps_t[("a0", tg, 2 + i)])
                else:
                    p.wait("sync", "pv", b3_vv[i - 2])
                ld = p.dma("sync", xb[i % 2][:], xs[tt * P:(tt + 1) * P, :],
                           sname=f"xld{i % 2}")
                if i == 0:
                    p.wait("scalar", "stv", vo)
                    p.wait("scalar", "sts", so)
                if i >= 2:
                    p.wait("scalar", "pv", b3_vv[i - 2])
                sc = p.on("scalar", lambda e, i=i: e.activation(
                    yb[i % 2][:], db[i][:], AF.Identity, bias=o_nmr[:, i:i + 1],
                    scale=o_rstd[:, i:i + 1]), inc=("ps", 1))
                p.wait("vector", "ps", sc)
                p.wait("vector", f"xld{i % 2}", ld)
                vv = p.on("vector", lambda e, i=i: e.tensor_add(
                    xob[i % 2][:], xb[i % 2][:], yb[i % 2][:]), inc=("pv", 1))
                b3_vv[i] = vv
                p.wait("sync", "pv", vv)
                st = p.dma("sync", xo[tt * P:(tt + 1) * P, :], xob[i % 2][:],
                           sname=f"xst{i % 2}")
                p.wait("sync", f"xst{i % 2}", st)

    p.wait("sync", "xst0", p.cnt.get("xst0", 0))
    p.wait("sync", "xst1", p.cnt.get("xst1", 0))
    return p


def _hadamard(n):
    i = np.arange(n)
    return np.where(np.bitwise_count(i[:, None] & i[None, :]) % 2 == 0, 1.0, -1.0)


def l0ffn_host_inputs(w):
    D, FF, NC, NFC = 2048, 8192, 16, 64
    H = _hadamard(D) * (2.0 ** -12)
    hs_slab = np.ascontiguousarray(
        H.reshape(NC, P, NC, P).transpose(2, 1, 0, 3).reshape(NC, P, NC * P)
    ).astype(ml_dtypes.bfloat16)
    wslab = lambda m: np.ascontiguousarray(
        np.asarray(m, np.float32).reshape(NFC, P, NC, P).transpose(0, 3, 2, 1)
        .reshape(NFC, P, NC * P))
    wsu = np.asarray(w["Wscale_ug"], np.float32)
    return {
        "hs_slab": hs_slab,
        "h128": _hadamard(P).astype(ml_dtypes.bfloat16),
        "h64": _hadamard(64).astype(ml_dtypes.bfloat16),
        "su_row": np.asarray(w["SU_ug"], np.float32).reshape(1, D),
        "su_dp_fm": np.ascontiguousarray(
            np.asarray(w["SU_dp"], np.float32).reshape(NFC, P).T),
        "wsu_up": np.ascontiguousarray(wsu[:FF].reshape(NFC, P).T),
        "wsu_g": np.ascontiguousarray(wsu[FF:].reshape(NFC, P).T),
        "sdp_row": np.asarray(w["Wscale_dp"], np.float32).reshape(1, D),
        "w_up": wslab(w["W_up"]),
        "w_gate": wslab(w["W_gate"]),
        "w_down": np.ascontiguousarray(np.asarray(w["W_down"], np.float32)),
        "ident": np.eye(P, dtype=ml_dtypes.bfloat16),
    }


def declare_l0ffn_inputs(nc, pfx="i_"):
    D, FF, NC, NFC = 2048, 8192, 16, 64
    f32 = {
        "su_row": [1, D], "su_dp_fm": [P, NFC], "wsu_up": [P, NFC], "wsu_g": [P, NFC],
        "sdp_row": [1, D], "w_up": [NFC, P, NC * P], "w_gate": [NFC, P, NC * P],
        "w_down": [D, FF],
    }
    bf = {"hs_slab": [NC, P, NC * P], "h128": [P, P], "h64": [64, 64], "ident": [P, P]}
    win = {k: nc.dram_tensor(pfx + k, sh, F32, kind="ExternalInput").ap()
           for k, sh in f32.items()}
    win.update({k: nc.dram_tensor(pfx + k, sh, BF16, kind="ExternalInput").ap()
                for k, sh in bf.items()})
    return win



def build_l0attn(p, T, x_in, x_out, win, nseq=2, dbg=None):
    nc = p.nc
    D = 2048
    NT = T // 128
    assert NT % 4 == 0
    NC = 16
    NH, NKV, DH = 16, 4, 128
    EPS_LN = 1e-5
    SCL = 1.0 / (DH ** 0.5)
    NGLM = 2 * (NT - 3) if NT >= 3 else 0

    wq_bf = nc.dram_tensor("a_wq_bf", [NH, P, NC * P], BF16, kind="Internal").ap()
    wkvd_bf = nc.dram_tensor("a_wkvd_bf", [NKV, P, NC * P], BF16, kind="Internal").ap()
    wku_bf = nc.dram_tensor("a_wku_bf", [NKV, P, NKV * P], BF16, kind="Internal").ap()
    wvu_bf = nc.dram_tensor("a_wvu_bf", [NKV * P, NKV * P], BF16, kind="Internal").ap()
    wo_bf = nc.dram_tensor("a_wo_bf", [D, D], BF16, kind="Internal").ap()
    ctx_dram = nc.dram_tensor("a_ctx", [T, D], BF16, kind="Internal").ap()

    ident = p.sbuf("a_ident", [P, P], BF16)
    cst = p.sbuf("a_cst", [P, 1], F32)
    prot = p.sbuf("a_prot", [P, P], BF16)
    cosf = p.sbuf("a_cosf", [P, T], BF16)
    sinf = p.sbuf("a_sinf", [P, T], BF16)
    bb = p.sbuf("a_bb", [P, 384], BF16)
    gt_sb = p.sbuf("a_gt", [P, NT * 32], BF16)
    wvu_sb = p.sbuf("a_wvu", [P, NKV * 512], BF16)

    lnf = p.sbuf("a_lnf", [P, NC * T], BF16)
    cf = p.sbuf("a_cf", [P, NKV * T], BF16)
    kro = p.sbuf("a_kro", [P, NKV * T], BF16)
    vtm = p.sbuf("a_vtm", [P, max(NT, 16) * 512], BF16)
    kglob = p.sbuf("a_kglob", [P, NKV * 32], BF16)
    vglob = p.sbuf("a_vglob", [32, 512], BF16)
    qraw = p.sbuf("a_qraw", [P, T], BF16)
    kraw = qraw
    qro = p.sbuf("a_qro", [P, T], BF16)
    tmb = [p.sbuf(f"a_tmb{i}", [P, 512], BF16) for i in range(2)]
    sc1 = p.sbuf("a_sc1", [P, 416], F32)
    probs = p.sbuf("a_probs", [P, 416], BF16)
    pbT = [p.sbuf("a_pbT0", [P, 512], BF16)]
    ctxh = p.sbuf("a_ctxh", [P, NT * P], BF16)
    nmc = p.sbuf("a_nmc", [P, 4], F32)

    xb = [p.sbuf(f"a_xb{i}", [P, D], F32) for i in range(2)]
    junkb = None
    xob = [p.sbuf(f"a_xob{i}", [P, D], F32) for i in range(2)]
    _yb0 = p.sbuf("a_yb0", [P, D], BF16)
    yb = [_yb0, _yb0]
    db = [p.sbuf(f"a_db{i}", [P, D], BF16) for i in range(4)]
    garing = [p.sbuf(f"a_garing{i}", [P, NC * P], BF16) for i in range(2)]
    wring = [p.sbuf(f"a_wring{i}", [P, 512], BF16) for i in range(2)]

    s_sum = p.sbuf("a_s_sum", [P, NT], F32)
    s_sq = p.sbuf("a_s_sq", [P, NT], F32)
    s_m = p.sbuf("a_s_m", [P, NT], F32)
    s_var = p.sbuf("a_s_var", [P, NT], F32)
    s_rstd = p.sbuf("a_s_rstd", [P, NT], F32)
    s_nmr = p.sbuf("a_s_nmr", [P, NT], F32)
    o_sum = p.sbuf("a_o_sum", [P, 4], F32)
    o_sq = p.sbuf("a_o_sq", [P, 4], F32)
    o_m = p.sbuf("a_o_m", [P, 4], F32)
    o_var = p.sbuf("a_o_var", [P, 4], F32)
    o_rstd = p.sbuf("a_o_rstd", [P, 4], F32)
    o_nmr = p.sbuf("a_o_nmr", [P, 4], F32)

    pp = [p.psum(f"a_pp{i}", [P, 512], F32) for i in range(6)]
    pt2 = [p.psum(f"a_pt{i}", [P, 512], BF16) for i in range(2)]

    p.on("vector", lambda e: e.memset(cst[:, 0:1], EPS_LN), inc=("vprep", 1))
    p.wait("scalar", "vprep")
    for dst, key in [(ident, "ident"), (prot, "prot"), (cosf, "cos_fm"), (sinf, "sin_fm"),
                     (bb, "band_bias")]:
        p.dma("sync", dst[:], win[key], sname="cload")
    for kt in range(NT):
        p.dma("sync", gt_sb[:, kt * 32:(kt + 1) * 32],
              win["gt"][kt * P:(kt + 1) * P, :], sname="cload")
    for k in range(NKV):
        p.dma("gpsimd", wvu_sb[:, k * 512:(k + 1) * 512],
              win["wv_up"][k * P:(k + 1) * P, :], sname="wcast")
    for (dst, src) in [(wq_bf, win["wq"]), (wkvd_bf, win["wkv_down"]),
                       (wku_bf, win["wk_up"]), (wo_bf, win["wo"])]:
        p.dma("gpsimd", dst, src, sname="wcast")
    wcast_done = p.cnt["wcast"]
    cload_done = p.cnt["cload"]
    p.wait("tensor", "cload", cload_done)
    p.wait("tensor", "wcast", wcast_done)
    p.wait("gpsimd", "wcast", wcast_done)

    def dbg_tap(slot, src_ap, waits, barrier=True):
        if dbg is None:
            return
        for sn, v in waits:
            p.wait("gpsimd", sn, v)
        dst = dbg[slot]
        if tuple(dst.shape) != tuple(src_ap.shape):
            dst = dst[:, 0:src_ap.shape[1]]
        p.dma("gpsimd", dst, src_ap, sname=f"dbgst{slot}")
        if barrier:
            for eng in ("vector", "scalar", "tensor", "sync"):
                p.wait(eng, f"dbgst{slot}")

    b3_vv = {}
    b3_sc = {}
    ps_t = {}
    pv_t = {}
    pbread = None
    bandcp = None
    globcp = None
    for seq in range(nseq):
        xs = x_in[seq]
        xo = x_out[seq]

        for tt in range(NT):
            if tt >= 2:
                p.wait("sync", "pv", pv_t[tt - 2])
                p.wait("sync", "ps", ps_t[tt - 2])
            ld = p.dma("sync", xb[tt % 2][:], xs[tt * P:(tt + 1) * P, :],
                       sname=f"xld{tt % 2}")
            p.wait("vector", f"xld{tt % 2}", ld)
            p.wait("vector", "pv")
            pv_t[tt] = p.on("vector", lambda e, tt=tt: e.reduce_sum(
                s_sum[:, tt:tt + 1], xb[tt % 2][:], axis=AX.X), inc=("pv", 1))
            p.wait("scalar", f"xld{tt % 2}", ld)
            p.wait("scalar", "ps")
            ps_t[tt] = p.on("scalar", lambda e, tt=tt: e.activation(
                yb[0][:], xb[tt % 2][:], AF.Square, accum_out=s_sq[:, tt:tt + 1]),
                inc=("ps", 1))
        p.wait("vector", "ps")
        vln, sln = emit_ln_stats_math(p, s_sum[:], s_sq[:], s_m[:], s_var[:],
                                      s_rstd[:], s_nmr[:], D, cst[:, 0:1])

        a0_sc = {}
        a0_tr = {}
        for tt in range(NT):
            i = tt % 2
            if tt >= 2:
                p.wait("sync", "ps", a0_sc[tt - 2])
            if tt == 0:
                p.wait("sync", "ps", ps_t[NT - 1])
                p.wait("sync", "pv", pv_t[NT - 1])
            ld = p.dma("sync", xb[i][:], xs[tt * P:(tt + 1) * P, :], sname=f"xld{i}")
            p.wait("scalar", f"xld{i}", ld)
            if tt == 0:
                p.wait("scalar", "stv", vln)
                p.wait("scalar", "sts", sln)
            if tt >= 1:
                p.wait("scalar", "trT", a0_tr[tt - 1])
            a0_sc[tt] = p.on("scalar", lambda e, i=i, tt=tt: e.activation(
                yb[i][:], xb[i][:], AF.Identity,
                bias=s_nmr[:, tt:tt + 1], scale=s_rstd[:, tt:tt + 1]), inc=("ps", 1))
            p.wait("tensor", "ps", a0_sc[tt])
            emit_transpose_tiles(p, [(lambda c, i=i: yb[i][:, c * P:(c + 1) * P])],
                                 lnf, pt2, ident, list(range(NC)), T, coloff=tt * P)
            a0_tr[tt] = p.cnt["trT"]

        p.wait("vector", "a3m", p.cnt.get("a3m", 0))
        p.wait("vector", "sxm", p.cnt.get("sxm", 0))
        for m in range(NKV):
            slot = m % 2
            if m < 2:
                p.wait("gpsimd", "gau", p.cnt.get("gau", 0))
                p.wait("gpsimd", "qmu", p.cnt.get("qmu", 0))
            else:
                p.wait("gpsimd", "gau", p.cnt["gau"] - NT // 4)
            ld = p.dma("gpsimd", garing[slot][:], wkvd_bf[m], sname=f"gal{slot}")
            p.wait("tensor", f"gal{slot}", ld)
            for ng in range(T // 512):
                gidx = p.cnt.get("a1v", 0)
                bank = gidx % 2
                if gidx >= 2:
                    p.wait("tensor", "a1v", gidx - 1)
                for k in range(NC):
                    p.on("tensor", lambda e, slot=slot, k=k, ng=ng, bank=bank: e.matmul(
                        pp[bank][:], garing[slot][:, k * P:(k + 1) * P],
                        lnf[:, k * T + ng * 512: k * T + (ng + 1) * 512],
                        start=(k == 0), stop=(k == NC - 1)),
                        inc=("gau", 1) if k == NC - 1 else None)
                p.wait("vector", "gau")
                p.on("vector", lambda e, m=m, ng=ng, bank=bank: e.tensor_copy(
                    cf[:, m * T + ng * 512: m * T + (ng + 1) * 512], pp[bank][:]),
                    inc=("a1v", 1))
        p.wait("tensor", "a1v")

        p.wait("vector", "sxm", p.cnt.get("sxm", 0))
        p.wait("vector", "qrv", p.cnt.get("qrv", 0))
        a2slab = {}
        for m in range(NKV):
            slot = m % 2
            if m < 2:
                p.wait("gpsimd", "gau", p.cnt.get("gau", 0))
            else:
                p.wait("gpsimd", "a2m", a2slab[m - 2])
            ld = p.dma("gpsimd", garing[slot][:, 0:NKV * P], wku_bf[m], sname=f"gal{slot}")
            p.wait("tensor", f"gal{slot}", ld)
            for ng in range(T // 512):
                if p.cnt.get("a2v", 0) >= 1:
                    p.wait("tensor", "a2v", p.cnt["a2v"])
                for k in range(NKV):
                    t_ = p.on("tensor", lambda e, slot=slot, k=k, ng=ng: e.matmul(
                        pp[0][:], garing[slot][:, k * P:(k + 1) * P],
                        cf[:, k * T + ng * 512: k * T + (ng + 1) * 512],
                        start=(k == 0), stop=(k == NKV - 1)),
                        inc=("a2m", 1) if k == NKV - 1 else None)
                    if k == NKV - 1:
                        a2slab[m] = t_
                p.wait("vector", "a2m")
                p.on("vector", lambda e, ng=ng: e.tensor_copy(
                    kraw[:, ng * 512:(ng + 1) * 512], pp[0][:]), inc=("a2v", 1))
                p.wait("tensor", "a2v")
                p.on("tensor", lambda e, ng=ng: e.matmul(
                    pp[1][:], prot[:], kraw[:, ng * 512:(ng + 1) * 512],
                    start=True, stop=True), inc=("a2m", 1))
                p.wait("vector", "a2m")
                p.wait("vector", "rpv")
                p.on("vector", lambda e, ng=ng: e.tensor_mul(
                    tmb[0][:], kraw[:, ng * 512:(ng + 1) * 512],
                    cosf[:, ng * 512:(ng + 1) * 512]), inc=("rpv", 1))
                p.on("vector", lambda e, ng=ng: e.tensor_mul(
                    tmb[1][:], pp[1][:], sinf[:, ng * 512:(ng + 1) * 512]),
                    inc=("rpv", 1))
                p.wait("vector", "rpv")
                p.on("vector", lambda e, m=m, ng=ng: e.tensor_add(
                    kro[:, m * T + ng * 512: m * T + (ng + 1) * 512],
                    tmb[0][:], tmb[1][:]), inc=("rpv", 1))
        p.wait("tensor", "rpv")

        p.wait("vector", "pvm", p.cnt.get("pvm", 0))
        p.wait("vector", "wou", p.cnt.get("wou", 0))
        for tt in range(NT):
            if p.cnt.get("a3v", 0) >= 2:
                p.wait("tensor", "a3v", p.cnt["a3v"] - 1)
            for k in range(NKV):
                p.on("tensor", lambda e, k=k, tt=tt: e.matmul(
                    pp[2 + tt % 2][:], cf[:, k * T + tt * P: k * T + (tt + 1) * P],
                    wvu_sb[:, k * 512:(k + 1) * 512],
                    start=(k == 0), stop=(k == NKV - 1)),
                    inc=("a3m", 1) if k == NKV - 1 else None)
            p.wait("vector", "a3m")
            p.on("vector", lambda e, tt=tt: e.tensor_copy(
                vtm[:, tt * 512:(tt + 1) * 512], pp[2 + tt % 2][:]), inc=("a3v", 1))
        p.wait("tensor", "a3v")

        NGL_ALL = T // 64
        for m in range(NKV):
            p.wait("vector", "rpv")
            p.on("vector", lambda e, m=m: e.tensor_copy(
                kglob[:, m * 32:m * 32 + NGL_ALL],
                kro[:, m * T:(m + 1) * T].rearrange("p (g s) -> p g s", s=64)[:, :, 0]),
                inc=("a4v", 1))
        for kt in range(NT):
            p.on("tensor", lambda e, kt=kt: e.matmul(
                pp[4][0:32, :], gt_sb[:, kt * 32:kt * 32 + 32],
                vtm[:, kt * 512:(kt + 1) * 512],
                start=(kt == 0), stop=(kt == NT - 1)), inc=("a4m", 1) if kt == NT - 1 else None)
        p.wait("vector", "a4m")
        p.on("vector", lambda e: e.tensor_copy(vglob[:, :], pp[4][0:32, :]), inc=("a4v", 1))
        a4_done = p.cnt["a4v"]

        for h in range(NH):
            g = h // 4
            slot = h % 2
            if h < 2:
                p.wait("gpsimd", "a2m", p.cnt.get("a2m", 0))
                if seq > 0:
                    for _ci in range(4):
                        p.wait("gpsimd", f"cxl{_ci}", p.cnt.get(f"cxl{_ci}", 0))
            if h >= 2:
                p.wait("gpsimd", "qmu", p.cnt["qmu"] - 1)
            ld = p.dma("gpsimd", garing[slot][:], wq_bf[h], sname=f"gal{slot}")
            p.wait("tensor", f"gal{slot}", ld)
            if h == 0:
                p.wait("tensor", "a4v", a4_done)
            for ng in range(T // 512):
                if h > 0 or ng > 0:
                    p.wait("tensor", "qrv", p.cnt.get("qrv", 0))
                for k in range(NC):
                    p.on("tensor", lambda e, slot=slot, k=k, ng=ng: e.matmul(
                        pp[0][:], garing[slot][:, k * P:(k + 1) * P],
                        lnf[:, k * T + ng * 512: k * T + (ng + 1) * 512],
                        start=(k == 0), stop=(k == NC - 1)),
                        inc=("qmu", 1) if k == NC - 1 else None)
                p.wait("vector", "qmu")
                p.on("vector", lambda e, ng=ng: e.tensor_copy(
                    qraw[:, ng * 512:(ng + 1) * 512], pp[0][:]), inc=("qmv", 1))
                p.wait("tensor", "qmv")
                p.on("tensor", lambda e, ng=ng: e.matmul(
                    pp[1][:], prot[:], qraw[:, ng * 512:(ng + 1) * 512],
                    start=True, stop=True), inc=("qrm", 1))
                p.wait("vector", "qrm")
                p.wait("vector", "rpv")
                if ng == 0:
                    p.wait("vector", "sxm", p.cnt.get("sxm", 0))
                p.on("vector", lambda e, ng=ng: e.tensor_mul(
                    tmb[0][:], qraw[:, ng * 512:(ng + 1) * 512],
                    cosf[:, ng * 512:(ng + 1) * 512]), inc=("rpv", 1))
                p.on("vector", lambda e, ng=ng: e.tensor_mul(
                    tmb[1][:], pp[1][:], sinf[:, ng * 512:(ng + 1) * 512]), inc=("rpv", 1))
                p.wait("vector", "rpv")
                p.on("vector", lambda e, ng=ng: e.tensor_add(
                    qro[:, ng * 512:(ng + 1) * 512], tmb[0][:], tmb[1][:]),
                    inc=("qrv", 1))
            p.wait("tensor", "qrv", p.cnt.get("qrv", 0))
            def l0_tail_a(st):
                nonlocal bandcp, globcp
                qt0, se0, nb0, ngl0, bc00 = st
                p.wait("vector", "ps", se0)
                p.wait("vector", "ctv", p.cnt.get("ctv", 0))
                p.on("vector", lambda e: e.reciprocal(nmc[:, 3:4], nmc[:, 2:3]),
                     inc=("expd", 1))
                p.wait("tensor", "ps", se0)
                if bandcp is not None:
                    p.wait("tensor", "ptv", bandcp)
                for b in range(nb0):
                    p.on("tensor", lambda e, b=b, nb0=nb0: e.matmul(
                        pt2[0][:, b * P:(b + 1) * P],
                        probs[:, b * P:(b + 1) * P], ident[:],
                        start=(b == 0), stop=(b == nb0 - 1), is_transpose=True),
                        inc=("ptm", 1) if b == nb0 - 1 else None)
                if ngl0 > 0:
                    if globcp is not None:
                        p.wait("tensor", "ptv", globcp)
                    p.on("tensor", lambda e, nb0=nb0, ngl0=ngl0: e.matmul(
                        pt2[1][0:ngl0, 0:P], probs[:, nb0 * P: nb0 * P + ngl0],
                        ident[:], start=True, stop=True, is_transpose=True),
                        inc=("ptm", 1))

            def l0_tail_b(st):
                nonlocal bandcp, globcp, pbread
                qt0, se0, nb0, ngl0, bc00 = st
                nch0 = nb0 + (1 if ngl0 > 0 else 0)
                p.wait("vector", "ptm", p.cnt["ptm"])
                if pbread is not None:
                    p.wait("vector", "pvm", pbread)
                p.on("vector", lambda e, nb0=nb0: e.tensor_copy(
                    pbT[0][:, 0:nb0 * P], pt2[0][:, 0:nb0 * P]), inc=("ptv", 1))
                bandcp = p.cnt["ptv"]
                if ngl0 > 0:
                    p.on("vector", lambda e, nb0=nb0, ngl0=ngl0: e.tensor_copy(
                        pbT[0][0:ngl0, nb0 * P:(nb0 + 1) * P],
                        pt2[1][0:ngl0, 0:P]), inc=("ptv", 1))
                    globcp = p.cnt["ptv"]
                p.wait("tensor", "ptv", p.cnt["ptv"])
                p.wait("tensor", "ctv", p.cnt.get("ctv", 0))
                for b in range(nch0):
                    isg = b == nb0
                    wdt = ngl0 if isg else P
                    if isg:
                        p.on("tensor", lambda e, g=g, wdt=wdt, b=b: e.matmul(
                            pp[3][:, 0:P], pbT[0][0:wdt, b * P:(b + 1) * P],
                            vglob[0:wdt, g * P:(g + 1) * P],
                            start=(b == 0), stop=True), inc=("pvm", 1))
                    else:
                        p.on("tensor", lambda e, g=g, b=b, bc00=bc00, nb0=nb0,
                             ngl0=ngl0: e.matmul(
                            pp[3][:, 0:P], pbT[0][:, b * P:(b + 1) * P],
                            vtm[:, (bc00 + b) * 512 + g * P:
                                 (bc00 + b) * 512 + (g + 1) * P],
                            start=(b == 0), stop=(b == nb0 - 1 and ngl0 == 0)),
                            inc=("pvm", 1))
                pbread = p.cnt["pvm"]
                p.wait("vector", "pvm")
                p.wait("vector", "expd")
                p.on("vector", lambda e, qt0=qt0: e.tensor_scalar(
                    ctxh[:, qt0 * P:(qt0 + 1) * P], pp[3][:, 0:P], nmc[:, 3:4], None,
                    ALU.mult, ALU.bypass), inc=("ctv", 1))

            pend = None
            for qt in range(NT):
                bc0 = max(0, qt - 2)
                nb = qt - bc0 + 1
                ngl = max(0, 2 * (qt - 2))
                nw = nb * P + ngl
                p.wait("tensor", "sxv", p.cnt.get("sxv", 0))
                p.on("tensor", lambda e, qt=qt, g=g, bc0=bc0, nb=nb, ngl=ngl: e.matmul(
                    pp[2][:, 0:nb * P], qro[:, qt * P:(qt + 1) * P],
                    kro[:, g * T + bc0 * P: g * T + (qt + 1) * P],
                    start=True, stop=(ngl == 0)), inc=("sxm", 1))
                if ngl > 0:
                    p.on("tensor", lambda e, qt=qt, g=g, nb=nb, ngl=ngl: e.matmul(
                        pp[2][:, nb * P:nb * P + ngl], qro[:, qt * P:(qt + 1) * P],
                        kglob[:, g * 32: g * 32 + ngl],
                        start=False, stop=True), inc=("sxm", 1))
                p.wait("vector", "sxm")
                if pend is not None:
                    p.wait("vector", "ps", pend[1])
                p.on("vector", lambda e, nb=nb: e.tensor_add(
                    sc1[:, 0:nb * P], pp[2][:, 0:nb * P], bb[:, (3 - nb) * P:384]),
                    inc=("sxv", 1))
                if ngl > 0:
                    p.on("vector", lambda e, nb=nb, ngl=ngl: e.tensor_copy(
                        sc1[:, nb * P:nb * P + ngl], pp[2][:, nb * P:nb * P + ngl]),
                        inc=("sxv", 1))
                sx = p.cnt["sxv"]
                if pend is not None:
                    l0_tail_a(pend)
                p.wait("scalar", "sxv", sx)
                p.wait("scalar", "ptm", p.cnt.get("ptm", 0))
                p.wait("scalar", "expd", p.cnt.get("expd", 0))
                p.wait("scalar", "ps")
                se = p.on("scalar", lambda e, nw=nw: e.activation(
                    probs[:, 0:nw], sc1[:, 0:nw], AF.Exp, scale=SCL, bias=0.0,
                    accum_out=nmc[:, 2:3]), inc=("ps", 1))
                if pend is not None:
                    l0_tail_b(pend)
                pend = (qt, se, nb, ngl, bc0)
            l0_tail_a(pend)
            l0_tail_b(pend)
            p.wait("gpsimd", "ctv", p.cnt["ctv"])
            p.dma("gpsimd",
                  ctx_dram[:, h * P:(h + 1) * P].rearrange("(q p) c -> p q c", p=P),
                  ctxh[:].rearrange("p (q c) -> p q c", c=P), sname="cxs")
            p.wait("vector", "cxs", p.cnt["cxs"])

        p.wait("sync", "cxs", p.cnt["cxs"])
        for tgc in range(NT // 4):
            tts = [tgc * 4 + i for i in range(4)]
            for i, tt in enumerate(tts):
                if b3_sc.get(i) is not None:
                    p.wait("sync", "ps", b3_sc[i])
                ld = p.dma("sync", db[i][:], ctx_dram[tt * P:(tt + 1) * P, :],
                           sname=f"cxl{i}")
                p.wait("tensor", f"cxl{i}", ld)
            emit_transpose_tiles(p, [(lambda c, i=i: db[i][:, c * P:(c + 1) * P])
                                     for i in range(4)], vtm, pt2, ident,
                                 list(range(NC)), 512)
            for ng in range(4):
                for k in range(NC):
                    wu = p.cnt.get("wou", 0)
                    slot = wu % 2
                    if wu >= 2:
                        p.wait("gpsimd", "wou", wu - 1)
                    ld = p.dma("gpsimd", wring[slot][:],
                               wo_bf[k * P:(k + 1) * P, ng * 512:(ng + 1) * 512],
                               sname=f"wol{slot}")
                    p.wait("tensor", f"wol{slot}", ld)
                    if k == 0:
                        p.wait("tensor", "woe")
                    for i in range(4):
                        p.on("tensor", lambda e, k=k, i=i, slot=slot: e.matmul(
                            pp[i][:], vtm[:, k * 512 + i * P: k * 512 + (i + 1) * P],
                            wring[slot][:], start=(k == 0), stop=(k == NC - 1)),
                            inc=("wou", 1) if i == 3 else None)
                p.wait("vector", "wou")
                for i in range(4):
                    p.on("vector", lambda e, ng=ng, i=i: e.tensor_copy(
                        db[i][:, ng * 512:(ng + 1) * 512], pp[i][:]),
                        inc=("woe", 1) if i == 3 else None)
            p.wait("tensor", "woe")
            p.wait("vector", "woe")
            for i, tt in enumerate(tts):
                p.wait("vector", "pv")
                p.on("vector", lambda e, i=i: e.reduce_sum(
                    o_sum[:, i:i + 1], db[i][:], axis=AX.X), inc=("pv", 1))
                p.wait("scalar", "woe")
                p.wait("scalar", "ps")
                p.on("scalar", lambda e, i=i: e.activation(
                    yb[0][:], db[i][:], AF.Square, accum_out=o_sq[:, i:i + 1]),
                    inc=("ps", 1))
            p.wait("vector", "ps")
            vo, so = emit_ln_stats_math(p, o_sum[:], o_sq[:], o_m[:], o_var[:],
                                        o_rstd[:], o_nmr[:], D, cst[:, 0:1])
            for i, tt in enumerate(tts):
                if i >= 2:
                    p.wait("sync", "pv", b3_vv[i - 2])
                ld = p.dma("sync", xb[i % 2][:], xs[tt * P:(tt + 1) * P, :],
                           sname=f"xld{i % 2}")
                if i == 0:
                    p.wait("scalar", "stv", vo)
                    p.wait("scalar", "sts", so)
                if i >= 1:
                    p.wait("scalar", "pv", b3_vv[i - 1])
                sc = p.on("scalar", lambda e, i=i: e.activation(
                    yb[i % 2][:], db[i][:], AF.Identity, bias=o_nmr[:, i:i + 1],
                    scale=o_rstd[:, i:i + 1]), inc=("ps", 1))
                b3_sc[i] = sc
                p.wait("vector", "ps", sc)
                p.wait("vector", f"xld{i % 2}", ld)
                vv = p.on("vector", lambda e, i=i: e.tensor_add(
                    xob[i % 2][:], xb[i % 2][:], yb[i % 2][:]), inc=("pv", 1))
                b3_vv[i] = vv
                p.wait("sync", "pv", vv)
                st = p.dma("sync", xo[tt * P:(tt + 1) * P, :], xob[i % 2][:],
                           sname=f"xst{i % 2}")
                p.wait("sync", f"xst{i % 2}", st)

    p.wait("sync", "xst0", p.cnt.get("xst0", 0))
    p.wait("sync", "xst1", p.cnt.get("xst1", 0))
    return p


def l0attn_host_inputs(w, T):
    D, NC, NH, NKV, DH = 2048, 16, 16, 4, 128
    THETA = 10000.0

    def slab(m, nm):
        m = np.asarray(m, np.float32)
        din, dout = m.shape
        nk = din // P
        return np.ascontiguousarray(
            m.reshape(nk, P, nm, P).transpose(2, 1, 0, 3).reshape(nm, P, nk * P))

    freqs = 1.0 / THETA ** (np.arange(0, DH, 2) / DH)
    emb = np.arange(T)[:, None] * freqs[None, :]
    cos = np.concatenate([np.cos(emb), np.cos(emb)], -1).astype(np.float32)
    sin = np.concatenate([np.sin(emb), np.sin(emb)], -1).astype(np.float32)
    prot_P = np.zeros((P, P), np.float32)
    for pp_ in range(64):
        prot_P[pp_, pp_ + 64] = -1.0
        prot_P[pp_ + 64, pp_] = 1.0
    ir = np.arange(128)[:, None]
    jb = np.arange(384)[None, :]
    masked = (np.abs(ir + 256 - jb) > 256) & (jb % 64 != 0)
    bias = np.where(masked, -1e30, 0.0).astype(np.float32)
    gt = np.zeros((T, 32), np.float32)
    for gg in range(T // 64):
        gt[64 * gg, gg] = 1.0
    return {
        "wq": slab(w["Wq"], NH),
        "wkv_down": slab(w["Wkv_down"], NKV),
        "wk_up": slab(w["Wk_up"], NKV),
        "wv_up": np.ascontiguousarray(np.asarray(w["Wv_up"], np.float32)),
        "wo": np.ascontiguousarray(np.asarray(w["Wo"], np.float32)),
        "cos_fm": np.ascontiguousarray(cos.T).astype(ml_dtypes.bfloat16),
        "sin_fm": np.ascontiguousarray(sin.T).astype(ml_dtypes.bfloat16),
        "prot": prot_P.T.astype(ml_dtypes.bfloat16),
        "band_bias": bias.astype(ml_dtypes.bfloat16),
        "gt": gt.astype(ml_dtypes.bfloat16),
        "ident": np.eye(P, dtype=ml_dtypes.bfloat16),
    }


def declare_l0attn_inputs(nc, T, pfx="i_"):
    D, NC, NH, NKV = 2048, 16, 16, 4
    f32 = {"wq": [NH, P, NC * P], "wkv_down": [NKV, P, NC * P], "wk_up": [NKV, P, NKV * P],
           "wv_up": [NKV * P, NKV * P], "wo": [D, D]}
    bf = {"cos_fm": [P, T], "sin_fm": [P, T], "prot": [P, P], "gt": [T, 32],
          "ident": [P, P], "band_bias": [P, 384]}
    win = {k: nc.dram_tensor(pfx + k, sh, F32, kind="ExternalInput").ap()
           for k, sh in f32.items()}
    win.update({k: nc.dram_tensor(pfx + k, sh, BF16, kind="ExternalInput").ap()
                for k, sh in bf.items()})
    return win



def emit_attn_qt(p, NT, NG, scale, pp, pt2, ident, nm4, probs, pbT, vh, ctxh,
                 mm_first, mm_last):
    sesum_prev = {}
    ptm_bank = {ng: p.cnt.get("ptm", 0) for ng in range(NG)}
    pv_tgt = p.cnt.get("pv", 0)
    pvm_last = {}
    CPB = 512 // P

    def emit_pv(ng, jb, copy_tgt):
        for bb in range(CPB):
            b = ng * CPB + bb
            if bb == 0:
                p.wait("tensor", "ptv", copy_tgt)
                if b == 0:
                    p.wait("tensor", "ctv", p.cnt.get("ctv", 0))
            p.on("tensor", lambda e, jb=jb, bb=bb, b=b: e.matmul(
                pp[5][:, 0:P], pbT[jb][:, bb * P:(bb + 1) * P],
                vh[:, b * P:(b + 1) * P],
                start=(b == 0), stop=(b == NT - 1)), inc=("pvm", 1))
        pvm_last[jb] = p.cnt["pvm"]

    for qt in range(NT):
        smt = {}
        for ng in range(NG):
            if sesum_prev.get(ng) is not None:
                p.wait("tensor", "ps", sesum_prev[ng])
            if mm_first is not None:
                mm_first(qt, ng)
            smt[ng] = mm_last(qt, ng)
        p.wait("scalar", "pv", pv_tgt)
        sesum = {}
        prev = None
        for ng in range(NG):
            jb = ng % 2
            p.wait("scalar", "sxm", smt[ng])
            p.wait("scalar", "ptm", ptm_bank[ng])
            p.wait("scalar", "ps")
            sesum[ng] = p.on("scalar", lambda e, ng=ng, scale=scale: e.activation(
                probs[:, ng * 512:(ng + 1) * 512], pp[ng][:], AF.Exp,
                scale=scale, bias=0.0, accum_out=nm4[:, ng:ng + 1]),
                inc=("ps", 1))
            p.wait("tensor", "ps", sesum[ng])
            if p.cnt.get("ptm", 0) >= 2:
                p.wait("tensor", "ptv", p.cnt["ptv"] - 1)
            for bb in range(CPB):
                b = ng * CPB + bb
                p.on("tensor", lambda e, b=b, bb=bb, jb=jb: e.matmul(
                    pt2[jb][:, bb * P:(bb + 1) * P], probs[:, b * P:(b + 1) * P],
                    ident[:], start=(bb == 0), stop=(bb == CPB - 1),
                    is_transpose=True),
                    inc=("ptm", 1) if bb == CPB - 1 else None)
            ptm_bank[ng] = p.cnt["ptm"]
            p.wait("vector", "ptm", ptm_bank[ng])
            if pvm_last.get(jb) is not None:
                p.wait("vector", "pvm", pvm_last[jb])
            p.on("vector", lambda e, jb=jb: e.tensor_copy(
                pbT[jb][:], pt2[jb][:]), inc=("ptv", 1))
            cp_tgt = p.cnt["ptv"]
            if prev is not None:
                emit_pv(*prev)
            prev = (ng, jb, cp_tgt)
        emit_pv(*prev)
        sesum_prev = dict(sesum)
        p.wait("vector", "ps", sesum[NG - 1])
        p.on("vector", lambda e, NG=NG: e.reduce_sum(
            nm4[:, 6:7], nm4[:, 0:NG], axis=AX.X), inc=("pv", 1))
        pv_tgt = p.cnt["pv"]
        p.wait("vector", "pv")
        p.on("vector", lambda e: e.reciprocal(nm4[:, 7:8], nm4[:, 6:7]), inc=("expd", 1))
        p.wait("vector", "pvm")
        p.wait("vector", "expd")
        p.on("vector", lambda e, qt=qt: e.tensor_scalar(
            ctxh[:, qt * P:(qt + 1) * P], pp[5][:, 0:P], nm4[:, 7:8], None,
            ALU.mult, ALU.bypass), inc=("ctv", 1))



def build_l1attn(p, T, x_in, x_out, win, nseq=2, dbg=None):
    nc = p.nc
    D = 2048
    NT = T // 128
    assert NT % 4 == 0
    NC = 16
    H = 16
    QR, KVR = 1536, 512
    NQ = QR // 128
    NKVC = KVR // 128
    EPS_LN, EPS_RMS = 1e-5, 1e-6
    SCLM = 1.0 / (192.0 ** 0.5)
    SCLP = 1.0 / (128.0 ** 0.5)

    wqa_bf = nc.dram_tensor("m_wqa", [NQ, P, NC * P], BF16, kind="Internal").ap()
    wkva_bf = nc.dram_tensor("m_wkva", [NQ, P, NC * P], BF16, kind="Internal").ap()
    qbn_bf = nc.dram_tensor("m_qbn", [H, P, NQ * P], BF16, kind="Internal").ap()
    qbp_bf = nc.dram_tensor("m_qbp", [H, P, NQ * 64], BF16, kind="Internal").ap()
    kvbk_bf = nc.dram_tensor("m_kvbk", [H, P, NKVC * P], BF16, kind="Internal").ap()
    kvbv_bf = nc.dram_tensor("m_kvbv", [H, P, NKVC * P], BF16, kind="Internal").ap()
    ow_bf = nc.dram_tensor("m_ow", [D, D], BF16, kind="Internal").ap()
    spq_bf = nc.dram_tensor("m_spq", [H, P, NC * P], BF16, kind="Internal").ap()
    spk_bf = nc.dram_tensor("m_spk", [H, P, NC * P], BF16, kind="Internal").ap()
    spv_bf = nc.dram_tensor("m_spv", [H, P, NC * P], BF16, kind="Internal").ap()
    spo_bf = nc.dram_tensor("m_spo", [D, D], BF16, kind="Internal").ap()
    gw_bf = nc.dram_tensor("m_gw", [D, 2], BF16, kind="Internal").ap()
    ln_dram = nc.dram_tensor("m_ln", [P, NC * T], BF16, kind="Internal").ap()
    qm_dram = nc.dram_tensor("m_qm", [NQ, P, T], BF16, kind="Internal").ap()
    ctx_dram = nc.dram_tensor("m_ctx", [T, D], BF16, kind="Internal").ap()
    mo_dram = nc.dram_tensor("m_mo", [T, D], BF16, kind="Internal").ap()

    ident = p.sbuf("m_ident", [P, P], BF16)
    cst = p.sbuf("m_cst", [P, 2], F32)
    ones1b = p.sbuf("m_ones1b", [1, P], BF16)
    onescol = p.sbuf("m_onescol", [P, 1], BF16)
    prot1 = p.sbuf("m_prot1", [64, 64], BF16)
    sup = p.sbuf("m_sup", [P, 64], BF16)
    sdn = p.sbuf("m_sdn", [P, 64], BF16)
    cos64 = p.sbuf("m_cos64", [P, T], BF16)
    sin64 = p.sbuf("m_sin64", [P, T], BF16)
    lnw_q = p.sbuf("m_lnw_q", [P, NQ], F32)
    lnw_kv = p.sbuf("m_lnw_kv", [P, NKVC], F32)
    gb = p.sbuf("m_gb", [P, 2], F32)

    big0 = p.sbuf("m_big0", [P, NC * T], BF16)
    kvaf = p.sbuf("m_kvaf", [P, NQ * T], BF16)
    cbuf = p.sbuf("m_cbuf", [P, T], BF16)
    rb_fm = p.sbuf("m_rb", [P, max(T, 2048)], BF16)
    r2b = p.sbuf("m_r2b", [1, 512], BF16)
    kn = p.sbuf("m_kn", [P, D], BF16)
    mob = [kn, kn]
    vh = p.sbuf("m_vh", [P, NT * P], BF16)
    qn = p.sbuf("m_qn", [P, T], BF16)
    qpb = p.sbuf("m_qpb", [64, T], BF16)
    ts1 = p.sbuf("m_ts1", [P, D], BF16)
    kpe_h = ts1[0:64, 0:T] if False else ts1
    probs = p.sbuf("m_probs", [P, T], BF16)
    kpraw = probs
    sqs = probs[:, 0:512]
    pbT = [p.sbuf(f"m_pbT{i}", [P, 512], BF16) for i in range(2)]
    rp2 = [p.sbuf(f"m_rp{i}", [64, 512], BF16) for i in range(2)]
    ctxh = p.sbuf("m_ctxh", [P, NT * P], BF16)
    nm4 = p.sbuf("m_nm4", [P, 8], F32)
    g0s = p.sbuf("m_g0s", [P, NT], F32)
    g1s = p.sbuf("m_g1s", [P, NT], F32)
    gds = p.sbuf("m_gds", [P, NT], F32)
    s_var = gds

    garing = [p.sbuf(f"m_garing{i}", [P, NC * P], BF16) for i in range(2)]
    wring = [p.sbuf(f"m_wring{i}", [P, 512], BF16) for i in range(2)]
    qmring = wring

    xb = [p.sbuf(f"m_xb{i}", [P, D], F32) for i in range(2)]
    xob = p.sbuf("m_xob", [P, D], F32)
    rstg = xob
    _yb = p.sbuf("m_yb", [P, D], BF16)
    yb = [_yb, _yb]
    if 3 * T + 2 * D <= NQ * T:
        db = [rb_fm, kvaf[:, 3 * T:3 * T + D], kvaf[:, 3 * T + D:3 * T + 2 * D],
              p.sbuf("m_db0", [P, D], BF16)]
    else:
        db = [rb_fm, kvaf[:, 3 * T:3 * T + D]] + \
            [p.sbuf(f"m_db{i}", [P, D], BF16) for i in range(2)]

    s_sum = p.sbuf("m_s_sum", [P, NT], F32)
    s_sq = p.sbuf("m_s_sq", [P, NT], F32)
    s_m = p.sbuf("m_s_m", [P, NT], F32)
    s_rstd = p.sbuf("m_s_rstd", [P, NT], F32)
    s_nmr = p.sbuf("m_s_nmr", [P, NT], F32)
    o_sum = s_sum
    o_sq = s_sq
    o_m = nm4
    o_var = p.sbuf("m_o_var", [P, 4], F32)
    o_rstd = p.sbuf("m_o_rstd", [P, 4], F32)
    o_nmr = p.sbuf("m_o_nmr", [P, 4], F32)

    pp = [p.psum(f"m_pp{i}", [P, 512], F32) for i in range(6)]
    pt2 = [p.psum(f"m_pt{i}", [P, 512], BF16) for i in range(2)]

    p.on("vector", lambda e: e.memset(cst[:, 0:1], EPS_LN), inc=("vprep", 1))
    p.on("vector", lambda e: e.memset(cst[:, 1:2], EPS_RMS), inc=("vprep", 1))
    p.on("vector", lambda e: e.memset(ones1b[:], 1.0), inc=("vprep", 1))
    p.on("vector", lambda e: e.memset(onescol[:], 1.0), inc=("vprep", 1))
    p.wait("scalar", "vprep")
    p.wait("tensor", "vprep")
    for dst, key in [(ident, "ident"), (prot1, "prot1"), (sup, "sup"), (sdn, "sdn"),
                     (cos64, "cos64"), (sin64, "sin64")]:
        p.dma("sync", dst[:], win[key], sname="cload")
    p.dma("sync", lnw_q[:], win["lnw_q"], sname="cload")
    p.dma("sync", lnw_kv[:], win["lnw_kv"], sname="cload")
    p.dma("sync", gb[:], win["gate_b"].partition_broadcast(P), sname="cload")
    for (dst, src) in [(wqa_bf, win["q_a"]), (wkva_bf, win["kv_a"]), (qbn_bf, win["qbn"]),
                       (qbp_bf, win["qbp"]), (kvbk_bf, win["kvbk"]), (kvbv_bf, win["kvbv"]),
                       (ow_bf, win["ow"]), (spq_bf, win["spq"]), (spk_bf, win["spk"]),
                       (spv_bf, win["spv"]), (spo_bf, win["spo"]), (gw_bf, win["gw"])]:
        p.dma("gpsimd", dst, src, sname="wcast")
    wcast_done = p.cnt["wcast"]
    cload_done = p.cnt["cload"]
    p.wait("tensor", "cload", cload_done)
    p.wait("tensor", "wcast", wcast_done)
    p.wait("gpsimd", "wcast", wcast_done)
    p.wait("vector", "cload", cload_done)

    def dbg_tap(slot, src_ap, waits, barrier=True):
        if dbg is None:
            return
        for sn, v in waits:
            p.wait("gpsimd", sn, v)
        dst = dbg[slot]
        if tuple(dst.shape) != tuple(src_ap.shape):
            dst = dst[0:src_ap.shape[0], 0:src_ap.shape[1]]
        p.dma("gpsimd", dst, src_ap, sname=f"dbgst{slot}")
        if barrier:
            for eng in ("vector", "scalar", "tensor", "sync"):
                p.wait(eng, f"dbgst{slot}")

    NG = T // 512
    b3_vv = {}
    b3_sc = {}
    b3_mo = {}
    ps_t = {}
    pv_t = {}

    def fm_rms(src_fn, nchunk, denom, lnw_col_fn, dst_fn, eps_col, tag):
        for ng in range(NG):
            for j in range(nchunk):
                p.wait("scalar", "kvav", p.cnt.get("kvav", 0))
                p.wait("scalar", "ps")
                pst = p.on("scalar", lambda e, j=j, ng=ng: e.activation(
                    sqs[:], src_fn(j)[:, ng * 512:(ng + 1) * 512], AF.Square),
                    inc=("ps", 1))
                p.wait("tensor", "ps", pst)
                if j == 0 and p.cnt.get(tag + "r", 0) >= 1:
                    p.wait("tensor", tag + "r", p.cnt[tag + "r"])
                p.on("tensor", lambda e, j=j, nchunk=nchunk: e.matmul(
                    pp[2][0:1, :], onescol[:], sqs[:],
                    start=(j == 0), stop=(j == nchunk - 1)), inc=("ssm", 1))
                p.wait("scalar", "ssm")
                p.wait("scalar", "ps")
            p.wait("scalar", tag + "r", p.cnt.get(tag + "r", 0))
            s1 = p.on("scalar", lambda e, denom=denom, eps_col=eps_col: e.activation(
                rstg[0:1, 0:512], pp[2][0:1, :], AF.Ln, scale=1.0 / denom,
                bias=eps_col[0:1, :]), inc=("sts", 1))
            p.wait("scalar", "sts", s1)
            p.wait("scalar", "ssb", p.cnt.get("ssb", 0))
            s2 = p.on("scalar", lambda e, ng=ng: e.activation(
                r2b[0:1, 0:512], rstg[0:1, 0:512], AF.Exp, scale=-0.5),
                inc=(tag + "r", 1))
            p.wait("tensor", tag + "r", p.cnt[tag + "r"])
            if p.cnt.get("ssv", 0) >= 1:
                p.wait("tensor", "ssv", p.cnt["ssv"])
            p.on("tensor", lambda e, ng=ng: e.matmul(
                pp[3][:], ones1b[:], r2b[0:1, 0:512],
                start=True, stop=True), inc=("ssb", 1))
            p.wait("vector", "ssb")
            p.on("vector", lambda e, ng=ng: e.tensor_copy(
                rb_fm[:, ng * 512:(ng + 1) * 512], pp[3][:]), inc=("ssv", 1))
        p.wait("vector", "ssv")
        p.wait("tensor", "ssv")
        for j in range(nchunk):
            for ng in range(NG):
                p.wait("vector", tag + "a")
                p.on("vector", lambda e, j=j, ng=ng: e.scalar_tensor_tensor(
                    dst_fn(j)[:, ng * 512:(ng + 1) * 512],
                    src_fn(j)[:, ng * 512:(ng + 1) * 512],
                    lnw_col_fn(j), rb_fm[:, ng * 512:(ng + 1) * 512],
                    ALU.mult, ALU.mult), inc=(tag + "a", 1))

    for seq in range(nseq):
        xs = x_in[seq]
        xo = x_out[seq]

        for tt in range(NT):
            if tt >= 2:
                p.wait("sync", "pv", pv_t[tt - 2])
                p.wait("sync", "ps", ps_t[tt - 2])
            ld = p.dma("sync", xb[tt % 2][:], xs[tt * P:(tt + 1) * P, :],
                       sname=f"xld{tt % 2}")
            p.wait("vector", f"xld{tt % 2}", ld)
            p.wait("vector", "pv")
            pv_t[tt] = p.on("vector", lambda e, tt=tt: e.reduce_sum(
                s_sum[:, tt:tt + 1], xb[tt % 2][:], axis=AX.X), inc=("pv", 1))
            p.wait("scalar", f"xld{tt % 2}", ld)
            p.wait("scalar", "ps")
            ps_t[tt] = p.on("scalar", lambda e, tt=tt: e.activation(
                yb[0][:], xb[tt % 2][:], AF.Square, accum_out=s_sq[:, tt:tt + 1]),
                inc=("ps", 1))
        p.wait("vector", "ps")
        vln, sln = emit_ln_stats_math(p, s_sum[:], s_sq[:], s_m[:], s_var[:],
                                      s_rstd[:], s_nmr[:], D, cst[:, 0:1])

        a0_sc = {}
        a0_tr = {}
        for tt in range(NT):
            i = tt % 2
            if tt >= 2:
                p.wait("sync", "ps", a0_sc[tt - 2])
            if tt == 0:
                p.wait("sync", "ps", ps_t[NT - 1])
                p.wait("sync", "pv", pv_t[NT - 1])
                p.wait("scalar", "q2a", p.cnt.get("q2a", 0))
                p.wait("scalar", "spm", p.cnt.get("spm", 0))
            ld = p.dma("sync", xb[i][:], xs[tt * P:(tt + 1) * P, :], sname=f"xld{i}")
            p.wait("scalar", f"xld{i}", ld)
            if tt == 0:
                p.wait("scalar", "stv", vln)
                p.wait("scalar", "sts", sln)
            if tt >= 1:
                p.wait("scalar", "trT", a0_tr[tt - 1])
            a0_sc[tt] = p.on("scalar", lambda e, i=i, tt=tt: e.activation(
                yb[0][:], xb[i][:], AF.Identity,
                bias=s_nmr[:, tt:tt + 1], scale=s_rstd[:, tt:tt + 1]), inc=("ps", 1))
            p.wait("tensor", "ps", a0_sc[tt])
            if tt == 0:
                p.wait("tensor", "q2a", p.cnt.get("q2a", 0))
                p.wait("tensor", "spm", p.cnt.get("spm", 0))
            emit_transpose_tiles(p, [(lambda c: yb[0][:, c * P:(c + 1) * P])],
                                 big0, pt2, ident, list(range(NC)), T, coloff=tt * P)
            a0_tr[tt] = p.cnt["trT"]

        if True:
            gwt = garing[0]
            p.wait("gpsimd", "qau", p.cnt.get("qau", 0))
            p.wait("gpsimd", "qmu", p.cnt.get("qmu", 0))
            p.wait("gpsimd", "a1m", p.cnt.get("a1m", 0))
            p.wait("gpsimd", "a3m", p.cnt.get("a3m", 0))
            ld = p.dma("gpsimd", gwt[:, 0:2 * NC].rearrange("p (k c) -> p k c", c=2),
                       gw_bf.rearrange("(k p) c -> p k c", p=P), sname="gtl")
            gtl_done = p.cnt["gtl"]
            p.wait("tensor", "gtl", gtl_done)
            for tt in range(NT):
                gi = p.cnt.get("gtv", 0)
                if gi >= 2:
                    p.wait("tensor", "gtv", gi - 1)
                for k in range(NC):
                    p.on("tensor", lambda e, tt=tt, k=k, gi=gi: e.matmul(
                        pp[4 + gi % 2][:, 0:2], big0[:, k * T + tt * P:k * T + (tt + 1) * P],
                        gwt[:, 2 * k:2 * k + 2], start=(k == 0), stop=(k == NC - 1)),
                        inc=("gtm2", 1) if k == NC - 1 else None)
                p.wait("vector", "gtm2")
                p.wait("vector", "gtv")
                p.on("vector", lambda e, tt=tt, gi=gi: e.tensor_tensor(
                    nm4[:, 4:6], pp[4 + gi % 2][:, 0:2], gb[:, 0:2], ALU.add),
                    inc=("pv", 1))
                p.wait("vector", "pv")
                p.on("vector", lambda e, tt=tt: e.tensor_sub(
                    gds[:, tt:tt + 1], nm4[:, 4:5], nm4[:, 5:6]), inc=("gtv", 1))
            gv = p.cnt["gtv"]
            p.wait("scalar", "gtv", gv)
            sg1 = p.on("scalar", lambda e: e.activation(g0s[:], gds[:], AF.Sigmoid),
                       inc=("sts", 1))
            p.wait("vector", "sts", sg1)
            p.on("vector", lambda e: e.tensor_scalar(g1s[:], g0s[:], -1.0, 1.0,
                                                     ALU.mult, ALU.add), inc=("gsv", 1))

        for m in range(NQ):
            slot = m % 2
            if m < 2:
                p.wait("gpsimd", "kvau", p.cnt.get("kvau", 0))
                p.wait("gpsimd", "qau", p.cnt.get("qau", 0))
                p.wait("gpsimd", "gtm2", p.cnt.get("gtm2", 0))
            else:
                p.wait("gpsimd", "kvau", p.cnt["kvau"] - NG)
            ld = p.dma("gpsimd", garing[slot][:], wkva_bf[m], sname=f"gal{slot}")
            p.wait("tensor", f"gal{slot}", ld)
            for ng in range(NG):
                gidx = p.cnt.get("kvav", 0)
                bank = gidx % 2
                if gidx >= 2:
                    p.wait("tensor", "kvav", gidx - 1)
                for k in range(NC):
                    p.on("tensor", lambda e, slot=slot, k=k, ng=ng, bank=bank: e.matmul(
                        pp[bank][:], garing[slot][:, k * P:(k + 1) * P],
                        big0[:, k * T + ng * 512:k * T + (ng + 1) * 512],
                        start=(k == 0), stop=(k == NC - 1)),
                        inc=("kvau", 1) if k == NC - 1 else None)
                p.wait("vector", "kvau")
                p.on("vector", lambda e, m=m, ng=ng, bank=bank: e.tensor_copy(
                    kvaf[:, m * T + ng * 512:m * T + (ng + 1) * 512], pp[bank][:]),
                    inc=("kvav", 1))
        p.wait("tensor", "kvav")

        cdst = [cbuf] + [None, None, None]

        def csrc(j):
            return kvaf[:, j * T:(j + 1) * T]

        def cget(j):
            if j == 0:
                return cbuf[:]
            return kvaf[:, (j - 1) * T:j * T]

        fm_rms(lambda j: kvaf[:, j * T:(j + 1) * T], NKVC, KVR,
               lambda j: lnw_kv[:, j:j + 1], lambda j: cget(j), cst[:, 1:2], "ckv")
        ckva = p.cnt["ckva"]
        p.wait("tensor", "ckva", ckva)

        for m in range(NQ):
            slot = m % 2
            if m < 2:
                p.wait("gpsimd", "kvau", p.cnt["kvau"])
                p.wait("gpsimd", "ssm", p.cnt.get("ssm", 0))
            else:
                p.wait("gpsimd", "qau", p.cnt["qau"] - NG)
            ld = p.dma("gpsimd", garing[slot][:], wqa_bf[m], sname=f"gal{slot}")
            p.wait("tensor", f"gal{slot}", ld)
            for ng in range(NG):
                gidx = p.cnt.get("qav", 0)
                bank = gidx % 2
                if gidx >= 2:
                    p.wait("tensor", "qav", gidx - 1)
                    p.wait("tensor", "ps")
                for k in range(NC):
                    p.on("tensor", lambda e, slot=slot, k=k, ng=ng, bank=bank: e.matmul(
                        pp[bank][:], garing[slot][:, k * P:(k + 1) * P],
                        big0[:, k * T + ng * 512:k * T + (ng + 1) * 512],
                        start=(k == 0), stop=(k == NC - 1)),
                        inc=("qau", 1) if k == NC - 1 else None)
                p.wait("vector", "qau")
                if p.cnt.get(f"qms{bank}", 0) >= 16:
                    p.wait("vector", f"qms{bank}", p.cnt[f"qms{bank}"])
                p.on("vector", lambda e, ng=ng, bank=bank: e.tensor_copy(
                    qmring[bank][:], pp[bank][:]), inc=("qav", 1))
                p.wait("gpsimd", "qav", p.cnt["qav"])
                p.dma("gpsimd", qm_dram[m, :, ng * 512:(ng + 1) * 512], qmring[bank][:],
                      sname=f"qms{bank}")
                p.wait("scalar", "qav", p.cnt["qav"])
                p.wait("scalar", "ps")
                pst = p.on("scalar", lambda e, bank=bank: e.activation(
                    sqs[:], pp[bank][:], AF.Square), inc=("ps", 1))
                p.wait("tensor", "ps", pst)
                p.on("tensor", lambda e, m=m, ng=ng: e.matmul(
                    pp[2 + ng][0:1, :], onescol[:], sqs[:],
                    start=(m == 0), stop=(m == NQ - 1)), inc=("ssm", 1))
                p.wait("scalar", "ssm")

        for ng in range(NG):
            p.wait("scalar", "ckvr", p.cnt.get("ckvr", 0))
            s1 = p.on("scalar", lambda e, ng=ng: e.activation(
                rstg[0:1, 0:512], pp[2 + ng][0:1, :], AF.Ln, scale=1.0 / QR,
                bias=cst[0:1, 1:2]), inc=("sts", 1))
            p.wait("scalar", "sts", s1)
            p.wait("scalar", "ssb", p.cnt.get("ssb", 0))
            s2 = p.on("scalar", lambda e, ng=ng: e.activation(
                r2b[0:1, 0:512], rstg[0:1, 0:512], AF.Exp, scale=-0.5),
                inc=("ckvr", 1))
            p.wait("tensor", "ckvr", p.cnt["ckvr"])
            p.wait("tensor", "ckva", ckva)
            p.wait("tensor", "qav", p.cnt["qav"])
            if p.cnt.get("ssv", 0) >= 1:
                p.wait("tensor", "ssv", p.cnt["ssv"])
            p.on("tensor", lambda e, ng=ng: e.matmul(
                pp[0][:], ones1b[:], r2b[0:1, 0:512],
                start=True, stop=True), inc=("ssb", 1))
            p.wait("vector", "ssb")
            p.wait("vector", "ckva", ckva)
            p.on("vector", lambda e, ng=ng: e.tensor_copy(
                rb_fm[:, ng * 512:(ng + 1) * 512], pp[0][:]), inc=("ssv", 1))
        p.wait("vector", "ssv")

        p.wait("gpsimd", "qau", p.cnt["qau"])
        p.wait("gpsimd", "gtm2", p.cnt.get("gtm2", 0))
        p.dma("gpsimd", ln_dram, big0[:], sname="lnsp")
        lnsp = p.cnt["lnsp"]
        p.wait("sync", "qms0", p.cnt.get("qms0", 0))
        p.wait("sync", "qms1", p.cnt.get("qms1", 0))
        p.wait("sync", "lnsp", lnsp)
        for m in range(NQ):
            for ng in range(NG):
                qi = p.cnt.get("q2a", 0)
                slot = qi % 2
                if qi >= 2:
                    p.wait("sync", "q2a", qi - 1)
                ld = p.dma("sync", qmring[slot][:], qm_dram[m, :, ng * 512:(ng + 1) * 512],
                           sname=f"q2l{slot}")
                p.wait("vector", f"q2l{slot}", ld)
                p.wait("vector", "ssv")
                p.on("vector", lambda e, m=m, ng=ng, slot=slot: e.scalar_tensor_tensor(
                    big0[:, m * T + ng * 512:m * T + (ng + 1) * 512], qmring[slot][:],
                    lnw_q[:, m:m + 1], rb_fm[:, ng * 512:(ng + 1) * 512],
                    ALU.mult, ALU.mult), inc=("q2a", 1))
        q2a_done = p.cnt["q2a"]

        for h in range(H):
            hs = 0
            if h == 0:
                p.wait("gpsimd", "kvav", p.cnt["kvav"])
                p.wait("gpsimd", "qau", p.cnt["qau"])
            else:
                p.wait("gpsimd", "qmu", p.cnt["qmu"])
                p.wait("gpsimd", "knm", p.cnt["knm"])
                p.wait("gpsimd", "vhm", p.cnt["vhm"])
                p.wait("gpsimd", "a2m", p.cnt["a2m"])
            ld1 = p.dma("gpsimd", garing[0][:, 0:NQ * P], qbn_bf[h], sname="hbl0")
            ld2 = p.dma("gpsimd", garing[1][:, 0:NQ * 64], qbp_bf[h], sname="hbl1")
            ld3 = p.dma("gpsimd", garing[1][:, NQ * 64:NQ * 64 + NKVC * P],
                        kvbk_bf[h], sname="hbl1")
            ld4 = p.dma("gpsimd", garing[1][:, NQ * 64 + NKVC * P:NQ * 64 + 2 * NKVC * P],
                        kvbv_bf[h], sname="hbl1")
            p.wait("tensor", "hbl0", p.cnt["hbl0"])
            p.wait("tensor", "hbl1", p.cnt["hbl1"])
            kboff, vboff = NQ * 64, NQ * 64 + NKVC * P
            if h == 0:
                p.wait("tensor", "q2a", q2a_done)

            for ng in range(NG):
                if p.cnt.get("knv", 0) >= 1:
                    p.wait("tensor", "knv", p.cnt["knv"])
                for k in range(NKVC):
                    p.on("tensor", lambda e, k=k, ng=ng, kboff=kboff: e.matmul(
                        pp[0][:], garing[1][:, kboff + k * P:kboff + (k + 1) * P],
                        cget(k)[:, ng * 512:(ng + 1) * 512],
                        start=(k == 0), stop=(k == NKVC - 1)),
                        inc=("knm", 1) if k == NKVC - 1 else None)
                p.wait("vector", "knm")
                p.wait("vector", "ps")
                p.on("vector", lambda e, ng=ng: e.tensor_copy(
                    kn[:, ng * 512:(ng + 1) * 512], pp[0][:]), inc=("knv", 1))
            for tt in range(NT):
                if p.cnt.get("vhv", 0) >= 1:
                    p.wait("tensor", "vhv", p.cnt["vhv"])
                for k in range(NKVC):
                    p.on("tensor", lambda e, k=k, tt=tt, vboff=vboff: e.matmul(
                        pp[1][:, 0:P], cget(k)[:, tt * P:(tt + 1) * P],
                        garing[1][:, vboff + k * P:vboff + (k + 1) * P],
                        start=(k == 0), stop=(k == NKVC - 1)),
                        inc=("vhm", 1) if k == NKVC - 1 else None)
                p.wait("vector", "vhm")
                p.on("vector", lambda e, tt=tt: e.tensor_copy(
                    vh[:, tt * P:(tt + 1) * P], pp[1][:, 0:P]), inc=("vhv", 1))
            kchunk = 4 + h // 2
            shmat = sup if h % 2 == 0 else sdn
            p.wait("tensor", "knv", p.cnt.get("knv", 0))
            p.wait("tensor", "vhv", p.cnt.get("vhv", 0))
            for ng in range(NG):
                if p.cnt.get("a2v", 0) >= 1:
                    p.wait("tensor", "a2v", p.cnt["a2v"])
                p.on("tensor", lambda e, kchunk=kchunk, ng=ng, shmat=shmat: e.matmul(
                    pp[0][0:64, :], shmat[:],
                    kvaf[:, kchunk * T + ng * 512:kchunk * T + (ng + 1) * 512],
                    start=True, stop=True), inc=("a2m", 1))
                p.wait("vector", "a2m")
                p.on("vector", lambda e, ng=ng: e.tensor_copy(
                    kpraw[0:64, ng * 512:(ng + 1) * 512], pp[0][0:64, :]),
                    inc=("a2v", 1))
                p.wait("tensor", "a2v")
                p.on("tensor", lambda e, ng=ng: e.matmul(
                    pp[1][0:64, :], prot1[:], kpraw[0:64, ng * 512:(ng + 1) * 512],
                    start=True, stop=True), inc=("a2m", 1))
                p.wait("vector", "a2m")
                p.wait("vector", "rpv")
                p.on("vector", lambda e, ng=ng: e.tensor_mul(
                    rp2[0][:], kpraw[0:64, ng * 512:(ng + 1) * 512],
                    cos64[0:64, ng * 512:(ng + 1) * 512]), inc=("rpv", 1))
                p.on("vector", lambda e, ng=ng: e.tensor_mul(
                    rp2[1][:], pp[1][0:64, :],
                    sin64[0:64, ng * 512:(ng + 1) * 512]), inc=("rpv", 1))
                p.wait("vector", "rpv")
                p.on("vector", lambda e, ng=ng: e.tensor_add(
                    kpe_h[0:64, ng * 512:(ng + 1) * 512], rp2[0][:],
                    rp2[1][:]), inc=("a2v", 1))
            p.wait("tensor", "a2v", p.cnt.get("a2v", 0))
            for ng in range(NG):
                if p.cnt.get("qmv", 0) >= 1:
                    p.wait("tensor", "qmv", p.cnt["qmv"])
                for k in range(NQ):
                    p.on("tensor", lambda e, k=k, ng=ng: e.matmul(
                        pp[0][:], garing[0][:, k * P:(k + 1) * P],
                        big0[:, k * T + ng * 512:k * T + (ng + 1) * 512],
                        start=(k == 0), stop=(k == NQ - 1)),
                        inc=("qmu", 1) if k == NQ - 1 else None)
                p.wait("vector", "qmu")
                p.on("vector", lambda e, ng=ng: e.tensor_copy(
                    qn[:, ng * 512:(ng + 1) * 512], pp[0][:]), inc=("qmv", 1))
                for k in range(NQ):
                    p.on("tensor", lambda e, k=k, ng=ng: e.matmul(
                        pp[1][0:64, :], garing[1][:, k * 64:(k + 1) * 64],
                        big0[:, k * T + ng * 512:k * T + (ng + 1) * 512],
                        start=(k == 0), stop=(k == NQ - 1)),
                        inc=("qmu", 1) if k == NQ - 1 else None)
                p.wait("vector", "qmu")
                p.on("vector", lambda e, ng=ng: e.tensor_copy(
                    kpraw[0:64, ng * 512:(ng + 1) * 512], pp[1][0:64, :]), inc=("qmv", 1))
                p.wait("tensor", "qmv")
                p.on("tensor", lambda e, ng=ng: e.matmul(
                    pp[1][0:64, :], prot1[:], kpraw[0:64, ng * 512:(ng + 1) * 512],
                    start=True, stop=True), inc=("qpm", 1))
                p.wait("vector", "qpm")
                p.wait("vector", "rpv")
                p.on("vector", lambda e, ng=ng: e.tensor_mul(
                    rp2[0][:], kpraw[0:64, ng * 512:(ng + 1) * 512],
                    cos64[0:64, ng * 512:(ng + 1) * 512]), inc=("rpv", 1))
                p.on("vector", lambda e, ng=ng: e.tensor_mul(
                    rp2[1][:], pp[1][0:64, :],
                    sin64[0:64, ng * 512:(ng + 1) * 512]), inc=("rpv", 1))
                p.wait("vector", "rpv")
                p.on("vector", lambda e, ng=ng: e.tensor_add(
                    qpb[0:64, ng * 512:(ng + 1) * 512], rp2[0][:],
                    rp2[1][:]), inc=("qmv", 1))
            p.wait("tensor", "qmv")
            p.wait("tensor", "a2v")

            emit_attn_qt(p, NT, NG, SCLM, pp, pt2, ident, nm4, probs, pbT, vh, ctxh,
                         lambda qt, ng: p.on("tensor",
                             lambda e, qt=qt, ng=ng: e.matmul(
                                 pp[ng][:], qn[:, qt * P:(qt + 1) * P],
                                 kn[:, ng * 512:(ng + 1) * 512], start=True, stop=False)),
                         lambda qt, ng: p.on("tensor",
                             lambda e, qt=qt, ng=ng: e.matmul(
                                 pp[ng][:], qpb[0:64, qt * P:(qt + 1) * P],
                                 kpe_h[0:64, ng * 512:(ng + 1) * 512],
                                 start=False, stop=True), inc=("sxm", 1)))
            p.wait("gpsimd", "qmu", p.cnt["qmu"])
            p.on("gpsimd", lambda e: e.engine_nop(), inc=("hwu", 1))
            p.wait("gpsimd", "ctv", p.cnt["ctv"])
            p.dma("gpsimd",
                  ctx_dram[:, h * P:(h + 1) * P].rearrange("(q p) c -> p q c", p=P),
                  ctxh[:].rearrange("p (q c) -> p q c", c=P), sname="cxs")
            p.wait("vector", "cxs", p.cnt["cxs"])

        cxf = big0
        for tgc in range(NT // 4):
            tts = [tgc * 4 + i for i in range(4)]
            if tgc == 0:
                p.wait("sync", "cxs", p.cnt["cxs"])
                p.wait("sync", "knm", p.cnt["knm"])
                p.wait("sync", "sxm", p.cnt["sxm"])
                p.wait("sync", "q2a", q2a_done)
            for i, tt in enumerate(tts):
                if b3_sc.get(("c", i)) is not None:
                    p.wait("sync", f"mov{i}", b3_sc[("c", i)])
                ld = p.dma("sync", db[i][:], ctx_dram[tt * P:(tt + 1) * P, :],
                           sname=f"cxl{i}")
                p.wait("tensor", f"cxl{i}", ld)
            emit_transpose_tiles(p, [(lambda c, i=i: db[i][:, c * P:(c + 1) * P])
                                     for i in range(4)], cxf, pt2, ident,
                                 list(range(NC)), 512)
            for ng in range(4):
                for k in range(NC):
                    wu = p.cnt.get("wou", 0)
                    slot = wu % 2
                    if wu >= 2:
                        p.wait("gpsimd", "wou", wu - 1)
                    ld = p.dma("gpsimd", wring[slot][:],
                               ow_bf[k * P:(k + 1) * P, ng * 512:(ng + 1) * 512],
                               sname=f"wol{slot}")
                    p.wait("tensor", f"wol{slot}", ld)
                    if k == 0:
                        p.wait("tensor", "woe")
                    for i in range(4):
                        p.on("tensor", lambda e, k=k, i=i, slot=slot: e.matmul(
                            pp[i][:], cxf[:, k * 512 + i * P:k * 512 + (i + 1) * P],
                            wring[slot][:], start=(k == 0), stop=(k == NC - 1)),
                            inc=("wou", 1) if i == 3 else None)
                p.wait("vector", "wou")
                for i in range(4):
                    p.on("vector", lambda e, ng=ng, i=i: e.tensor_copy(
                        db[i][:, ng * 512:(ng + 1) * 512], pp[i][:]),
                        inc=("woe", 1) if i == 3 else None)
            p.wait("tensor", "woe")
            p.wait("vector", "woe")
            for i, tt in enumerate(tts):
                p.wait("gpsimd", "woe", p.cnt["woe"])
                st = p.dma("gpsimd", mo_dram[tt * P:(tt + 1) * P, :], db[i][:],
                           sname=f"mov{i}")
                b3_sc[("c", i)] = st
            for i in range(4):
                p.wait("sync", f"mov{i}", p.cnt.get(f"mov{i}", 0))

        p.wait("sync", "qmu", p.cnt["qmu"])
        p.wait("sync", "wou", p.cnt.get("wou", 0))
        p.wait("sync", "lnsp", p.cnt["lnsp"])
        p.wait("sync", "gtm2", p.cnt.get("gtm2", 0))
        ld = p.dma("sync", big0[:], ln_dram, sname="lnrl")
        lnrl = p.cnt["lnrl"]
        for h in range(H):
            hs = h % 2
            if h >= 1:
                p.wait("gpsimd", "a1m", p.cnt["a1m"])
            ldq = p.dma("gpsimd", garing[hs][:], spq_bf[h], sname=f"pql{hs}")
            p.wait("tensor", f"pql{hs}", ldq)
            if h == 0:
                p.wait("tensor", "lnrl", lnrl)
                for _mi in range(4):
                    p.wait("tensor", f"mov{_mi}", p.cnt.get(f"mov{_mi}", 0))
            for ng in range(NG):
                if p.cnt.get("a3v", 0) >= 1:
                    p.wait("tensor", "a3v", p.cnt["a3v"])
                for k in range(NC):
                    p.on("tensor", lambda e, hs=hs, k=k, ng=ng: e.matmul(
                        pp[4][:], garing[hs][:, k * P:(k + 1) * P],
                        big0[:, k * T + ng * 512:k * T + (ng + 1) * 512],
                        start=(k == 0), stop=(k == NC - 1)),
                        inc=("a3m", 1) if k == NC - 1 else None)
                p.wait("vector", "a3m")
                p.on("vector", lambda e, ng=ng: e.tensor_copy(
                    qn[:, ng * 512:(ng + 1) * 512], pp[4][:]), inc=("a3v", 1))
            p.wait("gpsimd", "a3m", p.cnt["a3m"])
            ldk = p.dma("gpsimd", garing[hs][:], spk_bf[h], sname=f"pql{hs}")
            p.wait("tensor", f"pql{hs}", ldk)
            p.wait("tensor", "a3v", p.cnt.get("a3v", 0))
            for ng in range(NG):
                if p.cnt.get("a4v", 0) >= 1:
                    p.wait("tensor", "a4v", p.cnt["a4v"])
                for k in range(NC):
                    p.on("tensor", lambda e, hs=hs, k=k, ng=ng: e.matmul(
                        pp[4][:], garing[hs][:, k * P:(k + 1) * P],
                        big0[:, k * T + ng * 512:k * T + (ng + 1) * 512],
                        start=(k == 0), stop=(k == NC - 1)),
                        inc=("a4m", 1) if k == NC - 1 else None)
                p.wait("vector", "a4m")
                p.on("vector", lambda e, ng=ng: e.tensor_copy(
                    kn[:, ng * 512:(ng + 1) * 512], pp[4][:]), inc=("a4v", 1))
            p.wait("gpsimd", "a4m", p.cnt["a4m"])
            ldv = p.dma("gpsimd", garing[hs][:], spv_bf[h], sname=f"pql{hs}")
            p.wait("tensor", f"pql{hs}", ldv)
            for tt in range(NT):
                if p.cnt.get("a1v", 0) >= 1:
                    p.wait("tensor", "a1v", p.cnt["a1v"])
                for k in range(NC):
                    p.on("tensor", lambda e, hs=hs, k=k, tt=tt: e.matmul(
                        pp[5][:, 0:P], big0[:, k * T + tt * P:k * T + (tt + 1) * P],
                        garing[hs][:, k * P:(k + 1) * P],
                        start=(k == 0), stop=(k == NC - 1)),
                        inc=("a1m", 1) if k == NC - 1 else None)
                p.wait("vector", "a1m")
                p.on("vector", lambda e, tt=tt: e.tensor_copy(
                    vh[:, tt * P:(tt + 1) * P], pp[5][:, 0:P]), inc=("a1v", 1))
            p.wait("tensor", "a1v")
            p.wait("tensor", "a4v")
            p.wait("tensor", "a3v")
            emit_attn_qt(p, NT, NG, SCLP, pp, pt2, ident, nm4, probs, pbT, vh, ctxh,
                         None,
                         lambda qt, ng: p.on("tensor",
                             lambda e, qt=qt, ng=ng: e.matmul(
                                 pp[ng][:], qn[:, qt * P:(qt + 1) * P],
                                 kn[:, ng * 512:(ng + 1) * 512], start=True, stop=True),
                             inc=("sxm", 1)))
            p.wait("gpsimd", "ctv", p.cnt["ctv"])
            if seq == 0 and h == 0:
                pass
            p.wait("gpsimd", "cxl0", p.cnt.get("cxl0", 0))
            p.wait("gpsimd", "cxl1", p.cnt.get("cxl1", 0))
            p.wait("gpsimd", "cxl2", p.cnt.get("cxl2", 0))
            p.wait("gpsimd", "cxl3", p.cnt.get("cxl3", 0))
            p.dma("gpsimd",
                  ctx_dram[:, h * P:(h + 1) * P].rearrange("(q p) c -> p q c", p=P),
                  ctxh[:].rearrange("p (q c) -> p q c", c=P), sname="cxs")
            p.wait("vector", "cxs", p.cnt["cxs"])

        for tgc in range(NT // 4):
            tts = [tgc * 4 + i for i in range(4)]
            if tgc == 0:
                p.wait("sync", "cxs", p.cnt["cxs"])
                p.wait("sync", "a4m", p.cnt["a4m"])
                for _mi in range(4):
                    p.wait("sync", f"mov{_mi}", p.cnt.get(f"mov{_mi}", 0))
            for i, tt in enumerate(tts):
                if b3_sc.get(("e", i)) is not None:
                    p.wait("sync", "pv", b3_sc[("e", i)])
                ld = p.dma("sync", db[i][:], ctx_dram[tt * P:(tt + 1) * P, :],
                           sname=f"cxl{i}")
                p.wait("tensor", f"cxl{i}", ld)
            emit_transpose_tiles(p, [(lambda c, i=i: db[i][:, c * P:(c + 1) * P])
                                     for i in range(4)], cxf, pt2, ident,
                                 list(range(NC)), 512)
            for ng in range(4):
                for k in range(NC):
                    wu = p.cnt.get("wou", 0)
                    slot = wu % 2
                    if wu >= 2:
                        p.wait("gpsimd", "wou", wu - 1)
                    ld = p.dma("gpsimd", wring[slot][:],
                               spo_bf[k * P:(k + 1) * P, ng * 512:(ng + 1) * 512],
                               sname=f"wol{slot}")
                    p.wait("tensor", f"wol{slot}", ld)
                    if k == 0:
                        p.wait("tensor", "woe")
                    for i in range(4):
                        p.on("tensor", lambda e, k=k, i=i, slot=slot: e.matmul(
                            pp[i][:], cxf[:, k * 512 + i * P:k * 512 + (i + 1) * P],
                            wring[slot][:], start=(k == 0), stop=(k == NC - 1)),
                            inc=("wou", 1) if i == 3 else None)
                p.wait("vector", "wou")
                for i in range(4):
                    p.on("vector", lambda e, ng=ng, i=i: e.tensor_copy(
                        db[i][:, ng * 512:(ng + 1) * 512], pp[i][:]),
                        inc=("woe", 1) if i == 3 else None)
            p.wait("tensor", "woe")
            p.wait("vector", "woe")
            for i, tt in enumerate(tts):
                mslot = i % 2
                if b3_mo.get(mslot) is not None:
                    p.wait("sync", "ps", b3_mo[mslot])
                ldm = p.dma("sync", mob[mslot][:], mo_dram[tt * P:(tt + 1) * P, :],
                            sname=f"q2l{mslot}")
                p.wait("vector", f"q2l{mslot}", ldm)
                if i == 0 and tgc == 0:
                    p.wait("vector", "gsv", p.cnt["gsv"])
                p.wait("vector", "gtv", p.cnt.get("gtv", 0))
                p.on("vector", lambda e, tt=tt, mslot=mslot: e.tensor_scalar(
                    ts1[:], mob[mslot][:], g0s[:, tt:tt + 1], None, ALU.mult,
                    ALU.bypass), inc=("gtv", 1))
                p.wait("vector", "gtv")
                vv0 = p.on("vector", lambda e, i=i, tt=tt, mslot=mslot: e.scalar_tensor_tensor(
                    mob[mslot][:], db[i][:], g1s[:, tt:tt + 1], ts1[:],
                    ALU.mult, ALU.add), inc=("gtv", 1))
                b3_sc[("e", i)] = vv0
                p.wait("vector", "gtv")
                p.on("vector", lambda e, i=i, mslot=mslot: e.reduce_sum(
                    o_sum[:, i:i + 1], mob[mslot][:], axis=AX.X), inc=("pv", 1))
                p.wait("scalar", "gtv", vv0)
                p.wait("scalar", "ps")
                p.on("scalar", lambda e, i=i, mslot=mslot: e.activation(
                    yb[0][:], mob[mslot][:], AF.Square, accum_out=o_sq[:, i:i + 1]),
                    inc=("ps", 1))
                p.wait("vector", "ps")
                vo, so = emit_ln_stats_math(p, o_sum[:, i:i + 1], o_sq[:, i:i + 1],
                                            o_m[:, i:i + 1], o_var[:, i:i + 1],
                                            o_rstd[:, i:i + 1], o_nmr[:, i:i + 1],
                                            D, cst[:, 0:1])
                ld = p.dma("sync", xb[i % 2][:], xs[tt * P:(tt + 1) * P, :],
                           sname=f"xld{i % 2}")
                p.wait("scalar", "stv", vo)
                p.wait("scalar", "sts", so)
                if ("er", i - 1) in b3_vv:
                    p.wait("scalar", "pv", b3_vv[("er", i - 1)])
                sc = p.on("scalar", lambda e, i=i, mslot=mslot: e.activation(
                    yb[0][:], mob[mslot][:], AF.Identity, bias=o_nmr[:, i:i + 1],
                    scale=o_rstd[:, i:i + 1]), inc=("ps", 1))
                b3_mo[mslot] = sc
                p.wait("vector", "ps", sc)
                p.wait("vector", f"xld{i % 2}", ld)
                vv = p.on("vector", lambda e, i=i: e.tensor_add(
                    xob[:], xb[i % 2][:], yb[0][:]), inc=("pv", 1))
                b3_vv[("er", i)] = vv
                p.wait("sync", "pv", vv)
                st = p.dma("sync", xo[tt * P:(tt + 1) * P, :], xob[:],
                           sname=f"xst{i % 2}")
                p.wait("sync", f"xst{i % 2}", st)

    p.wait("sync", "xst0", p.cnt.get("xst0", 0))
    p.wait("sync", "xst1", p.cnt.get("xst1", 0))
    return p


def l1attn_host_inputs(w, T):
    D, NC, H, NQ, NKVC = 2048, 16, 16, 12, 4
    THETA = 10000.0

    def slabB(m, nm):
        m = np.asarray(m, np.float32)
        dout, din = m.shape
        nk = din // P
        mt = np.ascontiguousarray(m.T)
        return np.ascontiguousarray(
            mt.reshape(nk, P, nm, dout // nm).transpose(2, 1, 0, 3)
            .reshape(nm, P, nk * (dout // nm)))

    qa = slabB(w["q_a_w"], NQ)
    kva = slabB(w["kv_a_w"], NQ)
    qb = np.asarray(w["q_b_w"], np.float32)
    qbt = qb.T
    qbn = np.zeros((H, P, NQ * P), np.float32)
    qbp = np.zeros((H, P, NQ * 64), np.float32)
    for h in range(H):
        blkq = qbt[:, h * 192:(h + 1) * 192]
        qbn[h] = blkq[:, 0:128].reshape(NQ, P, P).transpose(1, 0, 2).reshape(P, NQ * P)
        qbp[h] = blkq[:, 128:192].reshape(NQ, P, 64).transpose(1, 0, 2).reshape(P, NQ * 64)
    kvb = np.asarray(w["kv_b_w"], np.float32)
    kvbt = kvb.T
    kvbk = np.zeros((H, P, NKVC * P), np.float32)
    kvbv = np.zeros((H, P, NKVC * P), np.float32)
    for h in range(H):
        blkk = kvbt[:, h * 256:h * 256 + 128]
        blkv = kvbt[:, h * 256 + 128:(h + 1) * 256]
        kvbk[h] = blkk.reshape(NKVC, P, P).transpose(1, 0, 2).reshape(P, NKVC * P)
        kvbv[h] = blkv.reshape(NKVC, P, P).transpose(1, 0, 2).reshape(P, NKVC * P)

    def slabH(m):
        mt = np.ascontiguousarray(np.asarray(m, np.float32).T)
        return np.ascontiguousarray(
            mt.reshape(NC, P, H, P).transpose(2, 1, 0, 3).reshape(H, P, NC * P))

    inv = 1.0 / THETA ** (np.arange(0, 64, 2) / 64.0)
    t = np.arange(T)
    fr = np.outer(t, inv)
    emb = np.concatenate([fr, fr], -1)
    cos = np.cos(emb)
    sin = np.sin(emb)
    cos128 = np.tile(cos.T, (2, 1))
    sin128 = np.tile(sin.T, (2, 1))
    p1 = np.zeros((64, 64), np.float32)
    for j in range(32):
        p1[j, j + 32] = -1.0
        p1[j + 32, j] = 1.0
    supm = np.zeros((P, 64), np.float32)
    sdnm = np.zeros((P, 64), np.float32)
    for j in range(64):
        supm[j, j] = 1.0
        sdnm[64 + j, j] = 1.0
    return {
        "q_a": qa, "kv_a": kva, "qbn": qbn, "qbp": qbp, "kvbk": kvbk, "kvbv": kvbv,
        "ow": np.ascontiguousarray(np.asarray(w["o_w"], np.float32).T),
        "spq": slabH(w["sp_q_w"]), "spk": slabH(w["sp_k_w"]), "spv": slabH(w["sp_v_w"]),
        "spo": np.ascontiguousarray(np.asarray(w["sp_o_w"], np.float32).T),
        "gw": np.ascontiguousarray(np.asarray(w["gate_w"], np.float32).T),
        "gate_b": np.asarray(w["gate_b"], np.float32).reshape(1, 2),
        "lnw_q": np.ascontiguousarray(
            np.asarray(w["q_a_ln_w"], np.float32).reshape(NQ, P).T),
        "lnw_kv": np.ascontiguousarray(
            np.asarray(w["kv_a_ln_w"], np.float32).reshape(NKVC, P).T),
        "cos64": cos128.astype(ml_dtypes.bfloat16),
        "sin64": sin128.astype(ml_dtypes.bfloat16),
        "prot1": p1.T.astype(ml_dtypes.bfloat16),
        "sup": supm.astype(ml_dtypes.bfloat16),
        "sdn": sdnm.astype(ml_dtypes.bfloat16),
        "ident": np.eye(P, dtype=ml_dtypes.bfloat16),
    }


def declare_l1attn_inputs(nc, T, pfx="i_"):
    D, NC, H, NQ, NKVC = 2048, 16, 16, 12, 4
    f32 = {"q_a": [NQ, P, NC * P], "kv_a": [NQ, P, NC * P], "qbn": [H, P, NQ * P],
           "qbp": [H, P, NQ * 64], "kvbk": [H, P, NKVC * P], "kvbv": [H, P, NKVC * P],
           "ow": [D, D], "spq": [H, P, NC * P], "spk": [H, P, NC * P],
           "spv": [H, P, NC * P], "spo": [D, D], "gw": [D, 2], "gate_b": [1, 2],
           "lnw_q": [P, NQ], "lnw_kv": [P, NKVC]}
    bf = {"cos64": [P, T], "sin64": [P, T], "prot1": [64, 64], "sup": [P, 64],
          "sdn": [P, 64], "ident": [P, P]}
    win = {k: nc.dram_tensor(pfx + k, sh, F32, kind="ExternalInput").ap()
           for k, sh in f32.items()}
    win.update({k: nc.dram_tensor(pfx + k, sh, BF16, kind="ExternalInput").ap()
                for k, sh in bf.items()})
    return win

def _split_weights(inputs):
    w_l0a = {k: inputs["L0_attn_" + k] for k in
             ["Wq", "Wkv_down", "Wk_up", "Wv_up", "Wo"]}
    w_l0f = {k: inputs["L0_ffn_" + k] for k in
             ["SU_ug", "SU_dp", "Wscale_ug", "Wscale_dp", "W_up", "W_gate", "W_down"]}
    w_l1a = {k: inputs["L1_attn_" + k] for k in
             ["q_a_w", "q_a_ln_w", "q_b_w", "kv_a_w", "kv_a_ln_w", "kv_b_w", "o_w",
              "sp_q_w", "sp_k_w", "sp_v_w", "sp_o_w", "gate_w", "gate_b"]}
    w_l1f = {k: inputs["L1_ffn_" + k] for k in
             ["w_router", "b_router", "norm1_w", "wv", "bv", "wo", "norm2_w",
              "w_gate", "w_up", "w_down"]}
    return w_l0a, w_l0f, w_l1a, w_l1f


def build_fused(T, nseq):
    nc = make_nc()
    x_in = nc.dram_tensor("x_in", [nseq, T, 2048], F32, kind="ExternalInput").ap()
    x_out = nc.dram_tensor("x_out", [nseq, T, 2048], F32, kind="ExternalOutput").ap()
    h1 = nc.dram_tensor("hand1", [nseq, T, 2048], F32, kind="Internal").ap()
    h2 = nc.dram_tensor("hand2", [nseq, T, 2048], F32, kind="Internal").ap()
    h3 = nc.dram_tensor("hand3", [nseq, T, 2048], F32, kind="Internal").ap()
    win0a = declare_l0attn_inputs(nc, T, pfx="i0a_")
    win0f = declare_l0ffn_inputs(nc, pfx="i0f_")
    win1a = declare_l1attn_inputs(nc, T, pfx="i1a_")
    win1f = declare_l1ffn_inputs(nc, pfx="i1f_")
    p = Prog(nc)
    p.push_scope()
    build_l0attn(p, T, x_in, h1, win0a, nseq=nseq)
    p.pop_scope()
    p.barrier()
    p.push_scope()
    build_l0ffn(p, T, h1, h2, win0f, nseq=nseq)
    p.pop_scope()
    p.barrier()
    p.push_scope()
    build_l1attn(p, T, h2, h3, win1a, nseq=nseq)
    p.pop_scope()
    p.barrier()
    p.push_scope()
    build_l1ffn(p, T, h3, x_out, win1f, nseq=nseq)
    p.pop_scope()
    p.finalize()
    return nc, p


def kernel(**inputs):
    NCORES = 8
    T = 2048
    NSEQ = 2
    x = np.ascontiguousarray(np.asarray(inputs["L0_attn_x"], np.float32))
    assert x.shape[0] == NCORES * NSEQ
    w_l0a, w_l0f, w_l1a, w_l1f = _split_weights(inputs)
    nc, _ = build_fused(T, NSEQ)
    host = {}
    host.update({"i0a_" + k: v for k, v in l0attn_host_inputs(w_l0a, T).items()})
    host.update({"i0f_" + k: v for k, v in l0ffn_host_inputs(w_l0f).items()})
    host.update({"i1a_" + k: v for k, v in l1attn_host_inputs(w_l1a, T).items()})
    host.update({"i1f_" + k: v for k, v in l1ffn_host_inputs(w_l1f).items()})
    in_maps = []
    for c in range(NCORES):
        m = {"x_in": np.ascontiguousarray(x[c * NSEQ:(c + 1) * NSEQ])}
        m.update(host)
        in_maps.append(m)
    res = run_bass_kernel_spmd(nc, in_maps, core_ids=list(range(NCORES)))
    out = [np.asarray(res.results[c]["x_out"]) for c in range(NCORES)]
    return np.concatenate(out, axis=0).astype(np.float32)
